# version 15
# baseline (speedup 1.0000x reference)
import numpy as np
import ml_dtypes
from contextlib import ExitStack

import concourse.bass as bass
import concourse.mybir as mybir
from concourse.bass_utils import run_bass_kernel_spmd

F32 = mybir.dt.float32
BF16 = mybir.dt.bfloat16
AF = mybir.ActivationFunctionType
ALU = mybir.AluOpType
AX = mybir.AxisListType
BF = ml_dtypes.bfloat16

NCORES = 8
T = 2048
D = 1024
NH = 16
HD = 64
SEQ = 8192
NE0 = 64
CAP0 = 512
NE1 = 16
H1 = 8
D1 = 128
S1 = 256


class Prog:

    ENGS = ("t", "v", "s", "g", "y")
    SEMS = ("t", "v", "s", "g", "y", "gd", "yd")

    def __init__(self, nc):
        self.nc = nc
        self.streams = {e: [] for e in self.ENGS}
        self.cnt = {e: 0 for e in self.SEMS}

    CHAINED = ("v", "s", "g")

    def op(self, eng, fn, wait=(), inc=False, dma=False):
        ev = None
        wait = tuple(w for w in wait if w is not None)
        if not dma and eng in self.CHAINED:
            if self.cnt[eng] > 0:
                wait = wait + ((eng, self.cnt[eng]),)
            inc = True
        if dma:
            self.cnt[eng + "d"] += 16
            self.streams[eng].append([fn, wait, (eng + "d", self.cnt[eng + "d"]), True])
            if inc:
                return self.dma_ev(eng)
            return None
        if inc:
            self.cnt[eng] += 1
            ev = (eng, self.cnt[eng])
        self.streams[eng].append([fn, wait, ev, False])
        return ev

    def mark(self, eng):
        for entry in reversed(self.streams[eng]):
            fn, wait, ev, dma = entry
            if fn is None or fn == "MILESTONE":
                continue
            if ev is not None:
                return ev
            self.cnt[eng] += 1
            entry[2] = (eng, self.cnt[eng])
            return entry[2]
        return None

    def dma_ev(self, eng):
        semd = eng + "d"
        self.streams[eng].append([None, ((semd, self.cnt[semd]),), None, False])
        self.cnt[eng] += 1
        ev = (eng, self.cnt[eng])
        self.streams[eng].append(["MILESTONE", (), ev, False])
        return ev

    def t(self, fn, wait=(), inc=False):
        return self.op("t", fn, wait, inc)

    def v(self, fn, wait=(), inc=False):
        return self.op("v", fn, wait, inc)

    def s(self, fn, wait=(), inc=False):
        return self.op("s", fn, wait, inc)

    def g(self, fn, wait=(), inc=False, dma=False):
        return self.op("g", fn, wait, inc, dma)

    def y(self, fn, wait=(), inc=False, dma=True):
        return self.op("y", fn, wait, inc, dma)

    def barrier(self):
        evs = []
        for e in self.ENGS:
            if self.cnt.get(e + "d", 0) > 0:
                evs.append(self.dma_ev(e))
            self.mark(e)
            if self.cnt[e] > 0:
                evs.append((e, self.cnt[e]))
        for e in self.ENGS:
            self.streams[e].append([None, tuple(evs), None, False])
        return evs

    def emit(self):
        nc = self.nc
        ctx = ExitStack()
        sems = {e: ctx.enter_context(nc.semaphore(f"sem_{e}")) for e in self.SEMS}
        sdummy = ctx.enter_context(nc.sbuf_tensor("prog_sdummy", [1, 8], F32))
        with ctx:
            blk = ctx.enter_context(nc.Block())

            def run_stream(eng_name):
                def body(h):
                    waited = {}
                    for fn, wait, ev, dma in self.streams[eng_name]:
                        for (sname, val) in wait:
                            if waited.get(sname, -1) >= val:
                                continue
                            h.wait_ge(sems[sname], val)
                            waited[sname] = val
                        if fn is None:
                            continue
                        if fn == "MILESTONE":
                            if eng_name in ("v", "g"):
                                ins = h.engine_nop()
                            elif eng_name == "s":
                                ins = h.activation(sdummy[:], sdummy[:], AF.Copy)
                            else:
                                ins = h.nop()
                            ins.then_inc(sems[eng_name], 1)
                            continue
                        ins = fn(h)
                        if ev is not None:
                            ins.then_inc(sems[eng_name + "d"] if dma else sems[eng_name], 16 if dma else 1)
                return body

            blk.sync(run_stream("y"))
            blk.gpsimd(run_stream("g"))
            blk.tensor(run_stream("t"))
            blk.vector(run_stream("v"))
            blk.scalar(run_stream("s"))
        return nc

    def final_wait(self):
        evs = []
        for e in ("y", "g"):
            if self.cnt.get(e + "d", 0) > 0:
                evs.append((e + "d", self.cnt[e + "d"]))
        self.streams["y"].append([None, tuple(evs), None, False])


_PHASE_N = [0]


def _nc():
    return bass.Bass("TRN2", target_bir_lowering=False, debug=False)


CONST_SHAPES = {
    "c_ones_1x128": ([1, 128], F32),
    "c_ones_col": ([128, 1], F32),
    "c_ident_f32": ([128, 128], F32),
    "c_ident_bf": ([128, 128], BF16),
    "c_ltriu": ([128, 128], F32),
    "c_rev64": ([128, 64], F32),
    "c_rev16": ([128, 16], F32),
    "c_ones_colbf": ([128, 1], BF16),
    "c_ones_rowbf": ([1, 128], BF16),
}


def _consts_np():
    return {
        "c_ones_1x128": np.ones((1, 128), dtype=np.float32),
        "c_ones_col": np.ones((128, 1), dtype=np.float32),
        "c_ident_f32": np.eye(128, dtype=np.float32),
        "c_ident_bf": np.eye(128, dtype=BF),
        "c_ltriu": np.triu(np.ones((128, 128), dtype=np.float32), 1),
        "c_rev64": np.broadcast_to(64.0 - np.arange(64, dtype=np.float32), (128, 64)).copy(),
        "c_rev16": np.broadcast_to(16.0 - np.arange(16, dtype=np.float32), (128, 16)).copy(),
        "c_ones_colbf": np.ones((128, 1), dtype=BF),
        "c_ones_rowbf": np.ones((1, 128), dtype=BF),
    }


def _use_consts(nc, p, S, names):
    tiles = {}
    ev = None
    for n in names:
        sh, dt = CONST_SHAPES[n]
        ap = nc.dram_tensor(n, sh, dt, kind="ExternalInput").ap()
        tl = S(n + "_sb", list(sh), dt)
        tiles[n] = tl
        ev = p.y(lambda y, tl=tl, ap=ap: y.dma_start(out=tl[:], in_=ap), inc=True)
    return tiles, ev


def _ld_fm(p, hbm_ap, sbuf_t, cast=False, wait=()):
    src = hbm_ap.rearrange("(c p) t -> p c t", p=128)
    if cast:
        return p.g(lambda h: h.dma_start(out=sbuf_t[:], in_=src), wait=wait, inc=True, dma=True)
    return p.y(lambda h: h.dma_start(out=sbuf_t[:], in_=src), wait=wait, inc=True)



def emit_pre0(nc, p, io, cs, ld_c):
    x_h = io["x_fm"]
    wq_h = io["wq_t"]
    wk_h = io["wk_t"]
    wv_h = io["wv_t"]
    gw_h = io["gate_w"]

    xn_o = io["xn0"]
    q_o = io["q0"]
    k_o = io["k0"]
    v_o = io["v0"]
    qn2_o = io["qn2"]
    kmax_o = io["kmax2"]
    wd_o = io["wd"]
    pos_o = io["pos"]
    cnt_o = io["cnt"]

    ctx = ExitStack()
    _pfx = "ph%d_" % _PHASE_N[0]
    _PHASE_N[0] += 1
    S = lambda n, sh, dt=F32: ctx.enter_context(nc.sbuf_tensor(_pfx + n, sh, dt))
    PS = lambda n, sh, dt=F32: ctx.enter_context(nc.psum_tensor(_pfx + n, sh, dt))
    with ctx:
        x_sb = S("x_sb", [128, 8, T])
        xnb = S("xnb", [128, 8, T], BF16)
        xnf_t = S("xnf_t", [128, 8, 512])
        wq_sb = S("wq_sb", [128, 8, D], BF16)
        wk_sb = S("wk_sb", [128, 8, D], BF16)
        gw_sb = S("gw_sb", [128, 8, NE0])
        sq_full = S("sq_full", [128, 8, 512])
        rows = S("rows", [1, 512 * 6])
        qstage = S("qstage", [128, 2, 512], BF16)
        kstage = S("kstage", [128, 2, 512], BF16)
        vstage = S("vstage", [128, 2, 1024], BF16)
        sqq = S("sqq", [128, 2, 512])
        qn_stage = S("qn_stage", [1, 2, 512])
        kn_row = S("kn_row", [1, 512])
        kmax_sb = S("kmax_sb", [1, NH])
        tmp11 = S("tmp11", [1, 1])
        probs = S("probs", [128, NE0])
        eq_t = S("eq_t", [128, NE0])
        tmq = S("tmq", [128, NE0])
        sel8 = S("sel8", [128, NE0])
        wd_t = S("wd_t", [128, NE0])
        negm = S("negm", [128, 1])
        mx = S("mx", [128, 1])
        am = S("am", [128, 1])
        wsum = S("wsum", [128, 1])
        carry = S("carry", [1, NE0])
        wd_sb = S("wd_sb", [128, 16, NE0])
        pos_sb = S("pos_sb", [128, 16, NE0])

        pA = PS("pA", [1, 512])
        pB = PS("pB", [1, 512])
        pC = PS("pC", [128, 512])
        pD = PS("pD", [128, 512])
        pE = PS("pE", [128, 512])
        pF = PS("pF", [128, 512])

        ones_col = cs["c_ones_col"]
        ones_row = cs["c_ones_1x128"]
        ltriu = cs["c_ltriu"]
        rev64 = cs["c_rev64"]

        ld_x = _ld_fm(p, x_h, x_sb)
        ld_wq = _ld_fm(p, wq_h, wq_sb, cast=True)
        ld_wk = _ld_fm(p, wk_h, wk_sb, cast=True)
        ld_gw = _ld_fm(p, gw_h, gw_sb)

        mean_r = rows[0:1, 0:512]
        msq_r = rows[0:1, 512:1024]
        var_r = rows[0:1, 1024:1536]
        std_r = rows[0:1, 1536:2048]
        rstd_r = rows[0:1, 2048:2560]
        tmp_r = rows[0:1, 2560:3072]

        ev_store = []
        p.v(lambda v: v.memset(carry[:], 0.0))
        p.v(lambda v: v.memset(kmax_sb[:], -1e30))

        for tc in range(4):
            ts = slice(512 * tc, 512 * (tc + 1))
            sq_evs = []
            for ci in range(8):
                sq_evs.append(p.s(lambda s, ci=ci, ts=ts: s.activation(
                    sq_full[:, ci, :], x_sb[:, ci, ts], AF.Square), wait=(ld_x,), inc=True))
            for ci in range(8):
                p.t(lambda t, ci=ci, ts=ts: t.matmul(
                    pA[0:1, :], ones_col[:], x_sb[:, ci, ts],
                    start=(ci == 0), stop=(ci == 7)), wait=(ld_x, ld_c))
            mm_ev = None
            for ci in range(8):
                mm_ev = p.t(lambda t, ci=ci: t.matmul(
                    pB[0:1, :], ones_col[:], sq_full[:, ci, :],
                    start=(ci == 0), stop=(ci == 7)), wait=(sq_evs[ci],), inc=(ci == 7))
            e1 = p.v(lambda v: v.tensor_scalar(mean_r, pA[0:1, :], 1.0 / D, None, ALU.mult), wait=(mm_ev,), inc=True)
            p.v(lambda v: v.tensor_scalar(msq_r, pB[0:1, :], 1.0 / D, None, ALU.mult))
            p.v(lambda v: v.tensor_tensor(tmp_r, mean_r, mean_r, ALU.mult))
            p.v(lambda v: v.tensor_tensor(var_r, msq_r, tmp_r, ALU.subtract))
            e3 = p.v(lambda v: v.tensor_scalar(var_r, var_r, 1e-5, None, ALU.add), inc=True)
            e4 = p.s(lambda s: s.activation(std_r, var_r, AF.Sqrt), wait=(e3,), inc=True)
            e5 = p.v(lambda v: v.reciprocal(rstd_r, std_r), wait=(e4,), inc=True)
            b1 = p.t(lambda t: t.matmul(pC[:], ones_row[:], mean_r, start=True, stop=True),
                     wait=(e1, e5), inc=True)
            b2 = p.t(lambda t: t.matmul(pD[:], ones_row[:], rstd_r, start=True, stop=True),
                     wait=(e5,), inc=True)
            ev_xn = None
            for ci in range(8):
                p.v(lambda v, ci=ci, ts=ts: v.tensor_tensor(xnf_t[:, ci, :], x_sb[:, ci, ts], pC[:], ALU.subtract), wait=(b1, b2))
                p.v(lambda v, ci=ci: v.tensor_tensor(xnf_t[:, ci, :], xnf_t[:, ci, :], pD[:], ALU.mult))
                ev_xn = p.v(lambda v, ci=ci, ts=ts: v.tensor_copy(xnb[:, ci, ts], xnf_t[:, ci, :]), inc=True)

            for st4 in range(4):
                tt = 4 * tc + st4
                for ci in range(8):
                    p.t(lambda t, ci=ci, st4=st4: t.matmul(
                        pC[0:128, 0:64], xnf_t[:, ci, 128 * st4:128 * (st4 + 1)], gw_sb[:, ci, :],
                        start=(ci == 0), stop=(ci == 7)), wait=(ev_xn, ld_gw))
                mlg = p.mark("t")
                r1 = p.v(lambda v: v.reduce_max(mx[:], pC[0:128, 0:64], axis=AX.X), wait=(mlg,), inc=True)
                e2 = p.v(lambda v: v.tensor_scalar(negm[:], mx[:], -1.0, None, ALU.mult), wait=(r1,), inc=True)
                e3b = p.s(lambda s: s.activation(probs[:], pC[0:128, 0:64], AF.Exp, bias=negm[:]), wait=(e2,), inc=True)
                e4b = p.v(lambda v: v.reduce_sum(wsum[:], probs[:], axis=AX.X), wait=(e3b,), inc=True)
                e5b = p.v(lambda v: v.reciprocal(mx[:], wsum[:]), wait=(e4b,), inc=True)
                p.v(lambda v: v.tensor_scalar(probs[:], probs[:], mx[:], None, ALU.mult), wait=(e5b,))
                p.v(lambda v: v.memset(sel8[:], 0.0))
                p.v(lambda v: v.memset(wd_t[:], 0.0))
                ek = p.v(lambda v: v.memset(wsum[:], 0.0), inc=True)
                for kk in range(8):
                    rr1 = p.v(lambda v: v.reduce_max(mx[:], probs[:], axis=AX.X), wait=(ek,), inc=True)
                    p.v(lambda v: v.tensor_scalar(eq_t[:], probs[:], mx[:], None, ALU.is_ge), wait=(rr1,))
                    p.v(lambda v: v.tensor_tensor(tmq[:], eq_t[:], rev64[:], ALU.mult), wait=(ld_c,))
                    rr2 = p.v(lambda v: v.reduce_max(am[:], tmq[:], axis=AX.X), inc=True)
                    p.v(lambda v: v.tensor_scalar(eq_t[:], tmq[:], am[:], None, ALU.is_ge), wait=(rr2,))
                    p.v(lambda v: v.tensor_tensor(sel8[:], sel8[:], eq_t[:], ALU.add))
                    p.v(lambda v: v.scalar_tensor_tensor(wd_t[:], eq_t[:], mx[:], wd_t[:], ALU.mult, ALU.add))
                    p.v(lambda v: v.tensor_tensor(wsum[:], wsum[:], mx[:], ALU.add))
                    ek = p.v(lambda v: v.scalar_tensor_tensor(probs[:], eq_t[:], -2.0, probs[:], ALU.mult, ALU.add), inc=True)
                en = p.v(lambda v: v.reciprocal(mx[:], wsum[:]), wait=(ek,), inc=True)
                p.v(lambda v, tt=tt: v.tensor_scalar(wd_sb[:, tt, :], wd_t[:], mx[:], None, ALU.mult), wait=(en,))
                esel = p.v(lambda v: v.engine_nop(), inc=True)
                m1 = p.t(lambda t: t.matmul(pD[0:128, 0:64], ltriu[:], sel8[:], start=True, stop=False),
                         wait=(esel,))
                m2 = p.t(lambda t: t.matmul(pD[0:128, 0:64], ones_row[:], carry[:], start=False, stop=True), inc=True)
                m3 = p.t(lambda t: t.matmul(pB[0:1, 0:64], ones_col[:], sel8[:], start=True, stop=True),
                         wait=(esel,), inc=True)
                p.v(lambda v, tt=tt: v.tensor_copy(pos_sb[:, tt, :], pD[0:128, 0:64]), wait=(m2,))
                p.v(lambda v: v.tensor_tensor(carry[:], carry[:], pB[0:1, 0:64], ALU.add), wait=(m3,))
                p.barrier()
            p.barrier()

        bar = p.barrier()
        ev_store.append(p.y(lambda y: y.dma_start(out=xn_o, in_=xnb[:].rearrange("p c t -> p (c t)")), inc=True))
        ev_store.append(p.y(lambda y: y.dma_start(out=wd_o, in_=wd_sb[:].rearrange("p c t -> p (c t)")), inc=True))
        ev_store.append(p.y(lambda y: y.dma_start(out=pos_o, in_=pos_sb[:].rearrange("p c t -> p (c t)")), inc=True))
        ev_store.append(p.y(lambda y: y.dma_start(out=cnt_o, in_=carry[:]), inc=True))

        prev_ev = {0: None, 1: None}
        prev_store = {}
        prev_qn = {0: None, 1: None}
        prev_sqq = {0: None, 1: None}
        prev_qnst = {0: None, 1: None}
        for which, (w_sb, stage, out_hbm, ld_w, scale) in enumerate(
                [(wq_sb, qstage, q_o, ld_wq, 0.125), (wk_sb, kstage, k_o, ld_wk, 1.0)]):
            for co in range(8):
                for tc in range(4):
                    ts = slice(512 * tc, 512 * (tc + 1))
                    buf = (co * 4 + tc) % 2
                    pX = pE if buf == 0 else pF
                    for ci in range(8):
                        p.t(lambda t, ci=ci, co=co, ts=ts, w_sb=w_sb, pX=pX: t.matmul(
                            pX[:], w_sb[:, ci, 128 * co:128 * (co + 1)], xnb[:, ci, ts],
                            start=(ci == 0), stop=(ci == 7)),
                            wait=(ld_w,) + ((prev_ev[buf],) if prev_ev[buf] else ()))
                    mm = p.mark("t")
                    ev_ev = p.s(lambda s, buf=buf, scale=scale, stage=stage, pX=pX: s.activation(
                        stage[:, buf, :], pX[:], AF.Copy, scale=scale),
                        wait=(mm,) + ((prev_store.get((which, buf)),) if prev_store.get((which, buf)) else ()), inc=True)
                    prev_ev[buf] = ev_ev
                    prev_store[(which, buf)] = p.y(
                        lambda y, buf=buf, co=co, ts=ts, stage=stage, out_hbm=out_hbm:
                        y.dma_start(out=out_hbm.rearrange("p (c t) -> p c t", c=8)[:, co, ts],
                                    in_=stage[:, buf, :]), wait=(ev_ev,), inc=True)
                    sq_ev = p.s(lambda s, buf=buf, stage=stage: s.activation(
                        sqq[:, buf, :], stage[:, buf, :], AF.Square),
                        wait=(ev_ev,) + ((prev_sqq[buf],) if prev_sqq[buf] else ()), inc=True)
                    for hh in range(2):
                        h = 2 * co + hh
                        pN = pA if hh == 0 else pB
                        mqn = p.t(lambda t, hh=hh, buf=buf, pN=pN: t.matmul(
                            pN[0:1, :], ones_col[64 * hh:64 * (hh + 1), :], sqq[64 * hh:64 * (hh + 1), buf, :],
                            start=True, stop=True), wait=(sq_ev,) + ((prev_qn[hh],) if prev_qn[hh] else ()), inc=True)
                        prev_sqq[buf] = mqn
                        if which == 0:
                            cpe = p.v(lambda v, buf=buf, pN=pN: v.tensor_copy(qn_stage[0:1, buf, :], pN[0:1, :]),
                                      wait=(mqn,) + ((prev_qnst[buf],) if prev_qnst[buf] else ()), inc=True)
                            prev_qn[hh] = cpe
                            prev_qnst[buf] = p.y(lambda y, h=h, ts=ts, buf=buf: y.dma_start(
                                out=qn2_o[h:h + 1, ts], in_=qn_stage[0:1, buf, :]), wait=(cpe,), inc=True)
                        else:
                            cpe = p.v(lambda v, pN=pN: v.tensor_copy(kn_row[:], pN[0:1, :]), wait=(mqn,), inc=True)
                            rm = p.v(lambda v: v.reduce_max(tmp11[:], kn_row[:], axis=AX.X), wait=(cpe,), inc=True)
                            prev_qn[hh] = p.v(lambda v, h=h: v.tensor_tensor(
                                kmax_sb[0:1, h:h + 1], kmax_sb[0:1, h:h + 1], tmp11[:], ALU.max), wait=(rm,), inc=True)
            if which == 0:
                pmk = p.mark("t")
                ld_wv = _ld_fm(p, wv_h, wq_sb, cast=True, wait=(pmk,))
        ev_store.append(p.y(lambda y: y.dma_start(out=kmax_o, in_=kmax_sb[:]),
                            wait=(prev_qn[0], prev_qn[1]), inc=True))

        prev_pf = p.mark("s")
        prev_vst = {0: None, 1: None}
        for tt in range(16):
            buf = tt % 2
            ev_half = []
            for half in range(2):
                for ci in range(8):
                    p.t(lambda t, tt=tt, half=half, ci=ci: t.matmul(
                        pE[:], xnb[:, ci, 128 * tt:128 * (tt + 1)], wq_sb[:, ci, 512 * half:512 * (half + 1)],
                        start=(ci == 0), stop=(ci == 7)),
                        wait=(ld_wv,) + ((prev_pf,) if prev_pf else ())),
                mm = p.mark("t")
                ev_half.append(p.v(lambda v, buf=buf, half=half: v.tensor_copy(
                    vstage[:, buf, 512 * half:512 * (half + 1)], pE[:]),
                    wait=(mm,) + ((prev_vst[buf],) if prev_vst[buf] else ()), inc=True))
                prev_pf = ev_half[-1]
            ev_st = p.y(lambda y, tt=tt, buf=buf: y.dma_start(
                out=v_o.rearrange("p (c t) -> p c t", c=16)[:, tt, :], in_=vstage[:, buf, :]),
                wait=tuple(ev_half), inc=True)
            prev_vst[buf] = ev_st
        ev_store.append(ev_st)




def _host_split_fm(x):
    xf = np.ascontiguousarray(x.reshape(2 * SEQ, D))
    return [np.ascontiguousarray(xf[c * T:(c + 1) * T].T) for c in range(NCORES)]


def _rope_tables(core):
    pos = (core % 4) * T + np.arange(T, dtype=np.float64)
    inv = 1.0 / (10000.0 ** (np.arange(0, HD, 2, dtype=np.float64) / HD))
    ang = pos[None, :] * inv[(np.arange(HD) // 2)][:, None]
    cos = np.cos(ang).astype(np.float32)
    sin = np.sin(ang).astype(np.float32)
    sgn = np.where(np.arange(HD) % 2 == 0, -1.0, 1.0).astype(np.float32)[:, None]
    sin_s = sin * sgn
    return (np.ascontiguousarray(np.tile(cos, (2, 1))),
            np.ascontiguousarray(np.tile(sin_s, (2, 1))))


def _attn_inputs(r1s, core, qk, kk, vk, qnk=None, kmk=None):
    grp = range(0, 4) if core < 4 else range(4, 8)
    k_full = np.concatenate([r1s[c][kk].reshape(128, 8, T) for c in grp], axis=2)
    v_all = np.concatenate([r1s[c][vk].reshape(128, 16, 8, 128) for c in grp], axis=1)
    v_dev = np.ascontiguousarray(v_all.transpose(2, 0, 1, 3).reshape(8, 128, 64 * 128))
    m = {"kfull" if qnk else "k1full": np.ascontiguousarray(k_full.reshape(128, 8 * SEQ)),
         "vfull" if qnk else "v1full": v_dev}
    m["q0" if qnk else "q1"] = np.ascontiguousarray(r1s[core][qk])
    if qnk:
        m["qn2"] = np.ascontiguousarray(r1s[core][qnk])
        m["kmax2g"] = np.ascontiguousarray(
            np.concatenate([r1s[c][kmk][0] for c in grp]).reshape(1, 64))
    return m


def kernel(**inputs):
    consts = _consts_np()
    x = np.asarray(inputs["L0_attn_x"], dtype=np.float32)
    wqkv = np.asarray(inputs["L0_attn_W_qkv"], dtype=np.float32)
    gate_w = np.ascontiguousarray(np.asarray(inputs["L0_ffn_gate_w"], dtype=np.float32))
    sgw = np.asarray(inputs["L0_ffn_shared_gate_w"], dtype=np.float32)
    sguw = np.asarray(inputs["L0_ffn_shared_gate_up_w"], dtype=np.float32)
    sdw = np.ascontiguousarray(np.asarray(inputs["L0_ffn_shared_down_w"], dtype=np.float32))
    eguw = np.ascontiguousarray(np.asarray(inputs["L0_ffn_expert_gate_up_w"], dtype=np.float32))
    edww = np.ascontiguousarray(np.asarray(inputs["L0_ffn_expert_down_w"], dtype=np.float32))
    wq1 = np.asarray(inputs["L1_attn_Wq"], dtype=np.float32)
    wk1 = np.asarray(inputs["L1_attn_Wk"], dtype=np.float32)
    wv1 = np.asarray(inputs["L1_attn_Wv"], dtype=np.float32)
    wo1 = np.asarray(inputs["L1_attn_Wo"], dtype=np.float32)
    sqk = np.asarray(inputs["L1_attn_s_qk"], dtype=np.float32)
    wr1 = np.asarray(inputs["L1_ffn_w_router"], dtype=np.float32)
    wmh = np.asarray(inputs["L1_ffn_w_mh"], dtype=np.float32)
    wmg = np.asarray(inputs["L1_ffn_w_mg"], dtype=np.float32)
    kff = np.asarray(inputs["L1_ffn_k_ffwd"], dtype=np.float32)
    vff = np.ascontiguousarray(np.asarray(inputs["L1_ffn_v_ffwd"], dtype=np.float32))

    x_parts = _host_split_fm(x)
    wq_t = np.ascontiguousarray(wqkv[0:D].T)
    wk_t = np.ascontiguousarray(wqkv[D:2 * D].T)
    wv_t = np.ascontiguousarray(wqkv[2 * D:3 * D].T)
    sgu_pairs = np.zeros((D, 16, 256), dtype=np.float32)
    for i in range(16):
        sgu_pairs[:, i, 0:128] = sguw[:, 128 * i:128 * (i + 1)]
        sgu_pairs[:, i, 128:256] = sguw[:, 2048 + 128 * i:2048 + 128 * (i + 1)]
    sgu_pairs = np.ascontiguousarray(sgu_pairs.reshape(D, 16 * 256))
    sgate_r = np.ascontiguousarray(sgw.reshape(8, 128).T)
    wq1_t = np.ascontiguousarray(wq1.T)
    wk1_t = np.ascontiguousarray(wk1.T)
    wv1_t = np.ascontiguousarray(wv1.T)
    wo1_t = np.ascontiguousarray(wo1.T)
    sqk_t = np.ascontiguousarray(sqk.T)
    wr1_t = np.ascontiguousarray(wr1.T)
    wmh_t = np.ascontiguousarray(wmh.T)
    wmg_t = np.ascontiguousarray(wmg.T)
    kff_r = np.ascontiguousarray(kff.transpose(0, 1, 3, 2))

    cores = list(range(NCORES))

    nc1 = build_pre0()
    in1 = []
    for c in cores:
        m = {"x_fm": x_parts[c], "wq_t": wq_t, "wk_t": wk_t, "wv_t": wv_t, "gate_w": gate_w}
        for kname in ["c_ones_1x128", "c_ones_col", "c_ltriu", "c_rev64"]:
            m[kname] = consts[kname]
        in1.append(m)
    r1s = run_bass_kernel_spmd(nc1, in1, core_ids=cores).results

    nc2 = build_attn0()
    in2 = []
    for c in cores:
        m = _attn_inputs(r1s, c, "q0", "k0", "v0", "qn2", "kmax2")
        for kname in ["c_ones_1x128", "c_ones_colbf"]:
            m[kname] = consts[kname]
        in2.append(m)
    r2s = run_bass_kernel_spmd(nc2, in2, core_ids=cores).results

    cnt_all = np.ascontiguousarray(np.stack([r1s[c]["cnt"][0] for c in cores]))
    nc3 = build_moe0()
    in3 = []
    for c in cores:
        m = {"xn0": r1s[c]["xn0"], "x_fm": x_parts[c], "s0": r2s[c]["s0"],
             "wd": r1s[c]["wd"], "pos": r1s[c]["pos"], "cnt_all": cnt_all,
             "core_mask": np.ascontiguousarray((np.arange(8) < c).astype(np.float32).reshape(8, 1)),
             "sgu_pairs": sgu_pairs, "sdown": sdw, "sgate": sgate_r,
             "egu": eguw, "edw": edww}
        for kname in ["c_ones_1x128", "c_ident_f32"]:
            m[kname] = consts[kname]
        in3.append(m)
    r3s = run_bass_kernel_spmd(nc3, in3, core_ids=cores).results

    nc4 = build_pre1()
    in4 = []
    for c in cores:
        cos_t, sin_t = _rope_tables(c)
        m = {"x1": r3s[c]["x1"], "wq1_t": wq1_t, "wk1_t": wk1_t, "wv1_t": wv1_t,
             "wr1_t": wr1_t, "sqk_t": sqk_t, "rope_cos": cos_t, "rope_sin": sin_t}
        for kname in ["c_ones_1x128", "c_ones_col", "c_rev16", "c_ident_f32"]:
            m[kname] = consts[kname]
        in4.append(m)
    r4s = run_bass_kernel_spmd(nc4, in4, core_ids=cores).results

    nc5 = build_attn1()
    in5 = []
    for c in cores:
        m = _attn_inputs(r4s, c, "q1", "k1", "v1")
        m["wo_t"] = wo1_t
        for kname in ["c_ones_1x128", "c_ones_colbf", "c_ident_f32"]:
            m[kname] = consts[kname]
        in5.append(m)
    r5s = run_bass_kernel_spmd(nc5, in5, core_ids=cores).results

    nc6 = build_ffn1()
    in6 = []
    for c in cores:
        m = {"x1": r3s[c]["x1"], "x1n": r4s[c]["x1n"], "s1": r5s[c]["s1"],
             "wmh_t": wmh_t, "wmg_t": wmg_t, "kffwd_r": kff_r, "vffwd": vff,
             "g1d": r4s[c]["g1d"]}
        for kname in ["c_ones_1x128"]:
            m[kname] = consts[kname]
        in6.append(m)
    r6s = run_bass_kernel_spmd(nc6, in6, core_ids=cores).results

    out = np.empty((2 * SEQ, D), dtype=np.float32)
    for c in cores:
        out[c * T:(c + 1) * T] = r6s[c]["outp"].reshape(128, 8, T).transpose(2, 1, 0).reshape(T, D)
    return out.reshape(2, SEQ, D)



def emit_attn0(nc, p, io, cs, ld_c):
    q_h = io["q0"]
    k_h = io["kfull"]
    v_h = io["vfull"]
    qn2_h = io["qn2"]
    km_h = io["kmax2g"]

    s0_o = io["s0"]

    ctx = ExitStack()
    _pfx = "ph%d_" % _PHASE_N[0]
    _PHASE_N[0] += 1
    S = lambda n, sh, dt=F32: ctx.enter_context(nc.sbuf_tensor(_pfx + n, sh, dt))
    PS = lambda n, sh, dt=F32: ctx.enter_context(nc.psum_tensor(_pfx + n, sh, dt))
    with ctx:
        qfm = S("qfm", [128, 8, T], BF16)
        kbuf = S("kbuf", [128, 2, SEQ], BF16)
        vbuf = S("vbuf", [128, 2, 64 * 128], BF16)
        e_sb = S("e_sb", [128, 2, 2, 1536], BF16)
        sstage = S("sstage", [128, 2, 512])
        qn16 = S("qn16", [NH, T])
        qm4 = S("qm4", [NH, 4])
        t64 = S("t64", [1, 16, 4])
        negc_sb = S("negc_sb", [128, 64])
        km2 = S("km2", [1, NH])
        rz = S("rz", [1, 2, 512], BF16)
        rzf = S("rzf", [1, 2, 512])

        pSa = PS("pSa", [128, 1536])
        pSb = PS("pSb", [128, 1536])
        pAV = PS("pAV", [128, 512])
        pZ = PS("pZ", [128, 512])

        ones_row = cs["c_ones_1x128"]
        ones_rowbf = cs["c_ones_rowbf"]
        ones_colbf = cs["c_ones_colbf"]

        ld_q = p.y(lambda y: y.dma_start(out=qfm[:], in_=q_h.rearrange("p (c t) -> p c t", c=8)), inc=True)
        kmg = S("kmg", [1, 64])
        ld_km = p.y(lambda y: y.dma_start(out=kmg[:], in_=km_h), inc=True)
        ld_qn = p.y(lambda y: y.dma_start(out=qn16[:], in_=qn2_h), inc=True)
        e0 = p.v(lambda v: v.tensor_tensor(km2[:], kmg[0:1, 0:16], kmg[0:1, 16:32], ALU.max), wait=(ld_km,))
        p.v(lambda v: v.tensor_tensor(km2[:], km2[:], kmg[0:1, 32:48], ALU.max))
        p.v(lambda v: v.tensor_tensor(km2[:], km2[:], kmg[0:1, 48:64], ALU.max))
        ev_km2 = p.mark("v")

        NG = 22
        GS = [min(3, 64 - 3 * g) for g in range(NG)]

        ld_k = {}
        ld_v = {}
        for hp in range(2):
            ld_k[hp] = p.y(lambda y, hp=hp: y.dma_start(
                out=kbuf[:, hp % 2, :].rearrange("p (r t) -> p r t", r=4), in_=k_h[:, hp]), inc=True)
            ld_v[hp] = p.g(lambda g, hp=hp: g.dma_start(
                out=vbuf[:, hp % 2, :], in_=v_h[hp]), inc=True, dma=True)

        for j in range(4):
            p.v(lambda v, j=j: v.reduce_max(qm4[:, j:j + 1], qn16[:, 512 * j:512 * (j + 1)], axis=AX.X),
                wait=(ld_qn,) if j == 0 else ())
        km4mm = p.t(lambda t: t.matmul(pSb[0:16, 0:4], km2[0:1, :], ones_row[0:1, 0:4],
                                       start=True, stop=True), wait=(ev_km2,), inc=True)
        p.v(lambda v: v.tensor_tensor(qm4[:], qm4[:], pSb[0:16, 0:4], ALU.mult), wait=(km4mm,))
        sq_ev = p.s(lambda s: s.activation(qm4[:], qm4[:], AF.Sqrt), wait=(p.mark("v"),), inc=True)
        ngv = p.v(lambda v: v.tensor_scalar(qm4[:], qm4[:], -1.0, None, ALU.mult), wait=(sq_ev,), inc=True)
        dq = p.y(lambda y: y.dma_start(out=t64[:],
                                       in_=qm4[:].rearrange("p j -> p () j")), wait=(ngv,), inc=True)
        bcn = p.t(lambda t: t.matmul(pZ[:, 0:64], ones_row[:], t64[:].rearrange("o h j -> o (h j)"),
                                     start=True, stop=True), wait=(dq,), inc=True)
        nccA = p.v(lambda v: v.tensor_copy(negc_sb[:], pZ[:, 0:64]), wait=(bcn,), inc=True)

        prev_c1 = None
        prev_mres = nccA
        prev_expa_l = None
        prev_exp_l = None
        prev_store = {0: None, 1: None}
        pend = [None]
        for hp in range(8):
            kb = hp % 2
            ld_k_ev = ld_k[hp]
            ld_v_ev = ld_v[hp]
            for q4 in range(4):
                qsl = slice(512 * q4, 512 * (q4 + 1))
                wA0 = (ld_k_ev, ld_q) + ((prev_expa_l,) if prev_expa_l else ())
                wB0 = (ld_k_ev, ld_q) + ((prev_exp_l,) if prev_exp_l else ())
                qa0 = [None] * GS[0]
                qb0 = [None] * GS[0]
                for j in range(GS[0]):
                    p.t(lambda t, j=j, kb=kb, hp=hp, qsl=qsl: t.matmul(
                        pSa[:, 512 * j:512 * (j + 1)], kbuf[0:64, kb, 128 * j:128 * (j + 1)],
                        qfm[0:64, hp, qsl], start=True, stop=True, tile_position=(0, 0)),
                        wait=wA0 if j == 0 else ())
                    qa0[j] = p.mark("t")
                    p.t(lambda t, j=j, kb=kb, hp=hp, qsl=qsl: t.matmul(
                        pSb[:, 512 * j:512 * (j + 1)], kbuf[64:128, kb, 128 * j:128 * (j + 1)],
                        qfm[64:128, hp, qsl], start=True, stop=True, tile_position=(64, 0)),
                        wait=wB0 if j == 0 else ())
                    qb0[j] = p.mark("t")
                if pend[0] is not None:
                    m_ev, st_ev, sb_p = pend[0]()
                    prev_mres = m_ev
                    prev_store[sb_p] = st_ev
                    pend[0] = None

                ev_ea = {}
                ev_eb = {}
                ev_av = [None] * NG
                ca = 4 * 2 * hp + q4
                cb = 4 * (2 * hp + 1) + q4
                for g in range(NG):
                    gb = g % 2
                    if g >= 1:
                        qa0 = [None] * GS[g]
                        qb0 = [None] * GS[g]
                        for j in range(GS[g]):
                            ks = 3 * g + j
                            wA = (ev_ea[(g - 1, j)],) if j < GS[g - 1] else ()
                            wB = (ev_eb[(g - 1, j)],) if j < GS[g - 1] else ()
                            p.t(lambda t, ks=ks, j=j, kb=kb, hp=hp, qsl=qsl: t.matmul(
                                pSa[:, 512 * j:512 * (j + 1)], kbuf[0:64, kb, 128 * ks:128 * (ks + 1)],
                                qfm[0:64, hp, qsl], start=True, stop=True, tile_position=(0, 0)), wait=wA)
                            qa0[j] = p.mark("t")
                            p.t(lambda t, ks=ks, j=j, kb=kb, hp=hp, qsl=qsl: t.matmul(
                                pSb[:, 512 * j:512 * (j + 1)],
                                kbuf[64:128, kb, 128 * ks:128 * (ks + 1)],
                                qfm[64:128, hp, qsl], start=True, stop=True, tile_position=(64, 0)), wait=wB)
                            qb0[j] = p.mark("t")
                    for j in range(GS[g]):
                        w_exp = (qa0[j], nccA)
                        if g >= 2 and j == 0:
                            w_exp = w_exp + ((ev_av[g - 2],) if ev_av[g - 2] else ())
                        ev_ea[(g, j)] = p.s(lambda s, gb=gb, j=j, ca=ca: s.activation(
                            e_sb[:, gb, 0, 512 * j:512 * (j + 1)], pSa[:, 512 * j:512 * (j + 1)],
                            AF.Exp, bias=negc_sb[:, ca:ca + 1]), wait=w_exp, inc=True)
                        ev_eb[(g, j)] = p.s(lambda s, gb=gb, j=j, cb=cb: s.activation(
                            e_sb[:, gb, 1, 512 * j:512 * (j + 1)], pSb[:, 512 * j:512 * (j + 1)],
                            AF.Exp, bias=negc_sb[:, cb:cb + 1]), wait=(qb0[j],), inc=True)
                    for j in range(GS[g]):
                        ks = 3 * g + j
                        first = (g == 0 and j == 0)
                        last = (g == NG - 1 and j == GS[g] - 1)
                        wav = (ev_ea[(g, j)],) + ((ld_v_ev,) if j == 0 else ())
                        p.t(lambda t, ks=ks, j=j, gb=gb, kb=kb, first=first, last=last: t.matmul(
                            pAV[0:64, :], vbuf[:, kb, :].rearrange("p (tt d) -> p tt d", d=128)[:, ks, 0:64],
                            e_sb[:, gb, 0, 512 * j:512 * (j + 1)],
                            start=first, stop=last, tile_position=(0, 0)),
                            wait=wav + ((prev_c1,) if (first and prev_c1) else ()))
                        p.t(lambda t, ks=ks, j=j, gb=gb, kb=kb, first=first, last=last: t.matmul(
                            pAV[64:128, :], vbuf[:, kb, :].rearrange("p (tt d) -> p tt d", d=128)[:, ks, 64:128],
                            e_sb[:, gb, 1, 512 * j:512 * (j + 1)],
                            start=first, stop=last, tile_position=(0, 64), skip_group_check=True),
                            wait=(ev_eb[(g, j)],))
                        p.t(lambda t, ks=ks, j=j, gb=gb, first=first, last=last: t.matmul(
                            pZ[0:1, :], ones_colbf[:], e_sb[:, gb, 0, 512 * j:512 * (j + 1)],
                            start=first, stop=last, tile_position=(0, 0)),
                            wait=((prev_mres,) if (first and prev_mres) else ()))
                        p.t(lambda t, ks=ks, j=j, gb=gb, first=first, last=last: t.matmul(
                            pZ[64:65, :], ones_colbf[:], e_sb[:, gb, 1, 512 * j:512 * (j + 1)],
                            start=first, stop=last, tile_position=(0, 64), skip_group_check=True))
                    ev_av[g] = p.mark("t")
                sb = q4 % 2
                c1 = p.v(lambda v, sb=sb: v.tensor_copy(sstage[:, sb, :], pAV[:]),
                         wait=(ev_av[NG - 1],) + ((prev_store[sb],) if prev_store[sb] else ()), inc=True)
                p.v(lambda v: v.reciprocal(rzf[0:1, 0, :], pZ[0:1, :]))
                p.v(lambda v: v.reciprocal(rzf[0:1, 1, :], pZ[64:65, :]))
                rcast = p.v(lambda v: v.tensor_copy(rz[:].rearrange("o h t -> o (h t)"), rzf[:].rearrange("o h t -> o (h t)")), inc=True)
                prev_c1 = c1
                prev_expa_l = ev_ea[(NG - 1, 0)]
                prev_exp_l = ev_eb[(NG - 1, 0)]

                def mk_pend(sb=sb, hp=hp, qsl=qsl, rcast=rcast):
                    def emit():
                        p.t(lambda t: t.matmul(pZ[0:64, :], ones_rowbf[0:1, 0:64], rz[0:1, 0, :],
                                               start=True, stop=True, tile_position=(0, 0)),
                            wait=(rcast,))
                        b2 = p.t(lambda t: t.matmul(pZ[64:128, :], ones_rowbf[0:1, 0:64], rz[0:1, 1, :],
                                                    start=True, stop=True, tile_position=(0, 64),
                                                    skip_group_check=True), inc=True)
                        mres = p.v(lambda v: v.tensor_tensor(sstage[:, sb, :], sstage[:, sb, :],
                                                             pZ[:, 0:512], ALU.mult), wait=(b2,), inc=True)
                        st = p.y(lambda y: y.dma_start(
                            out=s0_o.rearrange("p (c t) -> p c t", c=8)[:, hp, qsl], in_=sstage[:, sb, :]),
                            wait=(mres,), inc=True)
                        return mres, st, sb
                    return emit
                pend[0] = mk_pend()
            if hp + 2 <= 7:
                nhp = hp + 2
                lastuse = p.mark("t")
                ld_k[nhp] = p.y(lambda y, nhp=nhp: y.dma_start(
                    out=kbuf[:, nhp % 2, :].rearrange("p (r t) -> p r t", r=4), in_=k_h[:, nhp]),
                    wait=(lastuse,), inc=True)
                ld_v[nhp] = p.g(lambda g, nhp=nhp: g.dma_start(
                    out=vbuf[:, nhp % 2, :], in_=v_h[nhp]),
                    wait=(lastuse,), inc=True, dma=True)
        if pend[0] is not None:
            pend[0]()
            pend[0] = None




def emit_moe0(nc, p, io, cs, ld_c):
    xn_h = io["xn0"]
    x_hh = io["x_fm"]
    s0_h = io["s0"]
    wd_h = io["wd"]
    pos_h = io["pos"]
    cnt_h = io["cnt_all"]
    mask_h = io["core_mask"]
    sgu_h = io["sgu_pairs"]
    sdw_h = io["sdown"]
    sgw_h = io["sgate"]
    egu_h = io["egu"]
    edw_h = io["edw"]

    x1_o = io["x1"]
    wkt_h = nc.dram_tensor("wkt_scratch", [NE0, T], F32, kind="Internal").ap()

    ctx = ExitStack()
    _pfx = "ph%d_" % _PHASE_N[0]
    _PHASE_N[0] += 1
    S = lambda n, sh, dt=F32: ctx.enter_context(nc.sbuf_tensor(_pfx + n, sh, dt))
    PS = lambda n, sh, dt=F32: ctx.enter_context(nc.psum_tensor(_pfx + n, sh, dt))
    with ctx:
        xnb = S("xnb", [128, 8, T], BF16)
        y_acc = S("y_acc", [128, 8, T])
        act_t = S("act_t", [128, 16, 512], BF16)
        sgu_db = S("sgu_db", [128, 2, 8, 256], BF16)
        sdw_db = S("sdw_db", [128, 2, D], BF16)
        keepw = S("keepw", [128, 16, NE0])
        grow = S("grow", [1, T])
        gate_bc = S("gate_bc", [128, T], BF16)
        egu_sb = S("egu_sb", [128, 2, 8, D], BF16)
        edw_sb = S("edw_sb", [128, 4, D], BF16)
        silu_t = S("silu_t", [128, 512])
        t1f = S("t1f", [128, T])
        base_row = S("base_row", [1, NE0])
        cnt_sb = S("cnt_sb", [8, NE0])
        mask_sb = S("mask_sb", [8, 1])
        x1st = S("x1st", [128, T])
        pos_t = S("pos_t", [128, NE0])

        pAll = PS("pAll", [128, 4096])

        def pb(i):
            return pAll[:, 512 * i:512 * (i + 1)]

        ones_row = cs["c_ones_1x128"]
        ident = cs["c_ident_f32"]

        ld_xn = p.y(lambda y: y.dma_start(out=xnb[:], in_=xn_h.rearrange("p (c t) -> p c t", c=8)), inc=True)
        ld_cnt = p.y(lambda y: y.dma_start(out=cnt_sb[:], in_=cnt_h), inc=True)
        ld_msk = p.y(lambda y: y.dma_start(out=mask_sb[:], in_=mask_h), inc=True)
        ld_wd = p.y(lambda y: y.dma_start(out=keepw[:], in_=wd_h.rearrange("p (c t) -> p c t", c=16)), inc=True)

        wkT_sb = t1f[0:64, :].rearrange("p (c x) -> p c x", x=128)
        mb = p.t(lambda t: t.matmul(pAll[0:1, 0:NE0], mask_sb[:], cnt_sb[:], start=True, stop=True),
                 wait=(ld_cnt, ld_msk), inc=True)
        e1 = p.v(lambda v: v.tensor_copy(base_row[:], pAll[0:1, 0:NE0]), wait=(mb,), inc=True)
        bb = p.t(lambda t: t.matmul(pAll[0:128, 0:NE0], ones_row[:], base_row[:], start=True, stop=True),
                 wait=(e1, ld_c), inc=True)
        for tt in range(16):
            lp = p.y(lambda y, tt=tt: y.dma_start(out=pos_t[:], in_=pos_h.rearrange("p (c t) -> p c t", c=16)[:, tt, :]),
                     wait=(p.mark("v"),), inc=True)
            p.v(lambda v: v.tensor_tensor(pos_t[:], pos_t[:], pAll[0:128, 0:NE0], ALU.add), wait=(lp, bb))
            p.v(lambda v: v.tensor_scalar(pos_t[:], pos_t[:], float(CAP0), None, ALU.is_lt))
            p.v(lambda v, tt=tt: v.tensor_tensor(keepw[:, tt, :], keepw[:, tt, :], pos_t[:], ALU.mult), wait=(ld_wd,))
        ev_keep = p.mark("v")
        for tt in range(16):
            trw = p.t(lambda t, tt=tt: t.transpose(pAll[0:64, 512:640], keepw[:, tt, :], ident[:]),
                      wait=(ev_keep,) + ((p.mark("v"),) if tt else ()), inc=True)
            p.v(lambda v, tt=tt: v.tensor_copy(wkT_sb[:, tt, :], pAll[0:64, 512:640]), wait=(trw,))
        ev_wkt = p.mark("v")
        st_wkt = p.y(lambda y: y.dma_start(out=wkt_h, in_=wkT_sb.rearrange("p c t -> p (c t)")), wait=(ev_wkt,), inc=True)

        sgw_sb = S("sgw_sb", [128, 8], BF16)
        ld_sgw = p.g(lambda g: g.dma_start(out=sgw_sb[:], in_=sgw_h), inc=True, dma=True)
        for tc in range(4):
            for ci in range(8):
                p.t(lambda t, tc=tc, ci=ci: t.matmul(
                    pAll[0:1, 1024 + 512 * tc:1024 + 512 * (tc + 1)], sgw_sb[:, ci:ci + 1],
                    xnb[:, ci, 512 * tc:512 * (tc + 1)],
                    start=(ci == 0), stop=(ci == 7)), wait=(ld_sgw, ld_xn, st_wkt))
        mg = p.mark("t")
        sg = p.s(lambda s: s.activation(grow[:], pAll[0:1, 1024:1024 + 2048], AF.Sigmoid), wait=(mg,), inc=True)
        for tc in range(4):
            p.t(lambda t, tc=tc: t.matmul(pb(tc), ones_row[:], grow[0:1, 512 * tc:512 * (tc + 1)],
                                          start=True, stop=True), wait=(sg,))
        mgb = p.mark("t")
        ev_free = p.v(lambda v: v.tensor_copy(gate_bc[:], pAll[:, 0:2048]), wait=(mgb,), inc=True)
        ev_grow_free = mgb

        prev_y = None
        for tg in range(4):
            tsl = slice(512 * tg, 512 * (tg + 1))
            for i in range(16):
                dbuf = i % 2
                lw = p.g(lambda g, i=i, dbuf=dbuf: g.dma_start(
                    out=sgu_db[:, dbuf, :, :],
                    in_=sgu_h.rearrange("(c p) (i x) -> p c i x", p=128, i=16)[:, :, i, :]),
                    wait=(p.mark("t"),), inc=True, dma=True)
                for ci in range(8):
                    p.t(lambda t, i=i, tsl=tsl, ci=ci, dbuf=dbuf: t.matmul(
                        pb(0), sgu_db[:, dbuf, ci, 0:128], xnb[:, ci, tsl],
                        start=(ci == 0), stop=(ci == 7)), wait=(lw, ev_free))
                for ci in range(8):
                    p.t(lambda t, i=i, tsl=tsl, ci=ci, dbuf=dbuf: t.matmul(
                        pb(1), sgu_db[:, dbuf, ci, 128:256], xnb[:, ci, tsl],
                        start=(ci == 0), stop=(ci == 7)))
                mmi = p.mark("t")
                sv = p.s(lambda s: s.activation(silu_t[:], pb(0), AF.Silu), wait=(mmi,), inc=True)
                p.v(lambda v: v.tensor_tensor(silu_t[:], silu_t[:], pb(1), ALU.mult), wait=(sv,))
                ev_free = p.v(lambda v, i=i, tsl=tsl: v.tensor_tensor(
                    act_t[:, i, :], silu_t[:], gate_bc[:, tsl], ALU.mult), inc=True)
            for ifc in range(16):
                dbuf = ifc % 2
                lwi = p.g(lambda g, ifc=ifc, dbuf=dbuf: g.dma_start(
                    out=sdw_db[:, dbuf, :],
                    in_=sdw_h.rearrange("(c p) d -> p c d", p=128)[:, ifc, :]),
                    wait=(p.mark("t"),), inc=True, dma=True)
                for fc in range(8):
                    w = (lwi, ev_free) + ((prev_y,) if prev_y else ())
                    p.t(lambda t, ifc=ifc, fc=fc, dbuf=dbuf: t.matmul(
                        pb(fc), sdw_db[:, dbuf, 128 * fc:128 * (fc + 1)], act_t[:, ifc, :],
                        start=(ifc == 0), stop=(ifc == 15)), wait=w)
            mfc = p.mark("t")
            for fc in range(8):
                prev_y = p.v(lambda v, fc=fc, tsl=tsl: v.tensor_copy(y_acc[:, fc, tsl], pb(fc)),
                             wait=(mfc,), inc=True)
            ev_free = prev_y

        wrow_db = grow
        w_bc = gate_bc
        prev_wread = None
        lgu = {0: p.g(lambda g: g.dma_start(
            out=egu_sb[:, 0], in_=egu_h[0].rearrange("(c p) d -> p c d", p=128)),
            wait=(p.mark("t"),), inc=True, dma=True)}
        ldw = {0: p.g(lambda g: g.dma_start(
            out=edw_sb[:], in_=edw_h[0].rearrange("(c p) d -> p c d", p=128)),
            inc=True, dma=True)}
        lwr = {0: p.y(lambda y: y.dma_start(out=wrow_db[0:1, :], in_=wkt_h[0:1, :]),
                      wait=(st_wkt, ev_grow_free), inc=True)}
        ev_bankpair = {}
        for e in range(NE0):
            eb = e % 2
            for tc in range(4):
                p.t(lambda t, tc=tc, eb=eb: t.matmul(
                    pb(4 + tc), ones_row[:], wrow_db[0:1, 512 * tc:512 * (tc + 1)],
                    start=True, stop=True), wait=(lwr[e], ev_free))
            prev_wread = p.mark("t")
            ev_wbc = p.v(lambda v: v.tensor_copy(w_bc[:], pAll[:, 2048:4096]), wait=(prev_wread, ev_free), inc=True)
            if e + 1 < NE0:
                lwr[e + 1] = p.y(lambda y, e=e: y.dma_start(
                    out=wrow_db[0:1, :], in_=wkt_h[e + 1:e + 2, :]),
                    wait=(prev_wread,), inc=True)
            ev_act = [ev_wbc, ev_wbc]
            for tg in range(4):
                tsl = slice(512 * tg, 512 * (tg + 1))
                for i in range(4):
                    gb2 = 2 * (i % 2)
                    for ci in range(8):
                        bp = ev_bankpair.get(i % 2) if (i < 2 and ev_bankpair) else None
                        w = (lgu[e], ev_wbc, ev_act[i % 2]) + ((bp,) if bp else ())
                        p.t(lambda t, i=i, tsl=tsl, ci=ci, eb=eb, gb2=gb2: t.matmul(
                            pb(gb2), egu_sb[:, eb, ci, 128 * i:128 * (i + 1)], xnb[:, ci, tsl],
                            start=(ci == 0), stop=(ci == 7)), wait=w)
                    for ci in range(8):
                        p.t(lambda t, i=i, tsl=tsl, ci=ci, eb=eb, gb2=gb2: t.matmul(
                            pb(gb2 + 1), egu_sb[:, eb, ci, 512 + 128 * i:512 + 128 * (i + 1)], xnb[:, ci, tsl],
                            start=(ci == 0), stop=(ci == 7)))
                    mmi = p.mark("t")
                    sv = p.s(lambda s, gb2=gb2: s.activation(silu_t[:], pb(gb2), AF.Silu), wait=(mmi,), inc=True)
                    p.v(lambda v, gb2=gb2: v.tensor_tensor(silu_t[:], silu_t[:], pb(gb2 + 1), ALU.mult), wait=(sv,))
                    ev_act[i % 2] = p.v(lambda v, i=i, tsl=tsl: v.tensor_tensor(
                        act_t[:, i, :], silu_t[:], w_bc[:, tsl], ALU.mult), inc=True)
                if tg == 3:
                    gu_last = p.mark("t")
                    if e + 1 < NE0:
                        lgu[e + 1] = p.g(lambda g, e=e: g.dma_start(
                            out=egu_sb[:, (e + 1) % 2], in_=egu_h[e + 1].rearrange("(c p) d -> p c d", p=128)),
                            wait=(gu_last,), inc=True, dma=True)
                for ifc in range(4):
                    for fc in range(8):
                        p.t(lambda t, ifc=ifc, fc=fc, eb=eb: t.matmul(
                            pb(fc), edw_sb[:, ifc, 128 * fc:128 * (fc + 1)], act_t[:, ifc, :],
                            start=(ifc == 0), stop=(ifc == 3)), wait=(ldw[e], ev_act[1]))
                mfc = p.mark("t")
                if tg == 3 and e + 1 < NE0:
                    ldw[e + 1] = p.g(lambda g, e=e: g.dma_start(
                        out=edw_sb[:], in_=edw_h[e + 1].rearrange("(c p) d -> p c d", p=128)),
                        wait=(mfc,), inc=True, dma=True)
                y_add = {}
                for fc in range(8):
                    y_add[fc] = p.v(lambda v, fc=fc, tsl=tsl: v.tensor_tensor(
                        y_acc[:, fc, tsl], y_acc[:, fc, tsl], pb(fc), ALU.add), wait=(mfc,), inc=True)
                prev_y = y_add[7]
                ev_bankpair = {0: y_add[1], 1: y_add[3]}
                ev_free = prev_y

        xchunk = silu_t
        prev_st = None
        for fc in range(8):
            lx = p.y(lambda y, fc=fc: y.dma_start(
                out=t1f[:], in_=x_hh.rearrange("(c p) t -> p c t", p=128)[:, fc, :]),
                wait=(p.mark("v"),) + ((prev_st,) if prev_st else ()), inc=True)
            ls = p.y(lambda y, fc=fc: y.dma_start(
                out=x1st[:], in_=s0_h.rearrange("p (c t) -> p c t", c=8)[:, fc, :]), inc=True)
            gg = p.s(lambda s, fc=fc: s.activation(y_acc[:, fc, :], y_acc[:, fc, :], AF.Silu),
                     wait=(prev_y,), inc=True)
            p.v(lambda v, fc=fc: v.tensor_tensor(x1st[:], x1st[:], y_acc[:, fc, :], ALU.mult), wait=(gg, ls))
            p.v(lambda v, fc=fc: v.tensor_tensor(x1st[:], x1st[:], t1f[:], ALU.add), wait=(lx,))
            ev_x1 = p.mark("v")
            prev_st = p.y(lambda y, fc=fc: y.dma_start(
                out=x1_o.rearrange("p (c t) -> p c t", c=8)[:, fc, :], in_=x1st[:]), wait=(ev_x1,), inc=True)




def emit_pre1(nc, p, io, cs, ld_c):
    x_h = io["x1"]
    wq_h = io["wq1_t"]
    wk_h = io["wk1_t"]
    wv_h = io["wv1_t"]
    wr_h = io["wr1_t"]
    sqk_h = io["sqk_t"]
    cos_h = io["rope_cos"]
    sin_h = io["rope_sin"]

    xn_o = io["x1n"]
    q_o = io["q1"]
    k_o = io["k1"]
    v_o = io["v1"]
    g_o = io["g1d"]

    SWAP = [(i + 1) if i % 2 == 0 else (i - 1) for i in range(32)]

    ctx = ExitStack()
    _pfx = "ph%d_" % _PHASE_N[0]
    _PHASE_N[0] += 1
    S = lambda n, sh, dt=F32: ctx.enter_context(nc.sbuf_tensor(_pfx + n, sh, dt))
    PS = lambda n, sh, dt=F32: ctx.enter_context(nc.psum_tensor(_pfx + n, sh, dt))
    with ctx:
        x_sb = S("x_sb", [128, 8, T])
        xnb = S("xnb", [128, 8, T], BF16)
        xnf_t = S("xnf_t", [128, 8, 512])
        sq_full = xnf_t
        wq_sb = S("wq_sb", [128, 8, D], BF16)
        wk_sb = S("wk_sb", [128, 8, D], BF16)
        wr_sb = S("wr_sb", [128, 8, NE1])
        sqk_sb = S("sqk_sb", [64, NH])
        qscol = S("qscol", [128, 1])
        kscol = S("kscol", [128, 1])
        cos_sb = S("cos_sb", [128, T])
        sin_sb = S("sin_sb", [128, T])
        rows = S("rows", [1, 512 * 6])
        qstage = S("qstage", [128, 2, 512], BF16)
        kstage = S("kstage", [128, 2, 512], BF16)
        vstage = S("vstage", [128, 2, 1024], BF16)
        qt = S("qt", [128, 512])
        qs = S("qs", [128, 512])
        qro = qt
        sqq = qs
        rn_rows = S("rn_rows", [1, 2, 512], BF16)
        rn_f = S("rn_f", [1, 2, 512])
        probs = S("probs", [128, NE1])
        sig_t = S("sig_t", [128, NE1])
        eq_t = S("eq_t", [128, NE1])
        tmq = S("tmq", [128, NE1])
        sel2 = S("sel2", [128, NE1])
        mx = S("mx", [128, 1])
        am = S("am", [128, 1])
        negm = S("negm", [128, 1])
        gT_sb = S("gT_sb", [16, 16, 128])

        pA = PS("pA", [1, 512])
        pB = PS("pB", [1, 512])
        pC = PS("pC", [128, 512])
        pD = PS("pD", [128, 512])
        pE = PS("pE", [128, 512])
        pF = PS("pF", [128, 512])

        ones_col = cs["c_ones_col"]
        ones_row = cs["c_ones_1x128"]
        rev16 = cs["c_rev16"]
        ident = cs["c_ident_f32"]

        ld_x = p.y(lambda y: y.dma_start(out=x_sb[:], in_=x_h.rearrange("p (c t) -> p c t", c=8)), inc=True)
        ld_wq = _ld_fm(p, wq_h, wq_sb, cast=True)
        ld_wk = _ld_fm(p, wk_h, wk_sb, cast=True)
        ld_wr = _ld_fm(p, wr_h, wr_sb)
        ld_sqk = p.y(lambda y: y.dma_start(out=sqk_sb[:], in_=sqk_h), inc=True)
        ld_cos = p.y(lambda y: y.dma_start(out=cos_sb[:], in_=cos_h), inc=True)
        ld_sin = p.y(lambda y: y.dma_start(out=sin_sb[:], in_=sin_h), inc=True)

        mean_r = rows[0:1, 0:512]
        msq_r = rows[0:1, 512:1024]
        var_r = rows[0:1, 1024:1536]
        std_r = rows[0:1, 1536:2048]
        rstd_r = rows[0:1, 2048:2560]
        tmp_r = rows[0:1, 2560:3072]

        for tc in range(4):
            ts = slice(512 * tc, 512 * (tc + 1))
            sq_evs = []
            for ci in range(8):
                sq_evs.append(p.s(lambda s, ci=ci, ts=ts: s.activation(
                    sq_full[:, ci, :], x_sb[:, ci, ts], AF.Square), wait=(ld_x,), inc=True))
            for ci in range(8):
                p.t(lambda t, ci=ci, ts=ts: t.matmul(
                    pA[0:1, :], ones_col[:], x_sb[:, ci, ts],
                    start=(ci == 0), stop=(ci == 7)), wait=(ld_x, ld_c))
            mm_ev = None
            for ci in range(8):
                mm_ev = p.t(lambda t, ci=ci: t.matmul(
                    pB[0:1, :], ones_col[:], sq_full[:, ci, :],
                    start=(ci == 0), stop=(ci == 7)), wait=(sq_evs[ci],), inc=(ci == 7))
            e1 = p.v(lambda v: v.tensor_scalar(mean_r, pA[0:1, :], 1.0 / D, None, ALU.mult), wait=(mm_ev,), inc=True)
            p.v(lambda v: v.tensor_scalar(msq_r, pB[0:1, :], 1.0 / D, None, ALU.mult))
            p.v(lambda v: v.tensor_tensor(tmp_r, mean_r, mean_r, ALU.mult))
            p.v(lambda v: v.tensor_tensor(var_r, msq_r, tmp_r, ALU.subtract))
            e3 = p.v(lambda v: v.tensor_scalar(var_r, var_r, 1e-5, None, ALU.add), inc=True)
            e4 = p.s(lambda s: s.activation(std_r, var_r, AF.Sqrt), wait=(e3,), inc=True)
            e5 = p.v(lambda v: v.reciprocal(rstd_r, std_r), wait=(e4,), inc=True)
            b1 = p.t(lambda t: t.matmul(pC[:], ones_row[:], mean_r, start=True, stop=True),
                     wait=(e1, e5), inc=True)
            b2 = p.t(lambda t: t.matmul(pD[:], ones_row[:], rstd_r, start=True, stop=True),
                     wait=(e5,), inc=True)
            ev_xn = None
            for ci in range(8):
                p.v(lambda v, ci=ci, ts=ts: v.tensor_tensor(xnf_t[:, ci, :], x_sb[:, ci, ts], pC[:], ALU.subtract), wait=(b1, b2))
                p.v(lambda v, ci=ci: v.tensor_tensor(xnf_t[:, ci, :], xnf_t[:, ci, :], pD[:], ALU.mult))
                ev_xn = p.v(lambda v, ci=ci, ts=ts: v.tensor_copy(xnb[:, ci, ts], xnf_t[:, ci, :]), inc=True)

            for st4 in range(4):
                tt = 4 * tc + st4
                for ci in range(8):
                    p.t(lambda t, ci=ci, st4=st4: t.matmul(
                        pC[0:128, 0:NE1], xnf_t[:, ci, 128 * st4:128 * (st4 + 1)], wr_sb[:, ci, :],
                        start=(ci == 0), stop=(ci == 7)), wait=(ev_xn, ld_wr))
                mlg = p.mark("t")
                sig = p.s(lambda s: s.activation(sig_t[:], pC[0:128, 0:NE1], AF.Sigmoid), wait=(mlg,), inc=True)
                cp = p.v(lambda v: v.tensor_copy(probs[:], pC[0:128, 0:NE1]), wait=(mlg,), inc=True)
                p.v(lambda v: v.memset(sel2[:], 0.0))
                ek = p.v(lambda v: v.engine_nop(), inc=True)
                for kk in range(2):
                    rr1 = p.v(lambda v: v.reduce_max(mx[:], probs[:], axis=AX.X), wait=(ek,), inc=True)
                    p.v(lambda v: v.tensor_scalar(eq_t[:], probs[:], mx[:], None, ALU.is_ge), wait=(rr1,))
                    p.v(lambda v: v.tensor_tensor(tmq[:], eq_t[:], rev16[:], ALU.mult), wait=(ld_c,))
                    rr2 = p.v(lambda v: v.reduce_max(am[:], tmq[:], axis=AX.X), inc=True)
                    p.v(lambda v: v.tensor_scalar(eq_t[:], tmq[:], am[:], None, ALU.is_ge), wait=(rr2,))
                    p.v(lambda v: v.tensor_tensor(sel2[:], sel2[:], eq_t[:], ALU.add))
                    ek = p.v(lambda v: v.scalar_tensor_tensor(probs[:], eq_t[:], -1e30, probs[:], ALU.mult, ALU.add), inc=True)
                gsel = p.v(lambda v: v.tensor_tensor(sel2[:], sel2[:], sig_t[:], ALU.mult), wait=(sig, ek), inc=True)
                trg = p.t(lambda t, tt=tt: t.transpose(pD[0:16, 0:128], sel2[:], ident[:]), wait=(gsel,), inc=True)
                p.v(lambda v, tt=tt: v.tensor_copy(gT_sb[:, tt, :], pD[0:16, 0:128]), wait=(trg,))
        ev_g = p.mark("v")
        st_g = p.y(lambda y: y.dma_start(out=g_o, in_=gT_sb[:].rearrange("p c t -> p (c t)")), wait=(ev_g,), inc=True)
        xn_all = ev_g
        ev_store = [st_g]
        ev_store.append(p.y(lambda y: y.dma_start(out=xn_o, in_=xnb[:].rearrange("p c t -> p (c t)")), wait=(xn_all,), inc=True))

        prev_pe = None
        prev_store = {0: None, 1: None}
        for which, (w_sb, stage, out_hbm, ld_w, scol, sfac) in enumerate(
                [(wq_sb, qstage, q_o, ld_wq, qscol, 256.0), (wk_sb, kstage, k_o, ld_wk, kscol, 32.0)]):
            for co in range(8):
                sc1 = p.v(lambda v, co=co, sfac=sfac, scol=scol: v.tensor_scalar(
                    scol[0:64, :], sqk_sb[:, 2 * co:2 * co + 1], sfac, None, ALU.mult), wait=(ld_sqk,), inc=True)
                sc2 = p.v(lambda v, co=co, sfac=sfac, scol=scol: v.tensor_scalar(
                    scol[64:128, :], sqk_sb[:, 2 * co + 1:2 * co + 2], sfac, None, ALU.mult), inc=True)
                for tc in range(4):
                    ts = slice(512 * tc, 512 * (tc + 1))
                    buf = (co * 4 + tc) % 2
                    for ci in range(8):
                        p.t(lambda t, ci=ci, co=co, ts=ts, w_sb=w_sb: t.matmul(
                            pE[:], w_sb[:, ci, 128 * co:128 * (co + 1)], xnb[:, ci, ts],
                            start=(ci == 0), stop=(ci == 7)),
                            wait=(xn_all, ld_w) + ((prev_pe,) if prev_pe else ()))
                    mm = p.mark("t")
                    c1 = p.v(lambda v: v.tensor_copy(qt[:], pE[:]), wait=(mm, ld_cos, ld_sin), inc=True)
                    prev_pe = c1
                    p.v(lambda v: v.stream_shuffle(qs[:], qt[:], mask=SWAP))
                    p.v(lambda v, ts=ts: v.tensor_tensor(qt[:], qt[:], cos_sb[:, ts], ALU.mult))
                    p.v(lambda v, ts=ts: v.tensor_tensor(qs[:], qs[:], sin_sb[:, ts], ALU.mult))
                    rr = p.v(lambda v: v.tensor_tensor(qt[:], qt[:], qs[:], ALU.add), inc=True)
                    sqe = p.s(lambda s: s.activation(sqq[:], qro[:], AF.Square), wait=(rr,), inc=True)
                    for hh in range(2):
                        pX = pA if hh == 0 else pB
                        mqn = p.t(lambda t, hh=hh, pX=pX: t.matmul(
                            pX[0:1, :], ones_col[64 * hh:64 * (hh + 1), :], sqq[64 * hh:64 * (hh + 1), :],
                            start=True, stop=True), wait=(sqe,), inc=True)
                        s1e = p.s(lambda s, hh=hh, pX=pX: s.activation(rn_f[0:1, hh, :], pX[0:1, :], AF.Sqrt), wait=(mqn,), inc=True)
                        p.v(lambda v, hh=hh: v.tensor_scalar(rn_f[0:1, hh, :], rn_f[0:1, hh, :], 1e-12, None, ALU.max), wait=(s1e,))
                        p.v(lambda v, hh=hh: v.reciprocal(rn_f[0:1, hh, :], rn_f[0:1, hh, :]))
                        p.v(lambda v, hh=hh: v.tensor_copy(rn_rows[0:1, hh, :], rn_f[0:1, hh, :]))
                    evr = p.mark("v")
                    bb1 = p.t(lambda t: t.matmul(pF[0:64, :], cs["c_ones_rowbf"][0:1, 0:64], rn_rows[0:1, 0, :],
                                                 start=True, stop=True, tile_position=(0, 0)), wait=(evr,), inc=True)
                    bb2 = p.t(lambda t: t.matmul(pF[64:128, :], cs["c_ones_rowbf"][0:1, 0:64], rn_rows[0:1, 1, :],
                                                 start=True, stop=True, tile_position=(0, 64), skip_group_check=True), inc=True)
                    p.v(lambda v: v.tensor_tensor(qro[:], qro[:], pF[:], ALU.mult), wait=(bb2, sc2))
                    fin = p.v(lambda v, buf=buf, stage=stage, scol=scol: v.tensor_scalar(
                        stage[:, buf, :], qro[:], scol[:], None, ALU.mult),
                        wait=((prev_store[buf],) if prev_store[buf] else ()), inc=True)
                    prev_store[buf] = p.y(lambda y, buf=buf, co=co, ts=ts, stage=stage, out_hbm=out_hbm:
                        y.dma_start(out=out_hbm.rearrange("p (c t) -> p c t", c=8)[:, co, ts],
                                    in_=stage[:, buf, :]), wait=(fin,), inc=True)
        ld_wv = _ld_fm(p, wv_h, wq_sb, cast=True, wait=(p.mark("t"),))
        prev_pf = p.mark("v")
        prev_vst = {0: None, 1: None}
        for tt in range(16):
            buf = tt % 2
            ev_half = []
            for half in range(2):
                for ci in range(8):
                    p.t(lambda t, tt=tt, half=half, ci=ci: t.matmul(
                        pF[:], xnb[:, ci, 128 * tt:128 * (tt + 1)], wq_sb[:, ci, 512 * half:512 * (half + 1)],
                        start=(ci == 0), stop=(ci == 7)), wait=(ld_wv,) + ((prev_pf,) if prev_pf else ()))
                mm = p.mark("t")
                ev_half.append(p.v(lambda v, buf=buf, half=half: v.tensor_copy(
                    vstage[:, buf, 512 * half:512 * (half + 1)], pF[:]),
                    wait=(mm,) + ((prev_vst[buf],) if prev_vst[buf] else ()), inc=True))
                prev_pf = ev_half[-1]
            ev_st = p.y(lambda y, tt=tt, buf=buf: y.dma_start(
                out=v_o.rearrange("p (c t) -> p c t", c=16)[:, tt, :], in_=vstage[:, buf, :]),
                wait=(ev_half[-1],), inc=True)
            prev_vst[buf] = ev_st
        ev_store.append(ev_st)




def emit_attn1(nc, p, io, cs, ld_c):
    q_h = io["q1"]
    k_h = io["k1full"]
    v_h = io["v1full"]
    wo_h = io["wo_t"]

    s1_o = io["s1"]

    ctx = ExitStack()
    _pfx = "ph%d_" % _PHASE_N[0]
    _PHASE_N[0] += 1
    S = lambda n, sh, dt=F32: ctx.enter_context(nc.sbuf_tensor(_pfx + n, sh, dt))
    PS = lambda n, sh, dt=F32: ctx.enter_context(nc.psum_tensor(_pfx + n, sh, dt))
    with ctx:
        qfm = S("qfm", [128, 8, T], BF16)
        kbuf = S("kbuf", [128, 2, SEQ], BF16)
        vbuf = S("vbuf", [128, 2, 64 * 128], BF16)
        e_sb = S("e_sb", [128, 2, 2, 1536], BF16)
        y1 = S("y1", [128, 8, T], BF16)
        wo_sb = S("wo_sb", [128, 8, D], BF16)
        sstage = S("sstage", [128, 2, 512])
        Mt = S("Mt", [128, 2, 512])
        Mt2 = S("Mt2", [128, 2, 1024])
        mcols = S("mcols", [128, 8])
        negm = S("negm", [1, 2, 512], BF16)
        rz = S("rz", [1, 2, 512], BF16)
        rzf = S("rzf", [1, 2, 512])

        pAll = PS("pAll", [128, 4096])

        def pa1(j):
            return pAll[:, 512 * j:512 * (j + 1)]

        def pb1(j):
            return pAll[:, 2048 + 512 * j:2048 + 512 * (j + 1)]

        pSa = pAll[:, 0:1536]
        pSb = pAll[:, 1536:3072]
        pAV = pAll[:, 3072:3584]
        pZ = pAll[:, 3584:4096]

        ones_row = cs["c_ones_1x128"]
        ones_rowbf = cs["c_ones_rowbf"]
        ones_colbf = cs["c_ones_colbf"]
        ident = cs["c_ident_f32"]

        ld_q = p.y(lambda y: y.dma_start(out=qfm[:], in_=q_h.rearrange("p (c t) -> p c t", c=8)), inc=True)
        ld_wo = _ld_fm(p, wo_h, wo_sb, cast=True)

        NG = 22
        GS = [min(3, 64 - 3 * g) for g in range(NG)]
        ld_k = {}
        ld_v = {}
        for hp in range(2):
            ld_k[hp] = p.y(lambda y, hp=hp: y.dma_start(
                out=kbuf[:, hp % 2, :].rearrange("p (r t) -> p r t", r=4), in_=k_h[:, hp]), inc=True)
            ld_v[hp] = p.g(lambda g, hp=hp: g.dma_start(
                out=vbuf[:, hp % 2, :], in_=v_h[hp]), inc=True, dma=True)

        prev_evac = None
        prev_c1 = None
        prev_expa_l = None
        prev_exp_l = None
        pend = [None]
        for hp in range(8):
            kb = hp % 2
            ld_k_ev = ld_k[hp]
            ld_v_ev = ld_v[hp]
            for q4 in range(4):
                qsl = slice(512 * q4, 512 * (q4 + 1))
                ev_mp = {(0, 0): None, (0, 1): None, (1, 0): None, (1, 1): None}
                for g2 in range(32):
                    par = g2 % 2
                    if g2 == 1 and pend[0] is not None:
                        prev_evac, prev_c1 = pend[0]()
                        pend[0] = None
                    for j in range(2):
                        ks = 2 * g2 + j
                        if g2 == 0:
                            wA = ((ld_k_ev, ld_q) + ((prev_expa_l,) if prev_expa_l else ())) if j == 0 else ()
                            wB = ((prev_exp_l,) if prev_exp_l else ()) if j == 0 else ()
                        elif g2 == 1:
                            wA = ()
                            wB = ((prev_c1,) if (j == 0 and prev_c1) else ()) + \
                                 ((prev_evac,) if (j == 1 and prev_evac) else ())
                        else:
                            wA = (ev_mp[(0, par)],) if (j == 0 and ev_mp[(0, par)]) else ()
                            wB = (ev_mp[(1, par)],) if (j == 0 and ev_mp[(1, par)]) else ()
                        p.t(lambda t, ks=ks, j=j, kb=kb, hp=hp, qsl=qsl, par=par: t.matmul(
                            pa1(2 * par + j), kbuf[0:64, kb, 128 * ks:128 * (ks + 1)],
                            qfm[0:64, hp, qsl], start=True, stop=True, tile_position=(0, 0)), wait=wA)
                        p.t(lambda t, ks=ks, j=j, kb=kb, hp=hp, qsl=qsl, par=par: t.matmul(
                            pb1(2 * par + j), kbuf[64:128, kb, 128 * ks:128 * (ks + 1)],
                            qfm[64:128, hp, qsl], start=True, stop=True, tile_position=(64, 0)), wait=wB)
                    mg2 = p.mark("t")
                    for hh in range(2):
                        base = 1024 * par if hh == 0 else 2048 + 1024 * par
                        psl = pAll[:, base:base + 1024]
                        if g2 == 0:
                            p.v(lambda v, hh=hh, psl=psl: v.tensor_copy(Mt2[:, hh, :], psl), wait=(mg2,))
                        else:
                            p.v(lambda v, hh=hh, psl=psl: v.tensor_tensor(
                                Mt2[:, hh, :], Mt2[:, hh, :], psl, ALU.max), wait=(mg2,))
                        ev_mp[(hh, par)] = p.mark("v")
                p.v(lambda v: v.tensor_tensor(Mt[:, 0, :], Mt2[:, 0, 0:512], Mt2[:, 0, 512:1024], ALU.max))
                ev_mfa = p.mark("v")
                p.v(lambda v: v.tensor_tensor(Mt[:, 1, :], Mt2[:, 1, 0:512], Mt2[:, 1, 512:1024], ALU.max))
                ev_mfb = p.mark("v")
                w0 = (ev_mp[(0, 0)], ev_mp[(0, 1)], ev_mp[(1, 0)])
                for j in range(GS[0]):
                    p.t(lambda t, j=j, kb=kb, hp=hp, qsl=qsl: t.matmul(
                        pSa[:, 512 * j:512 * (j + 1)], kbuf[0:64, kb, 128 * j:128 * (j + 1)],
                        qfm[0:64, hp, qsl], start=True, stop=False, tile_position=(0, 0)),
                        wait=w0 if j == 0 else ())
                    p.t(lambda t, j=j, kb=kb, hp=hp, qsl=qsl: t.matmul(
                        pSb[:, 512 * j:512 * (j + 1)], kbuf[64:128, kb, 128 * j:128 * (j + 1)],
                        qfm[64:128, hp, qsl], start=True, stop=False, tile_position=(64, 0)))
                tr1 = None
                for c in range(8):
                    hh, j = c // 4, c % 4
                    w = (ev_mfa, ev_mfb) if c == 0 else ()
                    tr1 = p.t(lambda t, c=c, hh=hh, j=j: t.transpose(
                        pAll[:, 3072 + 128 * c:3072 + 128 * (c + 1)], Mt[:, hh, 128 * j:128 * (j + 1)], ident[:]),
                        wait=w, inc=(c == 7))
                rd = None
                for c in range(8):
                    rd = p.v(lambda v, c=c: v.reduce_max(mcols[:, c:c + 1],
                                                         pAll[:, 3072 + 128 * c:3072 + 128 * (c + 1)], axis=AX.X),
                             wait=(tr1,) if c == 0 else (), inc=(c == 7))
                tr2 = None
                for c in range(8):
                    tr2 = p.t(lambda t, c=c: t.transpose(
                        pAll[0:1, 3072 + 128 * c:3072 + 128 * (c + 1)], mcols[:, c:c + 1], ident[:]),
                        wait=(rd,) if c == 0 else (), inc=(c == 7))
                ev_negm = p.v(lambda v: v.tensor_scalar(negm[:].rearrange("o h t -> o (h t)"),
                                                        pAll[0:1, 3072:4096], -1.0, None, ALU.mult),
                              wait=(tr2,), inc=True)
                ev_ea = {}
                ev_eb = {}
                ev_av = [None] * NG
                for g in range(NG):
                    gb = g % 2
                    qa0 = [None] * GS[g]
                    qb0 = [None] * GS[g]
                    if g == 0:
                        for j in range(GS[0]):
                            p.t(lambda t, j=j: t.matmul(
                                pSa[:, 512 * j:512 * (j + 1)], ones_rowbf[:], negm[0:1, 0, :],
                                start=False, stop=True), wait=(ev_negm,) if j == 0 else ())
                            qa0[j] = p.mark("t")
                            p.t(lambda t, j=j: t.matmul(
                                pSb[:, 512 * j:512 * (j + 1)], ones_rowbf[:], negm[0:1, 1, :],
                                start=False, stop=True))
                            qb0[j] = p.mark("t")
                    else:
                        for j in range(GS[g]):
                            ks = 3 * g + j
                            p.t(lambda t, ks=ks, j=j, kb=kb, hp=hp, qsl=qsl: t.matmul(
                                pSa[:, 512 * j:512 * (j + 1)], kbuf[0:64, kb, 128 * ks:128 * (ks + 1)],
                                qfm[0:64, hp, qsl], start=True, stop=False, tile_position=(0, 0)),
                                wait=(ev_ea[(g - 1, j)],) if j < GS[g - 1] else ())
                            p.t(lambda t, j=j: t.matmul(
                                pSa[:, 512 * j:512 * (j + 1)], ones_rowbf[:], negm[0:1, 0, :],
                                start=False, stop=True))
                            qa0[j] = p.mark("t")
                            p.t(lambda t, ks=ks, j=j, kb=kb, hp=hp, qsl=qsl: t.matmul(
                                pSb[:, 512 * j:512 * (j + 1)], kbuf[64:128, kb, 128 * ks:128 * (ks + 1)],
                                qfm[64:128, hp, qsl], start=True, stop=False, tile_position=(64, 0)),
                                wait=(ev_eb[(g - 1, j)],) if j < GS[g - 1] else ())
                            p.t(lambda t, j=j: t.matmul(
                                pSb[:, 512 * j:512 * (j + 1)], ones_rowbf[:], negm[0:1, 1, :],
                                start=False, stop=True))
                            qb0[j] = p.mark("t")
                    for j in range(GS[g]):
                        w_exp = (qa0[j],)
                        if g >= 2 and j == 0:
                            w_exp = w_exp + ((ev_av[g - 2],) if ev_av[g - 2] else ())
                        ev_ea[(g, j)] = p.s(lambda s, gb=gb, j=j: s.activation(
                            e_sb[:, gb, 0, 512 * j:512 * (j + 1)], pSa[:, 512 * j:512 * (j + 1)],
                            AF.Exp), wait=w_exp, inc=True)
                        ev_eb[(g, j)] = p.s(lambda s, gb=gb, j=j: s.activation(
                            e_sb[:, gb, 1, 512 * j:512 * (j + 1)], pSb[:, 512 * j:512 * (j + 1)],
                            AF.Exp), wait=(qb0[j],), inc=True)
                    for j in range(GS[g]):
                        ks = 3 * g + j
                        first = (g == 0 and j == 0)
                        last = (g == NG - 1 and j == GS[g] - 1)
                        p.t(lambda t, ks=ks, j=j, gb=gb, kb=kb, first=first, last=last: t.matmul(
                            pAV[0:64, :], vbuf[:, kb, :].rearrange("p (tt d) -> p tt d", d=128)[:, ks, 0:64],
                            e_sb[:, gb, 0, 512 * j:512 * (j + 1)],
                            start=first, stop=last, tile_position=(0, 0)),
                            wait=(ev_ea[(g, j)],) + ((ld_v_ev,) if j == 0 else ()))
                        p.t(lambda t, ks=ks, j=j, gb=gb, kb=kb, first=first, last=last: t.matmul(
                            pAV[64:128, :], vbuf[:, kb, :].rearrange("p (tt d) -> p tt d", d=128)[:, ks, 64:128],
                            e_sb[:, gb, 1, 512 * j:512 * (j + 1)],
                            start=first, stop=last, tile_position=(0, 64), skip_group_check=True),
                            wait=(ev_eb[(g, j)],))
                        p.t(lambda t, j=j, gb=gb, first=first, last=last: t.matmul(
                            pZ[0:1, :], ones_colbf[:], e_sb[:, gb, 0, 512 * j:512 * (j + 1)],
                            start=first, stop=last, tile_position=(0, 0)))
                        p.t(lambda t, j=j, gb=gb, first=first, last=last: t.matmul(
                            pZ[64:65, :], ones_colbf[:], e_sb[:, gb, 1, 512 * j:512 * (j + 1)],
                            start=first, stop=last, tile_position=(0, 64), skip_group_check=True))
                    ev_av[g] = p.mark("t")
                sb2 = q4 % 2
                c1 = p.v(lambda v, sb2=sb2: v.tensor_copy(sstage[:, sb2, :], pAV[:]), wait=(ev_av[NG - 1],), inc=True)
                p.v(lambda v: v.reciprocal(rzf[0:1, 0, :], pZ[0:1, :]))
                p.v(lambda v: v.reciprocal(rzf[0:1, 1, :], pZ[64:65, :]))
                rcast = p.v(lambda v: v.tensor_copy(rz[:].rearrange("o h t -> o (h t)"), rzf[:].rearrange("o h t -> o (h t)")), inc=True)
                prev_expa_l = ev_ea[(NG - 1, 0)]
                prev_exp_l = ev_eb[(NG - 1, 0)]

                def mk_pend(sb2=sb2, hp=hp, qsl=qsl, rcast=rcast, c1=c1):
                    def emit():
                        p.t(lambda t: t.matmul(pZ[0:64, :], ones_rowbf[0:1, 0:64], rz[0:1, 0, :],
                                               start=True, stop=True, tile_position=(0, 0)),
                            wait=(rcast,))
                        b2 = p.t(lambda t: t.matmul(pZ[64:128, :], ones_rowbf[0:1, 0:64], rz[0:1, 1, :],
                                                    start=True, stop=True, tile_position=(0, 64),
                                                    skip_group_check=True), inc=True)
                        ev = p.v(lambda v: v.tensor_tensor(
                            y1[:, hp, qsl], sstage[:, sb2, :], pZ[:, 0:512], ALU.mult), wait=(b2,), inc=True)
                        return ev, c1
                    return emit
                pend[0] = mk_pend()
            if hp + 2 <= 7:
                nhp = hp + 2
                lastuse = p.mark("t")
                ld_k[nhp] = p.y(lambda y, nhp=nhp: y.dma_start(
                    out=kbuf[:, nhp % 2, :].rearrange("p (r t) -> p r t", r=4), in_=k_h[:, nhp]),
                    wait=(lastuse,), inc=True)
                ld_v[nhp] = p.g(lambda g, nhp=nhp: g.dma_start(
                    out=vbuf[:, nhp % 2, :], in_=v_h[nhp]),
                    wait=(lastuse,), inc=True, dma=True)
        ev_y1, _ = pend[0]()
        pend[0] = None

        prev_pe = None
        prev_sst = {0: None, 1: None}
        for co in range(8):
            for tc in range(4):
                ts = slice(512 * tc, 512 * (tc + 1))
                sb2 = (co * 4 + tc) % 2
                for ci in range(8):
                    p.t(lambda t, ci=ci, co=co, ts=ts: t.matmul(
                        pAll[:, 0:512], wo_sb[:, ci, 128 * co:128 * (co + 1)], y1[:, ci, ts],
                        start=(ci == 0), stop=(ci == 7)),
                        wait=(ev_y1, ld_wo) + ((prev_pe,) if prev_pe else ()))
                mm = p.mark("t")
                prev_pe = p.v(lambda v, sb2=sb2: v.tensor_copy(sstage[:, sb2, :], pAll[:, 0:512]),
                              wait=(mm,) + ((prev_sst[sb2],) if prev_sst[sb2] else ()), inc=True)
                prev_sst[sb2] = p.y(lambda y, co=co, ts=ts, sb2=sb2: y.dma_start(
                    out=s1_o.rearrange("p (c t) -> p c t", c=8)[:, co, ts], in_=sstage[:, sb2, :]),
                    wait=(prev_pe,), inc=True)




def emit_ffn1(nc, p, io, cs, ld_c):
    x_h = io["x1"]
    xn_h = io["x1n"]
    s1_h = io["s1"]
    wmh_h = io["wmh_t"]
    wmg_h = io["wmg_t"]
    kf_h = io["kffwd_r"]
    vf_h = io["vffwd"]
    g_h = io["g1d"]

    out_o = io["outp"]

    ctx = ExitStack()
    _pfx = "ph%d_" % _PHASE_N[0]
    _PHASE_N[0] += 1
    S = lambda n, sh, dt=F32: ctx.enter_context(nc.sbuf_tensor(_pfx + n, sh, dt))
    PS = lambda n, sh, dt=F32: ctx.enter_context(nc.psum_tensor(_pfx + n, sh, dt))
    with ctx:
        xnb = S("xnb", [128, 8, T], BF16)
        qm = S("qm", [128, 8, T], BF16)
        y_fm = S("y_fm", [128, 8, T], BF16)
        w_sb = S("w_sb", [128, 8, D], BF16)
        kv_k = S("kv_k", [128, 2, 16, S1], BF16)
        kv_v = S("kv_v", [128, 2, 16, 2, D1], BF16)
        g_bc = S("g_bc", [128, 16, 512], BF16)
        g_row16 = S("g_row16", [1, 16, 512], BF16)
        gel = S("gel", [128, 2, 512], BF16)
        stg = S("stg", [128, 2, 512])
        x1st = S("x1st", [128, T])
        t1f = S("t1f", [128, T])

        pAll = PS("pAll", [128, 4096])

        ones_row = cs["c_ones_1x128"]

        ld_xn = p.y(lambda y: y.dma_start(out=xnb[:], in_=xn_h.rearrange("p (c t) -> p c t", c=8)), inc=True)
        ld_w = _ld_fm(p, wmh_h, w_sb, cast=True)

        prev_pe = {0: None, 1: None}
        idx = 0
        for co in range(8):
            for tc in range(4):
                ts = slice(512 * tc, 512 * (tc + 1))
                bk = idx % 2
                base = 2048 + 512 * bk
                for ci in range(8):
                    w = ((ld_xn, ld_w) if idx == 0 and ci == 0 else ()) +                         ((prev_pe[bk],) if (ci == 0 and prev_pe[bk]) else ())
                    p.t(lambda t, ci=ci, co=co, ts=ts, base=base: t.matmul(
                        pAll[:, base:base + 512], w_sb[:, ci, 128 * co:128 * (co + 1)], xnb[:, ci, ts],
                        start=(ci == 0), stop=(ci == 7)), wait=w)
                mm = p.mark("t")
                prev_pe[bk] = p.v(lambda v, co=co, ts=ts, base=base: v.tensor_copy(
                    qm[:, co, ts], pAll[:, base:base + 512]), wait=(mm,), inc=True)
                idx += 1
        ev_qm = p.mark("v")

        prev_kv = {0: None, 1: None}
        prev_y = None
        prev_bg = None
        gcp = {2560: ev_qm, 3584: ev_qm}
        for tc in range(4):
            ts = slice(512 * tc, 512 * (tc + 1))
            lgs = [None] * NE1
            for e in range(NE1):
                lgs[e] = p.g(lambda g, e=e, ts=ts: g.dma_start(out=g_row16[0:1, e, :], in_=g_h[e:e + 1, ts]),
                             wait=(prev_bg,) if (e == 0 and prev_bg) else (), inc=True, dma=True)
            for e in range(NE1):
                base = 3584 if e % 2 == 0 else 2560
                bg = p.t(lambda t, e=e, base=base: t.matmul(
                    pAll[:, base:base + 512], cs["c_ones_rowbf"][:], g_row16[0:1, e, :],
                    start=True, stop=True), wait=(lgs[e],) + ((gcp[base],) if gcp[base] else ()), inc=True)
                gcp[base] = p.v(lambda v, e=e, base=base: v.tensor_copy(g_bc[:, e, :], pAll[:, base:base + 512]),
                                wait=(bg,), inc=True)
                if e == NE1 - 1:
                    prev_bg = bg
            ev_gbc = p.mark("v")
            for h in range(H1):
                hb = h % 2
                lk = p.g(lambda g, h=h, hb=hb: g.dma_start(
                    out=kv_k[:, hb], in_=kf_h[h].rearrange("e p s -> p e s")),
                    wait=(prev_kv[hb],) if prev_kv[hb] else (), inc=True, dma=True)
                lv = p.g(lambda g, h=h, hb=hb: g.dma_start(
                    out=kv_v[:, hb], in_=vf_h[h].rearrange("e (sc p) d -> p e sc d", p=128)),
                    inc=True, dma=True)
                ev_exp = {}
                for e in range(NE1):
                    sbk = e % 2
                    for sc in range(2):
                        w = (lk, ev_gbc) if (e == 0 and sc == 0) else ()
                        if e >= 2 and sc == 0:
                            w = w + (ev_exp[e - 2],)
                        p.t(lambda t, e=e, sc=sc, h=h, hb=hb, ts=ts, sbk=sbk: t.matmul(
                            pAll[:, 1024 * sbk + 512 * sc:1024 * sbk + 512 * (sc + 1)],
                            kv_k[:, hb, e, 128 * sc:128 * (sc + 1)], qm[:, h, ts],
                            start=True, stop=True), wait=w)
                    msc = p.mark("t")
                    ge1 = p.s(lambda s, sbk=sbk: s.activation(
                        gel[:, 0, :], pAll[:, 1024 * sbk:1024 * sbk + 512], AF.Gelu), wait=(msc,), inc=True)
                    ge2 = p.s(lambda s, sbk=sbk: s.activation(
                        gel[:, 1, :], pAll[:, 1024 * sbk + 512:1024 * sbk + 1024], AF.Gelu), inc=True)
                    p.v(lambda v, e=e: v.tensor_tensor(gel[:, 0, :], gel[:, 0, :], g_bc[:, e, :], ALU.mult), wait=(ge1,))
                    ev_exp[e] = p.v(lambda v, e=e: v.tensor_tensor(gel[:, 1, :], gel[:, 1, :], g_bc[:, e, :], ALU.mult), wait=(ge2,), inc=True)
                    for sc in range(2):
                        first = (e == 0 and sc == 0)
                        last = (e == NE1 - 1 and sc == 1)
                        w = ((ev_exp[e],) if sc == 0 else ()) + ((lv, prev_y) if (first and prev_y) else ((lv,) if first else ()))
                        p.t(lambda t, e=e, sc=sc, hb=hb, first=first, last=last: t.matmul(
                            pAll[:, 3072:3584], kv_v[:, hb, e, sc, :], gel[:, sc, :],
                            start=first, stop=last), wait=w)
                my = p.mark("t")
                prev_y = p.v(lambda v, h=h, ts=ts: v.tensor_copy(y_fm[:, h, ts], pAll[:, 3072:3584]), wait=(my,), inc=True)
                prev_kv[hb] = my
        ev_yfm = prev_y

        ld_wg = _ld_fm(p, wmg_h, w_sb, cast=True, wait=(p.mark("t"),))
        prev_st = None
        prev_sil = {0: None, 1: None}
        prev_ost = {0: None, 1: None}
        idx = 0
        for co in range(8):
            for tc in range(4):
                ts = slice(512 * tc, 512 * (tc + 1))
                sb2 = idx % 2
                base = 512 * sb2
                for ci in range(8):
                    w = ((ev_yfm, ld_wg) if idx == 0 and ci == 0 else ()) +                         ((prev_sil[sb2],) if (ci == 0 and prev_sil[sb2]) else ())
                    p.t(lambda t, ci=ci, co=co, ts=ts, base=base: t.matmul(
                        pAll[:, base:base + 512], w_sb[:, ci, 128 * co:128 * (co + 1)], y_fm[:, ci, ts],
                        start=(ci == 0), stop=(ci == 7)), wait=w)
                mm = p.mark("t")
                lx = p.y(lambda y, co=co, ts=ts: y.dma_start(
                    out=t1f[:, 0:512], in_=x_h.rearrange("p (c t) -> p c t", c=8)[:, co, ts]),
                    wait=(prev_st,) if prev_st else (), inc=True)
                ls = p.y(lambda y, co=co, ts=ts: y.dma_start(
                    out=x1st[:, 0:512], in_=s1_h.rearrange("p (c t) -> p c t", c=8)[:, co, ts]), inc=True)
                gg = p.s(lambda s, sb2=sb2, base=base: s.activation(stg[:, sb2, :], pAll[:, base:base + 512], AF.Silu),
                         wait=(mm,) + ((prev_ost[sb2],) if prev_ost[sb2] else ()), inc=True)
                prev_sil[sb2] = gg
                p.v(lambda v, sb2=sb2: v.tensor_tensor(stg[:, sb2, :], stg[:, sb2, :], x1st[:, 0:512], ALU.mult), wait=(gg, ls))
                p.v(lambda v, sb2=sb2: v.tensor_tensor(stg[:, sb2, :], stg[:, sb2, :], t1f[:, 0:512], ALU.add), wait=(lx,))
                ev_o = p.mark("v")
                prev_st = p.y(lambda y, co=co, ts=ts, sb2=sb2: y.dma_start(
                    out=out_o.rearrange("p (c t) -> p c t", c=8)[:, co, ts], in_=stg[:, sb2, :]),
                    wait=(ev_o,), inc=True)
                prev_ost[sb2] = prev_st
                idx += 1




ALL_CONSTS = ["c_ones_1x128", "c_ones_col", "c_ltriu", "c_rev64",
              "c_ident_f32", "c_rev16", "c_ones_colbf", "c_ones_rowbf"]


def build_fused(upto=99, dbg=False):
    nc = _nc()
    EI = lambda n, sh, dt=F32: nc.dram_tensor(n, sh, dt, kind="ExternalInput").ap()
    IN = lambda n, sh, dt=F32: nc.dram_tensor(n, sh, dt, kind="Internal").ap()

    x_fm = EI("x_fm", [D, T])
    wq_t = EI("wq_t", [D, D]); wk_t = EI("wk_t", [D, D]); wv_t = EI("wv_t", [D, D])
    gate_w = EI("gate_w", [D, NE0])
    core_mask = EI("core_mask", [8, 1])
    sgu_pairs = EI("sgu_pairs", [D, 16 * 256]); sdown = EI("sdown", [2 * D, D])
    sgate = EI("sgate", [128, 8])
    egu = EI("egu", [NE0, D, D]); edw = EI("edw", [NE0, CAP0, D])
    wq1_t = EI("wq1_t", [D, D]); wk1_t = EI("wk1_t", [D, D]); wv1_t = EI("wv1_t", [D, D])
    wo_t = EI("wo_t", [D, D]); sqk_t = EI("sqk_t", [64, NH])
    wr1_t = EI("wr1_t", [D, NE1])
    rope_cos = EI("rope_cos", [128, T]); rope_sin = EI("rope_sin", [128, T])
    wmh_t = EI("wmh_t", [D, D]); wmg_t = EI("wmg_t", [D, D])
    kffwd_r = EI("kffwd_r", [H1, NE1, D1, S1]); vffwd = EI("vffwd", [H1, NE1, S1, D1])

    out_o = nc.dram_tensor("outp", [128, 8 * T], F32, kind="ExternalOutput").ap()

    DI = (lambda n, sh, dt=F32: nc.dram_tensor(n, sh, dt, kind="ExternalOutput").ap()) if dbg else IN
    xn0 = DI("xn0", [128, 8 * T], BF16)
    q0 = DI("q0", [128, 8 * T], BF16)
    k0 = IN("k0", [128, 8 * T], BF16)
    v0 = IN("v0", [128, 16 * D], BF16)
    qn2 = DI("qn2", [NH, T])
    kmax2 = IN("kmax2", [1, NH])
    wd = DI("wd", [128, 16 * NE0])
    pos = DI("pos", [128, 16 * NE0])
    cnt = IN("cnt", [1, NE0])
    s0 = DI("s0", [128, 8 * T])
    x1 = DI("x1", [128, 8 * T])
    x1n = DI("x1n", [128, 8 * T], BF16)
    q1 = DI("q1", [128, 8 * T], BF16)
    k1 = IN("k1", [128, 8 * T], BF16)
    v1 = IN("v1", [128, 16 * D], BF16)
    g1d = DI("g1d", [NE1, T])
    s1 = DI("s1", [128, 8 * T])
    k0_ag = IN("k0_ag", [4 * 128, 8 * T], BF16)
    v0_ag = IN("v0_ag", [4 * 128, 16 * D], BF16)
    km_ag = IN("km_ag", [4, NH])
    cnt_ag = IN("cnt_ag", [8, NE0])
    v0_hp = IN("v0_hp", [8, 128, 64 * 128], BF16)
    k1_ag = IN("k1_ag", [4 * 128, 8 * T], BF16)
    v1_ag = IN("v1_ag", [4 * 128, 16 * D], BF16)
    v1_hp = IN("v1_hp", [8, 128, 64 * 128], BF16)

    GROUPS4 = [[0, 1, 2, 3], [4, 5, 6, 7]]
    GROUPS8 = [list(range(8))]

    ctx = ExitStack()
    with ctx:
        p = Prog(nc)
        S = lambda n, sh, dt=F32: ctx.enter_context(nc.sbuf_tensor(n, sh, dt))
        cs = {}
        ld_c = None
        for n in ALL_CONSTS:
            sh, dt = CONST_SHAPES[n]
            ap = nc.dram_tensor(n, sh, dt, kind="ExternalInput").ap()
            tl = S(n + "_sb", list(sh), dt)
            cs[n] = tl
            ld_c = p.y(lambda y, tl=tl, ap=ap: y.dma_start(out=tl[:], in_=ap), inc=True)

        emit_pre0(nc, p, {"x_fm": x_fm, "wq_t": wq_t, "wk_t": wk_t, "wv_t": wv_t,
                          "gate_w": gate_w, "xn0": xn0, "q0": q0, "k0": k0, "v0": v0,
                          "qn2": qn2, "kmax2": kmax2, "wd": wd, "pos": pos, "cnt": cnt},
                  cs, ld_c)
        p.barrier()
        if upto < 2:
            p.final_wait()
            p.emit()
            return nc
        p.g(lambda g: g.collective_compute("AllGather", ALU.bypass, replica_groups=GROUPS4,
                                           ins=[k0.opt()], outs=[k0_ag.opt()]))
        agv0 = p.g(lambda g: g.collective_compute("AllGather", ALU.bypass, replica_groups=GROUPS4,
                                           ins=[v0.opt()], outs=[v0_ag.opt()]))
        p.g(lambda g: g.collective_compute("AllGather", ALU.bypass, replica_groups=GROUPS4,
                                           ins=[kmax2.opt()], outs=[km_ag.opt()]))
        p.g(lambda g: g.collective_compute("AllGather", ALU.bypass, replica_groups=GROUPS8,
                                           ins=[cnt.opt()], outs=[cnt_ag.opt()]))
        for _h in range(8):
            for _r in range(4):
                p.g(lambda g, _h=_h, _r=_r: g.dma_start(
                    out=v0_hp.rearrange("h p (r tt x) -> h p r (tt x)", r=4, x=128)[_h][:, _r].rearrange("p (tt x) -> p tt x", x=128),
                    in_=v0_ag.rearrange("(r p) (tt h x) -> r h p tt x", p=128, h=8, x=128)[_r][_h]),
                    wait=(agv0,), inc=(_h == 7 and _r == 3), dma=True)
        p.barrier()
        if upto < 3:
            p.final_wait()
            p.emit()
            return nc
        emit_attn0(nc, p, {"q0": q0,
                           "kfull": k0_ag.rearrange("(r p) (c t) -> p c r t", p=128, c=8),
                           "vfull": v0_hp, "qn2": qn2,
                           "kmax2g": km_ag.rearrange("r h -> (r h)").rearrange("(o x) -> o x", o=1),
                           "s0": s0}, cs, ld_c)
        p.barrier()
        if upto < 4:
            p.final_wait()
            p.emit()
            return nc
        emit_moe0(nc, p, {"xn0": xn0, "x_fm": x_fm, "s0": s0, "wd": wd, "pos": pos,
                          "cnt_all": cnt_ag, "core_mask": core_mask,
                          "sgu_pairs": sgu_pairs, "sdown": sdown, "sgate": sgate,
                          "egu": egu, "edw": edw, "x1": x1}, cs, ld_c)
        p.barrier()
        if upto < 5:
            p.final_wait()
            p.emit()
            return nc
        emit_pre1(nc, p, {"x1": x1, "wq1_t": wq1_t, "wk1_t": wk1_t, "wv1_t": wv1_t,
                          "wr1_t": wr1_t, "sqk_t": sqk_t, "rope_cos": rope_cos,
                          "rope_sin": rope_sin, "x1n": x1n, "q1": q1, "k1": k1,
                          "v1": v1, "g1d": g1d}, cs, ld_c)
        p.barrier()
        if upto < 6:
            p.final_wait()
            p.emit()
            return nc
        p.g(lambda g: g.collective_compute("AllGather", ALU.bypass, replica_groups=GROUPS4,
                                           ins=[k1.opt()], outs=[k1_ag.opt()]))
        agv1 = p.g(lambda g: g.collective_compute("AllGather", ALU.bypass, replica_groups=GROUPS4,
                                           ins=[v1.opt()], outs=[v1_ag.opt()]))
        for _h in range(8):
            for _r in range(4):
                p.g(lambda g, _h=_h, _r=_r: g.dma_start(
                    out=v1_hp.rearrange("h p (r tt x) -> h p r (tt x)", r=4, x=128)[_h][:, _r].rearrange("p (tt x) -> p tt x", x=128),
                    in_=v1_ag.rearrange("(r p) (tt h x) -> r h p tt x", p=128, h=8, x=128)[_r][_h]),
                    wait=(agv1,), inc=(_h == 7 and _r == 3), dma=True)
        p.barrier()
        if upto < 7:
            p.final_wait()
            p.emit()
            return nc
        emit_attn1(nc, p, {"q1": q1,
                           "k1full": k1_ag.rearrange("(r p) (c t) -> p c r t", p=128, c=8),
                           "v1full": v1_hp, "wo_t": wo_t, "s1": s1}, cs, ld_c)
        p.barrier()
        if upto < 8:
            p.final_wait()
            p.emit()
            return nc
        emit_ffn1(nc, p, {"x1": x1, "x1n": x1n, "s1": s1, "wmh_t": wmh_t, "wmg_t": wmg_t,
                          "kffwd_r": kffwd_r, "vffwd": vffwd, "g1d": g1d, "outp": out_o},
                  cs, ld_c)
        p.final_wait()
        p.emit()
    return nc


def kernel(**inputs):
    consts = _consts_np()
    x = np.asarray(inputs["L0_attn_x"], dtype=np.float32)
    wqkv = np.asarray(inputs["L0_attn_W_qkv"], dtype=np.float32)
    gate_w = np.ascontiguousarray(np.asarray(inputs["L0_ffn_gate_w"], dtype=np.float32))
    sgw = np.asarray(inputs["L0_ffn_shared_gate_w"], dtype=np.float32)
    sguw = np.asarray(inputs["L0_ffn_shared_gate_up_w"], dtype=np.float32)
    sdw = np.ascontiguousarray(np.asarray(inputs["L0_ffn_shared_down_w"], dtype=np.float32))
    eguw = np.ascontiguousarray(np.asarray(inputs["L0_ffn_expert_gate_up_w"], dtype=np.float32))
    edww = np.ascontiguousarray(np.asarray(inputs["L0_ffn_expert_down_w"], dtype=np.float32))
    wq1 = np.asarray(inputs["L1_attn_Wq"], dtype=np.float32)
    wk1 = np.asarray(inputs["L1_attn_Wk"], dtype=np.float32)
    wv1 = np.asarray(inputs["L1_attn_Wv"], dtype=np.float32)
    wo1 = np.asarray(inputs["L1_attn_Wo"], dtype=np.float32)
    sqk = np.asarray(inputs["L1_attn_s_qk"], dtype=np.float32)
    wr1 = np.asarray(inputs["L1_ffn_w_router"], dtype=np.float32)
    wmh = np.asarray(inputs["L1_ffn_w_mh"], dtype=np.float32)
    wmg = np.asarray(inputs["L1_ffn_w_mg"], dtype=np.float32)
    kff = np.asarray(inputs["L1_ffn_k_ffwd"], dtype=np.float32)
    vff = np.ascontiguousarray(np.asarray(inputs["L1_ffn_v_ffwd"], dtype=np.float32))

    x_parts = _host_split_fm(x)
    sgu_pairs = np.zeros((D, 16, 256), dtype=np.float32)
    for i in range(16):
        sgu_pairs[:, i, 0:128] = sguw[:, 128 * i:128 * (i + 1)]
        sgu_pairs[:, i, 128:256] = sguw[:, 2048 + 128 * i:2048 + 128 * (i + 1)]
    shared = {
        "wq_t": np.ascontiguousarray(wqkv[0:D].T),
        "wk_t": np.ascontiguousarray(wqkv[D:2 * D].T),
        "wv_t": np.ascontiguousarray(wqkv[2 * D:3 * D].T),
        "gate_w": gate_w,
        "sgu_pairs": np.ascontiguousarray(sgu_pairs.reshape(D, 16 * 256)),
        "sdown": sdw,
        "sgate": np.ascontiguousarray(sgw.reshape(8, 128).T),
        "egu": eguw, "edw": edww,
        "wq1_t": np.ascontiguousarray(wq1.T),
        "wk1_t": np.ascontiguousarray(wk1.T),
        "wv1_t": np.ascontiguousarray(wv1.T),
        "wo_t": np.ascontiguousarray(wo1.T),
        "sqk_t": np.ascontiguousarray(sqk.T),
        "wr1_t": np.ascontiguousarray(wr1.T),
        "wmh_t": np.ascontiguousarray(wmh.T),
        "wmg_t": np.ascontiguousarray(wmg.T),
        "kffwd_r": np.ascontiguousarray(kff.transpose(0, 1, 3, 2)),
        "vffwd": vff,
    }
    for kname in ALL_CONSTS:
        shared[kname] = consts[kname]

    ncf = build_fused()
    in_maps = []
    for c in range(NCORES):
        cos_t, sin_t = _rope_tables(c)
        m = dict(shared)
        m["x_fm"] = x_parts[c]
        m["core_mask"] = np.ascontiguousarray((np.arange(8) < c).astype(np.float32).reshape(8, 1))
        m["rope_cos"] = cos_t
        m["rope_sin"] = sin_t
        in_maps.append(m)
    res = run_bass_kernel_spmd(ncf, in_maps, core_ids=list(range(NCORES)))
    out = np.empty((2 * SEQ, D), dtype=np.float32)
    for c in range(NCORES):
        out[c * T:(c + 1) * T] = res.results[c]["outp"].reshape(128, 8, T).transpose(2, 1, 0).reshape(T, D)
    return out.reshape(2, SEQ, D)


# version 16
# speedup vs baseline: 1.1921x; 1.1921x over previous
import numpy as np
import ml_dtypes
from contextlib import ExitStack

import concourse.bass as bass
import concourse.mybir as mybir
from concourse.bass_utils import run_bass_kernel_spmd

F32 = mybir.dt.float32
BF16 = mybir.dt.bfloat16
AF = mybir.ActivationFunctionType
ALU = mybir.AluOpType
AX = mybir.AxisListType
BF = ml_dtypes.bfloat16

NCORES = 8
T = 2048
D = 1024
NH = 16
HD = 64
SEQ = 8192
NE0 = 64
CAP0 = 512
NE1 = 16
H1 = 8
D1 = 128
S1 = 256


class Prog:

    ENGS = ("t", "v", "s", "g", "y")
    SEMS = ("t", "v", "s", "g", "y", "gd", "yd")

    def __init__(self, nc):
        self.nc = nc
        self.streams = {e: [] for e in self.ENGS}
        self.cnt = {e: 0 for e in self.SEMS}

    CHAINED = ("v", "s", "g")

    def op(self, eng, fn, wait=(), inc=False, dma=False):
        ev = None
        wait = tuple(w for w in wait if w is not None)
        if not dma and eng in self.CHAINED:
            if self.cnt[eng] > 0:
                wait = wait + ((eng, self.cnt[eng]),)
            inc = True
        if dma:
            self.cnt[eng + "d"] += 16
            self.streams[eng].append([fn, wait, (eng + "d", self.cnt[eng + "d"]), True])
            if inc:
                return self.dma_ev(eng)
            return None
        if inc:
            self.cnt[eng] += 1
            ev = (eng, self.cnt[eng])
        self.streams[eng].append([fn, wait, ev, False])
        return ev

    def mark(self, eng):
        for entry in reversed(self.streams[eng]):
            fn, wait, ev, dma = entry
            if fn is None or fn == "MILESTONE":
                continue
            if ev is not None:
                return ev
            self.cnt[eng] += 1
            entry[2] = (eng, self.cnt[eng])
            return entry[2]
        return None

    def dma_ev(self, eng):
        semd = eng + "d"
        self.streams[eng].append([None, ((semd, self.cnt[semd]),), None, False])
        self.cnt[eng] += 1
        ev = (eng, self.cnt[eng])
        self.streams[eng].append(["MILESTONE", (), ev, False])
        return ev

    def t(self, fn, wait=(), inc=False):
        return self.op("t", fn, wait, inc)

    def v(self, fn, wait=(), inc=False):
        return self.op("v", fn, wait, inc)

    def s(self, fn, wait=(), inc=False):
        return self.op("s", fn, wait, inc)

    def g(self, fn, wait=(), inc=False, dma=False):
        return self.op("g", fn, wait, inc, dma)

    def y(self, fn, wait=(), inc=False, dma=True):
        return self.op("y", fn, wait, inc, dma)

    def barrier(self):
        evs = []
        for e in self.ENGS:
            if self.cnt.get(e + "d", 0) > 0:
                evs.append(self.dma_ev(e))
            self.mark(e)
            if self.cnt[e] > 0:
                evs.append((e, self.cnt[e]))
        for e in self.ENGS:
            self.streams[e].append([None, tuple(evs), None, False])
        return evs

    def emit(self):
        nc = self.nc
        ctx = ExitStack()
        sems = {e: ctx.enter_context(nc.semaphore(f"sem_{e}")) for e in self.SEMS}
        sdummy = ctx.enter_context(nc.sbuf_tensor("prog_sdummy", [1, 8], F32))
        with ctx:
            blk = ctx.enter_context(nc.Block())

            def run_stream(eng_name):
                def body(h):
                    waited = {}
                    for fn, wait, ev, dma in self.streams[eng_name]:
                        for (sname, val) in wait:
                            if waited.get(sname, -1) >= val:
                                continue
                            h.wait_ge(sems[sname], val)
                            waited[sname] = val
                        if fn is None:
                            continue
                        if fn == "MILESTONE":
                            if eng_name in ("v", "g"):
                                ins = h.engine_nop()
                            elif eng_name == "s":
                                ins = h.activation(sdummy[:], sdummy[:], AF.Copy)
                            else:
                                ins = h.nop()
                            ins.then_inc(sems[eng_name], 1)
                            continue
                        ins = fn(h)
                        if ev is not None:
                            ins.then_inc(sems[eng_name + "d"] if dma else sems[eng_name], 16 if dma else 1)
                return body

            blk.sync(run_stream("y"))
            blk.gpsimd(run_stream("g"))
            blk.tensor(run_stream("t"))
            blk.vector(run_stream("v"))
            blk.scalar(run_stream("s"))
        return nc

    def final_wait(self):
        evs = []
        for e in ("y", "g"):
            if self.cnt.get(e + "d", 0) > 0:
                evs.append((e + "d", self.cnt[e + "d"]))
        self.streams["y"].append([None, tuple(evs), None, False])


_PHASE_N = [0]


def _nc():
    return bass.Bass("TRN2", target_bir_lowering=False, debug=False)


CONST_SHAPES = {
    "c_ones_1x128": ([1, 128], F32),
    "c_ones_col": ([128, 1], F32),
    "c_ident_f32": ([128, 128], F32),
    "c_ident_bf": ([128, 128], BF16),
    "c_ltriu": ([128, 128], F32),
    "c_rev64": ([128, 64], F32),
    "c_rev16": ([128, 16], F32),
    "c_ones_colbf": ([128, 1], BF16),
    "c_ones_rowbf": ([1, 128], BF16),
}


def _consts_np():
    return {
        "c_ones_1x128": np.ones((1, 128), dtype=np.float32),
        "c_ones_col": np.ones((128, 1), dtype=np.float32),
        "c_ident_f32": np.eye(128, dtype=np.float32),
        "c_ident_bf": np.eye(128, dtype=BF),
        "c_ltriu": np.triu(np.ones((128, 128), dtype=np.float32), 1),
        "c_rev64": np.broadcast_to(64.0 - np.arange(64, dtype=np.float32), (128, 64)).copy(),
        "c_rev16": np.broadcast_to(16.0 - np.arange(16, dtype=np.float32), (128, 16)).copy(),
        "c_ones_colbf": np.ones((128, 1), dtype=BF),
        "c_ones_rowbf": np.ones((1, 128), dtype=BF),
    }


def _use_consts(nc, p, S, names):
    tiles = {}
    ev = None
    for n in names:
        sh, dt = CONST_SHAPES[n]
        ap = nc.dram_tensor(n, sh, dt, kind="ExternalInput").ap()
        tl = S(n + "_sb", list(sh), dt)
        tiles[n] = tl
        ev = p.y(lambda y, tl=tl, ap=ap: y.dma_start(out=tl[:], in_=ap), inc=True)
    return tiles, ev


def _ld_fm(p, hbm_ap, sbuf_t, cast=False, wait=()):
    src = hbm_ap.rearrange("(c p) t -> p c t", p=128)
    if cast:
        return p.g(lambda h: h.dma_start(out=sbuf_t[:], in_=src), wait=wait, inc=True, dma=True)
    return p.y(lambda h: h.dma_start(out=sbuf_t[:], in_=src), wait=wait, inc=True)



def emit_pre0(nc, p, io, cs, ld_c):
    x_h = io["x_fm"]
    wq_h = io["wq_t"]
    wk_h = io["wk_t"]
    wv_h = io["wv_t"]
    gw_h = io["gate_w"]

    xn_o = io["xn0"]
    q_o = io["q0"]
    k_o = io["k0"]
    v_o = io["v0"]
    qn2_o = io["qn2"]
    kmax_o = io["kmax2"]
    wd_o = io["wd"]
    pos_o = io["pos"]
    cnt_o = io["cnt"]

    ctx = ExitStack()
    _pfx = "ph%d_" % _PHASE_N[0]
    _PHASE_N[0] += 1
    S = lambda n, sh, dt=F32: ctx.enter_context(nc.sbuf_tensor(_pfx + n, sh, dt))
    PS = lambda n, sh, dt=F32: ctx.enter_context(nc.psum_tensor(_pfx + n, sh, dt))
    with ctx:
        x_sb = S("x_sb", [128, 8, T])
        xnb = S("xnb", [128, 8, T], BF16)
        xnf_t = S("xnf_t", [128, 8, 512])
        wq_sb = S("wq_sb", [128, 8, D], BF16)
        wk_sb = S("wk_sb", [128, 8, D], BF16)
        gw_sb = S("gw_sb", [128, 8, NE0])
        sq_full = S("sq_full", [128, 8, 512])
        rows = S("rows", [1, 512 * 6])
        qstage = S("qstage", [128, 2, 512], BF16)
        kstage = S("kstage", [128, 2, 512], BF16)
        vstage = S("vstage", [128, 2, 1024], BF16)
        sqq = S("sqq", [128, 2, 512])
        qn_stage = S("qn_stage", [1, 2, 512])
        kn_row = S("kn_row", [1, 512])
        kmax_sb = S("kmax_sb", [1, NH])
        tmp11 = S("tmp11", [1, 1])
        probs = S("probs", [128, NE0])
        eq_t = S("eq_t", [128, NE0])
        tmq = S("tmq", [128, NE0])
        sel8 = S("sel8", [128, NE0])
        wd_t = S("wd_t", [128, NE0])
        negm = S("negm", [128, 1])
        mx = S("mx", [128, 1])
        am = S("am", [128, 1])
        wsum = S("wsum", [128, 1])
        carry = S("carry", [1, NE0])
        wd_sb = S("wd_sb", [128, 16, NE0])
        pos_sb = S("pos_sb", [128, 16, NE0])

        pA = PS("pA", [1, 512])
        pB = PS("pB", [1, 512])
        pC = PS("pC", [128, 512])
        pD = PS("pD", [128, 512])
        pE = PS("pE", [128, 512])
        pF = PS("pF", [128, 512])

        ones_col = cs["c_ones_col"]
        ones_row = cs["c_ones_1x128"]
        ltriu = cs["c_ltriu"]
        rev64 = cs["c_rev64"]

        ld_x = _ld_fm(p, x_h, x_sb)
        ld_wq = _ld_fm(p, wq_h, wq_sb, cast=True)
        ld_wk = _ld_fm(p, wk_h, wk_sb, cast=True)
        ld_gw = _ld_fm(p, gw_h, gw_sb)

        mean_r = rows[0:1, 0:512]
        msq_r = rows[0:1, 512:1024]
        var_r = rows[0:1, 1024:1536]
        std_r = rows[0:1, 1536:2048]
        rstd_r = rows[0:1, 2048:2560]
        tmp_r = rows[0:1, 2560:3072]

        ev_store = []
        p.v(lambda v: v.memset(carry[:], 0.0))
        p.v(lambda v: v.memset(kmax_sb[:], -1e30))

        for tc in range(4):
            ts = slice(512 * tc, 512 * (tc + 1))
            sq_evs = []
            for ci in range(8):
                sq_evs.append(p.s(lambda s, ci=ci, ts=ts: s.activation(
                    sq_full[:, ci, :], x_sb[:, ci, ts], AF.Square), wait=(ld_x,), inc=True))
            for ci in range(8):
                p.t(lambda t, ci=ci, ts=ts: t.matmul(
                    pA[0:1, :], ones_col[:], x_sb[:, ci, ts],
                    start=(ci == 0), stop=(ci == 7)), wait=(ld_x, ld_c))
            mm_ev = None
            for ci in range(8):
                mm_ev = p.t(lambda t, ci=ci: t.matmul(
                    pB[0:1, :], ones_col[:], sq_full[:, ci, :],
                    start=(ci == 0), stop=(ci == 7)), wait=(sq_evs[ci],), inc=(ci == 7))
            e1 = p.v(lambda v: v.tensor_scalar(mean_r, pA[0:1, :], 1.0 / D, None, ALU.mult), wait=(mm_ev,), inc=True)
            p.v(lambda v: v.tensor_scalar(msq_r, pB[0:1, :], 1.0 / D, None, ALU.mult))
            p.v(lambda v: v.tensor_tensor(tmp_r, mean_r, mean_r, ALU.mult))
            p.v(lambda v: v.tensor_tensor(var_r, msq_r, tmp_r, ALU.subtract))
            e3 = p.v(lambda v: v.tensor_scalar(var_r, var_r, 1e-5, None, ALU.add), inc=True)
            e4 = p.s(lambda s: s.activation(std_r, var_r, AF.Sqrt), wait=(e3,), inc=True)
            e5 = p.v(lambda v: v.reciprocal(rstd_r, std_r), wait=(e4,), inc=True)
            b1 = p.t(lambda t: t.matmul(pC[:], ones_row[:], mean_r, start=True, stop=True),
                     wait=(e1, e5), inc=True)
            b2 = p.t(lambda t: t.matmul(pD[:], ones_row[:], rstd_r, start=True, stop=True),
                     wait=(e5,), inc=True)
            ev_xn = None
            for ci in range(8):
                p.v(lambda v, ci=ci, ts=ts: v.tensor_tensor(xnf_t[:, ci, :], x_sb[:, ci, ts], pC[:], ALU.subtract), wait=(b1, b2))
                p.v(lambda v, ci=ci: v.tensor_tensor(xnf_t[:, ci, :], xnf_t[:, ci, :], pD[:], ALU.mult))
                ev_xn = p.v(lambda v, ci=ci, ts=ts: v.tensor_copy(xnb[:, ci, ts], xnf_t[:, ci, :]), inc=True)

            for st4 in range(4):
                tt = 4 * tc + st4
                for ci in range(8):
                    p.t(lambda t, ci=ci, st4=st4: t.matmul(
                        pC[0:128, 0:64], xnf_t[:, ci, 128 * st4:128 * (st4 + 1)], gw_sb[:, ci, :],
                        start=(ci == 0), stop=(ci == 7)), wait=(ev_xn, ld_gw))
                mlg = p.mark("t")
                r1 = p.v(lambda v: v.reduce_max(mx[:], pC[0:128, 0:64], axis=AX.X), wait=(mlg,), inc=True)
                e2 = p.v(lambda v: v.tensor_scalar(negm[:], mx[:], -1.0, None, ALU.mult), wait=(r1,), inc=True)
                e3b = p.s(lambda s: s.activation(probs[:], pC[0:128, 0:64], AF.Exp, bias=negm[:]), wait=(e2,), inc=True)
                e4b = p.v(lambda v: v.reduce_sum(wsum[:], probs[:], axis=AX.X), wait=(e3b,), inc=True)
                e5b = p.v(lambda v: v.reciprocal(mx[:], wsum[:]), wait=(e4b,), inc=True)
                p.v(lambda v: v.tensor_scalar(probs[:], probs[:], mx[:], None, ALU.mult), wait=(e5b,))
                p.v(lambda v: v.memset(sel8[:], 0.0))
                p.v(lambda v: v.memset(wd_t[:], 0.0))
                ek = p.v(lambda v: v.memset(wsum[:], 0.0), inc=True)
                for kk in range(8):
                    rr1 = p.v(lambda v: v.reduce_max(mx[:], probs[:], axis=AX.X), wait=(ek,), inc=True)
                    p.v(lambda v: v.tensor_scalar(eq_t[:], probs[:], mx[:], None, ALU.is_ge), wait=(rr1,))
                    p.v(lambda v: v.tensor_tensor(tmq[:], eq_t[:], rev64[:], ALU.mult), wait=(ld_c,))
                    rr2 = p.v(lambda v: v.reduce_max(am[:], tmq[:], axis=AX.X), inc=True)
                    p.v(lambda v: v.tensor_scalar(eq_t[:], tmq[:], am[:], None, ALU.is_ge), wait=(rr2,))
                    p.v(lambda v: v.tensor_tensor(sel8[:], sel8[:], eq_t[:], ALU.add))
                    p.v(lambda v: v.scalar_tensor_tensor(wd_t[:], eq_t[:], mx[:], wd_t[:], ALU.mult, ALU.add))
                    p.v(lambda v: v.tensor_tensor(wsum[:], wsum[:], mx[:], ALU.add))
                    ek = p.v(lambda v: v.scalar_tensor_tensor(probs[:], eq_t[:], -2.0, probs[:], ALU.mult, ALU.add), inc=True)
                en = p.v(lambda v: v.reciprocal(mx[:], wsum[:]), wait=(ek,), inc=True)
                p.v(lambda v, tt=tt: v.tensor_scalar(wd_sb[:, tt, :], wd_t[:], mx[:], None, ALU.mult), wait=(en,))
                esel = p.v(lambda v: v.engine_nop(), inc=True)
                m1 = p.t(lambda t: t.matmul(pD[0:128, 0:64], ltriu[:], sel8[:], start=True, stop=False),
                         wait=(esel,))
                m2 = p.t(lambda t: t.matmul(pD[0:128, 0:64], ones_row[:], carry[:], start=False, stop=True), inc=True)
                m3 = p.t(lambda t: t.matmul(pB[0:1, 0:64], ones_col[:], sel8[:], start=True, stop=True),
                         wait=(esel,), inc=True)
                p.v(lambda v, tt=tt: v.tensor_copy(pos_sb[:, tt, :], pD[0:128, 0:64]), wait=(m2,))
                p.v(lambda v: v.tensor_tensor(carry[:], carry[:], pB[0:1, 0:64], ALU.add), wait=(m3,))
                p.barrier()
            p.barrier()

        bar = p.barrier()
        ev_store.append(p.y(lambda y: y.dma_start(out=xn_o, in_=xnb[:].rearrange("p c t -> p (c t)")), inc=True))
        ev_store.append(p.y(lambda y: y.dma_start(out=wd_o, in_=wd_sb[:].rearrange("p c t -> p (c t)")), inc=True))
        ev_store.append(p.y(lambda y: y.dma_start(out=pos_o, in_=pos_sb[:].rearrange("p c t -> p (c t)")), inc=True))
        ev_store.append(p.y(lambda y: y.dma_start(out=cnt_o, in_=carry[:]), inc=True))

        prev_ev = {0: None, 1: None}
        prev_store = {}
        prev_qn = {0: None, 1: None}
        prev_sqq = {0: None, 1: None}
        prev_qnst = {0: None, 1: None}
        for which, (w_sb, stage, out_hbm, ld_w, scale) in enumerate(
                [(wq_sb, qstage, q_o, ld_wq, 0.125), (wk_sb, kstage, k_o, ld_wk, 1.0)]):
            for co in range(8):
                for tc in range(4):
                    ts = slice(512 * tc, 512 * (tc + 1))
                    buf = (co * 4 + tc) % 2
                    pX = pE if buf == 0 else pF
                    for ci in range(8):
                        p.t(lambda t, ci=ci, co=co, ts=ts, w_sb=w_sb, pX=pX: t.matmul(
                            pX[:], w_sb[:, ci, 128 * co:128 * (co + 1)], xnb[:, ci, ts],
                            start=(ci == 0), stop=(ci == 7)),
                            wait=(ld_w,) + ((prev_ev[buf],) if prev_ev[buf] else ()))
                    mm = p.mark("t")
                    ev_ev = p.s(lambda s, buf=buf, scale=scale, stage=stage, pX=pX: s.activation(
                        stage[:, buf, :], pX[:], AF.Copy, scale=scale),
                        wait=(mm,) + ((prev_store.get((which, buf)),) if prev_store.get((which, buf)) else ()), inc=True)
                    prev_ev[buf] = ev_ev
                    prev_store[(which, buf)] = p.y(
                        lambda y, buf=buf, co=co, ts=ts, stage=stage, out_hbm=out_hbm:
                        y.dma_start(out=out_hbm.rearrange("p (c t) -> p c t", c=8)[:, co, ts],
                                    in_=stage[:, buf, :]), wait=(ev_ev,), inc=True)
                    sq_ev = p.s(lambda s, buf=buf, stage=stage: s.activation(
                        sqq[:, buf, :], stage[:, buf, :], AF.Square),
                        wait=(ev_ev,) + ((prev_sqq[buf],) if prev_sqq[buf] else ()), inc=True)
                    for hh in range(2):
                        h = 2 * co + hh
                        pN = pA if hh == 0 else pB
                        mqn = p.t(lambda t, hh=hh, buf=buf, pN=pN: t.matmul(
                            pN[0:1, :], ones_col[64 * hh:64 * (hh + 1), :], sqq[64 * hh:64 * (hh + 1), buf, :],
                            start=True, stop=True), wait=(sq_ev,) + ((prev_qn[hh],) if prev_qn[hh] else ()), inc=True)
                        prev_sqq[buf] = mqn
                        if which == 0:
                            cpe = p.v(lambda v, buf=buf, pN=pN: v.tensor_copy(qn_stage[0:1, buf, :], pN[0:1, :]),
                                      wait=(mqn,) + ((prev_qnst[buf],) if prev_qnst[buf] else ()), inc=True)
                            prev_qn[hh] = cpe
                            prev_qnst[buf] = p.y(lambda y, h=h, ts=ts, buf=buf: y.dma_start(
                                out=qn2_o[h:h + 1, ts], in_=qn_stage[0:1, buf, :]), wait=(cpe,), inc=True)
                        else:
                            cpe = p.v(lambda v, pN=pN: v.tensor_copy(kn_row[:], pN[0:1, :]), wait=(mqn,), inc=True)
                            rm = p.v(lambda v: v.reduce_max(tmp11[:], kn_row[:], axis=AX.X), wait=(cpe,), inc=True)
                            prev_qn[hh] = p.v(lambda v, h=h: v.tensor_tensor(
                                kmax_sb[0:1, h:h + 1], kmax_sb[0:1, h:h + 1], tmp11[:], ALU.max), wait=(rm,), inc=True)
            if which == 0:
                pmk = p.mark("t")
                ld_wv = _ld_fm(p, wv_h, wq_sb, cast=True, wait=(pmk,))
        ev_store.append(p.y(lambda y: y.dma_start(out=kmax_o, in_=kmax_sb[:]),
                            wait=(prev_qn[0], prev_qn[1]), inc=True))

        prev_pf = p.mark("s")
        prev_vst = {0: None, 1: None}
        for tt in range(16):
            buf = tt % 2
            ev_half = []
            for half in range(2):
                for ci in range(8):
                    p.t(lambda t, tt=tt, half=half, ci=ci: t.matmul(
                        pE[:], xnb[:, ci, 128 * tt:128 * (tt + 1)], wq_sb[:, ci, 512 * half:512 * (half + 1)],
                        start=(ci == 0), stop=(ci == 7)),
                        wait=(ld_wv,) + ((prev_pf,) if prev_pf else ())),
                mm = p.mark("t")
                ev_half.append(p.v(lambda v, buf=buf, half=half: v.tensor_copy(
                    vstage[:, buf, 512 * half:512 * (half + 1)], pE[:]),
                    wait=(mm,) + ((prev_vst[buf],) if prev_vst[buf] else ()), inc=True))
                prev_pf = ev_half[-1]
            ev_st = p.y(lambda y, tt=tt, buf=buf: y.dma_start(
                out=v_o.rearrange("p (c t) -> p c t", c=16)[:, tt, :], in_=vstage[:, buf, :]),
                wait=tuple(ev_half), inc=True)
            prev_vst[buf] = ev_st
        ev_store.append(ev_st)




def _host_split_fm(x):
    xf = np.ascontiguousarray(x.reshape(2 * SEQ, D))
    return [np.ascontiguousarray(xf[c * T:(c + 1) * T].T) for c in range(NCORES)]


def _rope_tables(core):
    pos = (core % 4) * T + np.arange(T, dtype=np.float64)
    inv = 1.0 / (10000.0 ** (np.arange(0, HD, 2, dtype=np.float64) / HD))
    ang = pos[None, :] * inv[(np.arange(HD) // 2)][:, None]
    cos = np.cos(ang).astype(np.float32)
    sin = np.sin(ang).astype(np.float32)
    sgn = np.where(np.arange(HD) % 2 == 0, -1.0, 1.0).astype(np.float32)[:, None]
    sin_s = sin * sgn
    return (np.ascontiguousarray(np.tile(cos, (2, 1))),
            np.ascontiguousarray(np.tile(sin_s, (2, 1))))


def _attn_inputs(r1s, core, qk, kk, vk, qnk=None, kmk=None):
    grp = range(0, 4) if core < 4 else range(4, 8)
    k_full = np.concatenate([r1s[c][kk].reshape(128, 8, T) for c in grp], axis=2)
    v_all = np.concatenate([r1s[c][vk].reshape(128, 16, 8, 128) for c in grp], axis=1)
    v_dev = np.ascontiguousarray(v_all.transpose(2, 0, 1, 3).reshape(8, 128, 64 * 128))
    m = {"kfull" if qnk else "k1full": np.ascontiguousarray(k_full.reshape(128, 8 * SEQ)),
         "vfull" if qnk else "v1full": v_dev}
    m["q0" if qnk else "q1"] = np.ascontiguousarray(r1s[core][qk])
    if qnk:
        m["qn2"] = np.ascontiguousarray(r1s[core][qnk])
        m["kmax2g"] = np.ascontiguousarray(
            np.concatenate([r1s[c][kmk][0] for c in grp]).reshape(1, 64))
    return m


def kernel(**inputs):
    consts = _consts_np()
    x = np.asarray(inputs["L0_attn_x"], dtype=np.float32)
    wqkv = np.asarray(inputs["L0_attn_W_qkv"], dtype=np.float32)
    gate_w = np.ascontiguousarray(np.asarray(inputs["L0_ffn_gate_w"], dtype=np.float32))
    sgw = np.asarray(inputs["L0_ffn_shared_gate_w"], dtype=np.float32)
    sguw = np.asarray(inputs["L0_ffn_shared_gate_up_w"], dtype=np.float32)
    sdw = np.ascontiguousarray(np.asarray(inputs["L0_ffn_shared_down_w"], dtype=np.float32))
    eguw = np.ascontiguousarray(np.asarray(inputs["L0_ffn_expert_gate_up_w"], dtype=np.float32))
    edww = np.ascontiguousarray(np.asarray(inputs["L0_ffn_expert_down_w"], dtype=np.float32))
    wq1 = np.asarray(inputs["L1_attn_Wq"], dtype=np.float32)
    wk1 = np.asarray(inputs["L1_attn_Wk"], dtype=np.float32)
    wv1 = np.asarray(inputs["L1_attn_Wv"], dtype=np.float32)
    wo1 = np.asarray(inputs["L1_attn_Wo"], dtype=np.float32)
    sqk = np.asarray(inputs["L1_attn_s_qk"], dtype=np.float32)
    wr1 = np.asarray(inputs["L1_ffn_w_router"], dtype=np.float32)
    wmh = np.asarray(inputs["L1_ffn_w_mh"], dtype=np.float32)
    wmg = np.asarray(inputs["L1_ffn_w_mg"], dtype=np.float32)
    kff = np.asarray(inputs["L1_ffn_k_ffwd"], dtype=np.float32)
    vff = np.ascontiguousarray(np.asarray(inputs["L1_ffn_v_ffwd"], dtype=np.float32))

    x_parts = _host_split_fm(x)
    wq_t = np.ascontiguousarray(wqkv[0:D].T)
    wk_t = np.ascontiguousarray(wqkv[D:2 * D].T)
    wv_t = np.ascontiguousarray(wqkv[2 * D:3 * D].T)
    sgu_pairs = np.zeros((D, 16, 256), dtype=np.float32)
    for i in range(16):
        sgu_pairs[:, i, 0:128] = sguw[:, 128 * i:128 * (i + 1)]
        sgu_pairs[:, i, 128:256] = sguw[:, 2048 + 128 * i:2048 + 128 * (i + 1)]
    sgu_pairs = np.ascontiguousarray(sgu_pairs.reshape(D, 16 * 256))
    sgate_r = np.ascontiguousarray(sgw.reshape(8, 128).T)
    wq1_t = np.ascontiguousarray(wq1.T)
    wk1_t = np.ascontiguousarray(wk1.T)
    wv1_t = np.ascontiguousarray(wv1.T)
    wo1_t = np.ascontiguousarray(wo1.T)
    sqk_t = np.ascontiguousarray(sqk.T)
    wr1_t = np.ascontiguousarray(wr1.T)
    wmh_t = np.ascontiguousarray(wmh.T)
    wmg_t = np.ascontiguousarray(wmg.T)
    kff_r = np.ascontiguousarray(kff.transpose(0, 1, 3, 2))

    cores = list(range(NCORES))

    nc1 = build_pre0()
    in1 = []
    for c in cores:
        m = {"x_fm": x_parts[c], "wq_t": wq_t, "wk_t": wk_t, "wv_t": wv_t, "gate_w": gate_w}
        for kname in ["c_ones_1x128", "c_ones_col", "c_ltriu", "c_rev64"]:
            m[kname] = consts[kname]
        in1.append(m)
    r1s = run_bass_kernel_spmd(nc1, in1, core_ids=cores).results

    nc2 = build_attn0()
    in2 = []
    for c in cores:
        m = _attn_inputs(r1s, c, "q0", "k0", "v0", "qn2", "kmax2")
        for kname in ["c_ones_1x128", "c_ones_colbf"]:
            m[kname] = consts[kname]
        in2.append(m)
    r2s = run_bass_kernel_spmd(nc2, in2, core_ids=cores).results

    cnt_all = np.ascontiguousarray(np.stack([r1s[c]["cnt"][0] for c in cores]))
    nc3 = build_moe0()
    in3 = []
    for c in cores:
        m = {"xn0": r1s[c]["xn0"], "x_fm": x_parts[c], "s0": r2s[c]["s0"],
             "wd": r1s[c]["wd"], "pos": r1s[c]["pos"], "cnt_all": cnt_all,
             "core_mask": np.ascontiguousarray((np.arange(8) < c).astype(np.float32).reshape(8, 1)),
             "sgu_pairs": sgu_pairs, "sdown": sdw, "sgate": sgate_r,
             "egu": eguw, "edw": edww}
        for kname in ["c_ones_1x128", "c_ident_f32"]:
            m[kname] = consts[kname]
        in3.append(m)
    r3s = run_bass_kernel_spmd(nc3, in3, core_ids=cores).results

    nc4 = build_pre1()
    in4 = []
    for c in cores:
        cos_t, sin_t = _rope_tables(c)
        m = {"x1": r3s[c]["x1"], "wq1_t": wq1_t, "wk1_t": wk1_t, "wv1_t": wv1_t,
             "wr1_t": wr1_t, "sqk_t": sqk_t, "rope_cos": cos_t, "rope_sin": sin_t}
        for kname in ["c_ones_1x128", "c_ones_col", "c_rev16", "c_ident_f32"]:
            m[kname] = consts[kname]
        in4.append(m)
    r4s = run_bass_kernel_spmd(nc4, in4, core_ids=cores).results

    nc5 = build_attn1()
    in5 = []
    for c in cores:
        m = _attn_inputs(r4s, c, "q1", "k1", "v1")
        m["wo_t"] = wo1_t
        for kname in ["c_ones_1x128", "c_ones_colbf", "c_ident_f32"]:
            m[kname] = consts[kname]
        in5.append(m)
    r5s = run_bass_kernel_spmd(nc5, in5, core_ids=cores).results

    nc6 = build_ffn1()
    in6 = []
    for c in cores:
        m = {"x1": r3s[c]["x1"], "x1n": r4s[c]["x1n"], "s1": r5s[c]["s1"],
             "wmh_t": wmh_t, "wmg_t": wmg_t, "kffwd_r": kff_r, "vffwd": vff,
             "g1d": r4s[c]["g1d"]}
        for kname in ["c_ones_1x128"]:
            m[kname] = consts[kname]
        in6.append(m)
    r6s = run_bass_kernel_spmd(nc6, in6, core_ids=cores).results

    out = np.empty((2 * SEQ, D), dtype=np.float32)
    for c in cores:
        out[c * T:(c + 1) * T] = r6s[c]["outp"].reshape(128, 8, T).transpose(2, 1, 0).reshape(T, D)
    return out.reshape(2, SEQ, D)



def emit_attn0(nc, p, io, cs, ld_c):
    q_h = io["q0"]
    k_h = io["kfull"]
    v_h = io["vfull"]
    qn2_h = io["qn2"]
    km_h = io["kmax2g"]

    s0_o = io["s0"]

    ctx = ExitStack()
    _pfx = "ph%d_" % _PHASE_N[0]
    _PHASE_N[0] += 1
    S = lambda n, sh, dt=F32: ctx.enter_context(nc.sbuf_tensor(_pfx + n, sh, dt))
    PS = lambda n, sh, dt=F32: ctx.enter_context(nc.psum_tensor(_pfx + n, sh, dt))
    with ctx:
        qfm = S("qfm", [128, 8, T], BF16)
        kbuf = S("kbuf", [128, 2, SEQ], BF16)
        vbuf = S("vbuf", [128, 2, 64 * 128], BF16)
        e_sb = S("e_sb", [128, 2, 2, 1536], BF16)
        sstage = S("sstage", [128, 2, 512])
        qn16 = S("qn16", [NH, T])
        qm4 = S("qm4", [NH, 4])
        t64 = S("t64", [1, 16, 4])
        negc_sb = S("negc_sb", [128, 64])
        km2 = S("km2", [1, NH])
        rz = S("rz", [1, 2, 512], BF16)
        rzf = S("rzf", [1, 2, 512])

        pSa = PS("pSa", [128, 1536])
        pSb = PS("pSb", [128, 1536])
        pAV = PS("pAV", [128, 512])
        pZ = PS("pZ", [128, 512])

        ones_row = cs["c_ones_1x128"]
        ones_rowbf = cs["c_ones_rowbf"]
        ones_colbf = cs["c_ones_colbf"]

        ld_q = p.y(lambda y: y.dma_start(out=qfm[:], in_=q_h.rearrange("p (c t) -> p c t", c=8)), inc=True)
        kmg = S("kmg", [1, 64])
        ld_km = p.y(lambda y: y.dma_start(out=kmg[:], in_=km_h), inc=True)
        ld_qn = p.y(lambda y: y.dma_start(out=qn16[:], in_=qn2_h), inc=True)
        e0 = p.v(lambda v: v.tensor_tensor(km2[:], kmg[0:1, 0:16], kmg[0:1, 16:32], ALU.max), wait=(ld_km,))
        p.v(lambda v: v.tensor_tensor(km2[:], km2[:], kmg[0:1, 32:48], ALU.max))
        p.v(lambda v: v.tensor_tensor(km2[:], km2[:], kmg[0:1, 48:64], ALU.max))
        ev_km2 = p.mark("v")

        NG = 22
        GS = [min(3, 64 - 3 * g) for g in range(NG)]

        ld_k = {}
        ld_v = {}
        for hp in range(2):
            ld_k[hp] = p.y(lambda y, hp=hp: y.dma_start(
                out=kbuf[:, hp % 2, :].rearrange("p (r t) -> p r t", r=4), in_=k_h[:, hp]), inc=True)
            ld_v[hp] = p.g(lambda g, hp=hp: g.dma_start(
                out=vbuf[:, hp % 2, :], in_=v_h[hp]), inc=True, dma=True)

        for j in range(4):
            p.v(lambda v, j=j: v.reduce_max(qm4[:, j:j + 1], qn16[:, 512 * j:512 * (j + 1)], axis=AX.X),
                wait=(ld_qn,) if j == 0 else ())
        km4mm = p.t(lambda t: t.matmul(pSb[0:16, 0:4], km2[0:1, :], ones_row[0:1, 0:4],
                                       start=True, stop=True), wait=(ev_km2,), inc=True)
        p.v(lambda v: v.tensor_tensor(qm4[:], qm4[:], pSb[0:16, 0:4], ALU.mult), wait=(km4mm,))
        sq_ev = p.s(lambda s: s.activation(qm4[:], qm4[:], AF.Sqrt), wait=(p.mark("v"),), inc=True)
        ngv = p.v(lambda v: v.tensor_scalar(qm4[:], qm4[:], -1.0, None, ALU.mult), wait=(sq_ev,), inc=True)
        dq = p.y(lambda y: y.dma_start(out=t64[:],
                                       in_=qm4[:].rearrange("p j -> p () j")), wait=(ngv,), inc=True)
        bcn = p.t(lambda t: t.matmul(pZ[:, 0:64], ones_row[:], t64[:].rearrange("o h j -> o (h j)"),
                                     start=True, stop=True), wait=(dq,), inc=True)
        nccA = p.v(lambda v: v.tensor_copy(negc_sb[:], pZ[:, 0:64]), wait=(bcn,), inc=True)

        prev_c1 = None
        prev_mres = nccA
        prev_expa_l = None
        prev_exp_l = None
        prev_store = {0: None, 1: None}
        pend = [None]
        for hp in range(8):
            kb = hp % 2
            ld_k_ev = ld_k[hp]
            ld_v_ev = ld_v[hp]
            for q4 in range(4):
                qsl = slice(512 * q4, 512 * (q4 + 1))
                wA0 = (ld_k_ev, ld_q) + ((prev_expa_l,) if prev_expa_l else ())
                wB0 = (ld_k_ev, ld_q) + ((prev_exp_l,) if prev_exp_l else ())
                qa0 = [None] * GS[0]
                qb0 = [None] * GS[0]
                for j in range(GS[0]):
                    p.t(lambda t, j=j, kb=kb, hp=hp, qsl=qsl: t.matmul(
                        pSa[:, 512 * j:512 * (j + 1)], kbuf[0:64, kb, 128 * j:128 * (j + 1)],
                        qfm[0:64, hp, qsl], start=True, stop=True, tile_position=(0, 0)),
                        wait=wA0 if j == 0 else ())
                    qa0[j] = p.mark("t")
                    p.t(lambda t, j=j, kb=kb, hp=hp, qsl=qsl: t.matmul(
                        pSb[:, 512 * j:512 * (j + 1)], kbuf[64:128, kb, 128 * j:128 * (j + 1)],
                        qfm[64:128, hp, qsl], start=True, stop=True, tile_position=(64, 0)),
                        wait=wB0 if j == 0 else ())
                    qb0[j] = p.mark("t")
                if pend[0] is not None:
                    m_ev, st_ev, sb_p = pend[0]()
                    prev_mres = m_ev
                    prev_store[sb_p] = st_ev
                    pend[0] = None

                ev_ea = {}
                ev_eb = {}
                ev_av = [None] * NG
                ca = 4 * 2 * hp + q4
                cb = 4 * (2 * hp + 1) + q4
                for g in range(NG):
                    gb = g % 2
                    if g >= 1:
                        qa0 = [None] * GS[g]
                        qb0 = [None] * GS[g]
                        for j in range(GS[g]):
                            ks = 3 * g + j
                            wA = (ev_ea[(g - 1, j)],) if j < GS[g - 1] else ()
                            wB = (ev_eb[(g - 1, j)],) if j < GS[g - 1] else ()
                            p.t(lambda t, ks=ks, j=j, kb=kb, hp=hp, qsl=qsl: t.matmul(
                                pSa[:, 512 * j:512 * (j + 1)], kbuf[0:64, kb, 128 * ks:128 * (ks + 1)],
                                qfm[0:64, hp, qsl], start=True, stop=True, tile_position=(0, 0)), wait=wA)
                            qa0[j] = p.mark("t")
                            p.t(lambda t, ks=ks, j=j, kb=kb, hp=hp, qsl=qsl: t.matmul(
                                pSb[:, 512 * j:512 * (j + 1)],
                                kbuf[64:128, kb, 128 * ks:128 * (ks + 1)],
                                qfm[64:128, hp, qsl], start=True, stop=True, tile_position=(64, 0)), wait=wB)
                            qb0[j] = p.mark("t")
                    for j in range(GS[g]):
                        w_exp = (qa0[j], nccA)
                        if g >= 2 and j == 0:
                            w_exp = w_exp + ((ev_av[g - 2],) if ev_av[g - 2] else ())
                        ev_ea[(g, j)] = p.s(lambda s, gb=gb, j=j, ca=ca: s.activation(
                            e_sb[:, gb, 0, 512 * j:512 * (j + 1)], pSa[:, 512 * j:512 * (j + 1)],
                            AF.Exp, bias=negc_sb[:, ca:ca + 1]), wait=w_exp, inc=True)
                        ev_eb[(g, j)] = p.s(lambda s, gb=gb, j=j, cb=cb: s.activation(
                            e_sb[:, gb, 1, 512 * j:512 * (j + 1)], pSb[:, 512 * j:512 * (j + 1)],
                            AF.Exp, bias=negc_sb[:, cb:cb + 1]), wait=(qb0[j],), inc=True)
                    for j in range(GS[g]):
                        ks = 3 * g + j
                        first = (g == 0 and j == 0)
                        last = (g == NG - 1 and j == GS[g] - 1)
                        wav = (ev_ea[(g, j)],) + ((ld_v_ev,) if j == 0 else ())
                        p.t(lambda t, ks=ks, j=j, gb=gb, kb=kb, first=first, last=last: t.matmul(
                            pAV[0:64, :], vbuf[:, kb, :].rearrange("p (tt d) -> p tt d", d=128)[:, ks, 0:64],
                            e_sb[:, gb, 0, 512 * j:512 * (j + 1)],
                            start=first, stop=last, tile_position=(0, 0)),
                            wait=wav + ((prev_c1,) if (first and prev_c1) else ()))
                        p.t(lambda t, ks=ks, j=j, gb=gb, kb=kb, first=first, last=last: t.matmul(
                            pAV[64:128, :], vbuf[:, kb, :].rearrange("p (tt d) -> p tt d", d=128)[:, ks, 64:128],
                            e_sb[:, gb, 1, 512 * j:512 * (j + 1)],
                            start=first, stop=last, tile_position=(0, 64), skip_group_check=True),
                            wait=(ev_eb[(g, j)],))
                        p.t(lambda t, ks=ks, j=j, gb=gb, first=first, last=last: t.matmul(
                            pZ[0:1, :], ones_colbf[:], e_sb[:, gb, 0, 512 * j:512 * (j + 1)],
                            start=first, stop=last, tile_position=(0, 0)),
                            wait=((prev_mres,) if (first and prev_mres) else ()))
                        p.t(lambda t, ks=ks, j=j, gb=gb, first=first, last=last: t.matmul(
                            pZ[64:65, :], ones_colbf[:], e_sb[:, gb, 1, 512 * j:512 * (j + 1)],
                            start=first, stop=last, tile_position=(0, 64), skip_group_check=True))
                    ev_av[g] = p.mark("t")
                sb = q4 % 2
                c1 = p.v(lambda v, sb=sb: v.tensor_copy(sstage[:, sb, :], pAV[:]),
                         wait=(ev_av[NG - 1],) + ((prev_store[sb],) if prev_store[sb] else ()), inc=True)
                p.v(lambda v: v.reciprocal(rzf[0:1, 0, :], pZ[0:1, :]))
                p.v(lambda v: v.reciprocal(rzf[0:1, 1, :], pZ[64:65, :]))
                rcast = p.v(lambda v: v.tensor_copy(rz[:].rearrange("o h t -> o (h t)"), rzf[:].rearrange("o h t -> o (h t)")), inc=True)
                prev_c1 = c1
                prev_expa_l = ev_ea[(NG - 1, 0)]
                prev_exp_l = ev_eb[(NG - 1, 0)]

                def mk_pend(sb=sb, hp=hp, qsl=qsl, rcast=rcast):
                    def emit():
                        p.t(lambda t: t.matmul(pZ[0:64, :], ones_rowbf[0:1, 0:64], rz[0:1, 0, :],
                                               start=True, stop=True, tile_position=(0, 0)),
                            wait=(rcast,))
                        b2 = p.t(lambda t: t.matmul(pZ[64:128, :], ones_rowbf[0:1, 0:64], rz[0:1, 1, :],
                                                    start=True, stop=True, tile_position=(0, 64),
                                                    skip_group_check=True), inc=True)
                        mres = p.v(lambda v: v.tensor_tensor(sstage[:, sb, :], sstage[:, sb, :],
                                                             pZ[:, 0:512], ALU.mult), wait=(b2,), inc=True)
                        st = p.y(lambda y: y.dma_start(
                            out=s0_o.rearrange("p (c t) -> p c t", c=8)[:, hp, qsl], in_=sstage[:, sb, :]),
                            wait=(mres,), inc=True)
                        return mres, st, sb
                    return emit
                pend[0] = mk_pend()
            if hp + 2 <= 7:
                nhp = hp + 2
                lastuse = p.mark("t")
                ld_k[nhp] = p.y(lambda y, nhp=nhp: y.dma_start(
                    out=kbuf[:, nhp % 2, :].rearrange("p (r t) -> p r t", r=4), in_=k_h[:, nhp]),
                    wait=(lastuse,), inc=True)
                ld_v[nhp] = p.g(lambda g, nhp=nhp: g.dma_start(
                    out=vbuf[:, nhp % 2, :], in_=v_h[nhp]),
                    wait=(lastuse,), inc=True, dma=True)
        if pend[0] is not None:
            pend[0]()
            pend[0] = None




def emit_moe0(nc, p, io, cs, ld_c):
    xn_h = io["xn0"]
    x_hh = io["x_fm"]
    s0_h = io["s0"]
    wd_h = io["wd"]
    pos_h = io["pos"]
    cnt_h = io["cnt_all"]
    mask_h = io["core_mask"]
    sgu_h = io["sgu_pairs"]
    sdw_h = io["sdown"]
    sgw_h = io["sgate"]
    egu_h = io["egu"]
    edw_h = io["edw"]

    x1_o = io["x1"]
    wkt_h = nc.dram_tensor("wkt_scratch", [NE0, T], F32, kind="Internal").ap()

    ctx = ExitStack()
    _pfx = "ph%d_" % _PHASE_N[0]
    _PHASE_N[0] += 1
    S = lambda n, sh, dt=F32: ctx.enter_context(nc.sbuf_tensor(_pfx + n, sh, dt))
    PS = lambda n, sh, dt=F32: ctx.enter_context(nc.psum_tensor(_pfx + n, sh, dt))
    with ctx:
        xnb = S("xnb", [128, 8, T], BF16)
        y_acc = S("y_acc", [128, 8, T])
        act_t = S("act_t", [128, 16, 512], BF16)
        sgu_db = S("sgu_db", [128, 2, 8, 256], BF16)
        sdw_db = S("sdw_db", [128, 2, D], BF16)
        keepw = S("keepw", [128, 16, NE0])
        grow = S("grow", [1, T])
        gate_bc = S("gate_bc", [128, T], BF16)
        egu_sb = S("egu_sb", [128, 2, 8, D], BF16)
        edw_sb = S("edw_sb", [128, 4, D], BF16)
        silu_t = S("silu_t", [128, 512])
        t1f = S("t1f", [128, T])
        base_row = S("base_row", [1, NE0])
        cnt_sb = S("cnt_sb", [8, NE0])
        mask_sb = S("mask_sb", [8, 1])
        x1st = S("x1st", [128, T])
        pos_t = S("pos_t", [128, NE0])

        pAll = PS("pAll", [128, 4096])

        def pb(i):
            return pAll[:, 512 * i:512 * (i + 1)]

        ones_row = cs["c_ones_1x128"]
        ident = cs["c_ident_f32"]

        ld_xn = p.y(lambda y: y.dma_start(out=xnb[:], in_=xn_h.rearrange("p (c t) -> p c t", c=8)), inc=True)
        ld_cnt = p.y(lambda y: y.dma_start(out=cnt_sb[:], in_=cnt_h), inc=True)
        ld_msk = p.y(lambda y: y.dma_start(out=mask_sb[:], in_=mask_h), inc=True)
        ld_wd = p.y(lambda y: y.dma_start(out=keepw[:], in_=wd_h.rearrange("p (c t) -> p c t", c=16)), inc=True)

        wkT_sb = t1f[0:64, :].rearrange("p (c x) -> p c x", x=128)
        mb = p.t(lambda t: t.matmul(pAll[0:1, 0:NE0], mask_sb[:], cnt_sb[:], start=True, stop=True),
                 wait=(ld_cnt, ld_msk), inc=True)
        e1 = p.v(lambda v: v.tensor_copy(base_row[:], pAll[0:1, 0:NE0]), wait=(mb,), inc=True)
        bb = p.t(lambda t: t.matmul(pAll[0:128, 0:NE0], ones_row[:], base_row[:], start=True, stop=True),
                 wait=(e1, ld_c), inc=True)
        for tt in range(16):
            lp = p.y(lambda y, tt=tt: y.dma_start(out=pos_t[:], in_=pos_h.rearrange("p (c t) -> p c t", c=16)[:, tt, :]),
                     wait=(p.mark("v"),), inc=True)
            p.v(lambda v: v.tensor_tensor(pos_t[:], pos_t[:], pAll[0:128, 0:NE0], ALU.add), wait=(lp, bb))
            p.v(lambda v: v.tensor_scalar(pos_t[:], pos_t[:], float(CAP0), None, ALU.is_lt))
            p.v(lambda v, tt=tt: v.tensor_tensor(keepw[:, tt, :], keepw[:, tt, :], pos_t[:], ALU.mult), wait=(ld_wd,))
        ev_keep = p.mark("v")
        for tt in range(16):
            trw = p.t(lambda t, tt=tt: t.transpose(pAll[0:64, 512:640], keepw[:, tt, :], ident[:]),
                      wait=(ev_keep,) + ((p.mark("v"),) if tt else ()), inc=True)
            p.v(lambda v, tt=tt: v.tensor_copy(wkT_sb[:, tt, :], pAll[0:64, 512:640]), wait=(trw,))
        ev_wkt = p.mark("v")
        st_wkt = p.y(lambda y: y.dma_start(out=wkt_h, in_=wkT_sb.rearrange("p c t -> p (c t)")), wait=(ev_wkt,), inc=True)

        sgw_sb = S("sgw_sb", [128, 8], BF16)
        ld_sgw = p.g(lambda g: g.dma_start(out=sgw_sb[:], in_=sgw_h), inc=True, dma=True)
        for tc in range(4):
            for ci in range(8):
                p.t(lambda t, tc=tc, ci=ci: t.matmul(
                    pAll[0:1, 1024 + 512 * tc:1024 + 512 * (tc + 1)], sgw_sb[:, ci:ci + 1],
                    xnb[:, ci, 512 * tc:512 * (tc + 1)],
                    start=(ci == 0), stop=(ci == 7)), wait=(ld_sgw, ld_xn, st_wkt))
        mg = p.mark("t")
        sg = p.s(lambda s: s.activation(grow[:], pAll[0:1, 1024:1024 + 2048], AF.Sigmoid), wait=(mg,), inc=True)
        for tc in range(4):
            p.t(lambda t, tc=tc: t.matmul(pb(tc), ones_row[:], grow[0:1, 512 * tc:512 * (tc + 1)],
                                          start=True, stop=True), wait=(sg,))
        mgb = p.mark("t")
        ev_free = p.v(lambda v: v.tensor_copy(gate_bc[:], pAll[:, 0:2048]), wait=(mgb,), inc=True)
        ev_grow_free = mgb

        prev_y = None
        for tg in range(4):
            tsl = slice(512 * tg, 512 * (tg + 1))
            for i in range(16):
                dbuf = i % 2
                lw = p.g(lambda g, i=i, dbuf=dbuf: g.dma_start(
                    out=sgu_db[:, dbuf, :, :],
                    in_=sgu_h.rearrange("(c p) (i x) -> p c i x", p=128, i=16)[:, :, i, :]),
                    wait=(p.mark("t"),), inc=True, dma=True)
                for ci in range(8):
                    p.t(lambda t, i=i, tsl=tsl, ci=ci, dbuf=dbuf: t.matmul(
                        pb(0), sgu_db[:, dbuf, ci, 0:128], xnb[:, ci, tsl],
                        start=(ci == 0), stop=(ci == 7)), wait=(lw, ev_free))
                for ci in range(8):
                    p.t(lambda t, i=i, tsl=tsl, ci=ci, dbuf=dbuf: t.matmul(
                        pb(1), sgu_db[:, dbuf, ci, 128:256], xnb[:, ci, tsl],
                        start=(ci == 0), stop=(ci == 7)))
                mmi = p.mark("t")
                sv = p.s(lambda s: s.activation(silu_t[:], pb(0), AF.Silu), wait=(mmi,), inc=True)
                p.v(lambda v: v.tensor_tensor(silu_t[:], silu_t[:], pb(1), ALU.mult), wait=(sv,))
                ev_free = p.v(lambda v, i=i, tsl=tsl: v.tensor_tensor(
                    act_t[:, i, :], silu_t[:], gate_bc[:, tsl], ALU.mult), inc=True)
            for ifc in range(16):
                dbuf = ifc % 2
                lwi = p.g(lambda g, ifc=ifc, dbuf=dbuf: g.dma_start(
                    out=sdw_db[:, dbuf, :],
                    in_=sdw_h.rearrange("(c p) d -> p c d", p=128)[:, ifc, :]),
                    wait=(p.mark("t"),), inc=True, dma=True)
                for fc in range(8):
                    w = (lwi, ev_free) + ((prev_y,) if prev_y else ())
                    p.t(lambda t, ifc=ifc, fc=fc, dbuf=dbuf: t.matmul(
                        pb(fc), sdw_db[:, dbuf, 128 * fc:128 * (fc + 1)], act_t[:, ifc, :],
                        start=(ifc == 0), stop=(ifc == 15)), wait=w)
            mfc = p.mark("t")
            for fc in range(8):
                prev_y = p.v(lambda v, fc=fc, tsl=tsl: v.tensor_copy(y_acc[:, fc, tsl], pb(fc)),
                             wait=(mfc,), inc=True)
            ev_free = prev_y

        wrow_db = S("wrow_bf", [1, 2048], BF16)
        w_bc = gate_bc
        prev_wread = None
        lgu = {0: p.g(lambda g: g.dma_start(
            out=egu_sb[:, 0], in_=egu_h[0].rearrange("(c p) d -> p c d", p=128)),
            wait=(p.mark("t"),), inc=True, dma=True)}
        ldw = {0: p.g(lambda g: g.dma_start(
            out=edw_sb[:], in_=edw_h[0].rearrange("(c p) d -> p c d", p=128)),
            inc=True, dma=True)}
        lwr = {0: p.g(lambda g: g.dma_start(out=wrow_db[0:1, :], in_=wkt_h[0:1, :]),
                      wait=(st_wkt,), inc=True, dma=True)}
        ev_bankpair = {}
        for e in range(NE0):
            eb = e % 2
            for tc in range(4):
                p.t(lambda t, tc=tc, eb=eb: t.matmul(
                    pb(4 + tc), cs["c_ones_rowbf"][:], wrow_db[0:1, 512 * tc:512 * (tc + 1)],
                    start=True, stop=True), wait=(lwr[e], ev_free))
            prev_wread = p.mark("t")
            ev_wbc0 = p.v(lambda v: v.tensor_copy(w_bc[:, 0:1024], pAll[:, 2048:3072]),
                          wait=(prev_wread, ev_free), inc=True)
            ev_wbc = p.v(lambda v: v.tensor_copy(w_bc[:, 1024:2048], pAll[:, 3072:4096]), inc=True)
            if e + 1 < NE0:
                lwr[e + 1] = p.g(lambda g, e=e: g.dma_start(
                    out=wrow_db[0:1, :], in_=wkt_h[e + 1:e + 2, :]),
                    wait=(prev_wread,), inc=True, dma=True)
            ev_act = [ev_wbc0, ev_wbc0]
            for tg in range(4):
                tsl = slice(512 * tg, 512 * (tg + 1))
                for i in range(4):
                    gb2 = 2 * (i % 2)
                    for ci in range(8):
                        bp = ev_bankpair.get(i % 2) if (i < 2 and ev_bankpair) else None
                        w = (lgu[e], (ev_wbc0 if tg < 2 else ev_wbc), ev_act[i % 2]) + ((bp,) if bp else ())
                        p.t(lambda t, i=i, tsl=tsl, ci=ci, eb=eb, gb2=gb2: t.matmul(
                            pb(gb2), egu_sb[:, eb, ci, 128 * i:128 * (i + 1)], xnb[:, ci, tsl],
                            start=(ci == 0), stop=(ci == 7)), wait=w)
                    for ci in range(8):
                        p.t(lambda t, i=i, tsl=tsl, ci=ci, eb=eb, gb2=gb2: t.matmul(
                            pb(gb2 + 1), egu_sb[:, eb, ci, 512 + 128 * i:512 + 128 * (i + 1)], xnb[:, ci, tsl],
                            start=(ci == 0), stop=(ci == 7)))
                    mmi = p.mark("t")
                    sv = p.s(lambda s, gb2=gb2: s.activation(silu_t[:], pb(gb2), AF.Silu), wait=(mmi,), inc=True)
                    p.v(lambda v, gb2=gb2: v.tensor_tensor(silu_t[:], silu_t[:], pb(gb2 + 1), ALU.mult), wait=(sv,))
                    ev_act[i % 2] = p.v(lambda v, i=i, tsl=tsl: v.tensor_tensor(
                        act_t[:, i, :], silu_t[:], w_bc[:, tsl], ALU.mult), inc=True)
                if tg == 3:
                    gu_last = p.mark("t")
                    if e + 1 < NE0:
                        lgu[e + 1] = p.g(lambda g, e=e: g.dma_start(
                            out=egu_sb[:, (e + 1) % 2], in_=egu_h[e + 1].rearrange("(c p) d -> p c d", p=128)),
                            wait=(gu_last,), inc=True, dma=True)
                y_add = {}
                for fc in range(8):
                    for ifc in range(4):
                        w = (ldw[e], ev_act[1]) if (fc == 0 and ifc == 0) else ()
                        p.t(lambda t, ifc=ifc, fc=fc, eb=eb: t.matmul(
                            pb(fc), edw_sb[:, ifc, 128 * fc:128 * (fc + 1)], act_t[:, ifc, :],
                            start=(ifc == 0), stop=(ifc == 3)), wait=w)
                    mfc = p.mark("t")
                    y_add[fc] = p.v(lambda v, fc=fc, tsl=tsl: v.tensor_tensor(
                        y_acc[:, fc, tsl], y_acc[:, fc, tsl], pb(fc), ALU.add), wait=(mfc,), inc=True)
                if tg == 3 and e + 1 < NE0:
                    ldw[e + 1] = p.g(lambda g, e=e: g.dma_start(
                        out=edw_sb[:], in_=edw_h[e + 1].rearrange("(c p) d -> p c d", p=128)),
                        wait=(mfc,), inc=True, dma=True)
                prev_y = y_add[7]
                ev_bankpair = {0: y_add[1], 1: y_add[3]}
                ev_free = prev_y

        xchunk = silu_t
        prev_st = None
        for fc in range(8):
            lx = p.y(lambda y, fc=fc: y.dma_start(
                out=t1f[:], in_=x_hh.rearrange("(c p) t -> p c t", p=128)[:, fc, :]),
                wait=(p.mark("v"),) + ((prev_st,) if prev_st else ()), inc=True)
            ls = p.y(lambda y, fc=fc: y.dma_start(
                out=x1st[:], in_=s0_h.rearrange("p (c t) -> p c t", c=8)[:, fc, :]), inc=True)
            gg = p.s(lambda s, fc=fc: s.activation(y_acc[:, fc, :], y_acc[:, fc, :], AF.Silu),
                     wait=(prev_y,), inc=True)
            p.v(lambda v, fc=fc: v.tensor_tensor(x1st[:], x1st[:], y_acc[:, fc, :], ALU.mult), wait=(gg, ls))
            p.v(lambda v, fc=fc: v.tensor_tensor(x1st[:], x1st[:], t1f[:], ALU.add), wait=(lx,))
            ev_x1 = p.mark("v")
            prev_st = p.y(lambda y, fc=fc: y.dma_start(
                out=x1_o.rearrange("p (c t) -> p c t", c=8)[:, fc, :], in_=x1st[:]), wait=(ev_x1,), inc=True)




def emit_pre1(nc, p, io, cs, ld_c):
    x_h = io["x1"]
    wq_h = io["wq1_t"]
    wk_h = io["wk1_t"]
    wv_h = io["wv1_t"]
    wr_h = io["wr1_t"]
    sqk_h = io["sqk_t"]
    cos_h = io["rope_cos"]
    sin_h = io["rope_sin"]

    xn_o = io["x1n"]
    q_o = io["q1"]
    k_o = io["k1"]
    v_o = io["v1"]
    g_o = io["g1d"]

    SWAP = [(i + 1) if i % 2 == 0 else (i - 1) for i in range(32)]

    ctx = ExitStack()
    _pfx = "ph%d_" % _PHASE_N[0]
    _PHASE_N[0] += 1
    S = lambda n, sh, dt=F32: ctx.enter_context(nc.sbuf_tensor(_pfx + n, sh, dt))
    PS = lambda n, sh, dt=F32: ctx.enter_context(nc.psum_tensor(_pfx + n, sh, dt))
    with ctx:
        x_sb = S("x_sb", [128, 8, T])
        xnb = S("xnb", [128, 8, T], BF16)
        xnf_t = S("xnf_t", [128, 8, 512])
        sq_full = xnf_t
        wq_sb = S("wq_sb", [128, 8, D], BF16)
        wk_sb = S("wk_sb", [128, 8, D], BF16)
        wr_sb = S("wr_sb", [128, 8, NE1])
        sqk_sb = S("sqk_sb", [64, NH])
        qscol = S("qscol", [128, 1])
        kscol = S("kscol", [128, 1])
        cos_sb = S("cos_sb", [128, T])
        sin_sb = S("sin_sb", [128, T])
        rows = S("rows", [1, 512 * 6])
        qstage = S("qstage", [128, 2, 512], BF16)
        kstage = S("kstage", [128, 2, 512], BF16)
        vstage = S("vstage", [128, 2, 1024], BF16)
        qt = S("qt", [128, 512])
        qs = S("qs", [128, 512])
        qro = qt
        sqq = qs
        rn_rows = S("rn_rows", [1, 2, 512], BF16)
        rn_f = S("rn_f", [1, 2, 512])
        probs = S("probs", [128, NE1])
        sig_t = S("sig_t", [128, NE1])
        eq_t = S("eq_t", [128, NE1])
        tmq = S("tmq", [128, NE1])
        sel2 = S("sel2", [128, NE1])
        mx = S("mx", [128, 1])
        am = S("am", [128, 1])
        negm = S("negm", [128, 1])
        gT_sb = S("gT_sb", [16, 16, 128])

        pA = PS("pA", [1, 512])
        pB = PS("pB", [1, 512])
        pC = PS("pC", [128, 512])
        pD = PS("pD", [128, 512])
        pE = PS("pE", [128, 512])
        pF = PS("pF", [128, 512])

        ones_col = cs["c_ones_col"]
        ones_row = cs["c_ones_1x128"]
        rev16 = cs["c_rev16"]
        ident = cs["c_ident_f32"]

        ld_x = p.y(lambda y: y.dma_start(out=x_sb[:], in_=x_h.rearrange("p (c t) -> p c t", c=8)), inc=True)
        ld_wq = _ld_fm(p, wq_h, wq_sb, cast=True)
        ld_wk = _ld_fm(p, wk_h, wk_sb, cast=True)
        ld_wr = _ld_fm(p, wr_h, wr_sb)
        ld_sqk = p.y(lambda y: y.dma_start(out=sqk_sb[:], in_=sqk_h), inc=True)
        ld_cos = p.y(lambda y: y.dma_start(out=cos_sb[:], in_=cos_h), inc=True)
        ld_sin = p.y(lambda y: y.dma_start(out=sin_sb[:], in_=sin_h), inc=True)

        mean_r = rows[0:1, 0:512]
        msq_r = rows[0:1, 512:1024]
        var_r = rows[0:1, 1024:1536]
        std_r = rows[0:1, 1536:2048]
        rstd_r = rows[0:1, 2048:2560]
        tmp_r = rows[0:1, 2560:3072]

        for tc in range(4):
            ts = slice(512 * tc, 512 * (tc + 1))
            sq_evs = []
            for ci in range(8):
                sq_evs.append(p.s(lambda s, ci=ci, ts=ts: s.activation(
                    sq_full[:, ci, :], x_sb[:, ci, ts], AF.Square), wait=(ld_x,), inc=True))
            for ci in range(8):
                p.t(lambda t, ci=ci, ts=ts: t.matmul(
                    pA[0:1, :], ones_col[:], x_sb[:, ci, ts],
                    start=(ci == 0), stop=(ci == 7)), wait=(ld_x, ld_c))
            mm_ev = None
            for ci in range(8):
                mm_ev = p.t(lambda t, ci=ci: t.matmul(
                    pB[0:1, :], ones_col[:], sq_full[:, ci, :],
                    start=(ci == 0), stop=(ci == 7)), wait=(sq_evs[ci],), inc=(ci == 7))
            e1 = p.v(lambda v: v.tensor_scalar(mean_r, pA[0:1, :], 1.0 / D, None, ALU.mult), wait=(mm_ev,), inc=True)
            p.v(lambda v: v.tensor_scalar(msq_r, pB[0:1, :], 1.0 / D, None, ALU.mult))
            p.v(lambda v: v.tensor_tensor(tmp_r, mean_r, mean_r, ALU.mult))
            p.v(lambda v: v.tensor_tensor(var_r, msq_r, tmp_r, ALU.subtract))
            e3 = p.v(lambda v: v.tensor_scalar(var_r, var_r, 1e-5, None, ALU.add), inc=True)
            e4 = p.s(lambda s: s.activation(std_r, var_r, AF.Sqrt), wait=(e3,), inc=True)
            e5 = p.v(lambda v: v.reciprocal(rstd_r, std_r), wait=(e4,), inc=True)
            b1 = p.t(lambda t: t.matmul(pC[:], ones_row[:], mean_r, start=True, stop=True),
                     wait=(e1, e5), inc=True)
            b2 = p.t(lambda t: t.matmul(pD[:], ones_row[:], rstd_r, start=True, stop=True),
                     wait=(e5,), inc=True)
            ev_xn = None
            for ci in range(8):
                p.v(lambda v, ci=ci, ts=ts: v.tensor_tensor(xnf_t[:, ci, :], x_sb[:, ci, ts], pC[:], ALU.subtract), wait=(b1, b2))
                p.v(lambda v, ci=ci: v.tensor_tensor(xnf_t[:, ci, :], xnf_t[:, ci, :], pD[:], ALU.mult))
                ev_xn = p.v(lambda v, ci=ci, ts=ts: v.tensor_copy(xnb[:, ci, ts], xnf_t[:, ci, :]), inc=True)

            for st4 in range(4):
                tt = 4 * tc + st4
                for ci in range(8):
                    p.t(lambda t, ci=ci, st4=st4: t.matmul(
                        pC[0:128, 0:NE1], xnf_t[:, ci, 128 * st4:128 * (st4 + 1)], wr_sb[:, ci, :],
                        start=(ci == 0), stop=(ci == 7)), wait=(ev_xn, ld_wr))
                mlg = p.mark("t")
                sig = p.s(lambda s: s.activation(sig_t[:], pC[0:128, 0:NE1], AF.Sigmoid), wait=(mlg,), inc=True)
                cp = p.v(lambda v: v.tensor_copy(probs[:], pC[0:128, 0:NE1]), wait=(mlg,), inc=True)
                p.v(lambda v: v.memset(sel2[:], 0.0))
                ek = p.v(lambda v: v.engine_nop(), inc=True)
                for kk in range(2):
                    rr1 = p.v(lambda v: v.reduce_max(mx[:], probs[:], axis=AX.X), wait=(ek,), inc=True)
                    p.v(lambda v: v.tensor_scalar(eq_t[:], probs[:], mx[:], None, ALU.is_ge), wait=(rr1,))
                    p.v(lambda v: v.tensor_tensor(tmq[:], eq_t[:], rev16[:], ALU.mult), wait=(ld_c,))
                    rr2 = p.v(lambda v: v.reduce_max(am[:], tmq[:], axis=AX.X), inc=True)
                    p.v(lambda v: v.tensor_scalar(eq_t[:], tmq[:], am[:], None, ALU.is_ge), wait=(rr2,))
                    p.v(lambda v: v.tensor_tensor(sel2[:], sel2[:], eq_t[:], ALU.add))
                    ek = p.v(lambda v: v.scalar_tensor_tensor(probs[:], eq_t[:], -1e30, probs[:], ALU.mult, ALU.add), inc=True)
                gsel = p.v(lambda v: v.tensor_tensor(sel2[:], sel2[:], sig_t[:], ALU.mult), wait=(sig, ek), inc=True)
                trg = p.t(lambda t, tt=tt: t.transpose(pD[0:16, 0:128], sel2[:], ident[:]), wait=(gsel,), inc=True)
                p.v(lambda v, tt=tt: v.tensor_copy(gT_sb[:, tt, :], pD[0:16, 0:128]), wait=(trg,))
        ev_g = p.mark("v")
        st_g = p.y(lambda y: y.dma_start(out=g_o, in_=gT_sb[:].rearrange("p c t -> p (c t)")), wait=(ev_g,), inc=True)
        xn_all = ev_g
        ev_store = [st_g]
        ev_store.append(p.y(lambda y: y.dma_start(out=xn_o, in_=xnb[:].rearrange("p c t -> p (c t)")), wait=(xn_all,), inc=True))

        prev_pe = None
        prev_store = {0: None, 1: None}
        for which, (w_sb, stage, out_hbm, ld_w, scol, sfac) in enumerate(
                [(wq_sb, qstage, q_o, ld_wq, qscol, 256.0), (wk_sb, kstage, k_o, ld_wk, kscol, 32.0)]):
            for co in range(8):
                sc1 = p.v(lambda v, co=co, sfac=sfac, scol=scol: v.tensor_scalar(
                    scol[0:64, :], sqk_sb[:, 2 * co:2 * co + 1], sfac, None, ALU.mult), wait=(ld_sqk,), inc=True)
                sc2 = p.v(lambda v, co=co, sfac=sfac, scol=scol: v.tensor_scalar(
                    scol[64:128, :], sqk_sb[:, 2 * co + 1:2 * co + 2], sfac, None, ALU.mult), inc=True)
                for tc in range(4):
                    ts = slice(512 * tc, 512 * (tc + 1))
                    buf = (co * 4 + tc) % 2
                    for ci in range(8):
                        p.t(lambda t, ci=ci, co=co, ts=ts, w_sb=w_sb: t.matmul(
                            pE[:], w_sb[:, ci, 128 * co:128 * (co + 1)], xnb[:, ci, ts],
                            start=(ci == 0), stop=(ci == 7)),
                            wait=(xn_all, ld_w) + ((prev_pe,) if prev_pe else ()))
                    mm = p.mark("t")
                    c1 = p.v(lambda v: v.tensor_copy(qt[:], pE[:]), wait=(mm, ld_cos, ld_sin), inc=True)
                    prev_pe = c1
                    p.v(lambda v: v.stream_shuffle(qs[:], qt[:], mask=SWAP))
                    p.v(lambda v, ts=ts: v.tensor_tensor(qt[:], qt[:], cos_sb[:, ts], ALU.mult))
                    p.v(lambda v, ts=ts: v.tensor_tensor(qs[:], qs[:], sin_sb[:, ts], ALU.mult))
                    rr = p.v(lambda v: v.tensor_tensor(qt[:], qt[:], qs[:], ALU.add), inc=True)
                    sqe = p.s(lambda s: s.activation(sqq[:], qro[:], AF.Square), wait=(rr,), inc=True)
                    for hh in range(2):
                        pX = pA if hh == 0 else pB
                        mqn = p.t(lambda t, hh=hh, pX=pX: t.matmul(
                            pX[0:1, :], ones_col[64 * hh:64 * (hh + 1), :], sqq[64 * hh:64 * (hh + 1), :],
                            start=True, stop=True), wait=(sqe,), inc=True)
                        s1e = p.s(lambda s, hh=hh, pX=pX: s.activation(rn_f[0:1, hh, :], pX[0:1, :], AF.Sqrt), wait=(mqn,), inc=True)
                        p.v(lambda v, hh=hh: v.tensor_scalar(rn_f[0:1, hh, :], rn_f[0:1, hh, :], 1e-12, None, ALU.max), wait=(s1e,))
                        p.v(lambda v, hh=hh: v.reciprocal(rn_f[0:1, hh, :], rn_f[0:1, hh, :]))
                        p.v(lambda v, hh=hh: v.tensor_copy(rn_rows[0:1, hh, :], rn_f[0:1, hh, :]))
                    evr = p.mark("v")
                    bb1 = p.t(lambda t: t.matmul(pF[0:64, :], cs["c_ones_rowbf"][0:1, 0:64], rn_rows[0:1, 0, :],
                                                 start=True, stop=True, tile_position=(0, 0)), wait=(evr,), inc=True)
                    bb2 = p.t(lambda t: t.matmul(pF[64:128, :], cs["c_ones_rowbf"][0:1, 0:64], rn_rows[0:1, 1, :],
                                                 start=True, stop=True, tile_position=(0, 64), skip_group_check=True), inc=True)
                    p.v(lambda v: v.tensor_tensor(qro[:], qro[:], pF[:], ALU.mult), wait=(bb2, sc2))
                    fin = p.v(lambda v, buf=buf, stage=stage, scol=scol: v.tensor_scalar(
                        stage[:, buf, :], qro[:], scol[:], None, ALU.mult),
                        wait=((prev_store[buf],) if prev_store[buf] else ()), inc=True)
                    prev_store[buf] = p.y(lambda y, buf=buf, co=co, ts=ts, stage=stage, out_hbm=out_hbm:
                        y.dma_start(out=out_hbm.rearrange("p (c t) -> p c t", c=8)[:, co, ts],
                                    in_=stage[:, buf, :]), wait=(fin,), inc=True)
        ld_wv = _ld_fm(p, wv_h, wq_sb, cast=True, wait=(p.mark("t"),))
        prev_pf = p.mark("v")
        prev_vst = {0: None, 1: None}
        for tt in range(16):
            buf = tt % 2
            ev_half = []
            for half in range(2):
                for ci in range(8):
                    p.t(lambda t, tt=tt, half=half, ci=ci: t.matmul(
                        pF[:], xnb[:, ci, 128 * tt:128 * (tt + 1)], wq_sb[:, ci, 512 * half:512 * (half + 1)],
                        start=(ci == 0), stop=(ci == 7)), wait=(ld_wv,) + ((prev_pf,) if prev_pf else ()))
                mm = p.mark("t")
                ev_half.append(p.v(lambda v, buf=buf, half=half: v.tensor_copy(
                    vstage[:, buf, 512 * half:512 * (half + 1)], pF[:]),
                    wait=(mm,) + ((prev_vst[buf],) if prev_vst[buf] else ()), inc=True))
                prev_pf = ev_half[-1]
            ev_st = p.y(lambda y, tt=tt, buf=buf: y.dma_start(
                out=v_o.rearrange("p (c t) -> p c t", c=16)[:, tt, :], in_=vstage[:, buf, :]),
                wait=(ev_half[-1],), inc=True)
            prev_vst[buf] = ev_st
        ev_store.append(ev_st)




def emit_attn1(nc, p, io, cs, ld_c):
    q_h = io["q1"]
    k_h = io["k1full"]
    v_h = io["v1full"]
    wo_h = io["wo_t"]

    s1_o = io["s1"]

    ctx = ExitStack()
    _pfx = "ph%d_" % _PHASE_N[0]
    _PHASE_N[0] += 1
    S = lambda n, sh, dt=F32: ctx.enter_context(nc.sbuf_tensor(_pfx + n, sh, dt))
    PS = lambda n, sh, dt=F32: ctx.enter_context(nc.psum_tensor(_pfx + n, sh, dt))
    with ctx:
        qfm = S("qfm", [128, 8, T], BF16)
        kbuf = S("kbuf", [128, 2, SEQ], BF16)
        vbuf = S("vbuf", [128, 2, 64 * 128], BF16)
        e_sb = S("e_sb", [128, 2, 2, 1536], BF16)
        y1 = S("y1", [128, 8, T], BF16)
        wo_sb = S("wo_sb", [128, 8, D], BF16)
        sstage = S("sstage", [128, 2, 512])
        Mt = S("Mt", [128, 2, 512])
        Mt2 = S("Mt2", [128, 2, 1024])
        mcols = S("mcols", [128, 8])
        negm = S("negm", [1, 2, 512], BF16)
        rz = S("rz", [1, 2, 512], BF16)
        rzf = S("rzf", [1, 2, 512])

        pAll = PS("pAll", [128, 4096])

        def pa1(j):
            return pAll[:, 512 * j:512 * (j + 1)]

        def pb1(j):
            return pAll[:, 2048 + 512 * j:2048 + 512 * (j + 1)]

        pSa = pAll[:, 0:1536]
        pSb = pAll[:, 1536:3072]
        pAV = pAll[:, 3072:3584]
        pZ = pAll[:, 3584:4096]

        ones_row = cs["c_ones_1x128"]
        ones_rowbf = cs["c_ones_rowbf"]
        ones_colbf = cs["c_ones_colbf"]
        ident = cs["c_ident_f32"]

        ld_q = p.y(lambda y: y.dma_start(out=qfm[:], in_=q_h.rearrange("p (c t) -> p c t", c=8)), inc=True)
        ld_wo = _ld_fm(p, wo_h, wo_sb, cast=True)

        NG = 22
        GS = [min(3, 64 - 3 * g) for g in range(NG)]
        ld_k = {}
        ld_v = {}
        for hp in range(2):
            ld_k[hp] = p.y(lambda y, hp=hp: y.dma_start(
                out=kbuf[:, hp % 2, :].rearrange("p (r t) -> p r t", r=4), in_=k_h[:, hp]), inc=True)
            ld_v[hp] = p.g(lambda g, hp=hp: g.dma_start(
                out=vbuf[:, hp % 2, :], in_=v_h[hp]), inc=True, dma=True)

        prev_evac = None
        prev_c1 = None
        prev_expa_l = None
        prev_exp_l = None
        pend = [None]
        for hp in range(8):
            kb = hp % 2
            ld_k_ev = ld_k[hp]
            ld_v_ev = ld_v[hp]
            for q4 in range(4):
                qsl = slice(512 * q4, 512 * (q4 + 1))
                ev_mp = {(0, 0): None, (0, 1): None, (1, 0): None, (1, 1): None}
                for g2 in range(32):
                    par = g2 % 2
                    if g2 == 1 and pend[0] is not None:
                        prev_evac, prev_c1 = pend[0]()
                        pend[0] = None
                    for j in range(2):
                        ks = 2 * g2 + j
                        if g2 == 0:
                            wA = ((ld_k_ev, ld_q) + ((prev_expa_l,) if prev_expa_l else ())) if j == 0 else ()
                            wB = ((prev_exp_l,) if prev_exp_l else ()) if j == 0 else ()
                        elif g2 == 1:
                            wA = ()
                            wB = ((prev_c1,) if (j == 0 and prev_c1) else ()) + \
                                 ((prev_evac,) if (j == 1 and prev_evac) else ())
                        else:
                            wA = (ev_mp[(0, par)],) if (j == 0 and ev_mp[(0, par)]) else ()
                            wB = (ev_mp[(1, par)],) if (j == 0 and ev_mp[(1, par)]) else ()
                        p.t(lambda t, ks=ks, j=j, kb=kb, hp=hp, qsl=qsl, par=par: t.matmul(
                            pa1(2 * par + j), kbuf[0:64, kb, 128 * ks:128 * (ks + 1)],
                            qfm[0:64, hp, qsl], start=True, stop=True, tile_position=(0, 0)), wait=wA)
                        p.t(lambda t, ks=ks, j=j, kb=kb, hp=hp, qsl=qsl, par=par: t.matmul(
                            pb1(2 * par + j), kbuf[64:128, kb, 128 * ks:128 * (ks + 1)],
                            qfm[64:128, hp, qsl], start=True, stop=True, tile_position=(64, 0)), wait=wB)
                    mg2 = p.mark("t")
                    for hh in range(2):
                        base = 1024 * par if hh == 0 else 2048 + 1024 * par
                        psl = pAll[:, base:base + 1024]
                        if g2 == 0:
                            p.v(lambda v, hh=hh, psl=psl: v.tensor_copy(Mt2[:, hh, :], psl), wait=(mg2,))
                        else:
                            p.v(lambda v, hh=hh, psl=psl: v.tensor_tensor(
                                Mt2[:, hh, :], Mt2[:, hh, :], psl, ALU.max), wait=(mg2,))
                        ev_mp[(hh, par)] = p.mark("v")
                p.v(lambda v: v.tensor_tensor(Mt[:, 0, :], Mt2[:, 0, 0:512], Mt2[:, 0, 512:1024], ALU.max))
                ev_mfa = p.mark("v")
                p.v(lambda v: v.tensor_tensor(Mt[:, 1, :], Mt2[:, 1, 0:512], Mt2[:, 1, 512:1024], ALU.max))
                ev_mfb = p.mark("v")
                w0 = (ev_mp[(0, 0)], ev_mp[(0, 1)], ev_mp[(1, 0)])
                for j in range(GS[0]):
                    p.t(lambda t, j=j, kb=kb, hp=hp, qsl=qsl: t.matmul(
                        pSa[:, 512 * j:512 * (j + 1)], kbuf[0:64, kb, 128 * j:128 * (j + 1)],
                        qfm[0:64, hp, qsl], start=True, stop=False, tile_position=(0, 0)),
                        wait=w0 if j == 0 else ())
                    p.t(lambda t, j=j, kb=kb, hp=hp, qsl=qsl: t.matmul(
                        pSb[:, 512 * j:512 * (j + 1)], kbuf[64:128, kb, 128 * j:128 * (j + 1)],
                        qfm[64:128, hp, qsl], start=True, stop=False, tile_position=(64, 0)))
                tr1 = None
                for c in range(8):
                    hh, j = c // 4, c % 4
                    w = (ev_mfa, ev_mfb) if c == 0 else ()
                    tr1 = p.t(lambda t, c=c, hh=hh, j=j: t.transpose(
                        pAll[:, 3072 + 128 * c:3072 + 128 * (c + 1)], Mt[:, hh, 128 * j:128 * (j + 1)], ident[:]),
                        wait=w, inc=(c == 7))
                rd = None
                for c in range(8):
                    rd = p.v(lambda v, c=c: v.reduce_max(mcols[:, c:c + 1],
                                                         pAll[:, 3072 + 128 * c:3072 + 128 * (c + 1)], axis=AX.X),
                             wait=(tr1,) if c == 0 else (), inc=(c == 7))
                tr2 = None
                for c in range(8):
                    tr2 = p.t(lambda t, c=c: t.transpose(
                        pAll[0:1, 3072 + 128 * c:3072 + 128 * (c + 1)], mcols[:, c:c + 1], ident[:]),
                        wait=(rd,) if c == 0 else (), inc=(c == 7))
                ev_negm = p.v(lambda v: v.tensor_scalar(negm[:].rearrange("o h t -> o (h t)"),
                                                        pAll[0:1, 3072:4096], -1.0, None, ALU.mult),
                              wait=(tr2,), inc=True)
                ev_ea = {}
                ev_eb = {}
                ev_av = [None] * NG
                for g in range(NG):
                    gb = g % 2
                    qa0 = [None] * GS[g]
                    qb0 = [None] * GS[g]
                    if g == 0:
                        for j in range(GS[0]):
                            p.t(lambda t, j=j: t.matmul(
                                pSa[:, 512 * j:512 * (j + 1)], ones_rowbf[:], negm[0:1, 0, :],
                                start=False, stop=True), wait=(ev_negm,) if j == 0 else ())
                            qa0[j] = p.mark("t")
                            p.t(lambda t, j=j: t.matmul(
                                pSb[:, 512 * j:512 * (j + 1)], ones_rowbf[:], negm[0:1, 1, :],
                                start=False, stop=True))
                            qb0[j] = p.mark("t")
                    else:
                        for j in range(GS[g]):
                            ks = 3 * g + j
                            p.t(lambda t, ks=ks, j=j, kb=kb, hp=hp, qsl=qsl: t.matmul(
                                pSa[:, 512 * j:512 * (j + 1)], kbuf[0:64, kb, 128 * ks:128 * (ks + 1)],
                                qfm[0:64, hp, qsl], start=True, stop=False, tile_position=(0, 0)),
                                wait=(ev_ea[(g - 1, j)],) if j < GS[g - 1] else ())
                            p.t(lambda t, j=j: t.matmul(
                                pSa[:, 512 * j:512 * (j + 1)], ones_rowbf[:], negm[0:1, 0, :],
                                start=False, stop=True))
                            qa0[j] = p.mark("t")
                            p.t(lambda t, ks=ks, j=j, kb=kb, hp=hp, qsl=qsl: t.matmul(
                                pSb[:, 512 * j:512 * (j + 1)], kbuf[64:128, kb, 128 * ks:128 * (ks + 1)],
                                qfm[64:128, hp, qsl], start=True, stop=False, tile_position=(64, 0)),
                                wait=(ev_eb[(g - 1, j)],) if j < GS[g - 1] else ())
                            p.t(lambda t, j=j: t.matmul(
                                pSb[:, 512 * j:512 * (j + 1)], ones_rowbf[:], negm[0:1, 1, :],
                                start=False, stop=True))
                            qb0[j] = p.mark("t")
                    for j in range(GS[g]):
                        w_exp = (qa0[j],)
                        if g >= 2 and j == 0:
                            w_exp = w_exp + ((ev_av[g - 2],) if ev_av[g - 2] else ())
                        ev_ea[(g, j)] = p.s(lambda s, gb=gb, j=j: s.activation(
                            e_sb[:, gb, 0, 512 * j:512 * (j + 1)], pSa[:, 512 * j:512 * (j + 1)],
                            AF.Exp), wait=w_exp, inc=True)
                        ev_eb[(g, j)] = p.s(lambda s, gb=gb, j=j: s.activation(
                            e_sb[:, gb, 1, 512 * j:512 * (j + 1)], pSb[:, 512 * j:512 * (j + 1)],
                            AF.Exp), wait=(qb0[j],), inc=True)
                    for j in range(GS[g]):
                        ks = 3 * g + j
                        first = (g == 0 and j == 0)
                        last = (g == NG - 1 and j == GS[g] - 1)
                        p.t(lambda t, ks=ks, j=j, gb=gb, kb=kb, first=first, last=last: t.matmul(
                            pAV[0:64, :], vbuf[:, kb, :].rearrange("p (tt d) -> p tt d", d=128)[:, ks, 0:64],
                            e_sb[:, gb, 0, 512 * j:512 * (j + 1)],
                            start=first, stop=last, tile_position=(0, 0)),
                            wait=(ev_ea[(g, j)],) + ((ld_v_ev,) if j == 0 else ()))
                        p.t(lambda t, ks=ks, j=j, gb=gb, kb=kb, first=first, last=last: t.matmul(
                            pAV[64:128, :], vbuf[:, kb, :].rearrange("p (tt d) -> p tt d", d=128)[:, ks, 64:128],
                            e_sb[:, gb, 1, 512 * j:512 * (j + 1)],
                            start=first, stop=last, tile_position=(0, 64), skip_group_check=True),
                            wait=(ev_eb[(g, j)],))
                        p.t(lambda t, j=j, gb=gb, first=first, last=last: t.matmul(
                            pZ[0:1, :], ones_colbf[:], e_sb[:, gb, 0, 512 * j:512 * (j + 1)],
                            start=first, stop=last, tile_position=(0, 0)))
                        p.t(lambda t, j=j, gb=gb, first=first, last=last: t.matmul(
                            pZ[64:65, :], ones_colbf[:], e_sb[:, gb, 1, 512 * j:512 * (j + 1)],
                            start=first, stop=last, tile_position=(0, 64), skip_group_check=True))
                    ev_av[g] = p.mark("t")
                sb2 = q4 % 2
                c1 = p.v(lambda v, sb2=sb2: v.tensor_copy(sstage[:, sb2, :], pAV[:]), wait=(ev_av[NG - 1],), inc=True)
                p.v(lambda v: v.reciprocal(rzf[0:1, 0, :], pZ[0:1, :]))
                p.v(lambda v: v.reciprocal(rzf[0:1, 1, :], pZ[64:65, :]))
                rcast = p.v(lambda v: v.tensor_copy(rz[:].rearrange("o h t -> o (h t)"), rzf[:].rearrange("o h t -> o (h t)")), inc=True)
                prev_expa_l = ev_ea[(NG - 1, 0)]
                prev_exp_l = ev_eb[(NG - 1, 0)]

                def mk_pend(sb2=sb2, hp=hp, qsl=qsl, rcast=rcast, c1=c1):
                    def emit():
                        p.t(lambda t: t.matmul(pZ[0:64, :], ones_rowbf[0:1, 0:64], rz[0:1, 0, :],
                                               start=True, stop=True, tile_position=(0, 0)),
                            wait=(rcast,))
                        b2 = p.t(lambda t: t.matmul(pZ[64:128, :], ones_rowbf[0:1, 0:64], rz[0:1, 1, :],
                                                    start=True, stop=True, tile_position=(0, 64),
                                                    skip_group_check=True), inc=True)
                        ev = p.v(lambda v: v.tensor_tensor(
                            y1[:, hp, qsl], sstage[:, sb2, :], pZ[:, 0:512], ALU.mult), wait=(b2,), inc=True)
                        return ev, c1
                    return emit
                pend[0] = mk_pend()
            if hp + 2 <= 7:
                nhp = hp + 2
                lastuse = p.mark("t")
                ld_k[nhp] = p.y(lambda y, nhp=nhp: y.dma_start(
                    out=kbuf[:, nhp % 2, :].rearrange("p (r t) -> p r t", r=4), in_=k_h[:, nhp]),
                    wait=(lastuse,), inc=True)
                ld_v[nhp] = p.g(lambda g, nhp=nhp: g.dma_start(
                    out=vbuf[:, nhp % 2, :], in_=v_h[nhp]),
                    wait=(lastuse,), inc=True, dma=True)
        ev_y1, _ = pend[0]()
        pend[0] = None

        prev_pe = None
        prev_sst = {0: None, 1: None}
        for co in range(8):
            for tc in range(4):
                ts = slice(512 * tc, 512 * (tc + 1))
                sb2 = (co * 4 + tc) % 2
                for ci in range(8):
                    p.t(lambda t, ci=ci, co=co, ts=ts: t.matmul(
                        pAll[:, 0:512], wo_sb[:, ci, 128 * co:128 * (co + 1)], y1[:, ci, ts],
                        start=(ci == 0), stop=(ci == 7)),
                        wait=(ev_y1, ld_wo) + ((prev_pe,) if prev_pe else ()))
                mm = p.mark("t")
                prev_pe = p.v(lambda v, sb2=sb2: v.tensor_copy(sstage[:, sb2, :], pAll[:, 0:512]),
                              wait=(mm,) + ((prev_sst[sb2],) if prev_sst[sb2] else ()), inc=True)
                prev_sst[sb2] = p.y(lambda y, co=co, ts=ts, sb2=sb2: y.dma_start(
                    out=s1_o.rearrange("p (c t) -> p c t", c=8)[:, co, ts], in_=sstage[:, sb2, :]),
                    wait=(prev_pe,), inc=True)




def emit_ffn1(nc, p, io, cs, ld_c):
    x_h = io["x1"]
    xn_h = io["x1n"]
    s1_h = io["s1"]
    wmh_h = io["wmh_t"]
    wmg_h = io["wmg_t"]
    kf_h = io["kffwd_r"]
    vf_h = io["vffwd"]
    g_h = io["g1d"]

    out_o = io["outp"]

    ctx = ExitStack()
    _pfx = "ph%d_" % _PHASE_N[0]
    _PHASE_N[0] += 1
    S = lambda n, sh, dt=F32: ctx.enter_context(nc.sbuf_tensor(_pfx + n, sh, dt))
    PS = lambda n, sh, dt=F32: ctx.enter_context(nc.psum_tensor(_pfx + n, sh, dt))
    with ctx:
        xnb = S("xnb", [128, 8, T], BF16)
        qm = S("qm", [128, 8, T], BF16)
        y_fm = S("y_fm", [128, 8, T], BF16)
        w_sb = S("w_sb", [128, 8, D], BF16)
        kv_k = S("kv_k", [128, 2, 16, S1], BF16)
        kv_v = S("kv_v", [128, 2, 16, 2, D1], BF16)
        g_bc = S("g_bc", [128, 16, 512], BF16)
        g_row16 = S("g_row16", [1, 16, 512], BF16)
        gel = S("gel", [128, 2, 512], BF16)
        stg = S("stg", [128, 2, 512])
        x1st = S("x1st", [128, T])
        t1f = S("t1f", [128, T])

        pAll = PS("pAll", [128, 4096])

        ones_row = cs["c_ones_1x128"]

        ld_xn = p.y(lambda y: y.dma_start(out=xnb[:], in_=xn_h.rearrange("p (c t) -> p c t", c=8)), inc=True)
        ld_w = _ld_fm(p, wmh_h, w_sb, cast=True)

        prev_pe = {0: None, 1: None}
        idx = 0
        for co in range(8):
            for tc in range(4):
                ts = slice(512 * tc, 512 * (tc + 1))
                bk = idx % 2
                base = 2048 + 512 * bk
                for ci in range(8):
                    w = ((ld_xn, ld_w) if idx == 0 and ci == 0 else ()) +                         ((prev_pe[bk],) if (ci == 0 and prev_pe[bk]) else ())
                    p.t(lambda t, ci=ci, co=co, ts=ts, base=base: t.matmul(
                        pAll[:, base:base + 512], w_sb[:, ci, 128 * co:128 * (co + 1)], xnb[:, ci, ts],
                        start=(ci == 0), stop=(ci == 7)), wait=w)
                mm = p.mark("t")
                prev_pe[bk] = p.v(lambda v, co=co, ts=ts, base=base: v.tensor_copy(
                    qm[:, co, ts], pAll[:, base:base + 512]), wait=(mm,), inc=True)
                idx += 1
        ev_qm = p.mark("v")

        prev_kv = {0: None, 1: None}
        prev_y = None
        prev_bg = None
        gcp = {2560: ev_qm, 3584: ev_qm}
        for tc in range(4):
            ts = slice(512 * tc, 512 * (tc + 1))
            lgs = [None] * NE1
            for e in range(NE1):
                lgs[e] = p.g(lambda g, e=e, ts=ts: g.dma_start(out=g_row16[0:1, e, :], in_=g_h[e:e + 1, ts]),
                             wait=(prev_bg,) if (e == 0 and prev_bg) else (), inc=True, dma=True)
            for e in range(NE1):
                base = 3584 if e % 2 == 0 else 2560
                bg = p.t(lambda t, e=e, base=base: t.matmul(
                    pAll[:, base:base + 512], cs["c_ones_rowbf"][:], g_row16[0:1, e, :],
                    start=True, stop=True), wait=(lgs[e],) + ((gcp[base],) if gcp[base] else ()), inc=True)
                gcp[base] = p.v(lambda v, e=e, base=base: v.tensor_copy(g_bc[:, e, :], pAll[:, base:base + 512]),
                                wait=(bg,), inc=True)
                if e == NE1 - 1:
                    prev_bg = bg
            ev_gbc = p.mark("v")
            for h in range(H1):
                hb = h % 2
                lk = p.g(lambda g, h=h, hb=hb: g.dma_start(
                    out=kv_k[:, hb], in_=kf_h[h].rearrange("e p s -> p e s")),
                    wait=(prev_kv[hb],) if prev_kv[hb] else (), inc=True, dma=True)
                lv = p.g(lambda g, h=h, hb=hb: g.dma_start(
                    out=kv_v[:, hb], in_=vf_h[h].rearrange("e (sc p) d -> p e sc d", p=128)),
                    inc=True, dma=True)
                ev_exp = {}
                for e in range(NE1):
                    sbk = e % 2
                    for sc in range(2):
                        w = (lk, ev_gbc) if (e == 0 and sc == 0) else ()
                        if e >= 2 and sc == 0:
                            w = w + (ev_exp[e - 2],)
                        p.t(lambda t, e=e, sc=sc, h=h, hb=hb, ts=ts, sbk=sbk: t.matmul(
                            pAll[:, 1024 * sbk + 512 * sc:1024 * sbk + 512 * (sc + 1)],
                            kv_k[:, hb, e, 128 * sc:128 * (sc + 1)], qm[:, h, ts],
                            start=True, stop=True), wait=w)
                    msc = p.mark("t")
                    ge1 = p.s(lambda s, sbk=sbk: s.activation(
                        gel[:, 0, :], pAll[:, 1024 * sbk:1024 * sbk + 512], AF.Gelu), wait=(msc,), inc=True)
                    ge2 = p.s(lambda s, sbk=sbk: s.activation(
                        gel[:, 1, :], pAll[:, 1024 * sbk + 512:1024 * sbk + 1024], AF.Gelu), inc=True)
                    p.v(lambda v, e=e: v.tensor_tensor(gel[:, 0, :], gel[:, 0, :], g_bc[:, e, :], ALU.mult), wait=(ge1,))
                    ev_exp[e] = p.v(lambda v, e=e: v.tensor_tensor(gel[:, 1, :], gel[:, 1, :], g_bc[:, e, :], ALU.mult), wait=(ge2,), inc=True)
                    for sc in range(2):
                        first = (e == 0 and sc == 0)
                        last = (e == NE1 - 1 and sc == 1)
                        w = ((ev_exp[e],) if sc == 0 else ()) + ((lv, prev_y) if (first and prev_y) else ((lv,) if first else ()))
                        p.t(lambda t, e=e, sc=sc, hb=hb, first=first, last=last: t.matmul(
                            pAll[:, 3072:3584], kv_v[:, hb, e, sc, :], gel[:, sc, :],
                            start=first, stop=last), wait=w)
                my = p.mark("t")
                prev_y = p.v(lambda v, h=h, ts=ts: v.tensor_copy(y_fm[:, h, ts], pAll[:, 3072:3584]), wait=(my,), inc=True)
                prev_kv[hb] = my
        ev_yfm = prev_y

        ld_wg = _ld_fm(p, wmg_h, w_sb, cast=True, wait=(p.mark("t"),))
        prev_st = None
        prev_sil = {0: None, 1: None}
        prev_ost = {0: None, 1: None}
        idx = 0
        for co in range(8):
            for tc in range(4):
                ts = slice(512 * tc, 512 * (tc + 1))
                sb2 = idx % 2
                base = 512 * sb2
                for ci in range(8):
                    w = ((ev_yfm, ld_wg) if idx == 0 and ci == 0 else ()) +                         ((prev_sil[sb2],) if (ci == 0 and prev_sil[sb2]) else ())
                    p.t(lambda t, ci=ci, co=co, ts=ts, base=base: t.matmul(
                        pAll[:, base:base + 512], w_sb[:, ci, 128 * co:128 * (co + 1)], y_fm[:, ci, ts],
                        start=(ci == 0), stop=(ci == 7)), wait=w)
                mm = p.mark("t")
                lx = p.y(lambda y, co=co, ts=ts: y.dma_start(
                    out=t1f[:, 0:512], in_=x_h.rearrange("p (c t) -> p c t", c=8)[:, co, ts]),
                    wait=(prev_st,) if prev_st else (), inc=True)
                ls = p.y(lambda y, co=co, ts=ts: y.dma_start(
                    out=x1st[:, 0:512], in_=s1_h.rearrange("p (c t) -> p c t", c=8)[:, co, ts]), inc=True)
                gg = p.s(lambda s, sb2=sb2, base=base: s.activation(stg[:, sb2, :], pAll[:, base:base + 512], AF.Silu),
                         wait=(mm,) + ((prev_ost[sb2],) if prev_ost[sb2] else ()), inc=True)
                prev_sil[sb2] = gg
                p.v(lambda v, sb2=sb2: v.tensor_tensor(stg[:, sb2, :], stg[:, sb2, :], x1st[:, 0:512], ALU.mult), wait=(gg, ls))
                p.v(lambda v, sb2=sb2: v.tensor_tensor(stg[:, sb2, :], stg[:, sb2, :], t1f[:, 0:512], ALU.add), wait=(lx,))
                ev_o = p.mark("v")
                prev_st = p.y(lambda y, co=co, ts=ts, sb2=sb2: y.dma_start(
                    out=out_o.rearrange("p (c t) -> p c t", c=8)[:, co, ts], in_=stg[:, sb2, :]),
                    wait=(ev_o,), inc=True)
                prev_ost[sb2] = prev_st
                idx += 1




ALL_CONSTS = ["c_ones_1x128", "c_ones_col", "c_ltriu", "c_rev64",
              "c_ident_f32", "c_rev16", "c_ones_colbf", "c_ones_rowbf"]


def build_fused(upto=99, dbg=False):
    nc = _nc()
    EI = lambda n, sh, dt=F32: nc.dram_tensor(n, sh, dt, kind="ExternalInput").ap()
    IN = lambda n, sh, dt=F32: nc.dram_tensor(n, sh, dt, kind="Internal").ap()

    x_fm = EI("x_fm", [D, T])
    wq_t = EI("wq_t", [D, D]); wk_t = EI("wk_t", [D, D]); wv_t = EI("wv_t", [D, D])
    gate_w = EI("gate_w", [D, NE0])
    core_mask = EI("core_mask", [8, 1])
    sgu_pairs = EI("sgu_pairs", [D, 16 * 256]); sdown = EI("sdown", [2 * D, D])
    sgate = EI("sgate", [128, 8])
    egu = EI("egu", [NE0, D, D]); edw = EI("edw", [NE0, CAP0, D])
    wq1_t = EI("wq1_t", [D, D]); wk1_t = EI("wk1_t", [D, D]); wv1_t = EI("wv1_t", [D, D])
    wo_t = EI("wo_t", [D, D]); sqk_t = EI("sqk_t", [64, NH])
    wr1_t = EI("wr1_t", [D, NE1])
    rope_cos = EI("rope_cos", [128, T]); rope_sin = EI("rope_sin", [128, T])
    wmh_t = EI("wmh_t", [D, D]); wmg_t = EI("wmg_t", [D, D])
    kffwd_r = EI("kffwd_r", [H1, NE1, D1, S1]); vffwd = EI("vffwd", [H1, NE1, S1, D1])

    out_o = nc.dram_tensor("outp", [128, 8 * T], F32, kind="ExternalOutput").ap()

    DI = (lambda n, sh, dt=F32: nc.dram_tensor(n, sh, dt, kind="ExternalOutput").ap()) if dbg else IN
    xn0 = DI("xn0", [128, 8 * T], BF16)
    q0 = DI("q0", [128, 8 * T], BF16)
    k0 = IN("k0", [128, 8 * T], BF16)
    v0 = IN("v0", [128, 16 * D], BF16)
    qn2 = DI("qn2", [NH, T])
    kmax2 = IN("kmax2", [1, NH])
    wd = DI("wd", [128, 16 * NE0])
    pos = DI("pos", [128, 16 * NE0])
    cnt = IN("cnt", [1, NE0])
    s0 = DI("s0", [128, 8 * T])
    x1 = DI("x1", [128, 8 * T])
    x1n = DI("x1n", [128, 8 * T], BF16)
    q1 = DI("q1", [128, 8 * T], BF16)
    k1 = IN("k1", [128, 8 * T], BF16)
    v1 = IN("v1", [128, 16 * D], BF16)
    g1d = DI("g1d", [NE1, T])
    s1 = DI("s1", [128, 8 * T])
    k0_ag = IN("k0_ag", [4 * 128, 8 * T], BF16)
    v0_ag = IN("v0_ag", [4 * 128, 16 * D], BF16)
    km_ag = IN("km_ag", [4, NH])
    cnt_ag = IN("cnt_ag", [8, NE0])
    v0_hp = IN("v0_hp", [8, 128, 64 * 128], BF16)
    k1_ag = IN("k1_ag", [4 * 128, 8 * T], BF16)
    v1_ag = IN("v1_ag", [4 * 128, 16 * D], BF16)
    v1_hp = IN("v1_hp", [8, 128, 64 * 128], BF16)

    GROUPS4 = [[0, 1, 2, 3], [4, 5, 6, 7]]
    GROUPS8 = [list(range(8))]

    ctx = ExitStack()
    with ctx:
        p = Prog(nc)
        S = lambda n, sh, dt=F32: ctx.enter_context(nc.sbuf_tensor(n, sh, dt))
        cs = {}
        ld_c = None
        for n in ALL_CONSTS:
            sh, dt = CONST_SHAPES[n]
            ap = nc.dram_tensor(n, sh, dt, kind="ExternalInput").ap()
            tl = S(n + "_sb", list(sh), dt)
            cs[n] = tl
            ld_c = p.y(lambda y, tl=tl, ap=ap: y.dma_start(out=tl[:], in_=ap), inc=True)

        emit_pre0(nc, p, {"x_fm": x_fm, "wq_t": wq_t, "wk_t": wk_t, "wv_t": wv_t,
                          "gate_w": gate_w, "xn0": xn0, "q0": q0, "k0": k0, "v0": v0,
                          "qn2": qn2, "kmax2": kmax2, "wd": wd, "pos": pos, "cnt": cnt},
                  cs, ld_c)
        p.barrier()
        if upto < 2:
            p.final_wait()
            p.emit()
            return nc
        p.g(lambda g: g.collective_compute("AllGather", ALU.bypass, replica_groups=GROUPS4,
                                           ins=[k0.opt()], outs=[k0_ag.opt()]))
        agv0 = p.g(lambda g: g.collective_compute("AllGather", ALU.bypass, replica_groups=GROUPS4,
                                           ins=[v0.opt()], outs=[v0_ag.opt()]))
        p.g(lambda g: g.collective_compute("AllGather", ALU.bypass, replica_groups=GROUPS4,
                                           ins=[kmax2.opt()], outs=[km_ag.opt()]))
        p.g(lambda g: g.collective_compute("AllGather", ALU.bypass, replica_groups=GROUPS8,
                                           ins=[cnt.opt()], outs=[cnt_ag.opt()]))
        for _h in range(8):
            for _r in range(4):
                p.g(lambda g, _h=_h, _r=_r: g.dma_start(
                    out=v0_hp.rearrange("h p (r tt x) -> h p r (tt x)", r=4, x=128)[_h][:, _r].rearrange("p (tt x) -> p tt x", x=128),
                    in_=v0_ag.rearrange("(r p) (tt h x) -> r h p tt x", p=128, h=8, x=128)[_r][_h]),
                    wait=(agv0,), inc=(_h == 7 and _r == 3), dma=True)
        p.barrier()
        if upto < 3:
            p.final_wait()
            p.emit()
            return nc
        emit_attn0(nc, p, {"q0": q0,
                           "kfull": k0_ag.rearrange("(r p) (c t) -> p c r t", p=128, c=8),
                           "vfull": v0_hp, "qn2": qn2,
                           "kmax2g": km_ag.rearrange("r h -> (r h)").rearrange("(o x) -> o x", o=1),
                           "s0": s0}, cs, ld_c)
        p.barrier()
        if upto < 4:
            p.final_wait()
            p.emit()
            return nc
        emit_moe0(nc, p, {"xn0": xn0, "x_fm": x_fm, "s0": s0, "wd": wd, "pos": pos,
                          "cnt_all": cnt_ag, "core_mask": core_mask,
                          "sgu_pairs": sgu_pairs, "sdown": sdown, "sgate": sgate,
                          "egu": egu, "edw": edw, "x1": x1}, cs, ld_c)
        p.barrier()
        if upto < 5:
            p.final_wait()
            p.emit()
            return nc
        emit_pre1(nc, p, {"x1": x1, "wq1_t": wq1_t, "wk1_t": wk1_t, "wv1_t": wv1_t,
                          "wr1_t": wr1_t, "sqk_t": sqk_t, "rope_cos": rope_cos,
                          "rope_sin": rope_sin, "x1n": x1n, "q1": q1, "k1": k1,
                          "v1": v1, "g1d": g1d}, cs, ld_c)
        p.barrier()
        if upto < 6:
            p.final_wait()
            p.emit()
            return nc
        p.g(lambda g: g.collective_compute("AllGather", ALU.bypass, replica_groups=GROUPS4,
                                           ins=[k1.opt()], outs=[k1_ag.opt()]))
        agv1 = p.g(lambda g: g.collective_compute("AllGather", ALU.bypass, replica_groups=GROUPS4,
                                           ins=[v1.opt()], outs=[v1_ag.opt()]))
        for _h in range(8):
            for _r in range(4):
                p.g(lambda g, _h=_h, _r=_r: g.dma_start(
                    out=v1_hp.rearrange("h p (r tt x) -> h p r (tt x)", r=4, x=128)[_h][:, _r].rearrange("p (tt x) -> p tt x", x=128),
                    in_=v1_ag.rearrange("(r p) (tt h x) -> r h p tt x", p=128, h=8, x=128)[_r][_h]),
                    wait=(agv1,), inc=(_h == 7 and _r == 3), dma=True)
        p.barrier()
        if upto < 7:
            p.final_wait()
            p.emit()
            return nc
        emit_attn1(nc, p, {"q1": q1,
                           "k1full": k1_ag.rearrange("(r p) (c t) -> p c r t", p=128, c=8),
                           "v1full": v1_hp, "wo_t": wo_t, "s1": s1}, cs, ld_c)
        p.barrier()
        if upto < 8:
            p.final_wait()
            p.emit()
            return nc
        emit_ffn1(nc, p, {"x1": x1, "x1n": x1n, "s1": s1, "wmh_t": wmh_t, "wmg_t": wmg_t,
                          "kffwd_r": kffwd_r, "vffwd": vffwd, "g1d": g1d, "outp": out_o},
                  cs, ld_c)
        p.final_wait()
        p.emit()
    return nc


def kernel(**inputs):
    consts = _consts_np()
    x = np.asarray(inputs["L0_attn_x"], dtype=np.float32)
    wqkv = np.asarray(inputs["L0_attn_W_qkv"], dtype=np.float32)
    gate_w = np.ascontiguousarray(np.asarray(inputs["L0_ffn_gate_w"], dtype=np.float32))
    sgw = np.asarray(inputs["L0_ffn_shared_gate_w"], dtype=np.float32)
    sguw = np.asarray(inputs["L0_ffn_shared_gate_up_w"], dtype=np.float32)
    sdw = np.ascontiguousarray(np.asarray(inputs["L0_ffn_shared_down_w"], dtype=np.float32))
    eguw = np.ascontiguousarray(np.asarray(inputs["L0_ffn_expert_gate_up_w"], dtype=np.float32))
    edww = np.ascontiguousarray(np.asarray(inputs["L0_ffn_expert_down_w"], dtype=np.float32))
    wq1 = np.asarray(inputs["L1_attn_Wq"], dtype=np.float32)
    wk1 = np.asarray(inputs["L1_attn_Wk"], dtype=np.float32)
    wv1 = np.asarray(inputs["L1_attn_Wv"], dtype=np.float32)
    wo1 = np.asarray(inputs["L1_attn_Wo"], dtype=np.float32)
    sqk = np.asarray(inputs["L1_attn_s_qk"], dtype=np.float32)
    wr1 = np.asarray(inputs["L1_ffn_w_router"], dtype=np.float32)
    wmh = np.asarray(inputs["L1_ffn_w_mh"], dtype=np.float32)
    wmg = np.asarray(inputs["L1_ffn_w_mg"], dtype=np.float32)
    kff = np.asarray(inputs["L1_ffn_k_ffwd"], dtype=np.float32)
    vff = np.ascontiguousarray(np.asarray(inputs["L1_ffn_v_ffwd"], dtype=np.float32))

    x_parts = _host_split_fm(x)
    sgu_pairs = np.zeros((D, 16, 256), dtype=np.float32)
    for i in range(16):
        sgu_pairs[:, i, 0:128] = sguw[:, 128 * i:128 * (i + 1)]
        sgu_pairs[:, i, 128:256] = sguw[:, 2048 + 128 * i:2048 + 128 * (i + 1)]
    shared = {
        "wq_t": np.ascontiguousarray(wqkv[0:D].T),
        "wk_t": np.ascontiguousarray(wqkv[D:2 * D].T),
        "wv_t": np.ascontiguousarray(wqkv[2 * D:3 * D].T),
        "gate_w": gate_w,
        "sgu_pairs": np.ascontiguousarray(sgu_pairs.reshape(D, 16 * 256)),
        "sdown": sdw,
        "sgate": np.ascontiguousarray(sgw.reshape(8, 128).T),
        "egu": eguw, "edw": edww,
        "wq1_t": np.ascontiguousarray(wq1.T),
        "wk1_t": np.ascontiguousarray(wk1.T),
        "wv1_t": np.ascontiguousarray(wv1.T),
        "wo_t": np.ascontiguousarray(wo1.T),
        "sqk_t": np.ascontiguousarray(sqk.T),
        "wr1_t": np.ascontiguousarray(wr1.T),
        "wmh_t": np.ascontiguousarray(wmh.T),
        "wmg_t": np.ascontiguousarray(wmg.T),
        "kffwd_r": np.ascontiguousarray(kff.transpose(0, 1, 3, 2)),
        "vffwd": vff,
    }
    for kname in ALL_CONSTS:
        shared[kname] = consts[kname]

    ncf = build_fused()
    in_maps = []
    for c in range(NCORES):
        cos_t, sin_t = _rope_tables(c)
        m = dict(shared)
        m["x_fm"] = x_parts[c]
        m["core_mask"] = np.ascontiguousarray((np.arange(8) < c).astype(np.float32).reshape(8, 1))
        m["rope_cos"] = cos_t
        m["rope_sin"] = sin_t
        in_maps.append(m)
    res = run_bass_kernel_spmd(ncf, in_maps, core_ids=list(range(NCORES)))
    out = np.empty((2 * SEQ, D), dtype=np.float32)
    for c in range(NCORES):
        out[c * T:(c + 1) * T] = res.results[c]["outp"].reshape(128, 8, T).transpose(2, 1, 0).reshape(T, D)
    return out.reshape(2, SEQ, D)


# version 18
# speedup vs baseline: 1.2107x; 1.0157x over previous
import numpy as np
import ml_dtypes
from contextlib import ExitStack

import concourse.bass as bass
import concourse.mybir as mybir
from concourse.bass_utils import run_bass_kernel_spmd

F32 = mybir.dt.float32
BF16 = mybir.dt.bfloat16
AF = mybir.ActivationFunctionType
ALU = mybir.AluOpType
AX = mybir.AxisListType
BF = ml_dtypes.bfloat16

NCORES = 8
T = 2048
D = 1024
NH = 16
HD = 64
SEQ = 8192
NE0 = 64
CAP0 = 512
NE1 = 16
H1 = 8
D1 = 128
S1 = 256


class Prog:

    ENGS = ("t", "v", "s", "g", "y")
    SEMS = ("t", "v", "s", "g", "y", "gd", "yd")

    def __init__(self, nc):
        self.nc = nc
        self.streams = {e: [] for e in self.ENGS}
        self.cnt = {e: 0 for e in self.SEMS}

    CHAINED = ("v", "s", "g")

    def op(self, eng, fn, wait=(), inc=False, dma=False):
        ev = None
        wait = tuple(w for w in wait if w is not None)
        if not dma and eng in self.CHAINED:
            if self.cnt[eng] > 0:
                wait = wait + ((eng, self.cnt[eng]),)
            inc = True
        if dma:
            self.cnt[eng + "d"] += 16
            self.streams[eng].append([fn, wait, (eng + "d", self.cnt[eng + "d"]), True])
            if inc:
                return self.dma_ev(eng)
            return None
        if inc:
            self.cnt[eng] += 1
            ev = (eng, self.cnt[eng])
        self.streams[eng].append([fn, wait, ev, False])
        return ev

    def mark(self, eng):
        for entry in reversed(self.streams[eng]):
            fn, wait, ev, dma = entry
            if fn is None or fn == "MILESTONE":
                continue
            if ev is not None:
                return ev
            self.cnt[eng] += 1
            entry[2] = (eng, self.cnt[eng])
            return entry[2]
        return None

    def dma_ev(self, eng):
        semd = eng + "d"
        self.streams[eng].append([None, ((semd, self.cnt[semd]),), None, False])
        self.cnt[eng] += 1
        ev = (eng, self.cnt[eng])
        self.streams[eng].append(["MILESTONE", (), ev, False])
        return ev

    def t(self, fn, wait=(), inc=False):
        return self.op("t", fn, wait, inc)

    def v(self, fn, wait=(), inc=False):
        return self.op("v", fn, wait, inc)

    def s(self, fn, wait=(), inc=False):
        return self.op("s", fn, wait, inc)

    def g(self, fn, wait=(), inc=False, dma=False):
        return self.op("g", fn, wait, inc, dma)

    def y(self, fn, wait=(), inc=False, dma=True):
        return self.op("y", fn, wait, inc, dma)

    def barrier(self):
        evs = []
        for e in self.ENGS:
            if self.cnt.get(e + "d", 0) > 0:
                evs.append(self.dma_ev(e))
            self.mark(e)
            if self.cnt[e] > 0:
                evs.append((e, self.cnt[e]))
        for e in self.ENGS:
            self.streams[e].append([None, tuple(evs), None, False])
        return evs

    def emit(self):
        nc = self.nc
        ctx = ExitStack()
        sems = {e: ctx.enter_context(nc.semaphore(f"sem_{e}")) for e in self.SEMS}
        sdummy = ctx.enter_context(nc.sbuf_tensor("prog_sdummy", [1, 8], F32))
        with ctx:
            blk = ctx.enter_context(nc.Block())

            def run_stream(eng_name):
                def body(h):
                    waited = {}
                    for fn, wait, ev, dma in self.streams[eng_name]:
                        for (sname, val) in wait:
                            if waited.get(sname, -1) >= val:
                                continue
                            h.wait_ge(sems[sname], val)
                            waited[sname] = val
                        if fn is None:
                            continue
                        if fn == "MILESTONE":
                            if eng_name in ("v", "g"):
                                ins = h.engine_nop()
                            elif eng_name == "s":
                                ins = h.activation(sdummy[:], sdummy[:], AF.Copy)
                            else:
                                ins = h.nop()
                            ins.then_inc(sems[eng_name], 1)
                            continue
                        ins = fn(h)
                        if ev is not None:
                            ins.then_inc(sems[eng_name + "d"] if dma else sems[eng_name], 16 if dma else 1)
                return body

            blk.sync(run_stream("y"))
            blk.gpsimd(run_stream("g"))
            blk.tensor(run_stream("t"))
            blk.vector(run_stream("v"))
            blk.scalar(run_stream("s"))
        return nc

    def final_wait(self):
        evs = []
        for e in ("y", "g"):
            if self.cnt.get(e + "d", 0) > 0:
                evs.append((e + "d", self.cnt[e + "d"]))
        self.streams["y"].append([None, tuple(evs), None, False])


_PHASE_N = [0]


def _nc():
    return bass.Bass("TRN2", target_bir_lowering=False, debug=False)


CONST_SHAPES = {
    "c_ones_1x128": ([1, 128], F32),
    "c_ones_col": ([128, 1], F32),
    "c_ident_f32": ([128, 128], F32),
    "c_ident_bf": ([128, 128], BF16),
    "c_ltriu": ([128, 128], F32),
    "c_rev64": ([128, 64], F32),
    "c_rev16": ([128, 16], F32),
    "c_ones_colbf": ([128, 1], BF16),
    "c_ones_rowbf": ([1, 128], BF16),
}


def _consts_np():
    return {
        "c_ones_1x128": np.ones((1, 128), dtype=np.float32),
        "c_ones_col": np.ones((128, 1), dtype=np.float32),
        "c_ident_f32": np.eye(128, dtype=np.float32),
        "c_ident_bf": np.eye(128, dtype=BF),
        "c_ltriu": np.triu(np.ones((128, 128), dtype=np.float32), 1),
        "c_rev64": np.broadcast_to(64.0 - np.arange(64, dtype=np.float32), (128, 64)).copy(),
        "c_rev16": np.broadcast_to(16.0 - np.arange(16, dtype=np.float32), (128, 16)).copy(),
        "c_ones_colbf": np.ones((128, 1), dtype=BF),
        "c_ones_rowbf": np.ones((1, 128), dtype=BF),
    }


def _use_consts(nc, p, S, names):
    tiles = {}
    ev = None
    for n in names:
        sh, dt = CONST_SHAPES[n]
        ap = nc.dram_tensor(n, sh, dt, kind="ExternalInput").ap()
        tl = S(n + "_sb", list(sh), dt)
        tiles[n] = tl
        ev = p.y(lambda y, tl=tl, ap=ap: y.dma_start(out=tl[:], in_=ap), inc=True)
    return tiles, ev


def _ld_fm(p, hbm_ap, sbuf_t, cast=False, wait=()):
    src = hbm_ap.rearrange("(c p) t -> p c t", p=128)
    if cast:
        return p.g(lambda h: h.dma_start(out=sbuf_t[:], in_=src), wait=wait, inc=True, dma=True)
    return p.y(lambda h: h.dma_start(out=sbuf_t[:], in_=src), wait=wait, inc=True)



def emit_pre0(nc, p, io, cs, ld_c):
    x_h = io["x_fm"]
    wq_h = io["wq_t"]
    wk_h = io["wk_t"]
    wv_h = io["wv_t"]
    gw_h = io["gate_w"]

    xn_o = io["xn0"]
    q_o = io["q0"]
    k_o = io["k0"]
    v_o = io["v0"]
    qn2_o = io["qn2"]
    kmax_o = io["kmax2"]
    wd_o = io["wd"]
    pos_o = io["pos"]
    cnt_o = io["cnt"]

    ctx = ExitStack()
    _pfx = "ph%d_" % _PHASE_N[0]
    _PHASE_N[0] += 1
    S = lambda n, sh, dt=F32: ctx.enter_context(nc.sbuf_tensor(_pfx + n, sh, dt))
    PS = lambda n, sh, dt=F32: ctx.enter_context(nc.psum_tensor(_pfx + n, sh, dt))
    with ctx:
        x_sb = S("x_sb", [128, 8, T])
        xnb = S("xnb", [128, 8, T], BF16)
        xnf_t = S("xnf_t", [128, 8, 512])
        wq_sb = S("wq_sb", [128, 8, D], BF16)
        wk_sb = S("wk_sb", [128, 8, D], BF16)
        gw_sb = S("gw_sb", [128, 8, NE0])
        sq_full = S("sq_full", [128, 8, 512])
        rows = S("rows", [1, 512 * 6])
        qstage = S("qstage", [128, 2, 512], BF16)
        kstage = S("kstage", [128, 2, 512], BF16)
        vstage = S("vstage", [128, 2, 1024], BF16)
        sqq = S("sqq", [128, 2, 512])
        qn_stage = S("qn_stage", [1, 2, 512])
        kn_row = S("kn_row", [1, 512])
        kmax_sb = S("kmax_sb", [1, NH])
        tmp11 = S("tmp11", [1, 1])
        probs = S("probs", [128, NE0])
        eq_t = S("eq_t", [128, NE0])
        tmq = S("tmq", [128, NE0])
        sel8 = S("sel8", [128, NE0])
        wd_t = S("wd_t", [128, NE0])
        negm = S("negm", [128, 1])
        mx = S("mx", [128, 1])
        am = S("am", [128, 1])
        wsum = S("wsum", [128, 1])
        carry = S("carry", [1, NE0])
        wd_sb = S("wd_sb", [128, 16, NE0])
        pos_sb = S("pos_sb", [128, 16, NE0])

        pA = PS("pA", [1, 512])
        pB = PS("pB", [1, 512])
        pC = PS("pC", [128, 512])
        pD = PS("pD", [128, 512])
        pE = PS("pE", [128, 512])
        pF = PS("pF", [128, 512])

        ones_col = cs["c_ones_col"]
        ones_row = cs["c_ones_1x128"]
        ltriu = cs["c_ltriu"]
        rev64 = cs["c_rev64"]

        ld_x = _ld_fm(p, x_h, x_sb)
        ld_wq = _ld_fm(p, wq_h, wq_sb, cast=True)
        ld_wk = _ld_fm(p, wk_h, wk_sb, cast=True)
        ld_gw = _ld_fm(p, gw_h, gw_sb)

        mean_r = rows[0:1, 0:512]
        msq_r = rows[0:1, 512:1024]
        var_r = rows[0:1, 1024:1536]
        std_r = rows[0:1, 1536:2048]
        rstd_r = rows[0:1, 2048:2560]
        tmp_r = rows[0:1, 2560:3072]

        ev_store = []
        p.v(lambda v: v.memset(carry[:], 0.0))
        p.v(lambda v: v.memset(kmax_sb[:], -1e30))

        for tc in range(4):
            ts = slice(512 * tc, 512 * (tc + 1))
            sq_evs = []
            for ci in range(8):
                sq_evs.append(p.s(lambda s, ci=ci, ts=ts: s.activation(
                    sq_full[:, ci, :], x_sb[:, ci, ts], AF.Square), wait=(ld_x,), inc=True))
            for ci in range(8):
                p.t(lambda t, ci=ci, ts=ts: t.matmul(
                    pA[0:1, :], ones_col[:], x_sb[:, ci, ts],
                    start=(ci == 0), stop=(ci == 7)), wait=(ld_x, ld_c))
            mm_ev = None
            for ci in range(8):
                mm_ev = p.t(lambda t, ci=ci: t.matmul(
                    pB[0:1, :], ones_col[:], sq_full[:, ci, :],
                    start=(ci == 0), stop=(ci == 7)), wait=(sq_evs[ci],), inc=(ci == 7))
            e1 = p.v(lambda v: v.tensor_scalar(mean_r, pA[0:1, :], 1.0 / D, None, ALU.mult), wait=(mm_ev,), inc=True)
            p.v(lambda v: v.tensor_scalar(msq_r, pB[0:1, :], 1.0 / D, None, ALU.mult))
            p.v(lambda v: v.tensor_tensor(tmp_r, mean_r, mean_r, ALU.mult))
            p.v(lambda v: v.tensor_tensor(var_r, msq_r, tmp_r, ALU.subtract))
            e3 = p.v(lambda v: v.tensor_scalar(var_r, var_r, 1e-5, None, ALU.add), inc=True)
            e4 = p.s(lambda s: s.activation(std_r, var_r, AF.Sqrt), wait=(e3,), inc=True)
            e5 = p.v(lambda v: v.reciprocal(rstd_r, std_r), wait=(e4,), inc=True)
            b1 = p.t(lambda t: t.matmul(pC[:], ones_row[:], mean_r, start=True, stop=True),
                     wait=(e1, e5), inc=True)
            b2 = p.t(lambda t: t.matmul(pD[:], ones_row[:], rstd_r, start=True, stop=True),
                     wait=(e5,), inc=True)
            ev_xn = None
            for ci in range(8):
                p.v(lambda v, ci=ci, ts=ts: v.tensor_tensor(xnf_t[:, ci, :], x_sb[:, ci, ts], pC[:], ALU.subtract), wait=(b1, b2))
                p.v(lambda v, ci=ci: v.tensor_tensor(xnf_t[:, ci, :], xnf_t[:, ci, :], pD[:], ALU.mult))
                ev_xn = p.v(lambda v, ci=ci, ts=ts: v.tensor_copy(xnb[:, ci, ts], xnf_t[:, ci, :]), inc=True)

            for st4 in range(4):
                tt = 4 * tc + st4
                for ci in range(8):
                    p.t(lambda t, ci=ci, st4=st4: t.matmul(
                        pC[0:128, 0:64], xnf_t[:, ci, 128 * st4:128 * (st4 + 1)], gw_sb[:, ci, :],
                        start=(ci == 0), stop=(ci == 7)), wait=(ev_xn, ld_gw))
                mlg = p.mark("t")
                r1 = p.v(lambda v: v.reduce_max(mx[:], pC[0:128, 0:64], axis=AX.X), wait=(mlg,), inc=True)
                e2 = p.v(lambda v: v.tensor_scalar(negm[:], mx[:], -1.0, None, ALU.mult), wait=(r1,), inc=True)
                e3b = p.s(lambda s: s.activation(probs[:], pC[0:128, 0:64], AF.Exp, bias=negm[:]), wait=(e2,), inc=True)
                e4b = p.v(lambda v: v.reduce_sum(wsum[:], probs[:], axis=AX.X), wait=(e3b,), inc=True)
                e5b = p.v(lambda v: v.reciprocal(mx[:], wsum[:]), wait=(e4b,), inc=True)
                p.v(lambda v: v.tensor_scalar(probs[:], probs[:], mx[:], None, ALU.mult), wait=(e5b,))
                p.v(lambda v: v.memset(sel8[:], 0.0))
                p.v(lambda v: v.memset(wd_t[:], 0.0))
                ek = p.v(lambda v: v.memset(wsum[:], 0.0), inc=True)
                for kk in range(8):
                    rr1 = p.v(lambda v: v.reduce_max(mx[:], probs[:], axis=AX.X), wait=(ek,), inc=True)
                    p.v(lambda v: v.tensor_scalar(eq_t[:], probs[:], mx[:], None, ALU.is_ge), wait=(rr1,))
                    p.v(lambda v: v.tensor_tensor(tmq[:], eq_t[:], rev64[:], ALU.mult), wait=(ld_c,))
                    rr2 = p.v(lambda v: v.reduce_max(am[:], tmq[:], axis=AX.X), inc=True)
                    p.v(lambda v: v.tensor_scalar(eq_t[:], tmq[:], am[:], None, ALU.is_ge), wait=(rr2,))
                    p.v(lambda v: v.tensor_tensor(sel8[:], sel8[:], eq_t[:], ALU.add))
                    p.v(lambda v: v.scalar_tensor_tensor(wd_t[:], eq_t[:], mx[:], wd_t[:], ALU.mult, ALU.add))
                    p.v(lambda v: v.tensor_tensor(wsum[:], wsum[:], mx[:], ALU.add))
                    ek = p.v(lambda v: v.scalar_tensor_tensor(probs[:], eq_t[:], -2.0, probs[:], ALU.mult, ALU.add), inc=True)
                en = p.v(lambda v: v.reciprocal(mx[:], wsum[:]), wait=(ek,), inc=True)
                p.v(lambda v, tt=tt: v.tensor_scalar(wd_sb[:, tt, :], wd_t[:], mx[:], None, ALU.mult), wait=(en,))
                esel = p.v(lambda v: v.engine_nop(), inc=True)
                m1 = p.t(lambda t: t.matmul(pD[0:128, 0:64], ltriu[:], sel8[:], start=True, stop=False),
                         wait=(esel,))
                m2 = p.t(lambda t: t.matmul(pD[0:128, 0:64], ones_row[:], carry[:], start=False, stop=True), inc=True)
                m3 = p.t(lambda t: t.matmul(pB[0:1, 0:64], ones_col[:], sel8[:], start=True, stop=True),
                         wait=(esel,), inc=True)
                p.v(lambda v, tt=tt: v.tensor_copy(pos_sb[:, tt, :], pD[0:128, 0:64]), wait=(m2,))
                p.v(lambda v: v.tensor_tensor(carry[:], carry[:], pB[0:1, 0:64], ALU.add), wait=(m3,))
                p.barrier()
            p.barrier()

        bar = p.barrier()
        ev_store.append(p.y(lambda y: y.dma_start(out=xn_o, in_=xnb[:].rearrange("p c t -> p (c t)")), inc=True))
        ev_store.append(p.y(lambda y: y.dma_start(out=wd_o, in_=wd_sb[:].rearrange("p c t -> p (c t)")), inc=True))
        ev_store.append(p.y(lambda y: y.dma_start(out=pos_o, in_=pos_sb[:].rearrange("p c t -> p (c t)")), inc=True))
        ev_store.append(p.y(lambda y: y.dma_start(out=cnt_o, in_=carry[:]), inc=True))

        prev_ev = {0: None, 1: None}
        prev_store = {}
        prev_qn = {0: None, 1: None}
        prev_sqq = {0: None, 1: None}
        prev_qnst = {0: None, 1: None}
        for which, (w_sb, stage, out_hbm, ld_w, scale) in enumerate(
                [(wq_sb, qstage, q_o, ld_wq, 0.125), (wk_sb, kstage, k_o, ld_wk, 1.0)]):
            for co in range(8):
                for tc in range(4):
                    ts = slice(512 * tc, 512 * (tc + 1))
                    buf = (co * 4 + tc) % 2
                    pX = pE if buf == 0 else pF
                    for ci in range(8):
                        p.t(lambda t, ci=ci, co=co, ts=ts, w_sb=w_sb, pX=pX: t.matmul(
                            pX[:], w_sb[:, ci, 128 * co:128 * (co + 1)], xnb[:, ci, ts],
                            start=(ci == 0), stop=(ci == 7)),
                            wait=(ld_w,) + ((prev_ev[buf],) if prev_ev[buf] else ()))
                    mm = p.mark("t")
                    ev_ev = p.s(lambda s, buf=buf, scale=scale, stage=stage, pX=pX: s.activation(
                        stage[:, buf, :], pX[:], AF.Copy, scale=scale),
                        wait=(mm,) + ((prev_store.get((which, buf)),) if prev_store.get((which, buf)) else ()), inc=True)
                    prev_ev[buf] = ev_ev
                    prev_store[(which, buf)] = p.y(
                        lambda y, buf=buf, co=co, ts=ts, stage=stage, out_hbm=out_hbm:
                        y.dma_start(out=out_hbm.rearrange("p (c t) -> p c t", c=8)[:, co, ts],
                                    in_=stage[:, buf, :]), wait=(ev_ev,), inc=True)
                    sq_ev = p.s(lambda s, buf=buf, stage=stage: s.activation(
                        sqq[:, buf, :], stage[:, buf, :], AF.Square),
                        wait=(ev_ev,) + ((prev_sqq[buf],) if prev_sqq[buf] else ()), inc=True)
                    for hh in range(2):
                        h = 2 * co + hh
                        pN = pA if hh == 0 else pB
                        mqn = p.t(lambda t, hh=hh, buf=buf, pN=pN: t.matmul(
                            pN[0:1, :], ones_col[64 * hh:64 * (hh + 1), :], sqq[64 * hh:64 * (hh + 1), buf, :],
                            start=True, stop=True), wait=(sq_ev,) + ((prev_qn[hh],) if prev_qn[hh] else ()), inc=True)
                        prev_sqq[buf] = mqn
                        if which == 0:
                            cpe = p.v(lambda v, buf=buf, pN=pN: v.tensor_copy(qn_stage[0:1, buf, :], pN[0:1, :]),
                                      wait=(mqn,) + ((prev_qnst[buf],) if prev_qnst[buf] else ()), inc=True)
                            prev_qn[hh] = cpe
                            prev_qnst[buf] = p.y(lambda y, h=h, ts=ts, buf=buf: y.dma_start(
                                out=qn2_o[h:h + 1, ts], in_=qn_stage[0:1, buf, :]), wait=(cpe,), inc=True)
                        else:
                            cpe = p.v(lambda v, pN=pN: v.tensor_copy(kn_row[:], pN[0:1, :]), wait=(mqn,), inc=True)
                            rm = p.v(lambda v: v.reduce_max(tmp11[:], kn_row[:], axis=AX.X), wait=(cpe,), inc=True)
                            prev_qn[hh] = p.v(lambda v, h=h: v.tensor_tensor(
                                kmax_sb[0:1, h:h + 1], kmax_sb[0:1, h:h + 1], tmp11[:], ALU.max), wait=(rm,), inc=True)
            if which == 0:
                pmk = p.mark("t")
                ld_wv = _ld_fm(p, wv_h, wq_sb, cast=True, wait=(pmk,))
        ev_store.append(p.y(lambda y: y.dma_start(out=kmax_o, in_=kmax_sb[:]),
                            wait=(prev_qn[0], prev_qn[1]), inc=True))

        prev_pf = p.mark("s")
        prev_vst = {0: None, 1: None}
        for tt in range(16):
            buf = tt % 2
            ev_half = []
            for half in range(2):
                for ci in range(8):
                    p.t(lambda t, tt=tt, half=half, ci=ci: t.matmul(
                        pE[:], xnb[:, ci, 128 * tt:128 * (tt + 1)], wq_sb[:, ci, 512 * half:512 * (half + 1)],
                        start=(ci == 0), stop=(ci == 7)),
                        wait=(ld_wv,) + ((prev_pf,) if prev_pf else ())),
                mm = p.mark("t")
                ev_half.append(p.v(lambda v, buf=buf, half=half: v.tensor_copy(
                    vstage[:, buf, 512 * half:512 * (half + 1)], pE[:]),
                    wait=(mm,) + ((prev_vst[buf],) if prev_vst[buf] else ()), inc=True))
                prev_pf = ev_half[-1]
            ev_st = p.y(lambda y, tt=tt, buf=buf: y.dma_start(
                out=v_o.rearrange("p (c t) -> p c t", c=16)[:, tt, :], in_=vstage[:, buf, :]),
                wait=tuple(ev_half), inc=True)
            prev_vst[buf] = ev_st
        ev_store.append(ev_st)




def _host_split_fm(x):
    xf = np.ascontiguousarray(x.reshape(2 * SEQ, D))
    return [np.ascontiguousarray(xf[c * T:(c + 1) * T].T) for c in range(NCORES)]


def _rope_tables(core):
    pos = (core % 4) * T + np.arange(T, dtype=np.float64)
    inv = 1.0 / (10000.0 ** (np.arange(0, HD, 2, dtype=np.float64) / HD))
    ang = pos[None, :] * inv[(np.arange(HD) // 2)][:, None]
    cos = np.cos(ang).astype(np.float32)
    sin = np.sin(ang).astype(np.float32)
    sgn = np.where(np.arange(HD) % 2 == 0, -1.0, 1.0).astype(np.float32)[:, None]
    sin_s = sin * sgn
    return (np.ascontiguousarray(np.tile(cos, (2, 1))),
            np.ascontiguousarray(np.tile(sin_s, (2, 1))))


def _attn_inputs(r1s, core, qk, kk, vk, qnk=None, kmk=None):
    grp = range(0, 4) if core < 4 else range(4, 8)
    k_full = np.concatenate([r1s[c][kk].reshape(128, 8, T) for c in grp], axis=2)
    v_all = np.concatenate([r1s[c][vk].reshape(128, 16, 8, 128) for c in grp], axis=1)
    v_dev = np.ascontiguousarray(v_all.transpose(2, 0, 1, 3).reshape(8, 128, 64 * 128))
    m = {"kfull" if qnk else "k1full": np.ascontiguousarray(k_full.reshape(128, 8 * SEQ)),
         "vfull" if qnk else "v1full": v_dev}
    m["q0" if qnk else "q1"] = np.ascontiguousarray(r1s[core][qk])
    if qnk:
        m["qn2"] = np.ascontiguousarray(r1s[core][qnk])
        m["kmax2g"] = np.ascontiguousarray(
            np.concatenate([r1s[c][kmk][0] for c in grp]).reshape(1, 64))
    return m


def kernel(**inputs):
    consts = _consts_np()
    x = np.asarray(inputs["L0_attn_x"], dtype=np.float32)
    wqkv = np.asarray(inputs["L0_attn_W_qkv"], dtype=np.float32)
    gate_w = np.ascontiguousarray(np.asarray(inputs["L0_ffn_gate_w"], dtype=np.float32))
    sgw = np.asarray(inputs["L0_ffn_shared_gate_w"], dtype=np.float32)
    sguw = np.asarray(inputs["L0_ffn_shared_gate_up_w"], dtype=np.float32)
    sdw = np.ascontiguousarray(np.asarray(inputs["L0_ffn_shared_down_w"], dtype=np.float32))
    eguw = np.ascontiguousarray(np.asarray(inputs["L0_ffn_expert_gate_up_w"], dtype=np.float32))
    edww = np.ascontiguousarray(np.asarray(inputs["L0_ffn_expert_down_w"], dtype=np.float32))
    wq1 = np.asarray(inputs["L1_attn_Wq"], dtype=np.float32)
    wk1 = np.asarray(inputs["L1_attn_Wk"], dtype=np.float32)
    wv1 = np.asarray(inputs["L1_attn_Wv"], dtype=np.float32)
    wo1 = np.asarray(inputs["L1_attn_Wo"], dtype=np.float32)
    sqk = np.asarray(inputs["L1_attn_s_qk"], dtype=np.float32)
    wr1 = np.asarray(inputs["L1_ffn_w_router"], dtype=np.float32)
    wmh = np.asarray(inputs["L1_ffn_w_mh"], dtype=np.float32)
    wmg = np.asarray(inputs["L1_ffn_w_mg"], dtype=np.float32)
    kff = np.asarray(inputs["L1_ffn_k_ffwd"], dtype=np.float32)
    vff = np.ascontiguousarray(np.asarray(inputs["L1_ffn_v_ffwd"], dtype=np.float32))

    x_parts = _host_split_fm(x)
    wq_t = np.ascontiguousarray(wqkv[0:D].T)
    wk_t = np.ascontiguousarray(wqkv[D:2 * D].T)
    wv_t = np.ascontiguousarray(wqkv[2 * D:3 * D].T)
    sgu_pairs = np.zeros((D, 16, 256), dtype=np.float32)
    for i in range(16):
        sgu_pairs[:, i, 0:128] = sguw[:, 128 * i:128 * (i + 1)]
        sgu_pairs[:, i, 128:256] = sguw[:, 2048 + 128 * i:2048 + 128 * (i + 1)]
    sgu_pairs = np.ascontiguousarray(sgu_pairs.reshape(D, 16 * 256))
    sgate_r = np.ascontiguousarray(sgw.reshape(8, 128).T)
    wq1_t = np.ascontiguousarray(wq1.T)
    wk1_t = np.ascontiguousarray(wk1.T)
    wv1_t = np.ascontiguousarray(wv1.T)
    wo1_t = np.ascontiguousarray(wo1.T)
    sqk_t = np.ascontiguousarray(sqk.T)
    wr1_t = np.ascontiguousarray(wr1.T)
    wmh_t = np.ascontiguousarray(wmh.T)
    wmg_t = np.ascontiguousarray(wmg.T)
    kff_r = np.ascontiguousarray(kff.transpose(0, 1, 3, 2))

    cores = list(range(NCORES))

    nc1 = build_pre0()
    in1 = []
    for c in cores:
        m = {"x_fm": x_parts[c], "wq_t": wq_t, "wk_t": wk_t, "wv_t": wv_t, "gate_w": gate_w}
        for kname in ["c_ones_1x128", "c_ones_col", "c_ltriu", "c_rev64"]:
            m[kname] = consts[kname]
        in1.append(m)
    r1s = run_bass_kernel_spmd(nc1, in1, core_ids=cores).results

    nc2 = build_attn0()
    in2 = []
    for c in cores:
        m = _attn_inputs(r1s, c, "q0", "k0", "v0", "qn2", "kmax2")
        for kname in ["c_ones_1x128", "c_ones_colbf"]:
            m[kname] = consts[kname]
        in2.append(m)
    r2s = run_bass_kernel_spmd(nc2, in2, core_ids=cores).results

    cnt_all = np.ascontiguousarray(np.stack([r1s[c]["cnt"][0] for c in cores]))
    nc3 = build_moe0()
    in3 = []
    for c in cores:
        m = {"xn0": r1s[c]["xn0"], "x_fm": x_parts[c], "s0": r2s[c]["s0"],
             "wd": r1s[c]["wd"], "pos": r1s[c]["pos"], "cnt_all": cnt_all,
             "core_mask": np.ascontiguousarray((np.arange(8) < c).astype(np.float32).reshape(8, 1)),
             "sgu_pairs": sgu_pairs, "sdown": sdw, "sgate": sgate_r,
             "egu": eguw, "edw": edww}
        for kname in ["c_ones_1x128", "c_ident_f32"]:
            m[kname] = consts[kname]
        in3.append(m)
    r3s = run_bass_kernel_spmd(nc3, in3, core_ids=cores).results

    nc4 = build_pre1()
    in4 = []
    for c in cores:
        cos_t, sin_t = _rope_tables(c)
        m = {"x1": r3s[c]["x1"], "wq1_t": wq1_t, "wk1_t": wk1_t, "wv1_t": wv1_t,
             "wr1_t": wr1_t, "sqk_t": sqk_t, "rope_cos": cos_t, "rope_sin": sin_t}
        for kname in ["c_ones_1x128", "c_ones_col", "c_rev16", "c_ident_f32"]:
            m[kname] = consts[kname]
        in4.append(m)
    r4s = run_bass_kernel_spmd(nc4, in4, core_ids=cores).results

    nc5 = build_attn1()
    in5 = []
    for c in cores:
        m = _attn_inputs(r4s, c, "q1", "k1", "v1")
        m["wo_t"] = wo1_t
        for kname in ["c_ones_1x128", "c_ones_colbf", "c_ident_f32"]:
            m[kname] = consts[kname]
        in5.append(m)
    r5s = run_bass_kernel_spmd(nc5, in5, core_ids=cores).results

    nc6 = build_ffn1()
    in6 = []
    for c in cores:
        m = {"x1": r3s[c]["x1"], "x1n": r4s[c]["x1n"], "s1": r5s[c]["s1"],
             "wmh_t": wmh_t, "wmg_t": wmg_t, "kffwd_r": kff_r, "vffwd": vff,
             "g1d": r4s[c]["g1d"]}
        for kname in ["c_ones_1x128"]:
            m[kname] = consts[kname]
        in6.append(m)
    r6s = run_bass_kernel_spmd(nc6, in6, core_ids=cores).results

    out = np.empty((2 * SEQ, D), dtype=np.float32)
    for c in cores:
        out[c * T:(c + 1) * T] = r6s[c]["outp"].reshape(128, 8, T).transpose(2, 1, 0).reshape(T, D)
    return out.reshape(2, SEQ, D)



def emit_attn0(nc, p, io, cs, ld_c):
    q_h = io["q0"]
    k_h = io["kfull"]
    v_h = io["vfull"]
    qn2_h = io["qn2"]
    km_h = io["kmax2g"]

    s0_o = io["s0"]

    ctx = ExitStack()
    _pfx = "ph%d_" % _PHASE_N[0]
    _PHASE_N[0] += 1
    S = lambda n, sh, dt=F32: ctx.enter_context(nc.sbuf_tensor(_pfx + n, sh, dt))
    PS = lambda n, sh, dt=F32: ctx.enter_context(nc.psum_tensor(_pfx + n, sh, dt))
    with ctx:
        qfm = S("qfm", [128, 8, T], BF16)
        kbuf = S("kbuf", [128, 2, SEQ], BF16)
        vbuf = S("vbuf", [128, 2, 64 * 128], BF16)
        e_sb = S("e_sb", [128, 2, 2, 1536], BF16)
        sstage = S("sstage", [128, 2, 512])
        qn16 = S("qn16", [NH, T])
        qm4 = S("qm4", [NH, 4])
        t64 = S("t64", [1, 16, 4])
        negc_sb = S("negc_sb", [128, 64])
        km2 = S("km2", [1, NH])
        rz = S("rz", [1, 2, 512], BF16)
        rzf = S("rzf", [1, 2, 512])

        pSa = PS("pSa", [128, 1536])
        pSb = PS("pSb", [128, 1536])
        pAV = PS("pAV", [128, 512])
        pZ = PS("pZ", [128, 512])

        ones_row = cs["c_ones_1x128"]
        ones_rowbf = cs["c_ones_rowbf"]
        ones_colbf = cs["c_ones_colbf"]

        ld_q = p.y(lambda y: y.dma_start(out=qfm[:], in_=q_h.rearrange("p (c t) -> p c t", c=8)), inc=True)
        kmg = S("kmg", [1, 64])
        ld_km = p.y(lambda y: y.dma_start(out=kmg[:], in_=km_h), inc=True)
        ld_qn = p.y(lambda y: y.dma_start(out=qn16[:], in_=qn2_h), inc=True)
        e0 = p.v(lambda v: v.tensor_tensor(km2[:], kmg[0:1, 0:16], kmg[0:1, 16:32], ALU.max), wait=(ld_km,))
        p.v(lambda v: v.tensor_tensor(km2[:], km2[:], kmg[0:1, 32:48], ALU.max))
        p.v(lambda v: v.tensor_tensor(km2[:], km2[:], kmg[0:1, 48:64], ALU.max))
        ev_km2 = p.mark("v")

        NG = 22
        GS = [min(3, 64 - 3 * g) for g in range(NG)]

        ld_k = {}
        ld_v = {}
        for hp in range(2):
            ld_k[hp] = p.y(lambda y, hp=hp: y.dma_start(
                out=kbuf[:, hp % 2, :].rearrange("p (r t) -> p r t", r=4), in_=k_h[:, hp]), inc=True)
            ld_v[hp] = p.g(lambda g, hp=hp: g.dma_start(
                out=vbuf[:, hp % 2, :], in_=v_h[hp]), inc=True, dma=True)

        for j in range(4):
            p.v(lambda v, j=j: v.reduce_max(qm4[:, j:j + 1], qn16[:, 512 * j:512 * (j + 1)], axis=AX.X),
                wait=(ld_qn,) if j == 0 else ())
        km4mm = p.t(lambda t: t.matmul(pSb[0:16, 0:4], km2[0:1, :], ones_row[0:1, 0:4],
                                       start=True, stop=True), wait=(ev_km2,), inc=True)
        p.v(lambda v: v.tensor_tensor(qm4[:], qm4[:], pSb[0:16, 0:4], ALU.mult), wait=(km4mm,))
        sq_ev = p.s(lambda s: s.activation(qm4[:], qm4[:], AF.Sqrt), wait=(p.mark("v"),), inc=True)
        ngv = p.v(lambda v: v.tensor_scalar(qm4[:], qm4[:], -1.0, None, ALU.mult), wait=(sq_ev,), inc=True)
        dq = p.y(lambda y: y.dma_start(out=t64[:],
                                       in_=qm4[:].rearrange("p j -> p () j")), wait=(ngv,), inc=True)
        bcn = p.t(lambda t: t.matmul(pZ[:, 0:64], ones_row[:], t64[:].rearrange("o h j -> o (h j)"),
                                     start=True, stop=True), wait=(dq,), inc=True)
        nccA = p.v(lambda v: v.tensor_copy(negc_sb[:], pZ[:, 0:64]), wait=(bcn,), inc=True)

        prev_c1 = None
        prev_mres = nccA
        prev_expa_l = None
        prev_exp_l = None
        prev_store = {0: None, 1: None}
        pend = [None]
        for hp in range(8):
            kb = hp % 2
            ld_k_ev = ld_k[hp]
            ld_v_ev = ld_v[hp]
            for q4 in range(4):
                qsl = slice(512 * q4, 512 * (q4 + 1))
                wA0 = (ld_k_ev, ld_q) + ((prev_expa_l,) if prev_expa_l else ())
                wB0 = (ld_k_ev, ld_q) + ((prev_exp_l,) if prev_exp_l else ())
                qa0 = [None] * GS[0]
                qb0 = [None] * GS[0]
                for j in range(GS[0]):
                    p.t(lambda t, j=j, kb=kb, hp=hp, qsl=qsl: t.matmul(
                        pSa[:, 512 * j:512 * (j + 1)], kbuf[0:64, kb, 128 * j:128 * (j + 1)],
                        qfm[0:64, hp, qsl], start=True, stop=True, tile_position=(0, 0)),
                        wait=wA0 if j == 0 else ())
                    qa0[j] = p.mark("t")
                    p.t(lambda t, j=j, kb=kb, hp=hp, qsl=qsl: t.matmul(
                        pSb[:, 512 * j:512 * (j + 1)], kbuf[64:128, kb, 128 * j:128 * (j + 1)],
                        qfm[64:128, hp, qsl], start=True, stop=True, tile_position=(64, 0)),
                        wait=wB0 if j == 0 else ())
                    qb0[j] = p.mark("t")
                if pend[0] is not None:
                    m_ev, st_ev, sb_p = pend[0]()
                    prev_mres = m_ev
                    prev_store[sb_p] = st_ev
                    pend[0] = None

                ev_ea = {}
                ev_eb = {}
                ev_av = [None] * NG
                ca = 4 * 2 * hp + q4
                cb = 4 * (2 * hp + 1) + q4
                for g in range(NG):
                    gb = g % 2
                    if g >= 1:
                        qa0 = [None] * GS[g]
                        qb0 = [None] * GS[g]
                        for j in range(GS[g]):
                            ks = 3 * g + j
                            wA = (ev_ea[(g - 1, j)],) if j < GS[g - 1] else ()
                            wB = (ev_eb[(g - 1, j)],) if j < GS[g - 1] else ()
                            p.t(lambda t, ks=ks, j=j, kb=kb, hp=hp, qsl=qsl: t.matmul(
                                pSa[:, 512 * j:512 * (j + 1)], kbuf[0:64, kb, 128 * ks:128 * (ks + 1)],
                                qfm[0:64, hp, qsl], start=True, stop=True, tile_position=(0, 0)), wait=wA)
                            qa0[j] = p.mark("t")
                            p.t(lambda t, ks=ks, j=j, kb=kb, hp=hp, qsl=qsl: t.matmul(
                                pSb[:, 512 * j:512 * (j + 1)],
                                kbuf[64:128, kb, 128 * ks:128 * (ks + 1)],
                                qfm[64:128, hp, qsl], start=True, stop=True, tile_position=(64, 0)), wait=wB)
                            qb0[j] = p.mark("t")
                    for j in range(GS[g]):
                        w_exp = (qa0[j], nccA)
                        if g >= 2 and j == 0:
                            w_exp = w_exp + ((ev_av[g - 2],) if ev_av[g - 2] else ())
                        ev_ea[(g, j)] = p.s(lambda s, gb=gb, j=j, ca=ca: s.activation(
                            e_sb[:, gb, 0, 512 * j:512 * (j + 1)], pSa[:, 512 * j:512 * (j + 1)],
                            AF.Exp, bias=negc_sb[:, ca:ca + 1]), wait=w_exp, inc=True)
                        ev_eb[(g, j)] = p.s(lambda s, gb=gb, j=j, cb=cb: s.activation(
                            e_sb[:, gb, 1, 512 * j:512 * (j + 1)], pSb[:, 512 * j:512 * (j + 1)],
                            AF.Exp, bias=negc_sb[:, cb:cb + 1]), wait=(qb0[j],), inc=True)
                    for j in range(GS[g]):
                        ks = 3 * g + j
                        first = (g == 0 and j == 0)
                        last = (g == NG - 1 and j == GS[g] - 1)
                        wav = (ev_ea[(g, j)],) + ((ld_v_ev,) if j == 0 else ())
                        p.t(lambda t, ks=ks, j=j, gb=gb, kb=kb, first=first, last=last: t.matmul(
                            pAV[0:64, :], vbuf[:, kb, :].rearrange("p (tt d) -> p tt d", d=128)[:, ks, 0:64],
                            e_sb[:, gb, 0, 512 * j:512 * (j + 1)],
                            start=first, stop=last, tile_position=(0, 0)),
                            wait=wav + ((prev_c1,) if (first and prev_c1) else ()))
                        p.t(lambda t, ks=ks, j=j, gb=gb, kb=kb, first=first, last=last: t.matmul(
                            pAV[64:128, :], vbuf[:, kb, :].rearrange("p (tt d) -> p tt d", d=128)[:, ks, 64:128],
                            e_sb[:, gb, 1, 512 * j:512 * (j + 1)],
                            start=first, stop=last, tile_position=(0, 64), skip_group_check=True),
                            wait=(ev_eb[(g, j)],))
                        p.t(lambda t, ks=ks, j=j, gb=gb, first=first, last=last: t.matmul(
                            pZ[0:1, :], ones_colbf[:], e_sb[:, gb, 0, 512 * j:512 * (j + 1)],
                            start=first, stop=last, tile_position=(0, 0)),
                            wait=((prev_mres,) if (first and prev_mres) else ()))
                        p.t(lambda t, ks=ks, j=j, gb=gb, first=first, last=last: t.matmul(
                            pZ[64:65, :], ones_colbf[:], e_sb[:, gb, 1, 512 * j:512 * (j + 1)],
                            start=first, stop=last, tile_position=(0, 64), skip_group_check=True))
                    ev_av[g] = p.mark("t")
                sb = q4 % 2
                c1 = p.v(lambda v, sb=sb: v.tensor_copy(sstage[:, sb, :], pAV[:]),
                         wait=(ev_av[NG - 1],) + ((prev_store[sb],) if prev_store[sb] else ()), inc=True)
                p.v(lambda v: v.reciprocal(rzf[0:1, 0, :], pZ[0:1, :]))
                p.v(lambda v: v.reciprocal(rzf[0:1, 1, :], pZ[64:65, :]))
                rcast = p.v(lambda v: v.tensor_copy(rz[:].rearrange("o h t -> o (h t)"), rzf[:].rearrange("o h t -> o (h t)")), inc=True)
                prev_c1 = c1
                prev_expa_l = ev_ea[(NG - 1, 0)]
                prev_exp_l = ev_eb[(NG - 1, 0)]

                def mk_pend(sb=sb, hp=hp, qsl=qsl, rcast=rcast):
                    def emit():
                        p.t(lambda t: t.matmul(pZ[0:64, :], ones_rowbf[0:1, 0:64], rz[0:1, 0, :],
                                               start=True, stop=True, tile_position=(0, 0)),
                            wait=(rcast,))
                        b2 = p.t(lambda t: t.matmul(pZ[64:128, :], ones_rowbf[0:1, 0:64], rz[0:1, 1, :],
                                                    start=True, stop=True, tile_position=(0, 64),
                                                    skip_group_check=True), inc=True)
                        mres = p.v(lambda v: v.tensor_tensor(sstage[:, sb, :], sstage[:, sb, :],
                                                             pZ[:, 0:512], ALU.mult), wait=(b2,), inc=True)
                        st = p.y(lambda y: y.dma_start(
                            out=s0_o.rearrange("p (c t) -> p c t", c=8)[:, hp, qsl], in_=sstage[:, sb, :]),
                            wait=(mres,), inc=True)
                        return mres, st, sb
                    return emit
                pend[0] = mk_pend()
            if hp + 2 <= 7:
                nhp = hp + 2
                lastuse = p.mark("t")
                ld_k[nhp] = p.y(lambda y, nhp=nhp: y.dma_start(
                    out=kbuf[:, nhp % 2, :].rearrange("p (r t) -> p r t", r=4), in_=k_h[:, nhp]),
                    wait=(lastuse,), inc=True)
                ld_v[nhp] = p.g(lambda g, nhp=nhp: g.dma_start(
                    out=vbuf[:, nhp % 2, :], in_=v_h[nhp]),
                    wait=(lastuse,), inc=True, dma=True)
        if pend[0] is not None:
            pend[0]()
            pend[0] = None




def emit_moe0(nc, p, io, cs, ld_c):
    xn_h = io["xn0"]
    x_hh = io["x_fm"]
    s0_h = io["s0"]
    wd_h = io["wd"]
    pos_h = io["pos"]
    cnt_h = io["cnt_all"]
    mask_h = io["core_mask"]
    sgu_h = io["sgu_pairs"]
    sdw_h = io["sdown"]
    sgw_h = io["sgate"]
    egu_h = io["egu"]
    edw_h = io["edw"]

    x1_o = io["x1"]
    wkt_h = nc.dram_tensor("wkt_scratch", [NE0, T], F32, kind="Internal").ap()

    ctx = ExitStack()
    _pfx = "ph%d_" % _PHASE_N[0]
    _PHASE_N[0] += 1
    S = lambda n, sh, dt=F32: ctx.enter_context(nc.sbuf_tensor(_pfx + n, sh, dt))
    PS = lambda n, sh, dt=F32: ctx.enter_context(nc.psum_tensor(_pfx + n, sh, dt))
    with ctx:
        xnb = S("xnb", [128, 8, T], BF16)
        y_acc = S("y_acc", [128, 8, T])
        act_t = S("act_t", [128, 16, 512], BF16)
        sgu_db = S("sgu_db", [128, 2, 8, 256], BF16)
        sdw_db = S("sdw_db", [128, 2, D], BF16)
        keepw = S("keepw", [128, 16, NE0])
        grow = S("grow", [1, T])
        gate_bc = S("gate_bc", [128, T], BF16)
        egu_sb = S("egu_sb", [128, 2, 8, D], BF16)
        edw_sb = S("edw_sb", [128, 4, D], BF16)
        silu_t = S("silu_t", [128, 512])
        t1f = S("t1f", [128, T])
        base_row = S("base_row", [1, NE0])
        cnt_sb = S("cnt_sb", [8, NE0])
        mask_sb = S("mask_sb", [8, 1])
        x1st = S("x1st", [128, T])
        pos_t = S("pos_t", [128, NE0])

        pAll = PS("pAll", [128, 4096])

        def pb(i):
            return pAll[:, 512 * i:512 * (i + 1)]

        ones_row = cs["c_ones_1x128"]
        ident = cs["c_ident_f32"]

        ld_xn = p.y(lambda y: y.dma_start(out=xnb[:], in_=xn_h.rearrange("p (c t) -> p c t", c=8)), inc=True)
        ld_cnt = p.y(lambda y: y.dma_start(out=cnt_sb[:], in_=cnt_h), inc=True)
        ld_msk = p.y(lambda y: y.dma_start(out=mask_sb[:], in_=mask_h), inc=True)
        ld_wd = p.y(lambda y: y.dma_start(out=keepw[:], in_=wd_h.rearrange("p (c t) -> p c t", c=16)), inc=True)

        wkT_sb = t1f[0:64, :].rearrange("p (c x) -> p c x", x=128)
        mb = p.t(lambda t: t.matmul(pAll[0:1, 0:NE0], mask_sb[:], cnt_sb[:], start=True, stop=True),
                 wait=(ld_cnt, ld_msk), inc=True)
        e1 = p.v(lambda v: v.tensor_copy(base_row[:], pAll[0:1, 0:NE0]), wait=(mb,), inc=True)
        bb = p.t(lambda t: t.matmul(pAll[0:128, 0:NE0], ones_row[:], base_row[:], start=True, stop=True),
                 wait=(e1, ld_c), inc=True)
        for tt in range(16):
            lp = p.y(lambda y, tt=tt: y.dma_start(out=pos_t[:], in_=pos_h.rearrange("p (c t) -> p c t", c=16)[:, tt, :]),
                     wait=(p.mark("v"),), inc=True)
            p.v(lambda v: v.tensor_tensor(pos_t[:], pos_t[:], pAll[0:128, 0:NE0], ALU.add), wait=(lp, bb))
            p.v(lambda v: v.tensor_scalar(pos_t[:], pos_t[:], float(CAP0), None, ALU.is_lt))
            p.v(lambda v, tt=tt: v.tensor_tensor(keepw[:, tt, :], keepw[:, tt, :], pos_t[:], ALU.mult), wait=(ld_wd,))
        ev_keep = p.mark("v")
        for tt in range(16):
            trw = p.t(lambda t, tt=tt: t.transpose(pAll[0:64, 512:640], keepw[:, tt, :], ident[:]),
                      wait=(ev_keep,) + ((p.mark("v"),) if tt else ()), inc=True)
            p.v(lambda v, tt=tt: v.tensor_copy(wkT_sb[:, tt, :], pAll[0:64, 512:640]), wait=(trw,))
        ev_wkt = p.mark("v")
        st_wkt = p.y(lambda y: y.dma_start(out=wkt_h, in_=wkT_sb.rearrange("p c t -> p (c t)")), wait=(ev_wkt,), inc=True)

        sgw_sb = S("sgw_sb", [128, 8], BF16)
        ld_sgw = p.g(lambda g: g.dma_start(out=sgw_sb[:], in_=sgw_h), inc=True, dma=True)
        for tc in range(4):
            for ci in range(8):
                p.t(lambda t, tc=tc, ci=ci: t.matmul(
                    pAll[0:1, 1024 + 512 * tc:1024 + 512 * (tc + 1)], sgw_sb[:, ci:ci + 1],
                    xnb[:, ci, 512 * tc:512 * (tc + 1)],
                    start=(ci == 0), stop=(ci == 7)), wait=(ld_sgw, ld_xn, st_wkt))
        mg = p.mark("t")
        sg = p.s(lambda s: s.activation(grow[:], pAll[0:1, 1024:1024 + 2048], AF.Sigmoid), wait=(mg,), inc=True)
        for tc in range(4):
            p.t(lambda t, tc=tc: t.matmul(pb(tc), ones_row[:], grow[0:1, 512 * tc:512 * (tc + 1)],
                                          start=True, stop=True), wait=(sg,))
        mgb = p.mark("t")
        ev_free = p.v(lambda v: v.tensor_copy(gate_bc[:], pAll[:, 0:2048]), wait=(mgb,), inc=True)
        ev_grow_free = mgb

        prev_y = None
        for tg in range(4):
            tsl = slice(512 * tg, 512 * (tg + 1))
            for i in range(16):
                dbuf = i % 2
                lw = p.g(lambda g, i=i, dbuf=dbuf: g.dma_start(
                    out=sgu_db[:, dbuf, :, :],
                    in_=sgu_h.rearrange("(c p) (i x) -> p c i x", p=128, i=16)[:, :, i, :]),
                    wait=(p.mark("t"),), inc=True, dma=True)
                for ci in range(8):
                    p.t(lambda t, i=i, tsl=tsl, ci=ci, dbuf=dbuf: t.matmul(
                        pb(0), sgu_db[:, dbuf, ci, 0:128], xnb[:, ci, tsl],
                        start=(ci == 0), stop=(ci == 7)), wait=(lw, ev_free))
                for ci in range(8):
                    p.t(lambda t, i=i, tsl=tsl, ci=ci, dbuf=dbuf: t.matmul(
                        pb(1), sgu_db[:, dbuf, ci, 128:256], xnb[:, ci, tsl],
                        start=(ci == 0), stop=(ci == 7)))
                mmi = p.mark("t")
                sv = p.s(lambda s: s.activation(silu_t[:], pb(0), AF.Silu), wait=(mmi,), inc=True)
                p.v(lambda v: v.tensor_tensor(silu_t[:], silu_t[:], pb(1), ALU.mult), wait=(sv,))
                ev_free = p.v(lambda v, i=i, tsl=tsl: v.tensor_tensor(
                    act_t[:, i, :], silu_t[:], gate_bc[:, tsl], ALU.mult), inc=True)
            for ifc in range(16):
                dbuf = ifc % 2
                lwi = p.g(lambda g, ifc=ifc, dbuf=dbuf: g.dma_start(
                    out=sdw_db[:, dbuf, :],
                    in_=sdw_h.rearrange("(c p) d -> p c d", p=128)[:, ifc, :]),
                    wait=(p.mark("t"),), inc=True, dma=True)
                for fc in range(8):
                    w = (lwi, ev_free) + ((prev_y,) if prev_y else ())
                    p.t(lambda t, ifc=ifc, fc=fc, dbuf=dbuf: t.matmul(
                        pb(fc), sdw_db[:, dbuf, 128 * fc:128 * (fc + 1)], act_t[:, ifc, :],
                        start=(ifc == 0), stop=(ifc == 15)), wait=w)
            mfc = p.mark("t")
            for fc in range(8):
                prev_y = p.v(lambda v, fc=fc, tsl=tsl: v.tensor_copy(y_acc[:, fc, tsl], pb(fc)),
                             wait=(mfc,), inc=True)
            ev_free = prev_y

        wrow_db = S("wrow_bf", [1, 2048], BF16)
        w_bc = gate_bc
        prev_wread = None
        lgu = {0: p.g(lambda g: g.dma_start(
            out=egu_sb[:, 0], in_=egu_h[0].rearrange("(c p) d -> p c d", p=128)),
            wait=(p.mark("t"),), inc=True, dma=True)}
        ldw = {0: p.g(lambda g: g.dma_start(
            out=edw_sb[:], in_=edw_h[0].rearrange("(c p) d -> p c d", p=128)),
            inc=True, dma=True)}
        lwr = {0: p.g(lambda g: g.dma_start(out=wrow_db[0:1, :], in_=wkt_h[0:1, :]),
                      wait=(st_wkt,), inc=True, dma=True)}
        ev_bankpair = {}
        for e in range(NE0):
            eb = e % 2
            for tc in range(4):
                p.t(lambda t, tc=tc, eb=eb: t.matmul(
                    pb(4 + tc), cs["c_ones_rowbf"][:], wrow_db[0:1, 512 * tc:512 * (tc + 1)],
                    start=True, stop=True), wait=(lwr[e], ev_free))
            prev_wread = p.mark("t")
            ev_wbc0 = p.v(lambda v: v.tensor_copy(w_bc[:, 0:1024], pAll[:, 2048:3072]),
                          wait=(prev_wread, ev_free), inc=True)
            ev_wbc = p.v(lambda v: v.tensor_copy(w_bc[:, 1024:2048], pAll[:, 3072:4096]), inc=True)
            if e + 1 < NE0:
                lwr[e + 1] = p.g(lambda g, e=e: g.dma_start(
                    out=wrow_db[0:1, :], in_=wkt_h[e + 1:e + 2, :]),
                    wait=(prev_wread,), inc=True, dma=True)
            ev_act = [ev_wbc0, ev_wbc0]
            for tg in range(4):
                tsl = slice(512 * tg, 512 * (tg + 1))
                for i in range(4):
                    gb2 = 2 * (i % 2)
                    for ci in range(8):
                        bp = ev_bankpair.get(i % 2) if (i < 2 and ev_bankpair) else None
                        w = (lgu[e], (ev_wbc0 if tg < 2 else ev_wbc), ev_act[i % 2]) + ((bp,) if bp else ())
                        p.t(lambda t, i=i, tsl=tsl, ci=ci, eb=eb, gb2=gb2: t.matmul(
                            pb(gb2), egu_sb[:, eb, ci, 128 * i:128 * (i + 1)], xnb[:, ci, tsl],
                            start=(ci == 0), stop=(ci == 7)), wait=w)
                    for ci in range(8):
                        p.t(lambda t, i=i, tsl=tsl, ci=ci, eb=eb, gb2=gb2: t.matmul(
                            pb(gb2 + 1), egu_sb[:, eb, ci, 512 + 128 * i:512 + 128 * (i + 1)], xnb[:, ci, tsl],
                            start=(ci == 0), stop=(ci == 7)))
                    mmi = p.mark("t")
                    sv = p.s(lambda s, gb2=gb2: s.activation(silu_t[:], pb(gb2), AF.Silu), wait=(mmi,), inc=True)
                    p.v(lambda v, gb2=gb2: v.tensor_tensor(silu_t[:], silu_t[:], pb(gb2 + 1), ALU.mult), wait=(sv,))
                    ev_act[i % 2] = p.v(lambda v, i=i, tsl=tsl: v.tensor_tensor(
                        act_t[:, i, :], silu_t[:], w_bc[:, tsl], ALU.mult), inc=True)
                if tg == 3:
                    gu_last = p.mark("t")
                    if e + 1 < NE0:
                        lgu[e + 1] = p.g(lambda g, e=e: g.dma_start(
                            out=egu_sb[:, (e + 1) % 2], in_=egu_h[e + 1].rearrange("(c p) d -> p c d", p=128)),
                            wait=(gu_last,), inc=True, dma=True)
                y_add = {}
                for fc in range(8):
                    for ifc in range(4):
                        w = (ldw[e], ev_act[1]) if (fc == 0 and ifc == 0) else ()
                        p.t(lambda t, ifc=ifc, fc=fc, eb=eb: t.matmul(
                            pb(fc), edw_sb[:, ifc, 128 * fc:128 * (fc + 1)], act_t[:, ifc, :],
                            start=(ifc == 0), stop=(ifc == 3)), wait=w)
                    mfc = p.mark("t")
                    y_add[fc] = p.v(lambda v, fc=fc, tsl=tsl: v.tensor_tensor(
                        y_acc[:, fc, tsl], y_acc[:, fc, tsl], pb(fc), ALU.add), wait=(mfc,), inc=True)
                if tg == 3 and e + 1 < NE0:
                    ldw[e + 1] = p.g(lambda g, e=e: g.dma_start(
                        out=edw_sb[:], in_=edw_h[e + 1].rearrange("(c p) d -> p c d", p=128)),
                        wait=(mfc,), inc=True, dma=True)
                prev_y = y_add[7]
                ev_bankpair = {0: y_add[1], 1: y_add[3]}
                ev_free = prev_y

        xchunk = silu_t
        prev_st = None
        for fc in range(8):
            lx = p.y(lambda y, fc=fc: y.dma_start(
                out=t1f[:], in_=x_hh.rearrange("(c p) t -> p c t", p=128)[:, fc, :]),
                wait=(p.mark("v"),) + ((prev_st,) if prev_st else ()), inc=True)
            ls = p.y(lambda y, fc=fc: y.dma_start(
                out=x1st[:], in_=s0_h.rearrange("p (c t) -> p c t", c=8)[:, fc, :]), inc=True)
            gg = p.s(lambda s, fc=fc: s.activation(y_acc[:, fc, :], y_acc[:, fc, :], AF.Silu),
                     wait=(prev_y,), inc=True)
            p.v(lambda v, fc=fc: v.tensor_tensor(x1st[:], x1st[:], y_acc[:, fc, :], ALU.mult), wait=(gg, ls))
            p.v(lambda v, fc=fc: v.tensor_tensor(x1st[:], x1st[:], t1f[:], ALU.add), wait=(lx,))
            ev_x1 = p.mark("v")
            prev_st = p.y(lambda y, fc=fc: y.dma_start(
                out=x1_o.rearrange("p (c t) -> p c t", c=8)[:, fc, :], in_=x1st[:]), wait=(ev_x1,), inc=True)




def emit_pre1(nc, p, io, cs, ld_c):
    x_h = io["x1"]
    wq_h = io["wq1_t"]
    wk_h = io["wk1_t"]
    wv_h = io["wv1_t"]
    wr_h = io["wr1_t"]
    sqk_h = io["sqk_t"]
    cos_h = io["rope_cos"]
    sin_h = io["rope_sin"]

    xn_o = io["x1n"]
    q_o = io["q1"]
    k_o = io["k1"]
    v_o = io["v1"]
    g_o = io["g1d"]

    SWAP = [(i + 1) if i % 2 == 0 else (i - 1) for i in range(32)]

    ctx = ExitStack()
    _pfx = "ph%d_" % _PHASE_N[0]
    _PHASE_N[0] += 1
    S = lambda n, sh, dt=F32: ctx.enter_context(nc.sbuf_tensor(_pfx + n, sh, dt))
    PS = lambda n, sh, dt=F32: ctx.enter_context(nc.psum_tensor(_pfx + n, sh, dt))
    with ctx:
        x_sb = S("x_sb", [128, 8, T])
        xnb = S("xnb", [128, 8, T], BF16)
        xnf_t = S("xnf_t", [128, 8, 512])
        sq_full = xnf_t
        wq_sb = S("wq_sb", [128, 8, D], BF16)
        wk_sb = S("wk_sb", [128, 8, D], BF16)
        wr_sb = S("wr_sb", [128, 8, NE1])
        sqk_sb = S("sqk_sb", [64, NH])
        qscol = S("qscol", [128, 1])
        kscol = S("kscol", [128, 1])
        cos_sb = S("cos_sb", [128, T])
        sin_sb = S("sin_sb", [128, T])
        rows = S("rows", [1, 512 * 6])
        qstage = S("qstage", [128, 2, 512], BF16)
        kstage = S("kstage", [128, 2, 512], BF16)
        vstage = S("vstage", [128, 2, 1024], BF16)
        qt = S("qt", [128, 512])
        qs = S("qs", [128, 512])
        qro = qt
        sqq = qs
        rn_rows = S("rn_rows", [1, 2, 512], BF16)
        rn_f = S("rn_f", [1, 2, 512])
        probs = S("probs", [128, NE1])
        sig_t = S("sig_t", [128, NE1])
        eq_t = S("eq_t", [128, NE1])
        tmq = S("tmq", [128, NE1])
        sel2 = S("sel2", [128, NE1])
        mx = S("mx", [128, 1])
        am = S("am", [128, 1])
        negm = S("negm", [128, 1])
        gT_sb = S("gT_sb", [16, 16, 128])

        pA = PS("pA", [1, 512])
        pB = PS("pB", [1, 512])
        pC = PS("pC", [128, 512])
        pD = PS("pD", [128, 512])
        pE = PS("pE", [128, 512])
        pF = PS("pF", [128, 512])

        ones_col = cs["c_ones_col"]
        ones_row = cs["c_ones_1x128"]
        rev16 = cs["c_rev16"]
        ident = cs["c_ident_f32"]

        ld_x = p.y(lambda y: y.dma_start(out=x_sb[:], in_=x_h.rearrange("p (c t) -> p c t", c=8)), inc=True)
        ld_wq = _ld_fm(p, wq_h, wq_sb, cast=True)
        ld_wk = _ld_fm(p, wk_h, wk_sb, cast=True)
        ld_wr = _ld_fm(p, wr_h, wr_sb)
        ld_sqk = p.y(lambda y: y.dma_start(out=sqk_sb[:], in_=sqk_h), inc=True)
        ld_cos = p.y(lambda y: y.dma_start(out=cos_sb[:], in_=cos_h), inc=True)
        ld_sin = p.y(lambda y: y.dma_start(out=sin_sb[:], in_=sin_h), inc=True)

        mean_r = rows[0:1, 0:512]
        msq_r = rows[0:1, 512:1024]
        var_r = rows[0:1, 1024:1536]
        std_r = rows[0:1, 1536:2048]
        rstd_r = rows[0:1, 2048:2560]
        tmp_r = rows[0:1, 2560:3072]

        for tc in range(4):
            ts = slice(512 * tc, 512 * (tc + 1))
            sq_evs = []
            for ci in range(8):
                sq_evs.append(p.s(lambda s, ci=ci, ts=ts: s.activation(
                    sq_full[:, ci, :], x_sb[:, ci, ts], AF.Square), wait=(ld_x,), inc=True))
            for ci in range(8):
                p.t(lambda t, ci=ci, ts=ts: t.matmul(
                    pA[0:1, :], ones_col[:], x_sb[:, ci, ts],
                    start=(ci == 0), stop=(ci == 7)), wait=(ld_x, ld_c))
            mm_ev = None
            for ci in range(8):
                mm_ev = p.t(lambda t, ci=ci: t.matmul(
                    pB[0:1, :], ones_col[:], sq_full[:, ci, :],
                    start=(ci == 0), stop=(ci == 7)), wait=(sq_evs[ci],), inc=(ci == 7))
            e1 = p.v(lambda v: v.tensor_scalar(mean_r, pA[0:1, :], 1.0 / D, None, ALU.mult), wait=(mm_ev,), inc=True)
            p.v(lambda v: v.tensor_scalar(msq_r, pB[0:1, :], 1.0 / D, None, ALU.mult))
            p.v(lambda v: v.tensor_tensor(tmp_r, mean_r, mean_r, ALU.mult))
            p.v(lambda v: v.tensor_tensor(var_r, msq_r, tmp_r, ALU.subtract))
            e3 = p.v(lambda v: v.tensor_scalar(var_r, var_r, 1e-5, None, ALU.add), inc=True)
            e4 = p.s(lambda s: s.activation(std_r, var_r, AF.Sqrt), wait=(e3,), inc=True)
            e5 = p.v(lambda v: v.reciprocal(rstd_r, std_r), wait=(e4,), inc=True)
            b1 = p.t(lambda t: t.matmul(pC[:], ones_row[:], mean_r, start=True, stop=True),
                     wait=(e1, e5), inc=True)
            b2 = p.t(lambda t: t.matmul(pD[:], ones_row[:], rstd_r, start=True, stop=True),
                     wait=(e5,), inc=True)
            ev_xn = None
            for ci in range(8):
                p.v(lambda v, ci=ci, ts=ts: v.tensor_tensor(xnf_t[:, ci, :], x_sb[:, ci, ts], pC[:], ALU.subtract), wait=(b1, b2))
                p.v(lambda v, ci=ci: v.tensor_tensor(xnf_t[:, ci, :], xnf_t[:, ci, :], pD[:], ALU.mult))
                ev_xn = p.v(lambda v, ci=ci, ts=ts: v.tensor_copy(xnb[:, ci, ts], xnf_t[:, ci, :]), inc=True)

            for st4 in range(4):
                tt = 4 * tc + st4
                for ci in range(8):
                    p.t(lambda t, ci=ci, st4=st4: t.matmul(
                        pC[0:128, 0:NE1], xnf_t[:, ci, 128 * st4:128 * (st4 + 1)], wr_sb[:, ci, :],
                        start=(ci == 0), stop=(ci == 7)), wait=(ev_xn, ld_wr))
                mlg = p.mark("t")
                sig = p.s(lambda s: s.activation(sig_t[:], pC[0:128, 0:NE1], AF.Sigmoid), wait=(mlg,), inc=True)
                cp = p.v(lambda v: v.tensor_copy(probs[:], pC[0:128, 0:NE1]), wait=(mlg,), inc=True)
                p.v(lambda v: v.memset(sel2[:], 0.0))
                ek = p.v(lambda v: v.engine_nop(), inc=True)
                for kk in range(2):
                    rr1 = p.v(lambda v: v.reduce_max(mx[:], probs[:], axis=AX.X), wait=(ek,), inc=True)
                    p.v(lambda v: v.tensor_scalar(eq_t[:], probs[:], mx[:], None, ALU.is_ge), wait=(rr1,))
                    p.v(lambda v: v.tensor_tensor(tmq[:], eq_t[:], rev16[:], ALU.mult), wait=(ld_c,))
                    rr2 = p.v(lambda v: v.reduce_max(am[:], tmq[:], axis=AX.X), inc=True)
                    p.v(lambda v: v.tensor_scalar(eq_t[:], tmq[:], am[:], None, ALU.is_ge), wait=(rr2,))
                    p.v(lambda v: v.tensor_tensor(sel2[:], sel2[:], eq_t[:], ALU.add))
                    ek = p.v(lambda v: v.scalar_tensor_tensor(probs[:], eq_t[:], -1e30, probs[:], ALU.mult, ALU.add), inc=True)
                gsel = p.v(lambda v: v.tensor_tensor(sel2[:], sel2[:], sig_t[:], ALU.mult), wait=(sig, ek), inc=True)
                trg = p.t(lambda t, tt=tt: t.transpose(pD[0:16, 0:128], sel2[:], ident[:]), wait=(gsel,), inc=True)
                p.v(lambda v, tt=tt: v.tensor_copy(gT_sb[:, tt, :], pD[0:16, 0:128]), wait=(trg,))
        ev_g = p.mark("v")
        st_g = p.y(lambda y: y.dma_start(out=g_o, in_=gT_sb[:].rearrange("p c t -> p (c t)")), wait=(ev_g,), inc=True)
        xn_all = ev_g
        ev_store = [st_g]
        ev_store.append(p.y(lambda y: y.dma_start(out=xn_o, in_=xnb[:].rearrange("p c t -> p (c t)")), wait=(xn_all,), inc=True))

        st8 = {"prev_pe": None, "prev_store": {0: None, 1: None}}

        def do_proj(w_sb, stage, out_hbm, ld_w, scol, sfac):
            prev_pe = st8["prev_pe"]
            prev_store = st8["prev_store"]
            for co in range(8):
                sc1 = p.v(lambda v, co=co, sfac=sfac, scol=scol: v.tensor_scalar(
                    scol[0:64, :], sqk_sb[:, 2 * co:2 * co + 1], sfac, None, ALU.mult), wait=(ld_sqk,), inc=True)
                sc2 = p.v(lambda v, co=co, sfac=sfac, scol=scol: v.tensor_scalar(
                    scol[64:128, :], sqk_sb[:, 2 * co + 1:2 * co + 2], sfac, None, ALU.mult), inc=True)
                for tc in range(4):
                    ts = slice(512 * tc, 512 * (tc + 1))
                    buf = (co * 4 + tc) % 2
                    for ci in range(8):
                        p.t(lambda t, ci=ci, co=co, ts=ts, w_sb=w_sb: t.matmul(
                            pE[:], w_sb[:, ci, 128 * co:128 * (co + 1)], xnb[:, ci, ts],
                            start=(ci == 0), stop=(ci == 7)),
                            wait=(xn_all, ld_w) + ((prev_pe,) if prev_pe else ()))
                    mm = p.mark("t")
                    c1 = p.v(lambda v: v.tensor_copy(qt[:], pE[:]), wait=(mm, ld_cos, ld_sin), inc=True)
                    prev_pe = c1
                    p.v(lambda v: v.stream_shuffle(qs[:], qt[:], mask=SWAP))
                    p.v(lambda v, ts=ts: v.tensor_tensor(qt[:], qt[:], cos_sb[:, ts], ALU.mult))
                    p.v(lambda v, ts=ts: v.tensor_tensor(qs[:], qs[:], sin_sb[:, ts], ALU.mult))
                    rr = p.v(lambda v: v.tensor_tensor(qt[:], qt[:], qs[:], ALU.add), inc=True)
                    sqe = p.s(lambda s: s.activation(sqq[:], qro[:], AF.Square), wait=(rr,), inc=True)
                    for hh in range(2):
                        pX = pA if hh == 0 else pB
                        mqn = p.t(lambda t, hh=hh, pX=pX: t.matmul(
                            pX[0:1, :], ones_col[64 * hh:64 * (hh + 1), :], sqq[64 * hh:64 * (hh + 1), :],
                            start=True, stop=True), wait=(sqe,), inc=True)
                        s1e = p.s(lambda s, hh=hh, pX=pX: s.activation(rn_f[0:1, hh, :], pX[0:1, :], AF.Sqrt), wait=(mqn,), inc=True)
                        p.v(lambda v, hh=hh: v.tensor_scalar(rn_f[0:1, hh, :], rn_f[0:1, hh, :], 1e-12, None, ALU.max), wait=(s1e,))
                        p.v(lambda v, hh=hh: v.reciprocal(rn_f[0:1, hh, :], rn_f[0:1, hh, :]))
                        p.v(lambda v, hh=hh: v.tensor_copy(rn_rows[0:1, hh, :], rn_f[0:1, hh, :]))
                    evr = p.mark("v")
                    bb1 = p.t(lambda t: t.matmul(pF[0:64, :], cs["c_ones_rowbf"][0:1, 0:64], rn_rows[0:1, 0, :],
                                                 start=True, stop=True, tile_position=(0, 0)), wait=(evr,), inc=True)
                    bb2 = p.t(lambda t: t.matmul(pF[64:128, :], cs["c_ones_rowbf"][0:1, 0:64], rn_rows[0:1, 1, :],
                                                 start=True, stop=True, tile_position=(0, 64), skip_group_check=True), inc=True)
                    p.v(lambda v: v.tensor_tensor(qro[:], qro[:], pF[:], ALU.mult), wait=(bb2, sc2))
                    fin = p.v(lambda v, buf=buf, stage=stage, scol=scol: v.tensor_scalar(
                        stage[:, buf, :], qro[:], scol[:], None, ALU.mult),
                        wait=((prev_store[buf],) if prev_store[buf] else ()), inc=True)
                    prev_store[buf] = p.y(lambda y, buf=buf, co=co, ts=ts, stage=stage, out_hbm=out_hbm:
                        y.dma_start(out=out_hbm.rearrange("p (c t) -> p c t", c=8)[:, co, ts],
                                    in_=stage[:, buf, :]), wait=(fin,), inc=True)
            st8["prev_pe"] = prev_pe
            st8["prev_store"] = prev_store
            return (prev_store[0], prev_store[1])

        k_done = do_proj(wk_sb, kstage, k_o, ld_wk, kscol, 32.0)
        ld_wv = _ld_fm(p, wv_h, wk_sb, cast=True, wait=(p.mark("t"),))
        prev_pf = p.mark("v")
        prev_vst = {0: None, 1: None}
        for tt in range(16):
            buf = tt % 2
            ev_half = []
            for half in range(2):
                for ci in range(8):
                    p.t(lambda t, tt=tt, half=half, ci=ci: t.matmul(
                        pF[:], xnb[:, ci, 128 * tt:128 * (tt + 1)], wk_sb[:, ci, 512 * half:512 * (half + 1)],
                        start=(ci == 0), stop=(ci == 7)), wait=(ld_wv,) + ((prev_pf,) if prev_pf else ()))
                mm = p.mark("t")
                ev_half.append(p.v(lambda v, buf=buf, half=half: v.tensor_copy(
                    vstage[:, buf, 512 * half:512 * (half + 1)], pF[:]),
                    wait=(mm,) + ((prev_vst[buf],) if prev_vst[buf] else ()), inc=True))
                prev_pf = ev_half[-1]
            ev_st = p.y(lambda y, tt=tt, buf=buf: y.dma_start(
                out=v_o.rearrange("p (c t) -> p c t", c=16)[:, tt, :], in_=vstage[:, buf, :]),
                wait=(ev_half[-1],), inc=True)
            prev_vst[buf] = ev_st
        ev_store.append(ev_st)
        v_done = (prev_vst[0], prev_vst[1])
        do_proj(wq_sb, qstage, q_o, ld_wq, qscol, 256.0)
        return {"k": k_done, "v": v_done}




def emit_attn1(nc, p, io, cs, ld_c):
    q_h = io["q1"]
    k_h = io["k1full"]
    v_h = io["v1full"]
    wo_h = io["wo_t"]

    s1_o = io["s1"]

    ctx = ExitStack()
    _pfx = "ph%d_" % _PHASE_N[0]
    _PHASE_N[0] += 1
    S = lambda n, sh, dt=F32: ctx.enter_context(nc.sbuf_tensor(_pfx + n, sh, dt))
    PS = lambda n, sh, dt=F32: ctx.enter_context(nc.psum_tensor(_pfx + n, sh, dt))
    with ctx:
        qfm = S("qfm", [128, 8, T], BF16)
        kbuf = S("kbuf", [128, 2, SEQ], BF16)
        vbuf = S("vbuf", [128, 2, 64 * 128], BF16)
        e_sb = S("e_sb", [128, 2, 2, 1536], BF16)
        y1 = S("y1", [128, 8, T], BF16)
        wo_sb = S("wo_sb", [128, 8, D], BF16)
        sstage = S("sstage", [128, 2, 512])
        Mt = S("Mt", [128, 2, 512])
        Mt2 = S("Mt2", [128, 2, 1024])
        mcols = S("mcols", [128, 8])
        negm = S("negm", [1, 2, 512], BF16)
        rz = S("rz", [1, 2, 512], BF16)
        rzf = S("rzf", [1, 2, 512])

        pAll = PS("pAll", [128, 4096])

        def pa1(j):
            return pAll[:, 512 * j:512 * (j + 1)]

        def pb1(j):
            return pAll[:, 2048 + 512 * j:2048 + 512 * (j + 1)]

        pSa = pAll[:, 0:1536]
        pSb = pAll[:, 1536:3072]
        pAV = pAll[:, 3072:3584]
        pZ = pAll[:, 3584:4096]

        ones_row = cs["c_ones_1x128"]
        ones_rowbf = cs["c_ones_rowbf"]
        ones_colbf = cs["c_ones_colbf"]
        ident = cs["c_ident_f32"]

        ld_q = p.y(lambda y: y.dma_start(out=qfm[:], in_=q_h.rearrange("p (c t) -> p c t", c=8)), inc=True)
        ld_wo = _ld_fm(p, wo_h, wo_sb, cast=True)

        NG = 22
        GS = [min(3, 64 - 3 * g) for g in range(NG)]
        ld_k = {}
        ld_v = {}
        for hp in range(2):
            ld_k[hp] = p.y(lambda y, hp=hp: y.dma_start(
                out=kbuf[:, hp % 2, :].rearrange("p (r t) -> p r t", r=4), in_=k_h[:, hp]), inc=True)
            ld_v[hp] = p.g(lambda g, hp=hp: g.dma_start(
                out=vbuf[:, hp % 2, :], in_=v_h[hp]), inc=True, dma=True)

        prev_evac = None
        prev_c1 = None
        prev_expa_l = None
        prev_exp_l = None
        pend = [None]
        for hp in range(8):
            kb = hp % 2
            ld_k_ev = ld_k[hp]
            ld_v_ev = ld_v[hp]
            for q4 in range(4):
                qsl = slice(512 * q4, 512 * (q4 + 1))
                ev_mp = {(0, 0): None, (0, 1): None, (1, 0): None, (1, 1): None}
                for g2 in range(32):
                    par = g2 % 2
                    if g2 == 1 and pend[0] is not None:
                        prev_evac, prev_c1 = pend[0]()
                        pend[0] = None
                    for j in range(2):
                        ks = 2 * g2 + j
                        if g2 == 0:
                            wA = ((ld_k_ev, ld_q) + ((prev_expa_l,) if prev_expa_l else ())) if j == 0 else ()
                            wB = ((prev_exp_l,) if prev_exp_l else ()) if j == 0 else ()
                        elif g2 == 1:
                            wA = ()
                            wB = ((prev_c1,) if (j == 0 and prev_c1) else ()) + \
                                 ((prev_evac,) if (j == 1 and prev_evac) else ())
                        else:
                            wA = (ev_mp[(0, par)],) if (j == 0 and ev_mp[(0, par)]) else ()
                            wB = (ev_mp[(1, par)],) if (j == 0 and ev_mp[(1, par)]) else ()
                        p.t(lambda t, ks=ks, j=j, kb=kb, hp=hp, qsl=qsl, par=par: t.matmul(
                            pa1(2 * par + j), kbuf[0:64, kb, 128 * ks:128 * (ks + 1)],
                            qfm[0:64, hp, qsl], start=True, stop=True, tile_position=(0, 0)), wait=wA)
                        p.t(lambda t, ks=ks, j=j, kb=kb, hp=hp, qsl=qsl, par=par: t.matmul(
                            pb1(2 * par + j), kbuf[64:128, kb, 128 * ks:128 * (ks + 1)],
                            qfm[64:128, hp, qsl], start=True, stop=True, tile_position=(64, 0)), wait=wB)
                    mg2 = p.mark("t")
                    for hh in range(2):
                        base = 1024 * par if hh == 0 else 2048 + 1024 * par
                        psl = pAll[:, base:base + 1024]
                        if g2 == 0:
                            p.v(lambda v, hh=hh, psl=psl: v.tensor_copy(Mt2[:, hh, :], psl), wait=(mg2,))
                        else:
                            p.v(lambda v, hh=hh, psl=psl: v.tensor_tensor(
                                Mt2[:, hh, :], Mt2[:, hh, :], psl, ALU.max), wait=(mg2,))
                        ev_mp[(hh, par)] = p.mark("v")
                p.v(lambda v: v.tensor_tensor(Mt[:, 0, :], Mt2[:, 0, 0:512], Mt2[:, 0, 512:1024], ALU.max))
                ev_mfa = p.mark("v")
                p.v(lambda v: v.tensor_tensor(Mt[:, 1, :], Mt2[:, 1, 0:512], Mt2[:, 1, 512:1024], ALU.max))
                ev_mfb = p.mark("v")
                w0 = (ev_mp[(0, 0)], ev_mp[(0, 1)], ev_mp[(1, 0)])
                for j in range(GS[0]):
                    p.t(lambda t, j=j, kb=kb, hp=hp, qsl=qsl: t.matmul(
                        pSa[:, 512 * j:512 * (j + 1)], kbuf[0:64, kb, 128 * j:128 * (j + 1)],
                        qfm[0:64, hp, qsl], start=True, stop=False, tile_position=(0, 0)),
                        wait=w0 if j == 0 else ())
                    p.t(lambda t, j=j, kb=kb, hp=hp, qsl=qsl: t.matmul(
                        pSb[:, 512 * j:512 * (j + 1)], kbuf[64:128, kb, 128 * j:128 * (j + 1)],
                        qfm[64:128, hp, qsl], start=True, stop=False, tile_position=(64, 0)))
                tr1 = None
                for c in range(8):
                    hh, j = c // 4, c % 4
                    w = (ev_mfa, ev_mfb) if c == 0 else ()
                    tr1 = p.t(lambda t, c=c, hh=hh, j=j: t.transpose(
                        pAll[:, 3072 + 128 * c:3072 + 128 * (c + 1)], Mt[:, hh, 128 * j:128 * (j + 1)], ident[:]),
                        wait=w, inc=(c == 7))
                rd = None
                for c in range(8):
                    rd = p.v(lambda v, c=c: v.reduce_max(mcols[:, c:c + 1],
                                                         pAll[:, 3072 + 128 * c:3072 + 128 * (c + 1)], axis=AX.X),
                             wait=(tr1,) if c == 0 else (), inc=(c == 7))
                tr2 = None
                for c in range(8):
                    tr2 = p.t(lambda t, c=c: t.transpose(
                        pAll[0:1, 3072 + 128 * c:3072 + 128 * (c + 1)], mcols[:, c:c + 1], ident[:]),
                        wait=(rd,) if c == 0 else (), inc=(c == 7))
                ev_negm = p.v(lambda v: v.tensor_scalar(negm[:].rearrange("o h t -> o (h t)"),
                                                        pAll[0:1, 3072:4096], -1.0, None, ALU.mult),
                              wait=(tr2,), inc=True)
                ev_ea = {}
                ev_eb = {}
                ev_av = [None] * NG
                for g in range(NG):
                    gb = g % 2
                    qa0 = [None] * GS[g]
                    qb0 = [None] * GS[g]
                    if g == 0:
                        for j in range(GS[0]):
                            p.t(lambda t, j=j: t.matmul(
                                pSa[:, 512 * j:512 * (j + 1)], ones_rowbf[:], negm[0:1, 0, :],
                                start=False, stop=True), wait=(ev_negm,) if j == 0 else ())
                            qa0[j] = p.mark("t")
                            p.t(lambda t, j=j: t.matmul(
                                pSb[:, 512 * j:512 * (j + 1)], ones_rowbf[:], negm[0:1, 1, :],
                                start=False, stop=True))
                            qb0[j] = p.mark("t")
                    else:
                        for j in range(GS[g]):
                            ks = 3 * g + j
                            p.t(lambda t, ks=ks, j=j, kb=kb, hp=hp, qsl=qsl: t.matmul(
                                pSa[:, 512 * j:512 * (j + 1)], kbuf[0:64, kb, 128 * ks:128 * (ks + 1)],
                                qfm[0:64, hp, qsl], start=True, stop=False, tile_position=(0, 0)),
                                wait=(ev_ea[(g - 1, j)],) if j < GS[g - 1] else ())
                            p.t(lambda t, j=j: t.matmul(
                                pSa[:, 512 * j:512 * (j + 1)], ones_rowbf[:], negm[0:1, 0, :],
                                start=False, stop=True))
                            qa0[j] = p.mark("t")
                            p.t(lambda t, ks=ks, j=j, kb=kb, hp=hp, qsl=qsl: t.matmul(
                                pSb[:, 512 * j:512 * (j + 1)], kbuf[64:128, kb, 128 * ks:128 * (ks + 1)],
                                qfm[64:128, hp, qsl], start=True, stop=False, tile_position=(64, 0)),
                                wait=(ev_eb[(g - 1, j)],) if j < GS[g - 1] else ())
                            p.t(lambda t, j=j: t.matmul(
                                pSb[:, 512 * j:512 * (j + 1)], ones_rowbf[:], negm[0:1, 1, :],
                                start=False, stop=True))
                            qb0[j] = p.mark("t")
                    for j in range(GS[g]):
                        w_exp = (qa0[j],)
                        if g >= 2 and j == 0:
                            w_exp = w_exp + ((ev_av[g - 2],) if ev_av[g - 2] else ())
                        ev_ea[(g, j)] = p.s(lambda s, gb=gb, j=j: s.activation(
                            e_sb[:, gb, 0, 512 * j:512 * (j + 1)], pSa[:, 512 * j:512 * (j + 1)],
                            AF.Exp), wait=w_exp, inc=True)
                        ev_eb[(g, j)] = p.s(lambda s, gb=gb, j=j: s.activation(
                            e_sb[:, gb, 1, 512 * j:512 * (j + 1)], pSb[:, 512 * j:512 * (j + 1)],
                            AF.Exp), wait=(qb0[j],), inc=True)
                    for j in range(GS[g]):
                        ks = 3 * g + j
                        first = (g == 0 and j == 0)
                        last = (g == NG - 1 and j == GS[g] - 1)
                        p.t(lambda t, ks=ks, j=j, gb=gb, kb=kb, first=first, last=last: t.matmul(
                            pAV[0:64, :], vbuf[:, kb, :].rearrange("p (tt d) -> p tt d", d=128)[:, ks, 0:64],
                            e_sb[:, gb, 0, 512 * j:512 * (j + 1)],
                            start=first, stop=last, tile_position=(0, 0)),
                            wait=(ev_ea[(g, j)],) + ((ld_v_ev,) if j == 0 else ()))
                        p.t(lambda t, ks=ks, j=j, gb=gb, kb=kb, first=first, last=last: t.matmul(
                            pAV[64:128, :], vbuf[:, kb, :].rearrange("p (tt d) -> p tt d", d=128)[:, ks, 64:128],
                            e_sb[:, gb, 1, 512 * j:512 * (j + 1)],
                            start=first, stop=last, tile_position=(0, 64), skip_group_check=True),
                            wait=(ev_eb[(g, j)],))
                        p.t(lambda t, j=j, gb=gb, first=first, last=last: t.matmul(
                            pZ[0:1, :], ones_colbf[:], e_sb[:, gb, 0, 512 * j:512 * (j + 1)],
                            start=first, stop=last, tile_position=(0, 0)))
                        p.t(lambda t, j=j, gb=gb, first=first, last=last: t.matmul(
                            pZ[64:65, :], ones_colbf[:], e_sb[:, gb, 1, 512 * j:512 * (j + 1)],
                            start=first, stop=last, tile_position=(0, 64), skip_group_check=True))
                    ev_av[g] = p.mark("t")
                sb2 = q4 % 2
                c1 = p.v(lambda v, sb2=sb2: v.tensor_copy(sstage[:, sb2, :], pAV[:]), wait=(ev_av[NG - 1],), inc=True)
                p.v(lambda v: v.reciprocal(rzf[0:1, 0, :], pZ[0:1, :]))
                p.v(lambda v: v.reciprocal(rzf[0:1, 1, :], pZ[64:65, :]))
                rcast = p.v(lambda v: v.tensor_copy(rz[:].rearrange("o h t -> o (h t)"), rzf[:].rearrange("o h t -> o (h t)")), inc=True)
                prev_expa_l = ev_ea[(NG - 1, 0)]
                prev_exp_l = ev_eb[(NG - 1, 0)]

                def mk_pend(sb2=sb2, hp=hp, qsl=qsl, rcast=rcast, c1=c1):
                    def emit():
                        p.t(lambda t: t.matmul(pZ[0:64, :], ones_rowbf[0:1, 0:64], rz[0:1, 0, :],
                                               start=True, stop=True, tile_position=(0, 0)),
                            wait=(rcast,))
                        b2 = p.t(lambda t: t.matmul(pZ[64:128, :], ones_rowbf[0:1, 0:64], rz[0:1, 1, :],
                                                    start=True, stop=True, tile_position=(0, 64),
                                                    skip_group_check=True), inc=True)
                        ev = p.v(lambda v: v.tensor_tensor(
                            y1[:, hp, qsl], sstage[:, sb2, :], pZ[:, 0:512], ALU.mult), wait=(b2,), inc=True)
                        return ev, c1
                    return emit
                pend[0] = mk_pend()
            if hp + 2 <= 7:
                nhp = hp + 2
                lastuse = p.mark("t")
                ld_k[nhp] = p.y(lambda y, nhp=nhp: y.dma_start(
                    out=kbuf[:, nhp % 2, :].rearrange("p (r t) -> p r t", r=4), in_=k_h[:, nhp]),
                    wait=(lastuse,), inc=True)
                ld_v[nhp] = p.g(lambda g, nhp=nhp: g.dma_start(
                    out=vbuf[:, nhp % 2, :], in_=v_h[nhp]),
                    wait=(lastuse,), inc=True, dma=True)
        ev_y1, _ = pend[0]()
        pend[0] = None

        prev_pe = None
        prev_sst = {0: None, 1: None}
        for co in range(8):
            for tc in range(4):
                ts = slice(512 * tc, 512 * (tc + 1))
                sb2 = (co * 4 + tc) % 2
                for ci in range(8):
                    p.t(lambda t, ci=ci, co=co, ts=ts: t.matmul(
                        pAll[:, 0:512], wo_sb[:, ci, 128 * co:128 * (co + 1)], y1[:, ci, ts],
                        start=(ci == 0), stop=(ci == 7)),
                        wait=(ev_y1, ld_wo) + ((prev_pe,) if prev_pe else ()))
                mm = p.mark("t")
                prev_pe = p.v(lambda v, sb2=sb2: v.tensor_copy(sstage[:, sb2, :], pAll[:, 0:512]),
                              wait=(mm,) + ((prev_sst[sb2],) if prev_sst[sb2] else ()), inc=True)
                prev_sst[sb2] = p.y(lambda y, co=co, ts=ts, sb2=sb2: y.dma_start(
                    out=s1_o.rearrange("p (c t) -> p c t", c=8)[:, co, ts], in_=sstage[:, sb2, :]),
                    wait=(prev_pe,), inc=True)




def emit_ffn1(nc, p, io, cs, ld_c):
    x_h = io["x1"]
    xn_h = io["x1n"]
    s1_h = io["s1"]
    wmh_h = io["wmh_t"]
    wmg_h = io["wmg_t"]
    kf_h = io["kffwd_r"]
    vf_h = io["vffwd"]
    g_h = io["g1d"]

    out_o = io["outp"]

    ctx = ExitStack()
    _pfx = "ph%d_" % _PHASE_N[0]
    _PHASE_N[0] += 1
    S = lambda n, sh, dt=F32: ctx.enter_context(nc.sbuf_tensor(_pfx + n, sh, dt))
    PS = lambda n, sh, dt=F32: ctx.enter_context(nc.psum_tensor(_pfx + n, sh, dt))
    with ctx:
        xnb = S("xnb", [128, 8, T], BF16)
        qm = S("qm", [128, 8, T], BF16)
        y_fm = S("y_fm", [128, 8, T], BF16)
        w_sb = S("w_sb", [128, 8, D], BF16)
        kv_k = S("kv_k", [128, 2, 16, S1], BF16)
        kv_v = S("kv_v", [128, 2, 16, 2, D1], BF16)
        g_bc = S("g_bc", [128, 16, 512], BF16)
        g_row16 = S("g_row16", [1, 16, 512], BF16)
        gel = S("gel", [128, 2, 512], BF16)
        stg = S("stg", [128, 2, 512])
        x1st = S("x1st", [128, T])
        t1f = S("t1f", [128, T])

        pAll = PS("pAll", [128, 4096])

        ones_row = cs["c_ones_1x128"]

        ld_xn = p.y(lambda y: y.dma_start(out=xnb[:], in_=xn_h.rearrange("p (c t) -> p c t", c=8)), inc=True)
        ld_w = _ld_fm(p, wmh_h, w_sb, cast=True)

        prev_pe = {0: None, 1: None}
        idx = 0
        for co in range(8):
            for tc in range(4):
                ts = slice(512 * tc, 512 * (tc + 1))
                bk = idx % 2
                base = 2048 + 512 * bk
                for ci in range(8):
                    w = ((ld_xn, ld_w) if idx == 0 and ci == 0 else ()) +                         ((prev_pe[bk],) if (ci == 0 and prev_pe[bk]) else ())
                    p.t(lambda t, ci=ci, co=co, ts=ts, base=base: t.matmul(
                        pAll[:, base:base + 512], w_sb[:, ci, 128 * co:128 * (co + 1)], xnb[:, ci, ts],
                        start=(ci == 0), stop=(ci == 7)), wait=w)
                mm = p.mark("t")
                prev_pe[bk] = p.v(lambda v, co=co, ts=ts, base=base: v.tensor_copy(
                    qm[:, co, ts], pAll[:, base:base + 512]), wait=(mm,), inc=True)
                idx += 1
        ev_qm = p.mark("v")

        prev_kv = {0: None, 1: None}
        prev_y = None
        prev_bg = None
        gcp = {2560: ev_qm, 3584: ev_qm}
        for tc in range(4):
            ts = slice(512 * tc, 512 * (tc + 1))
            lgs = [None] * NE1
            for e in range(NE1):
                lgs[e] = p.g(lambda g, e=e, ts=ts: g.dma_start(out=g_row16[0:1, e, :], in_=g_h[e:e + 1, ts]),
                             wait=(prev_bg,) if (e == 0 and prev_bg) else (), inc=True, dma=True)
            for e in range(NE1):
                base = 3584 if e % 2 == 0 else 2560
                bg = p.t(lambda t, e=e, base=base: t.matmul(
                    pAll[:, base:base + 512], cs["c_ones_rowbf"][:], g_row16[0:1, e, :],
                    start=True, stop=True), wait=(lgs[e],) + ((gcp[base],) if gcp[base] else ()), inc=True)
                gcp[base] = p.v(lambda v, e=e, base=base: v.tensor_copy(g_bc[:, e, :], pAll[:, base:base + 512]),
                                wait=(bg,), inc=True)
                if e == NE1 - 1:
                    prev_bg = bg
            ev_gbc = p.mark("v")
            for h in range(H1):
                hb = h % 2
                lk = p.g(lambda g, h=h, hb=hb: g.dma_start(
                    out=kv_k[:, hb], in_=kf_h[h].rearrange("e p s -> p e s")),
                    wait=(prev_kv[hb],) if prev_kv[hb] else (), inc=True, dma=True)
                lv = p.g(lambda g, h=h, hb=hb: g.dma_start(
                    out=kv_v[:, hb], in_=vf_h[h].rearrange("e (sc p) d -> p e sc d", p=128)),
                    inc=True, dma=True)
                ev_exp = {}
                for e in range(NE1):
                    sbk = e % 2
                    for sc in range(2):
                        w = (lk, ev_gbc) if (e == 0 and sc == 0) else ()
                        if e >= 2 and sc == 0:
                            w = w + (ev_exp[e - 2],)
                        p.t(lambda t, e=e, sc=sc, h=h, hb=hb, ts=ts, sbk=sbk: t.matmul(
                            pAll[:, 1024 * sbk + 512 * sc:1024 * sbk + 512 * (sc + 1)],
                            kv_k[:, hb, e, 128 * sc:128 * (sc + 1)], qm[:, h, ts],
                            start=True, stop=True), wait=w)
                    msc = p.mark("t")
                    ge1 = p.s(lambda s, sbk=sbk: s.activation(
                        gel[:, 0, :], pAll[:, 1024 * sbk:1024 * sbk + 512], AF.Gelu), wait=(msc,), inc=True)
                    ge2 = p.s(lambda s, sbk=sbk: s.activation(
                        gel[:, 1, :], pAll[:, 1024 * sbk + 512:1024 * sbk + 1024], AF.Gelu), inc=True)
                    p.v(lambda v, e=e: v.tensor_tensor(gel[:, 0, :], gel[:, 0, :], g_bc[:, e, :], ALU.mult), wait=(ge1,))
                    ev_exp[e] = p.v(lambda v, e=e: v.tensor_tensor(gel[:, 1, :], gel[:, 1, :], g_bc[:, e, :], ALU.mult), wait=(ge2,), inc=True)
                    for sc in range(2):
                        first = (e == 0 and sc == 0)
                        last = (e == NE1 - 1 and sc == 1)
                        w = ((ev_exp[e],) if sc == 0 else ()) + ((lv, prev_y) if (first and prev_y) else ((lv,) if first else ()))
                        p.t(lambda t, e=e, sc=sc, hb=hb, first=first, last=last: t.matmul(
                            pAll[:, 3072:3584], kv_v[:, hb, e, sc, :], gel[:, sc, :],
                            start=first, stop=last), wait=w)
                my = p.mark("t")
                prev_y = p.v(lambda v, h=h, ts=ts: v.tensor_copy(y_fm[:, h, ts], pAll[:, 3072:3584]), wait=(my,), inc=True)
                prev_kv[hb] = my
        ev_yfm = prev_y

        ld_wg = _ld_fm(p, wmg_h, w_sb, cast=True, wait=(p.mark("t"),))
        prev_st = None
        prev_sil = {0: None, 1: None}
        prev_ost = {0: None, 1: None}
        idx = 0
        for co in range(8):
            for tc in range(4):
                ts = slice(512 * tc, 512 * (tc + 1))
                sb2 = idx % 2
                base = 512 * sb2
                for ci in range(8):
                    w = ((ev_yfm, ld_wg) if idx == 0 and ci == 0 else ()) +                         ((prev_sil[sb2],) if (ci == 0 and prev_sil[sb2]) else ())
                    p.t(lambda t, ci=ci, co=co, ts=ts, base=base: t.matmul(
                        pAll[:, base:base + 512], w_sb[:, ci, 128 * co:128 * (co + 1)], y_fm[:, ci, ts],
                        start=(ci == 0), stop=(ci == 7)), wait=w)
                mm = p.mark("t")
                lx = p.y(lambda y, co=co, ts=ts: y.dma_start(
                    out=t1f[:, 0:512], in_=x_h.rearrange("p (c t) -> p c t", c=8)[:, co, ts]),
                    wait=(prev_st,) if prev_st else (), inc=True)
                ls = p.y(lambda y, co=co, ts=ts: y.dma_start(
                    out=x1st[:, 0:512], in_=s1_h.rearrange("p (c t) -> p c t", c=8)[:, co, ts]), inc=True)
                gg = p.s(lambda s, sb2=sb2, base=base: s.activation(stg[:, sb2, :], pAll[:, base:base + 512], AF.Silu),
                         wait=(mm,) + ((prev_ost[sb2],) if prev_ost[sb2] else ()), inc=True)
                prev_sil[sb2] = gg
                p.v(lambda v, sb2=sb2: v.tensor_tensor(stg[:, sb2, :], stg[:, sb2, :], x1st[:, 0:512], ALU.mult), wait=(gg, ls))
                p.v(lambda v, sb2=sb2: v.tensor_tensor(stg[:, sb2, :], stg[:, sb2, :], t1f[:, 0:512], ALU.add), wait=(lx,))
                ev_o = p.mark("v")
                prev_st = p.y(lambda y, co=co, ts=ts, sb2=sb2: y.dma_start(
                    out=out_o.rearrange("p (c t) -> p c t", c=8)[:, co, ts], in_=stg[:, sb2, :]),
                    wait=(ev_o,), inc=True)
                prev_ost[sb2] = prev_st
                idx += 1




ALL_CONSTS = ["c_ones_1x128", "c_ones_col", "c_ltriu", "c_rev64",
              "c_ident_f32", "c_rev16", "c_ones_colbf", "c_ones_rowbf"]


def build_fused(upto=99, dbg=False):
    nc = _nc()
    EI = lambda n, sh, dt=F32: nc.dram_tensor(n, sh, dt, kind="ExternalInput").ap()
    IN = lambda n, sh, dt=F32: nc.dram_tensor(n, sh, dt, kind="Internal").ap()

    x_fm = EI("x_fm", [D, T])
    wq_t = EI("wq_t", [D, D]); wk_t = EI("wk_t", [D, D]); wv_t = EI("wv_t", [D, D])
    gate_w = EI("gate_w", [D, NE0])
    core_mask = EI("core_mask", [8, 1])
    sgu_pairs = EI("sgu_pairs", [D, 16 * 256]); sdown = EI("sdown", [2 * D, D])
    sgate = EI("sgate", [128, 8])
    egu = EI("egu", [NE0, D, D]); edw = EI("edw", [NE0, CAP0, D])
    wq1_t = EI("wq1_t", [D, D]); wk1_t = EI("wk1_t", [D, D]); wv1_t = EI("wv1_t", [D, D])
    wo_t = EI("wo_t", [D, D]); sqk_t = EI("sqk_t", [64, NH])
    wr1_t = EI("wr1_t", [D, NE1])
    rope_cos = EI("rope_cos", [128, T]); rope_sin = EI("rope_sin", [128, T])
    wmh_t = EI("wmh_t", [D, D]); wmg_t = EI("wmg_t", [D, D])
    kffwd_r = EI("kffwd_r", [H1, NE1, D1, S1]); vffwd = EI("vffwd", [H1, NE1, S1, D1])

    out_o = nc.dram_tensor("outp", [128, 8 * T], F32, kind="ExternalOutput").ap()

    DI = (lambda n, sh, dt=F32: nc.dram_tensor(n, sh, dt, kind="ExternalOutput").ap()) if dbg else IN
    xn0 = DI("xn0", [128, 8 * T], BF16)
    q0 = DI("q0", [128, 8 * T], BF16)
    k0 = IN("k0", [128, 8 * T], BF16)
    v0 = IN("v0", [128, 16 * D], BF16)
    qn2 = DI("qn2", [NH, T])
    kmax2 = IN("kmax2", [1, NH])
    wd = DI("wd", [128, 16 * NE0])
    pos = DI("pos", [128, 16 * NE0])
    cnt = IN("cnt", [1, NE0])
    s0 = DI("s0", [128, 8 * T])
    x1 = DI("x1", [128, 8 * T])
    x1n = DI("x1n", [128, 8 * T], BF16)
    q1 = DI("q1", [128, 8 * T], BF16)
    k1 = IN("k1", [128, 8 * T], BF16)
    v1 = IN("v1", [128, 16 * D], BF16)
    g1d = DI("g1d", [NE1, T])
    s1 = DI("s1", [128, 8 * T])
    k0_ag = IN("k0_ag", [4 * 128, 8 * T], BF16)
    v0_ag = IN("v0_ag", [4 * 128, 16 * D], BF16)
    km_ag = IN("km_ag", [4, NH])
    cnt_ag = IN("cnt_ag", [8, NE0])
    v0_hp = IN("v0_hp", [8, 128, 64 * 128], BF16)
    k1_ag = IN("k1_ag", [4 * 128, 8 * T], BF16)
    v1_ag = IN("v1_ag", [4 * 128, 16 * D], BF16)
    v1_hp = IN("v1_hp", [8, 128, 64 * 128], BF16)

    GROUPS4 = [[0, 1, 2, 3], [4, 5, 6, 7]]
    GROUPS8 = [list(range(8))]

    ctx = ExitStack()
    with ctx:
        p = Prog(nc)
        S = lambda n, sh, dt=F32: ctx.enter_context(nc.sbuf_tensor(n, sh, dt))
        cs = {}
        ld_c = None
        for n in ALL_CONSTS:
            sh, dt = CONST_SHAPES[n]
            ap = nc.dram_tensor(n, sh, dt, kind="ExternalInput").ap()
            tl = S(n + "_sb", list(sh), dt)
            cs[n] = tl
            ld_c = p.y(lambda y, tl=tl, ap=ap: y.dma_start(out=tl[:], in_=ap), inc=True)

        emit_pre0(nc, p, {"x_fm": x_fm, "wq_t": wq_t, "wk_t": wk_t, "wv_t": wv_t,
                          "gate_w": gate_w, "xn0": xn0, "q0": q0, "k0": k0, "v0": v0,
                          "qn2": qn2, "kmax2": kmax2, "wd": wd, "pos": pos, "cnt": cnt},
                  cs, ld_c)
        p.barrier()
        if upto < 2:
            p.final_wait()
            p.emit()
            return nc
        p.g(lambda g: g.collective_compute("AllGather", ALU.bypass, replica_groups=GROUPS4,
                                           ins=[k0.opt()], outs=[k0_ag.opt()]))
        agv0 = p.g(lambda g: g.collective_compute("AllGather", ALU.bypass, replica_groups=GROUPS4,
                                           ins=[v0.opt()], outs=[v0_ag.opt()]))
        p.g(lambda g: g.collective_compute("AllGather", ALU.bypass, replica_groups=GROUPS4,
                                           ins=[kmax2.opt()], outs=[km_ag.opt()]))
        p.g(lambda g: g.collective_compute("AllGather", ALU.bypass, replica_groups=GROUPS8,
                                           ins=[cnt.opt()], outs=[cnt_ag.opt()]))
        for _h in range(8):
            for _r in range(4):
                p.g(lambda g, _h=_h, _r=_r: g.dma_start(
                    out=v0_hp.rearrange("h p (r tt x) -> h p r (tt x)", r=4, x=128)[_h][:, _r].rearrange("p (tt x) -> p tt x", x=128),
                    in_=v0_ag.rearrange("(r p) (tt h x) -> r h p tt x", p=128, h=8, x=128)[_r][_h]),
                    wait=(agv0,), inc=(_h == 7 and _r == 3), dma=True)
        p.barrier()
        if upto < 3:
            p.final_wait()
            p.emit()
            return nc
        emit_attn0(nc, p, {"q0": q0,
                           "kfull": k0_ag.rearrange("(r p) (c t) -> p c r t", p=128, c=8),
                           "vfull": v0_hp, "qn2": qn2,
                           "kmax2g": km_ag.rearrange("r h -> (r h)").rearrange("(o x) -> o x", o=1),
                           "s0": s0}, cs, ld_c)
        p.barrier()
        if upto < 4:
            p.final_wait()
            p.emit()
            return nc
        emit_moe0(nc, p, {"xn0": xn0, "x_fm": x_fm, "s0": s0, "wd": wd, "pos": pos,
                          "cnt_all": cnt_ag, "core_mask": core_mask,
                          "sgu_pairs": sgu_pairs, "sdown": sdown, "sgate": sgate,
                          "egu": egu, "edw": edw, "x1": x1}, cs, ld_c)
        p.barrier()
        if upto < 5:
            p.final_wait()
            p.emit()
            return nc
        kv_done = emit_pre1(nc, p, {"x1": x1, "wq1_t": wq1_t, "wk1_t": wk1_t, "wv1_t": wv1_t,
                          "wr1_t": wr1_t, "sqk_t": sqk_t, "rope_cos": rope_cos,
                          "rope_sin": rope_sin, "x1n": x1n, "q1": q1, "k1": k1,
                          "v1": v1, "g1d": g1d}, cs, ld_c)
        if upto < 6:
            p.barrier()
            p.final_wait()
            p.emit()
            return nc
        p.g(lambda g: g.collective_compute("AllGather", ALU.bypass, replica_groups=GROUPS4,
                                           ins=[k1.opt()], outs=[k1_ag.opt()]), wait=kv_done["k"])
        agv1 = p.g(lambda g: g.collective_compute("AllGather", ALU.bypass, replica_groups=GROUPS4,
                                           ins=[v1.opt()], outs=[v1_ag.opt()]), wait=kv_done["v"])
        for _h in range(8):
            for _r in range(4):
                p.g(lambda g, _h=_h, _r=_r: g.dma_start(
                    out=v1_hp.rearrange("h p (r tt x) -> h p r (tt x)", r=4, x=128)[_h][:, _r].rearrange("p (tt x) -> p tt x", x=128),
                    in_=v1_ag.rearrange("(r p) (tt h x) -> r h p tt x", p=128, h=8, x=128)[_r][_h]),
                    wait=(agv1,), inc=(_h == 7 and _r == 3), dma=True)
        p.barrier()
        if upto < 7:
            p.final_wait()
            p.emit()
            return nc
        emit_attn1(nc, p, {"q1": q1,
                           "k1full": k1_ag.rearrange("(r p) (c t) -> p c r t", p=128, c=8),
                           "v1full": v1_hp, "wo_t": wo_t, "s1": s1}, cs, ld_c)
        p.barrier()
        if upto < 8:
            p.final_wait()
            p.emit()
            return nc
        emit_ffn1(nc, p, {"x1": x1, "x1n": x1n, "s1": s1, "wmh_t": wmh_t, "wmg_t": wmg_t,
                          "kffwd_r": kffwd_r, "vffwd": vffwd, "g1d": g1d, "outp": out_o},
                  cs, ld_c)
        p.final_wait()
        p.emit()
    return nc


def kernel(**inputs):
    consts = _consts_np()
    x = np.asarray(inputs["L0_attn_x"], dtype=np.float32)
    wqkv = np.asarray(inputs["L0_attn_W_qkv"], dtype=np.float32)
    gate_w = np.ascontiguousarray(np.asarray(inputs["L0_ffn_gate_w"], dtype=np.float32))
    sgw = np.asarray(inputs["L0_ffn_shared_gate_w"], dtype=np.float32)
    sguw = np.asarray(inputs["L0_ffn_shared_gate_up_w"], dtype=np.float32)
    sdw = np.ascontiguousarray(np.asarray(inputs["L0_ffn_shared_down_w"], dtype=np.float32))
    eguw = np.ascontiguousarray(np.asarray(inputs["L0_ffn_expert_gate_up_w"], dtype=np.float32))
    edww = np.ascontiguousarray(np.asarray(inputs["L0_ffn_expert_down_w"], dtype=np.float32))
    wq1 = np.asarray(inputs["L1_attn_Wq"], dtype=np.float32)
    wk1 = np.asarray(inputs["L1_attn_Wk"], dtype=np.float32)
    wv1 = np.asarray(inputs["L1_attn_Wv"], dtype=np.float32)
    wo1 = np.asarray(inputs["L1_attn_Wo"], dtype=np.float32)
    sqk = np.asarray(inputs["L1_attn_s_qk"], dtype=np.float32)
    wr1 = np.asarray(inputs["L1_ffn_w_router"], dtype=np.float32)
    wmh = np.asarray(inputs["L1_ffn_w_mh"], dtype=np.float32)
    wmg = np.asarray(inputs["L1_ffn_w_mg"], dtype=np.float32)
    kff = np.asarray(inputs["L1_ffn_k_ffwd"], dtype=np.float32)
    vff = np.ascontiguousarray(np.asarray(inputs["L1_ffn_v_ffwd"], dtype=np.float32))

    x_parts = _host_split_fm(x)
    sgu_pairs = np.zeros((D, 16, 256), dtype=np.float32)
    for i in range(16):
        sgu_pairs[:, i, 0:128] = sguw[:, 128 * i:128 * (i + 1)]
        sgu_pairs[:, i, 128:256] = sguw[:, 2048 + 128 * i:2048 + 128 * (i + 1)]
    shared = {
        "wq_t": np.ascontiguousarray(wqkv[0:D].T),
        "wk_t": np.ascontiguousarray(wqkv[D:2 * D].T),
        "wv_t": np.ascontiguousarray(wqkv[2 * D:3 * D].T),
        "gate_w": gate_w,
        "sgu_pairs": np.ascontiguousarray(sgu_pairs.reshape(D, 16 * 256)),
        "sdown": sdw,
        "sgate": np.ascontiguousarray(sgw.reshape(8, 128).T),
        "egu": eguw, "edw": edww,
        "wq1_t": np.ascontiguousarray(wq1.T),
        "wk1_t": np.ascontiguousarray(wk1.T),
        "wv1_t": np.ascontiguousarray(wv1.T),
        "wo_t": np.ascontiguousarray(wo1.T),
        "sqk_t": np.ascontiguousarray(sqk.T),
        "wr1_t": np.ascontiguousarray(wr1.T),
        "wmh_t": np.ascontiguousarray(wmh.T),
        "wmg_t": np.ascontiguousarray(wmg.T),
        "kffwd_r": np.ascontiguousarray(kff.transpose(0, 1, 3, 2)),
        "vffwd": vff,
    }
    for kname in ALL_CONSTS:
        shared[kname] = consts[kname]

    ncf = build_fused()
    in_maps = []
    for c in range(NCORES):
        cos_t, sin_t = _rope_tables(c)
        m = dict(shared)
        m["x_fm"] = x_parts[c]
        m["core_mask"] = np.ascontiguousarray((np.arange(8) < c).astype(np.float32).reshape(8, 1))
        m["rope_cos"] = cos_t
        m["rope_sin"] = sin_t
        in_maps.append(m)
    res = run_bass_kernel_spmd(ncf, in_maps, core_ids=list(range(NCORES)))
    out = np.empty((2 * SEQ, D), dtype=np.float32)
    for c in range(NCORES):
        out[c * T:(c + 1) * T] = res.results[c]["outp"].reshape(128, 8, T).transpose(2, 1, 0).reshape(T, D)
    return out.reshape(2, SEQ, D)


# version 19 (final)
# speedup vs baseline: 1.2165x; 1.0048x over previous
import numpy as np
import ml_dtypes
from contextlib import ExitStack

import concourse.bass as bass
import concourse.mybir as mybir
from concourse.bass_utils import run_bass_kernel_spmd

F32 = mybir.dt.float32
BF16 = mybir.dt.bfloat16
AF = mybir.ActivationFunctionType
ALU = mybir.AluOpType
AX = mybir.AxisListType
BF = ml_dtypes.bfloat16

NCORES = 8
T = 2048
D = 1024
NH = 16
HD = 64
SEQ = 8192
NE0 = 64
CAP0 = 512
NE1 = 16
H1 = 8
D1 = 128
S1 = 256


class Prog:

    ENGS = ("t", "v", "s", "g", "y")
    SEMS = ("t", "v", "s", "g", "y", "gd", "yd")

    def __init__(self, nc):
        self.nc = nc
        self.streams = {e: [] for e in self.ENGS}
        self.cnt = {e: 0 for e in self.SEMS}

    CHAINED = ("v", "s", "g")

    def op(self, eng, fn, wait=(), inc=False, dma=False):
        ev = None
        wait = tuple(w for w in wait if w is not None)
        if not dma and eng in self.CHAINED:
            if self.cnt[eng] > 0:
                wait = wait + ((eng, self.cnt[eng]),)
            inc = True
        if dma:
            self.cnt[eng + "d"] += 16
            self.streams[eng].append([fn, wait, (eng + "d", self.cnt[eng + "d"]), True])
            if inc:
                return self.dma_ev(eng)
            return None
        if inc:
            self.cnt[eng] += 1
            ev = (eng, self.cnt[eng])
        self.streams[eng].append([fn, wait, ev, False])
        return ev

    def mark(self, eng):
        for entry in reversed(self.streams[eng]):
            fn, wait, ev, dma = entry
            if fn is None or fn == "MILESTONE":
                continue
            if ev is not None:
                return ev
            self.cnt[eng] += 1
            entry[2] = (eng, self.cnt[eng])
            return entry[2]
        return None

    def dma_ev(self, eng):
        semd = eng + "d"
        self.streams[eng].append([None, ((semd, self.cnt[semd]),), None, False])
        self.cnt[eng] += 1
        ev = (eng, self.cnt[eng])
        self.streams[eng].append(["MILESTONE", (), ev, False])
        return ev

    def t(self, fn, wait=(), inc=False):
        return self.op("t", fn, wait, inc)

    def v(self, fn, wait=(), inc=False):
        return self.op("v", fn, wait, inc)

    def s(self, fn, wait=(), inc=False):
        return self.op("s", fn, wait, inc)

    def g(self, fn, wait=(), inc=False, dma=False):
        return self.op("g", fn, wait, inc, dma)

    def y(self, fn, wait=(), inc=False, dma=True):
        return self.op("y", fn, wait, inc, dma)

    def barrier(self):
        evs = []
        for e in self.ENGS:
            if self.cnt.get(e + "d", 0) > 0:
                evs.append(self.dma_ev(e))
            self.mark(e)
            if self.cnt[e] > 0:
                evs.append((e, self.cnt[e]))
        for e in self.ENGS:
            self.streams[e].append([None, tuple(evs), None, False])
        return evs

    def emit(self):
        nc = self.nc
        ctx = ExitStack()
        sems = {e: ctx.enter_context(nc.semaphore(f"sem_{e}")) for e in self.SEMS}
        sdummy = ctx.enter_context(nc.sbuf_tensor("prog_sdummy", [1, 8], F32))
        with ctx:
            blk = ctx.enter_context(nc.Block())

            def run_stream(eng_name):
                def body(h):
                    waited = {}
                    for fn, wait, ev, dma in self.streams[eng_name]:
                        for (sname, val) in wait:
                            if waited.get(sname, -1) >= val:
                                continue
                            h.wait_ge(sems[sname], val)
                            waited[sname] = val
                        if fn is None:
                            continue
                        if fn == "MILESTONE":
                            if eng_name in ("v", "g"):
                                ins = h.engine_nop()
                            elif eng_name == "s":
                                ins = h.activation(sdummy[:], sdummy[:], AF.Copy)
                            else:
                                ins = h.nop()
                            ins.then_inc(sems[eng_name], 1)
                            continue
                        ins = fn(h)
                        if ev is not None:
                            ins.then_inc(sems[eng_name + "d"] if dma else sems[eng_name], 16 if dma else 1)
                return body

            blk.sync(run_stream("y"))
            blk.gpsimd(run_stream("g"))
            blk.tensor(run_stream("t"))
            blk.vector(run_stream("v"))
            blk.scalar(run_stream("s"))
        return nc

    def final_wait(self):
        evs = []
        for e in ("y", "g"):
            if self.cnt.get(e + "d", 0) > 0:
                evs.append((e + "d", self.cnt[e + "d"]))
        self.streams["y"].append([None, tuple(evs), None, False])


_PHASE_N = [0]


def _nc():
    return bass.Bass("TRN2", target_bir_lowering=False, debug=False)


CONST_SHAPES = {
    "c_ones_1x128": ([1, 128], F32),
    "c_ones_col": ([128, 1], F32),
    "c_ident_f32": ([128, 128], F32),
    "c_ident_bf": ([128, 128], BF16),
    "c_ltriu": ([128, 128], F32),
    "c_rev64": ([128, 64], F32),
    "c_rev16": ([128, 16], F32),
    "c_ones_colbf": ([128, 1], BF16),
    "c_ones_rowbf": ([1, 128], BF16),
}


def _consts_np():
    return {
        "c_ones_1x128": np.ones((1, 128), dtype=np.float32),
        "c_ones_col": np.ones((128, 1), dtype=np.float32),
        "c_ident_f32": np.eye(128, dtype=np.float32),
        "c_ident_bf": np.eye(128, dtype=BF),
        "c_ltriu": np.triu(np.ones((128, 128), dtype=np.float32), 1),
        "c_rev64": np.broadcast_to(64.0 - np.arange(64, dtype=np.float32), (128, 64)).copy(),
        "c_rev16": np.broadcast_to(16.0 - np.arange(16, dtype=np.float32), (128, 16)).copy(),
        "c_ones_colbf": np.ones((128, 1), dtype=BF),
        "c_ones_rowbf": np.ones((1, 128), dtype=BF),
    }


def _use_consts(nc, p, S, names):
    tiles = {}
    ev = None
    for n in names:
        sh, dt = CONST_SHAPES[n]
        ap = nc.dram_tensor(n, sh, dt, kind="ExternalInput").ap()
        tl = S(n + "_sb", list(sh), dt)
        tiles[n] = tl
        ev = p.y(lambda y, tl=tl, ap=ap: y.dma_start(out=tl[:], in_=ap), inc=True)
    return tiles, ev


def _ld_fm(p, hbm_ap, sbuf_t, cast=False, wait=()):
    src = hbm_ap.rearrange("(c p) t -> p c t", p=128)
    if cast:
        return p.g(lambda h: h.dma_start(out=sbuf_t[:], in_=src), wait=wait, inc=True, dma=True)
    return p.y(lambda h: h.dma_start(out=sbuf_t[:], in_=src), wait=wait, inc=True)



def emit_pre0(nc, p, io, cs, ld_c):
    x_h = io["x_fm"]
    wq_h = io["wq_t"]
    wk_h = io["wk_t"]
    wv_h = io["wv_t"]
    gw_h = io["gate_w"]

    xn_o = io["xn0"]
    q_o = io["q0"]
    k_o = io["k0"]
    v_o = io["v0"]
    qn2_o = io["qn2"]
    kmax_o = io["kmax2"]
    wd_o = io["wd"]
    pos_o = io["pos"]
    cnt_o = io["cnt"]

    ctx = ExitStack()
    _pfx = "ph%d_" % _PHASE_N[0]
    _PHASE_N[0] += 1
    S = lambda n, sh, dt=F32: ctx.enter_context(nc.sbuf_tensor(_pfx + n, sh, dt))
    PS = lambda n, sh, dt=F32: ctx.enter_context(nc.psum_tensor(_pfx + n, sh, dt))
    with ctx:
        x_sb = S("x_sb", [128, 8, T])
        xnb = S("xnb", [128, 8, T], BF16)
        xnf_t = S("xnf_t", [128, 8, 512])
        wq_sb = S("wq_sb", [128, 8, D], BF16)
        wk_sb = S("wk_sb", [128, 8, D], BF16)
        gw_sb = S("gw_sb", [128, 8, NE0])
        sq_full = S("sq_full", [128, 8, 512])
        rows = S("rows", [1, 512 * 6])
        qstage = S("qstage", [128, 2, 512], BF16)
        kstage = S("kstage", [128, 2, 512], BF16)
        vstage = S("vstage", [128, 2, 1024], BF16)
        sqq = S("sqq", [128, 2, 512])
        qn_stage = S("qn_stage", [1, 2, 512])
        kn_row = S("kn_row", [1, 512])
        kmax_sb = S("kmax_sb", [1, NH])
        tmp11 = S("tmp11", [1, 1])
        probs = S("probs", [128, NE0])
        eq_t = S("eq_t", [128, NE0])
        tmq = S("tmq", [128, NE0])
        sel8 = S("sel8", [128, NE0])
        wd_t = S("wd_t", [128, NE0])
        negm = S("negm", [128, 1])
        mx = S("mx", [128, 1])
        am = S("am", [128, 1])
        wsum = S("wsum", [128, 1])
        carry = S("carry", [1, NE0])
        wd_sb = S("wd_sb", [128, 16, NE0])
        pos_sb = S("pos_sb", [128, 16, NE0])

        pA = PS("pA", [1, 512])
        pB = PS("pB", [1, 512])
        pC = PS("pC", [128, 512])
        pD = PS("pD", [128, 512])
        pE = PS("pE", [128, 512])
        pF = PS("pF", [128, 512])

        ones_col = cs["c_ones_col"]
        ones_row = cs["c_ones_1x128"]
        ltriu = cs["c_ltriu"]
        rev64 = cs["c_rev64"]

        ld_x = _ld_fm(p, x_h, x_sb)
        ld_wq = _ld_fm(p, wq_h, wq_sb, cast=True)
        ld_wk = _ld_fm(p, wk_h, wk_sb, cast=True)
        ld_gw = _ld_fm(p, gw_h, gw_sb)

        mean_r = rows[0:1, 0:512]
        msq_r = rows[0:1, 512:1024]
        var_r = rows[0:1, 1024:1536]
        std_r = rows[0:1, 1536:2048]
        rstd_r = rows[0:1, 2048:2560]
        tmp_r = rows[0:1, 2560:3072]

        ev_store = []
        p.v(lambda v: v.memset(carry[:], 0.0))
        p.v(lambda v: v.memset(kmax_sb[:], -1e30))

        for tc in range(4):
            ts = slice(512 * tc, 512 * (tc + 1))
            sq_evs = []
            for ci in range(8):
                sq_evs.append(p.s(lambda s, ci=ci, ts=ts: s.activation(
                    sq_full[:, ci, :], x_sb[:, ci, ts], AF.Square), wait=(ld_x,), inc=True))
            for ci in range(8):
                p.t(lambda t, ci=ci, ts=ts: t.matmul(
                    pA[0:1, :], ones_col[:], x_sb[:, ci, ts],
                    start=(ci == 0), stop=(ci == 7)), wait=(ld_x, ld_c))
            mm_ev = None
            for ci in range(8):
                mm_ev = p.t(lambda t, ci=ci: t.matmul(
                    pB[0:1, :], ones_col[:], sq_full[:, ci, :],
                    start=(ci == 0), stop=(ci == 7)), wait=(sq_evs[ci],), inc=(ci == 7))
            e1 = p.v(lambda v: v.tensor_scalar(mean_r, pA[0:1, :], 1.0 / D, None, ALU.mult), wait=(mm_ev,), inc=True)
            p.v(lambda v: v.tensor_scalar(msq_r, pB[0:1, :], 1.0 / D, None, ALU.mult))
            p.v(lambda v: v.tensor_tensor(tmp_r, mean_r, mean_r, ALU.mult))
            p.v(lambda v: v.tensor_tensor(var_r, msq_r, tmp_r, ALU.subtract))
            e3 = p.v(lambda v: v.tensor_scalar(var_r, var_r, 1e-5, None, ALU.add), inc=True)
            e4 = p.s(lambda s: s.activation(std_r, var_r, AF.Sqrt), wait=(e3,), inc=True)
            e5 = p.v(lambda v: v.reciprocal(rstd_r, std_r), wait=(e4,), inc=True)
            b1 = p.t(lambda t: t.matmul(pC[:], ones_row[:], mean_r, start=True, stop=True),
                     wait=(e1, e5), inc=True)
            b2 = p.t(lambda t: t.matmul(pD[:], ones_row[:], rstd_r, start=True, stop=True),
                     wait=(e5,), inc=True)
            ev_xn = None
            for ci in range(8):
                p.v(lambda v, ci=ci, ts=ts: v.tensor_tensor(xnf_t[:, ci, :], x_sb[:, ci, ts], pC[:], ALU.subtract), wait=(b1, b2))
                p.v(lambda v, ci=ci: v.tensor_tensor(xnf_t[:, ci, :], xnf_t[:, ci, :], pD[:], ALU.mult))
                ev_xn = p.v(lambda v, ci=ci, ts=ts: v.tensor_copy(xnb[:, ci, ts], xnf_t[:, ci, :]), inc=True)

            for st4 in range(4):
                tt = 4 * tc + st4
                for ci in range(8):
                    p.t(lambda t, ci=ci, st4=st4: t.matmul(
                        pC[0:128, 0:64], xnf_t[:, ci, 128 * st4:128 * (st4 + 1)], gw_sb[:, ci, :],
                        start=(ci == 0), stop=(ci == 7)), wait=(ev_xn, ld_gw))
                mlg = p.mark("t")
                r1 = p.v(lambda v: v.reduce_max(mx[:], pC[0:128, 0:64], axis=AX.X), wait=(mlg,), inc=True)
                e2 = p.v(lambda v: v.tensor_scalar(negm[:], mx[:], -1.0, None, ALU.mult), wait=(r1,), inc=True)
                e3b = p.s(lambda s: s.activation(probs[:], pC[0:128, 0:64], AF.Exp, bias=negm[:]), wait=(e2,), inc=True)
                e4b = p.v(lambda v: v.reduce_sum(wsum[:], probs[:], axis=AX.X), wait=(e3b,), inc=True)
                e5b = p.v(lambda v: v.reciprocal(mx[:], wsum[:]), wait=(e4b,), inc=True)
                p.v(lambda v: v.tensor_scalar(probs[:], probs[:], mx[:], None, ALU.mult), wait=(e5b,))
                p.v(lambda v: v.memset(sel8[:], 0.0))
                p.v(lambda v: v.memset(wd_t[:], 0.0))
                ek = p.v(lambda v: v.memset(wsum[:], 0.0), inc=True)
                for kk in range(8):
                    rr1 = p.v(lambda v: v.reduce_max(mx[:], probs[:], axis=AX.X), wait=(ek,), inc=True)
                    p.v(lambda v: v.tensor_scalar(eq_t[:], probs[:], mx[:], None, ALU.is_ge), wait=(rr1,))
                    p.v(lambda v: v.tensor_tensor(tmq[:], eq_t[:], rev64[:], ALU.mult), wait=(ld_c,))
                    rr2 = p.v(lambda v: v.reduce_max(am[:], tmq[:], axis=AX.X), inc=True)
                    p.v(lambda v: v.tensor_scalar(eq_t[:], tmq[:], am[:], None, ALU.is_ge), wait=(rr2,))
                    p.v(lambda v: v.tensor_tensor(sel8[:], sel8[:], eq_t[:], ALU.add))
                    p.v(lambda v: v.scalar_tensor_tensor(wd_t[:], eq_t[:], mx[:], wd_t[:], ALU.mult, ALU.add))
                    p.v(lambda v: v.tensor_tensor(wsum[:], wsum[:], mx[:], ALU.add))
                    ek = p.v(lambda v: v.scalar_tensor_tensor(probs[:], eq_t[:], -2.0, probs[:], ALU.mult, ALU.add), inc=True)
                en = p.v(lambda v: v.reciprocal(mx[:], wsum[:]), wait=(ek,), inc=True)
                p.v(lambda v, tt=tt: v.tensor_scalar(wd_sb[:, tt, :], wd_t[:], mx[:], None, ALU.mult), wait=(en,))
                esel = p.v(lambda v: v.engine_nop(), inc=True)
                m1 = p.t(lambda t: t.matmul(pD[0:128, 0:64], ltriu[:], sel8[:], start=True, stop=False),
                         wait=(esel,))
                m2 = p.t(lambda t: t.matmul(pD[0:128, 0:64], ones_row[:], carry[:], start=False, stop=True), inc=True)
                m3 = p.t(lambda t: t.matmul(pB[0:1, 0:64], ones_col[:], sel8[:], start=True, stop=True),
                         wait=(esel,), inc=True)
                p.v(lambda v, tt=tt: v.tensor_copy(pos_sb[:, tt, :], pD[0:128, 0:64]), wait=(m2,))
                p.v(lambda v: v.tensor_tensor(carry[:], carry[:], pB[0:1, 0:64], ALU.add), wait=(m3,))
                p.barrier()
            p.barrier()

        bar = p.barrier()
        ev_store.append(p.y(lambda y: y.dma_start(out=xn_o, in_=xnb[:].rearrange("p c t -> p (c t)")), inc=True))
        ev_store.append(p.y(lambda y: y.dma_start(out=wd_o, in_=wd_sb[:].rearrange("p c t -> p (c t)")), inc=True))
        ev_store.append(p.y(lambda y: y.dma_start(out=pos_o, in_=pos_sb[:].rearrange("p c t -> p (c t)")), inc=True))
        ev_cnt = p.y(lambda y: y.dma_start(out=cnt_o, in_=carry[:]), inc=True)
        ev_store.append(ev_cnt)

        prev_ev = {0: None, 1: None}
        prev_store = {}
        prev_qn = {0: None, 1: None}
        prev_sqq = {0: None, 1: None}
        prev_qnst = {0: None, 1: None}
        k_done = None
        ev_km = None
        for which, (w_sb, stage, out_hbm, ld_w, scale, is_q) in enumerate(
                [(wk_sb, kstage, k_o, ld_wk, 1.0, False), (wq_sb, qstage, q_o, ld_wq, 0.125, True)]):
            for co in range(8):
                for tc in range(4):
                    ts = slice(512 * tc, 512 * (tc + 1))
                    buf = (co * 4 + tc) % 2
                    pX = pE if buf == 0 else pF
                    for ci in range(8):
                        p.t(lambda t, ci=ci, co=co, ts=ts, w_sb=w_sb, pX=pX: t.matmul(
                            pX[:], w_sb[:, ci, 128 * co:128 * (co + 1)], xnb[:, ci, ts],
                            start=(ci == 0), stop=(ci == 7)),
                            wait=(ld_w,) + ((prev_ev[buf],) if prev_ev[buf] else ()))
                    mm = p.mark("t")
                    ev_ev = p.s(lambda s, buf=buf, scale=scale, stage=stage, pX=pX: s.activation(
                        stage[:, buf, :], pX[:], AF.Copy, scale=scale),
                        wait=(mm,) + ((prev_store.get((which, buf)),) if prev_store.get((which, buf)) else ()), inc=True)
                    prev_ev[buf] = ev_ev
                    prev_store[(which, buf)] = p.y(
                        lambda y, buf=buf, co=co, ts=ts, stage=stage, out_hbm=out_hbm:
                        y.dma_start(out=out_hbm.rearrange("p (c t) -> p c t", c=8)[:, co, ts],
                                    in_=stage[:, buf, :]), wait=(ev_ev,), inc=True)
                    sq_ev = p.s(lambda s, buf=buf, stage=stage: s.activation(
                        sqq[:, buf, :], stage[:, buf, :], AF.Square),
                        wait=(ev_ev,) + ((prev_sqq[buf],) if prev_sqq[buf] else ()), inc=True)
                    for hh in range(2):
                        h = 2 * co + hh
                        pN = pA if hh == 0 else pB
                        mqn = p.t(lambda t, hh=hh, buf=buf, pN=pN: t.matmul(
                            pN[0:1, :], ones_col[64 * hh:64 * (hh + 1), :], sqq[64 * hh:64 * (hh + 1), buf, :],
                            start=True, stop=True), wait=(sq_ev,) + ((prev_qn[hh],) if prev_qn[hh] else ()), inc=True)
                        prev_sqq[buf] = mqn
                        if is_q:
                            cpe = p.v(lambda v, buf=buf, pN=pN: v.tensor_copy(qn_stage[0:1, buf, :], pN[0:1, :]),
                                      wait=(mqn,) + ((prev_qnst[buf],) if prev_qnst[buf] else ()), inc=True)
                            prev_qn[hh] = cpe
                            prev_qnst[buf] = p.y(lambda y, h=h, ts=ts, buf=buf: y.dma_start(
                                out=qn2_o[h:h + 1, ts], in_=qn_stage[0:1, buf, :]), wait=(cpe,), inc=True)
                        else:
                            cpe = p.v(lambda v, pN=pN: v.tensor_copy(kn_row[:], pN[0:1, :]), wait=(mqn,), inc=True)
                            rm = p.v(lambda v: v.reduce_max(tmp11[:], kn_row[:], axis=AX.X), wait=(cpe,), inc=True)
                            prev_qn[hh] = p.v(lambda v, h=h: v.tensor_tensor(
                                kmax_sb[0:1, h:h + 1], kmax_sb[0:1, h:h + 1], tmp11[:], ALU.max), wait=(rm,), inc=True)
            if which == 0:
                ev_km = p.y(lambda y: y.dma_start(out=kmax_o, in_=kmax_sb[:]),
                            wait=(prev_qn[0], prev_qn[1]), inc=True)
                ev_store.append(ev_km)
                k_done = (prev_store[(0, 0)], prev_store[(0, 1)])
            else:
                pmk = p.mark("t")
                ld_wv = _ld_fm(p, wv_h, wq_sb, cast=True, wait=(pmk,))

        prev_pf = p.mark("s")
        prev_vst = {0: None, 1: None}
        for tt in range(16):
            buf = tt % 2
            ev_half = []
            for half in range(2):
                for ci in range(8):
                    p.t(lambda t, tt=tt, half=half, ci=ci: t.matmul(
                        pE[:], xnb[:, ci, 128 * tt:128 * (tt + 1)], wq_sb[:, ci, 512 * half:512 * (half + 1)],
                        start=(ci == 0), stop=(ci == 7)),
                        wait=(ld_wv,) + ((prev_pf,) if prev_pf else ())),
                mm = p.mark("t")
                ev_half.append(p.v(lambda v, buf=buf, half=half: v.tensor_copy(
                    vstage[:, buf, 512 * half:512 * (half + 1)], pE[:]),
                    wait=(mm,) + ((prev_vst[buf],) if prev_vst[buf] else ()), inc=True))
                prev_pf = ev_half[-1]
            ev_st = p.y(lambda y, tt=tt, buf=buf: y.dma_start(
                out=v_o.rearrange("p (c t) -> p c t", c=16)[:, tt, :], in_=vstage[:, buf, :]),
                wait=tuple(ev_half), inc=True)
            prev_vst[buf] = ev_st
        ev_store.append(ev_st)
        return {"k": k_done, "v": (prev_vst[0], prev_vst[1]), "km": (ev_km,), "cnt": (ev_cnt,)}




def _host_split_fm(x):
    xf = np.ascontiguousarray(x.reshape(2 * SEQ, D))
    return [np.ascontiguousarray(xf[c * T:(c + 1) * T].T) for c in range(NCORES)]


def _rope_tables(core):
    pos = (core % 4) * T + np.arange(T, dtype=np.float64)
    inv = 1.0 / (10000.0 ** (np.arange(0, HD, 2, dtype=np.float64) / HD))
    ang = pos[None, :] * inv[(np.arange(HD) // 2)][:, None]
    cos = np.cos(ang).astype(np.float32)
    sin = np.sin(ang).astype(np.float32)
    sgn = np.where(np.arange(HD) % 2 == 0, -1.0, 1.0).astype(np.float32)[:, None]
    sin_s = sin * sgn
    return (np.ascontiguousarray(np.tile(cos, (2, 1))),
            np.ascontiguousarray(np.tile(sin_s, (2, 1))))


def _attn_inputs(r1s, core, qk, kk, vk, qnk=None, kmk=None):
    grp = range(0, 4) if core < 4 else range(4, 8)
    k_full = np.concatenate([r1s[c][kk].reshape(128, 8, T) for c in grp], axis=2)
    v_all = np.concatenate([r1s[c][vk].reshape(128, 16, 8, 128) for c in grp], axis=1)
    v_dev = np.ascontiguousarray(v_all.transpose(2, 0, 1, 3).reshape(8, 128, 64 * 128))
    m = {"kfull" if qnk else "k1full": np.ascontiguousarray(k_full.reshape(128, 8 * SEQ)),
         "vfull" if qnk else "v1full": v_dev}
    m["q0" if qnk else "q1"] = np.ascontiguousarray(r1s[core][qk])
    if qnk:
        m["qn2"] = np.ascontiguousarray(r1s[core][qnk])
        m["kmax2g"] = np.ascontiguousarray(
            np.concatenate([r1s[c][kmk][0] for c in grp]).reshape(1, 64))
    return m


def kernel(**inputs):
    consts = _consts_np()
    x = np.asarray(inputs["L0_attn_x"], dtype=np.float32)
    wqkv = np.asarray(inputs["L0_attn_W_qkv"], dtype=np.float32)
    gate_w = np.ascontiguousarray(np.asarray(inputs["L0_ffn_gate_w"], dtype=np.float32))
    sgw = np.asarray(inputs["L0_ffn_shared_gate_w"], dtype=np.float32)
    sguw = np.asarray(inputs["L0_ffn_shared_gate_up_w"], dtype=np.float32)
    sdw = np.ascontiguousarray(np.asarray(inputs["L0_ffn_shared_down_w"], dtype=np.float32))
    eguw = np.ascontiguousarray(np.asarray(inputs["L0_ffn_expert_gate_up_w"], dtype=np.float32))
    edww = np.ascontiguousarray(np.asarray(inputs["L0_ffn_expert_down_w"], dtype=np.float32))
    wq1 = np.asarray(inputs["L1_attn_Wq"], dtype=np.float32)
    wk1 = np.asarray(inputs["L1_attn_Wk"], dtype=np.float32)
    wv1 = np.asarray(inputs["L1_attn_Wv"], dtype=np.float32)
    wo1 = np.asarray(inputs["L1_attn_Wo"], dtype=np.float32)
    sqk = np.asarray(inputs["L1_attn_s_qk"], dtype=np.float32)
    wr1 = np.asarray(inputs["L1_ffn_w_router"], dtype=np.float32)
    wmh = np.asarray(inputs["L1_ffn_w_mh"], dtype=np.float32)
    wmg = np.asarray(inputs["L1_ffn_w_mg"], dtype=np.float32)
    kff = np.asarray(inputs["L1_ffn_k_ffwd"], dtype=np.float32)
    vff = np.ascontiguousarray(np.asarray(inputs["L1_ffn_v_ffwd"], dtype=np.float32))

    x_parts = _host_split_fm(x)
    wq_t = np.ascontiguousarray(wqkv[0:D].T)
    wk_t = np.ascontiguousarray(wqkv[D:2 * D].T)
    wv_t = np.ascontiguousarray(wqkv[2 * D:3 * D].T)
    sgu_pairs = np.zeros((D, 16, 256), dtype=np.float32)
    for i in range(16):
        sgu_pairs[:, i, 0:128] = sguw[:, 128 * i:128 * (i + 1)]
        sgu_pairs[:, i, 128:256] = sguw[:, 2048 + 128 * i:2048 + 128 * (i + 1)]
    sgu_pairs = np.ascontiguousarray(sgu_pairs.reshape(D, 16 * 256))
    sgate_r = np.ascontiguousarray(sgw.reshape(8, 128).T)
    wq1_t = np.ascontiguousarray(wq1.T)
    wk1_t = np.ascontiguousarray(wk1.T)
    wv1_t = np.ascontiguousarray(wv1.T)
    wo1_t = np.ascontiguousarray(wo1.T)
    sqk_t = np.ascontiguousarray(sqk.T)
    wr1_t = np.ascontiguousarray(wr1.T)
    wmh_t = np.ascontiguousarray(wmh.T)
    wmg_t = np.ascontiguousarray(wmg.T)
    kff_r = np.ascontiguousarray(kff.transpose(0, 1, 3, 2))

    cores = list(range(NCORES))

    nc1 = build_pre0()
    in1 = []
    for c in cores:
        m = {"x_fm": x_parts[c], "wq_t": wq_t, "wk_t": wk_t, "wv_t": wv_t, "gate_w": gate_w}
        for kname in ["c_ones_1x128", "c_ones_col", "c_ltriu", "c_rev64"]:
            m[kname] = consts[kname]
        in1.append(m)
    r1s = run_bass_kernel_spmd(nc1, in1, core_ids=cores).results

    nc2 = build_attn0()
    in2 = []
    for c in cores:
        m = _attn_inputs(r1s, c, "q0", "k0", "v0", "qn2", "kmax2")
        for kname in ["c_ones_1x128", "c_ones_colbf"]:
            m[kname] = consts[kname]
        in2.append(m)
    r2s = run_bass_kernel_spmd(nc2, in2, core_ids=cores).results

    cnt_all = np.ascontiguousarray(np.stack([r1s[c]["cnt"][0] for c in cores]))
    nc3 = build_moe0()
    in3 = []
    for c in cores:
        m = {"xn0": r1s[c]["xn0"], "x_fm": x_parts[c], "s0": r2s[c]["s0"],
             "wd": r1s[c]["wd"], "pos": r1s[c]["pos"], "cnt_all": cnt_all,
             "core_mask": np.ascontiguousarray((np.arange(8) < c).astype(np.float32).reshape(8, 1)),
             "sgu_pairs": sgu_pairs, "sdown": sdw, "sgate": sgate_r,
             "egu": eguw, "edw": edww}
        for kname in ["c_ones_1x128", "c_ident_f32"]:
            m[kname] = consts[kname]
        in3.append(m)
    r3s = run_bass_kernel_spmd(nc3, in3, core_ids=cores).results

    nc4 = build_pre1()
    in4 = []
    for c in cores:
        cos_t, sin_t = _rope_tables(c)
        m = {"x1": r3s[c]["x1"], "wq1_t": wq1_t, "wk1_t": wk1_t, "wv1_t": wv1_t,
             "wr1_t": wr1_t, "sqk_t": sqk_t, "rope_cos": cos_t, "rope_sin": sin_t}
        for kname in ["c_ones_1x128", "c_ones_col", "c_rev16", "c_ident_f32"]:
            m[kname] = consts[kname]
        in4.append(m)
    r4s = run_bass_kernel_spmd(nc4, in4, core_ids=cores).results

    nc5 = build_attn1()
    in5 = []
    for c in cores:
        m = _attn_inputs(r4s, c, "q1", "k1", "v1")
        m["wo_t"] = wo1_t
        for kname in ["c_ones_1x128", "c_ones_colbf", "c_ident_f32"]:
            m[kname] = consts[kname]
        in5.append(m)
    r5s = run_bass_kernel_spmd(nc5, in5, core_ids=cores).results

    nc6 = build_ffn1()
    in6 = []
    for c in cores:
        m = {"x1": r3s[c]["x1"], "x1n": r4s[c]["x1n"], "s1": r5s[c]["s1"],
             "wmh_t": wmh_t, "wmg_t": wmg_t, "kffwd_r": kff_r, "vffwd": vff,
             "g1d": r4s[c]["g1d"]}
        for kname in ["c_ones_1x128"]:
            m[kname] = consts[kname]
        in6.append(m)
    r6s = run_bass_kernel_spmd(nc6, in6, core_ids=cores).results

    out = np.empty((2 * SEQ, D), dtype=np.float32)
    for c in cores:
        out[c * T:(c + 1) * T] = r6s[c]["outp"].reshape(128, 8, T).transpose(2, 1, 0).reshape(T, D)
    return out.reshape(2, SEQ, D)



def emit_attn0(nc, p, io, cs, ld_c):
    q_h = io["q0"]
    k_h = io["kfull"]
    v_h = io["vfull"]
    qn2_h = io["qn2"]
    km_h = io["kmax2g"]

    s0_o = io["s0"]

    ctx = ExitStack()
    _pfx = "ph%d_" % _PHASE_N[0]
    _PHASE_N[0] += 1
    S = lambda n, sh, dt=F32: ctx.enter_context(nc.sbuf_tensor(_pfx + n, sh, dt))
    PS = lambda n, sh, dt=F32: ctx.enter_context(nc.psum_tensor(_pfx + n, sh, dt))
    with ctx:
        qfm = S("qfm", [128, 8, T], BF16)
        kbuf = S("kbuf", [128, 2, SEQ], BF16)
        vbuf = S("vbuf", [128, 2, 64 * 128], BF16)
        e_sb = S("e_sb", [128, 2, 2, 1536], BF16)
        sstage = S("sstage", [128, 2, 512])
        qn16 = S("qn16", [NH, T])
        qm4 = S("qm4", [NH, 4])
        t64 = S("t64", [1, 16, 4])
        negc_sb = S("negc_sb", [128, 64])
        km2 = S("km2", [1, NH])
        rz = S("rz", [1, 2, 512], BF16)
        rzf = S("rzf", [1, 2, 512])

        pSa = PS("pSa", [128, 1536])
        pSb = PS("pSb", [128, 1536])
        pAV = PS("pAV", [128, 512])
        pZ = PS("pZ", [128, 512])

        ones_row = cs["c_ones_1x128"]
        ones_rowbf = cs["c_ones_rowbf"]
        ones_colbf = cs["c_ones_colbf"]

        ld_q = p.y(lambda y: y.dma_start(out=qfm[:], in_=q_h.rearrange("p (c t) -> p c t", c=8)), inc=True)
        kmg = S("kmg", [1, 64])
        ld_km = p.y(lambda y: y.dma_start(out=kmg[:], in_=km_h), inc=True)
        ld_qn = p.y(lambda y: y.dma_start(out=qn16[:], in_=qn2_h), inc=True)
        e0 = p.v(lambda v: v.tensor_tensor(km2[:], kmg[0:1, 0:16], kmg[0:1, 16:32], ALU.max), wait=(ld_km,))
        p.v(lambda v: v.tensor_tensor(km2[:], km2[:], kmg[0:1, 32:48], ALU.max))
        p.v(lambda v: v.tensor_tensor(km2[:], km2[:], kmg[0:1, 48:64], ALU.max))
        ev_km2 = p.mark("v")

        NG = 22
        GS = [min(3, 64 - 3 * g) for g in range(NG)]

        ld_k = {}
        ld_v = {}
        for hp in range(2):
            ld_k[hp] = p.y(lambda y, hp=hp: y.dma_start(
                out=kbuf[:, hp % 2, :].rearrange("p (r t) -> p r t", r=4), in_=k_h[:, hp]), inc=True)
            ld_v[hp] = p.g(lambda g, hp=hp: g.dma_start(
                out=vbuf[:, hp % 2, :], in_=v_h[hp]), inc=True, dma=True)

        for j in range(4):
            p.v(lambda v, j=j: v.reduce_max(qm4[:, j:j + 1], qn16[:, 512 * j:512 * (j + 1)], axis=AX.X),
                wait=(ld_qn,) if j == 0 else ())
        km4mm = p.t(lambda t: t.matmul(pSb[0:16, 0:4], km2[0:1, :], ones_row[0:1, 0:4],
                                       start=True, stop=True), wait=(ev_km2,), inc=True)
        p.v(lambda v: v.tensor_tensor(qm4[:], qm4[:], pSb[0:16, 0:4], ALU.mult), wait=(km4mm,))
        sq_ev = p.s(lambda s: s.activation(qm4[:], qm4[:], AF.Sqrt), wait=(p.mark("v"),), inc=True)
        ngv = p.v(lambda v: v.tensor_scalar(qm4[:], qm4[:], -1.0, None, ALU.mult), wait=(sq_ev,), inc=True)
        dq = p.y(lambda y: y.dma_start(out=t64[:],
                                       in_=qm4[:].rearrange("p j -> p () j")), wait=(ngv,), inc=True)
        bcn = p.t(lambda t: t.matmul(pZ[:, 0:64], ones_row[:], t64[:].rearrange("o h j -> o (h j)"),
                                     start=True, stop=True), wait=(dq,), inc=True)
        nccA = p.v(lambda v: v.tensor_copy(negc_sb[:], pZ[:, 0:64]), wait=(bcn,), inc=True)

        prev_c1 = None
        prev_mres = nccA
        prev_expa_l = None
        prev_exp_l = None
        prev_store = {0: None, 1: None}
        pend = [None]
        for hp in range(8):
            kb = hp % 2
            ld_k_ev = ld_k[hp]
            ld_v_ev = ld_v[hp]
            for q4 in range(4):
                qsl = slice(512 * q4, 512 * (q4 + 1))
                wA0 = (ld_k_ev, ld_q) + ((prev_expa_l,) if prev_expa_l else ())
                wB0 = (ld_k_ev, ld_q) + ((prev_exp_l,) if prev_exp_l else ())
                qa0 = [None] * GS[0]
                qb0 = [None] * GS[0]
                for j in range(GS[0]):
                    p.t(lambda t, j=j, kb=kb, hp=hp, qsl=qsl: t.matmul(
                        pSa[:, 512 * j:512 * (j + 1)], kbuf[0:64, kb, 128 * j:128 * (j + 1)],
                        qfm[0:64, hp, qsl], start=True, stop=True, tile_position=(0, 0)),
                        wait=wA0 if j == 0 else ())
                    qa0[j] = p.mark("t")
                    p.t(lambda t, j=j, kb=kb, hp=hp, qsl=qsl: t.matmul(
                        pSb[:, 512 * j:512 * (j + 1)], kbuf[64:128, kb, 128 * j:128 * (j + 1)],
                        qfm[64:128, hp, qsl], start=True, stop=True, tile_position=(64, 0)),
                        wait=wB0 if j == 0 else ())
                    qb0[j] = p.mark("t")
                if pend[0] is not None:
                    m_ev, st_ev, sb_p = pend[0]()
                    prev_mres = m_ev
                    prev_store[sb_p] = st_ev
                    pend[0] = None

                ev_ea = {}
                ev_eb = {}
                ev_av = [None] * NG
                ca = 4 * 2 * hp + q4
                cb = 4 * (2 * hp + 1) + q4
                for g in range(NG):
                    gb = g % 2
                    if g >= 1:
                        qa0 = [None] * GS[g]
                        qb0 = [None] * GS[g]
                        for j in range(GS[g]):
                            ks = 3 * g + j
                            wA = (ev_ea[(g - 1, j)],) if j < GS[g - 1] else ()
                            wB = (ev_eb[(g - 1, j)],) if j < GS[g - 1] else ()
                            p.t(lambda t, ks=ks, j=j, kb=kb, hp=hp, qsl=qsl: t.matmul(
                                pSa[:, 512 * j:512 * (j + 1)], kbuf[0:64, kb, 128 * ks:128 * (ks + 1)],
                                qfm[0:64, hp, qsl], start=True, stop=True, tile_position=(0, 0)), wait=wA)
                            qa0[j] = p.mark("t")
                            p.t(lambda t, ks=ks, j=j, kb=kb, hp=hp, qsl=qsl: t.matmul(
                                pSb[:, 512 * j:512 * (j + 1)],
                                kbuf[64:128, kb, 128 * ks:128 * (ks + 1)],
                                qfm[64:128, hp, qsl], start=True, stop=True, tile_position=(64, 0)), wait=wB)
                            qb0[j] = p.mark("t")
                    for j in range(GS[g]):
                        w_exp = (qa0[j], nccA)
                        if g >= 2 and j == 0:
                            w_exp = w_exp + ((ev_av[g - 2],) if ev_av[g - 2] else ())
                        ev_ea[(g, j)] = p.s(lambda s, gb=gb, j=j, ca=ca: s.activation(
                            e_sb[:, gb, 0, 512 * j:512 * (j + 1)], pSa[:, 512 * j:512 * (j + 1)],
                            AF.Exp, bias=negc_sb[:, ca:ca + 1]), wait=w_exp, inc=True)
                        ev_eb[(g, j)] = p.s(lambda s, gb=gb, j=j, cb=cb: s.activation(
                            e_sb[:, gb, 1, 512 * j:512 * (j + 1)], pSb[:, 512 * j:512 * (j + 1)],
                            AF.Exp, bias=negc_sb[:, cb:cb + 1]), wait=(qb0[j],), inc=True)
                    for j in range(GS[g]):
                        ks = 3 * g + j
                        first = (g == 0 and j == 0)
                        last = (g == NG - 1 and j == GS[g] - 1)
                        wav = (ev_ea[(g, j)],) + ((ld_v_ev,) if j == 0 else ())
                        p.t(lambda t, ks=ks, j=j, gb=gb, kb=kb, first=first, last=last: t.matmul(
                            pAV[0:64, :], vbuf[:, kb, :].rearrange("p (tt d) -> p tt d", d=128)[:, ks, 0:64],
                            e_sb[:, gb, 0, 512 * j:512 * (j + 1)],
                            start=first, stop=last, tile_position=(0, 0)),
                            wait=wav + ((prev_c1,) if (first and prev_c1) else ()))
                        p.t(lambda t, ks=ks, j=j, gb=gb, kb=kb, first=first, last=last: t.matmul(
                            pAV[64:128, :], vbuf[:, kb, :].rearrange("p (tt d) -> p tt d", d=128)[:, ks, 64:128],
                            e_sb[:, gb, 1, 512 * j:512 * (j + 1)],
                            start=first, stop=last, tile_position=(0, 64), skip_group_check=True),
                            wait=(ev_eb[(g, j)],))
                        p.t(lambda t, ks=ks, j=j, gb=gb, first=first, last=last: t.matmul(
                            pZ[0:1, :], ones_colbf[:], e_sb[:, gb, 0, 512 * j:512 * (j + 1)],
                            start=first, stop=last, tile_position=(0, 0)),
                            wait=((prev_mres,) if (first and prev_mres) else ()))
                        p.t(lambda t, ks=ks, j=j, gb=gb, first=first, last=last: t.matmul(
                            pZ[64:65, :], ones_colbf[:], e_sb[:, gb, 1, 512 * j:512 * (j + 1)],
                            start=first, stop=last, tile_position=(0, 64), skip_group_check=True))
                    ev_av[g] = p.mark("t")
                sb = q4 % 2
                c1 = p.v(lambda v, sb=sb: v.tensor_copy(sstage[:, sb, :], pAV[:]),
                         wait=(ev_av[NG - 1],) + ((prev_store[sb],) if prev_store[sb] else ()), inc=True)
                p.v(lambda v: v.reciprocal(rzf[0:1, 0, :], pZ[0:1, :]))
                p.v(lambda v: v.reciprocal(rzf[0:1, 1, :], pZ[64:65, :]))
                rcast = p.v(lambda v: v.tensor_copy(rz[:].rearrange("o h t -> o (h t)"), rzf[:].rearrange("o h t -> o (h t)")), inc=True)
                prev_c1 = c1
                prev_expa_l = ev_ea[(NG - 1, 0)]
                prev_exp_l = ev_eb[(NG - 1, 0)]

                def mk_pend(sb=sb, hp=hp, qsl=qsl, rcast=rcast):
                    def emit():
                        p.t(lambda t: t.matmul(pZ[0:64, :], ones_rowbf[0:1, 0:64], rz[0:1, 0, :],
                                               start=True, stop=True, tile_position=(0, 0)),
                            wait=(rcast,))
                        b2 = p.t(lambda t: t.matmul(pZ[64:128, :], ones_rowbf[0:1, 0:64], rz[0:1, 1, :],
                                                    start=True, stop=True, tile_position=(0, 64),
                                                    skip_group_check=True), inc=True)
                        mres = p.v(lambda v: v.tensor_tensor(sstage[:, sb, :], sstage[:, sb, :],
                                                             pZ[:, 0:512], ALU.mult), wait=(b2,), inc=True)
                        st = p.y(lambda y: y.dma_start(
                            out=s0_o.rearrange("p (c t) -> p c t", c=8)[:, hp, qsl], in_=sstage[:, sb, :]),
                            wait=(mres,), inc=True)
                        return mres, st, sb
                    return emit
                pend[0] = mk_pend()
            if hp + 2 <= 7:
                nhp = hp + 2
                lastuse = p.mark("t")
                ld_k[nhp] = p.y(lambda y, nhp=nhp: y.dma_start(
                    out=kbuf[:, nhp % 2, :].rearrange("p (r t) -> p r t", r=4), in_=k_h[:, nhp]),
                    wait=(lastuse,), inc=True)
                ld_v[nhp] = p.g(lambda g, nhp=nhp: g.dma_start(
                    out=vbuf[:, nhp % 2, :], in_=v_h[nhp]),
                    wait=(lastuse,), inc=True, dma=True)
        if pend[0] is not None:
            pend[0]()
            pend[0] = None




def emit_moe0(nc, p, io, cs, ld_c):
    xn_h = io["xn0"]
    x_hh = io["x_fm"]
    s0_h = io["s0"]
    wd_h = io["wd"]
    pos_h = io["pos"]
    cnt_h = io["cnt_all"]
    mask_h = io["core_mask"]
    sgu_h = io["sgu_pairs"]
    sdw_h = io["sdown"]
    sgw_h = io["sgate"]
    egu_h = io["egu"]
    edw_h = io["edw"]

    x1_o = io["x1"]
    wkt_h = nc.dram_tensor("wkt_scratch", [NE0, T], F32, kind="Internal").ap()

    ctx = ExitStack()
    _pfx = "ph%d_" % _PHASE_N[0]
    _PHASE_N[0] += 1
    S = lambda n, sh, dt=F32: ctx.enter_context(nc.sbuf_tensor(_pfx + n, sh, dt))
    PS = lambda n, sh, dt=F32: ctx.enter_context(nc.psum_tensor(_pfx + n, sh, dt))
    with ctx:
        xnb = S("xnb", [128, 8, T], BF16)
        y_acc = S("y_acc", [128, 8, T])
        act_t = S("act_t", [128, 16, 512], BF16)
        sgu_db = S("sgu_db", [128, 2, 8, 256], BF16)
        sdw_db = S("sdw_db", [128, 2, D], BF16)
        keepw = S("keepw", [128, 16, NE0])
        grow = S("grow", [1, T])
        gate_bc = S("gate_bc", [128, T], BF16)
        egu_sb = S("egu_sb", [128, 2, 8, D], BF16)
        edw_sb = S("edw_sb", [128, 4, D], BF16)
        silu_t = S("silu_t", [128, 512])
        t1f = S("t1f", [128, T])
        base_row = S("base_row", [1, NE0])
        cnt_sb = S("cnt_sb", [8, NE0])
        mask_sb = S("mask_sb", [8, 1])
        x1st = S("x1st", [128, T])
        pos_t = S("pos_t", [128, NE0])

        pAll = PS("pAll", [128, 4096])

        def pb(i):
            return pAll[:, 512 * i:512 * (i + 1)]

        ones_row = cs["c_ones_1x128"]
        ident = cs["c_ident_f32"]

        ld_xn = p.y(lambda y: y.dma_start(out=xnb[:], in_=xn_h.rearrange("p (c t) -> p c t", c=8)), inc=True)
        ld_cnt = p.y(lambda y: y.dma_start(out=cnt_sb[:], in_=cnt_h), inc=True)
        ld_msk = p.y(lambda y: y.dma_start(out=mask_sb[:], in_=mask_h), inc=True)
        ld_wd = p.y(lambda y: y.dma_start(out=keepw[:], in_=wd_h.rearrange("p (c t) -> p c t", c=16)), inc=True)

        wkT_sb = t1f[0:64, :].rearrange("p (c x) -> p c x", x=128)
        mb = p.t(lambda t: t.matmul(pAll[0:1, 0:NE0], mask_sb[:], cnt_sb[:], start=True, stop=True),
                 wait=(ld_cnt, ld_msk), inc=True)
        e1 = p.v(lambda v: v.tensor_copy(base_row[:], pAll[0:1, 0:NE0]), wait=(mb,), inc=True)
        bb = p.t(lambda t: t.matmul(pAll[0:128, 0:NE0], ones_row[:], base_row[:], start=True, stop=True),
                 wait=(e1, ld_c), inc=True)
        for tt in range(16):
            lp = p.y(lambda y, tt=tt: y.dma_start(out=pos_t[:], in_=pos_h.rearrange("p (c t) -> p c t", c=16)[:, tt, :]),
                     wait=(p.mark("v"),), inc=True)
            p.v(lambda v: v.tensor_tensor(pos_t[:], pos_t[:], pAll[0:128, 0:NE0], ALU.add), wait=(lp, bb))
            p.v(lambda v: v.tensor_scalar(pos_t[:], pos_t[:], float(CAP0), None, ALU.is_lt))
            p.v(lambda v, tt=tt: v.tensor_tensor(keepw[:, tt, :], keepw[:, tt, :], pos_t[:], ALU.mult), wait=(ld_wd,))
        ev_keep = p.mark("v")
        for tt in range(16):
            trw = p.t(lambda t, tt=tt: t.transpose(pAll[0:64, 512:640], keepw[:, tt, :], ident[:]),
                      wait=(ev_keep,) + ((p.mark("v"),) if tt else ()), inc=True)
            p.v(lambda v, tt=tt: v.tensor_copy(wkT_sb[:, tt, :], pAll[0:64, 512:640]), wait=(trw,))
        ev_wkt = p.mark("v")
        st_wkt = p.y(lambda y: y.dma_start(out=wkt_h, in_=wkT_sb.rearrange("p c t -> p (c t)")), wait=(ev_wkt,), inc=True)

        sgw_sb = S("sgw_sb", [128, 8], BF16)
        ld_sgw = p.g(lambda g: g.dma_start(out=sgw_sb[:], in_=sgw_h), inc=True, dma=True)
        for tc in range(4):
            for ci in range(8):
                p.t(lambda t, tc=tc, ci=ci: t.matmul(
                    pAll[0:1, 1024 + 512 * tc:1024 + 512 * (tc + 1)], sgw_sb[:, ci:ci + 1],
                    xnb[:, ci, 512 * tc:512 * (tc + 1)],
                    start=(ci == 0), stop=(ci == 7)), wait=(ld_sgw, ld_xn, st_wkt))
        mg = p.mark("t")
        sg = p.s(lambda s: s.activation(grow[:], pAll[0:1, 1024:1024 + 2048], AF.Sigmoid), wait=(mg,), inc=True)
        for tc in range(4):
            p.t(lambda t, tc=tc: t.matmul(pb(tc), ones_row[:], grow[0:1, 512 * tc:512 * (tc + 1)],
                                          start=True, stop=True), wait=(sg,))
        mgb = p.mark("t")
        ev_free = p.v(lambda v: v.tensor_copy(gate_bc[:], pAll[:, 0:2048]), wait=(mgb,), inc=True)
        ev_grow_free = mgb

        prev_y = None
        for tg in range(4):
            tsl = slice(512 * tg, 512 * (tg + 1))
            for i in range(16):
                dbuf = i % 2
                lw = p.g(lambda g, i=i, dbuf=dbuf: g.dma_start(
                    out=sgu_db[:, dbuf, :, :],
                    in_=sgu_h.rearrange("(c p) (i x) -> p c i x", p=128, i=16)[:, :, i, :]),
                    wait=(p.mark("t"),), inc=True, dma=True)
                for ci in range(8):
                    p.t(lambda t, i=i, tsl=tsl, ci=ci, dbuf=dbuf: t.matmul(
                        pb(0), sgu_db[:, dbuf, ci, 0:128], xnb[:, ci, tsl],
                        start=(ci == 0), stop=(ci == 7)), wait=(lw, ev_free))
                for ci in range(8):
                    p.t(lambda t, i=i, tsl=tsl, ci=ci, dbuf=dbuf: t.matmul(
                        pb(1), sgu_db[:, dbuf, ci, 128:256], xnb[:, ci, tsl],
                        start=(ci == 0), stop=(ci == 7)))
                mmi = p.mark("t")
                sv = p.s(lambda s: s.activation(silu_t[:], pb(0), AF.Silu), wait=(mmi,), inc=True)
                p.v(lambda v: v.tensor_tensor(silu_t[:], silu_t[:], pb(1), ALU.mult), wait=(sv,))
                ev_free = p.v(lambda v, i=i, tsl=tsl: v.tensor_tensor(
                    act_t[:, i, :], silu_t[:], gate_bc[:, tsl], ALU.mult), inc=True)
            for ifc in range(16):
                dbuf = ifc % 2
                lwi = p.g(lambda g, ifc=ifc, dbuf=dbuf: g.dma_start(
                    out=sdw_db[:, dbuf, :],
                    in_=sdw_h.rearrange("(c p) d -> p c d", p=128)[:, ifc, :]),
                    wait=(p.mark("t"),), inc=True, dma=True)
                for fc in range(8):
                    w = (lwi, ev_free) + ((prev_y,) if prev_y else ())
                    p.t(lambda t, ifc=ifc, fc=fc, dbuf=dbuf: t.matmul(
                        pb(fc), sdw_db[:, dbuf, 128 * fc:128 * (fc + 1)], act_t[:, ifc, :],
                        start=(ifc == 0), stop=(ifc == 15)), wait=w)
            mfc = p.mark("t")
            for fc in range(8):
                prev_y = p.v(lambda v, fc=fc, tsl=tsl: v.tensor_copy(y_acc[:, fc, tsl], pb(fc)),
                             wait=(mfc,), inc=True)
            ev_free = prev_y

        wrow_db = S("wrow_bf", [1, 2048], BF16)
        w_bc = gate_bc
        prev_wread = None
        lgu = {0: p.g(lambda g: g.dma_start(
            out=egu_sb[:, 0], in_=egu_h[0].rearrange("(c p) d -> p c d", p=128)),
            wait=(p.mark("t"),), inc=True, dma=True)}
        ldw = {0: p.g(lambda g: g.dma_start(
            out=edw_sb[:], in_=edw_h[0].rearrange("(c p) d -> p c d", p=128)),
            inc=True, dma=True)}
        lwr = {0: p.g(lambda g: g.dma_start(out=wrow_db[0:1, :], in_=wkt_h[0:1, :]),
                      wait=(st_wkt,), inc=True, dma=True)}
        ev_bankpair = {}
        for e in range(NE0):
            eb = e % 2
            for tc in range(4):
                p.t(lambda t, tc=tc, eb=eb: t.matmul(
                    pb(4 + tc), cs["c_ones_rowbf"][:], wrow_db[0:1, 512 * tc:512 * (tc + 1)],
                    start=True, stop=True), wait=(lwr[e], ev_free))
            prev_wread = p.mark("t")
            ev_wbc0 = p.v(lambda v: v.tensor_copy(w_bc[:, 0:1024], pAll[:, 2048:3072]),
                          wait=(prev_wread, ev_free), inc=True)
            ev_wbc = p.v(lambda v: v.tensor_copy(w_bc[:, 1024:2048], pAll[:, 3072:4096]), inc=True)
            if e + 1 < NE0:
                lwr[e + 1] = p.g(lambda g, e=e: g.dma_start(
                    out=wrow_db[0:1, :], in_=wkt_h[e + 1:e + 2, :]),
                    wait=(prev_wread,), inc=True, dma=True)
            ev_act = [ev_wbc0, ev_wbc0]
            for tg in range(4):
                tsl = slice(512 * tg, 512 * (tg + 1))
                for i in range(4):
                    gb2 = 2 * (i % 2)
                    for ci in range(8):
                        bp = ev_bankpair.get(i % 2) if (i < 2 and ev_bankpair) else None
                        w = (lgu[e], (ev_wbc0 if tg < 2 else ev_wbc), ev_act[i % 2]) + ((bp,) if bp else ())
                        p.t(lambda t, i=i, tsl=tsl, ci=ci, eb=eb, gb2=gb2: t.matmul(
                            pb(gb2), egu_sb[:, eb, ci, 128 * i:128 * (i + 1)], xnb[:, ci, tsl],
                            start=(ci == 0), stop=(ci == 7)), wait=w)
                    for ci in range(8):
                        p.t(lambda t, i=i, tsl=tsl, ci=ci, eb=eb, gb2=gb2: t.matmul(
                            pb(gb2 + 1), egu_sb[:, eb, ci, 512 + 128 * i:512 + 128 * (i + 1)], xnb[:, ci, tsl],
                            start=(ci == 0), stop=(ci == 7)))
                    mmi = p.mark("t")
                    sv = p.s(lambda s, gb2=gb2: s.activation(silu_t[:], pb(gb2), AF.Silu), wait=(mmi,), inc=True)
                    p.v(lambda v, gb2=gb2: v.tensor_tensor(silu_t[:], silu_t[:], pb(gb2 + 1), ALU.mult), wait=(sv,))
                    ev_act[i % 2] = p.v(lambda v, i=i, tsl=tsl: v.tensor_tensor(
                        act_t[:, i, :], silu_t[:], w_bc[:, tsl], ALU.mult), inc=True)
                if tg == 3:
                    gu_last = p.mark("t")
                    if e + 1 < NE0:
                        lgu[e + 1] = p.g(lambda g, e=e: g.dma_start(
                            out=egu_sb[:, (e + 1) % 2], in_=egu_h[e + 1].rearrange("(c p) d -> p c d", p=128)),
                            wait=(gu_last,), inc=True, dma=True)
                y_add = {}
                for fc in range(8):
                    for ifc in range(4):
                        w = (ldw[e], ev_act[1]) if (fc == 0 and ifc == 0) else ()
                        p.t(lambda t, ifc=ifc, fc=fc, eb=eb: t.matmul(
                            pb(fc), edw_sb[:, ifc, 128 * fc:128 * (fc + 1)], act_t[:, ifc, :],
                            start=(ifc == 0), stop=(ifc == 3)), wait=w)
                    mfc = p.mark("t")
                    y_add[fc] = p.v(lambda v, fc=fc, tsl=tsl: v.tensor_tensor(
                        y_acc[:, fc, tsl], y_acc[:, fc, tsl], pb(fc), ALU.add), wait=(mfc,), inc=True)
                if tg == 3 and e + 1 < NE0:
                    ldw[e + 1] = p.g(lambda g, e=e: g.dma_start(
                        out=edw_sb[:], in_=edw_h[e + 1].rearrange("(c p) d -> p c d", p=128)),
                        wait=(mfc,), inc=True, dma=True)
                prev_y = y_add[7]
                ev_bankpair = {0: y_add[1], 1: y_add[3]}
                ev_free = prev_y

        xchunk = silu_t
        prev_st = None
        for fc in range(8):
            lx = p.y(lambda y, fc=fc: y.dma_start(
                out=t1f[:], in_=x_hh.rearrange("(c p) t -> p c t", p=128)[:, fc, :]),
                wait=(p.mark("v"),) + ((prev_st,) if prev_st else ()), inc=True)
            ls = p.y(lambda y, fc=fc: y.dma_start(
                out=x1st[:], in_=s0_h.rearrange("p (c t) -> p c t", c=8)[:, fc, :]), inc=True)
            gg = p.s(lambda s, fc=fc: s.activation(y_acc[:, fc, :], y_acc[:, fc, :], AF.Silu),
                     wait=(prev_y,), inc=True)
            p.v(lambda v, fc=fc: v.tensor_tensor(x1st[:], x1st[:], y_acc[:, fc, :], ALU.mult), wait=(gg, ls))
            p.v(lambda v, fc=fc: v.tensor_tensor(x1st[:], x1st[:], t1f[:], ALU.add), wait=(lx,))
            ev_x1 = p.mark("v")
            prev_st = p.y(lambda y, fc=fc: y.dma_start(
                out=x1_o.rearrange("p (c t) -> p c t", c=8)[:, fc, :], in_=x1st[:]), wait=(ev_x1,), inc=True)




def emit_pre1(nc, p, io, cs, ld_c):
    x_h = io["x1"]
    wq_h = io["wq1_t"]
    wk_h = io["wk1_t"]
    wv_h = io["wv1_t"]
    wr_h = io["wr1_t"]
    sqk_h = io["sqk_t"]
    cos_h = io["rope_cos"]
    sin_h = io["rope_sin"]

    xn_o = io["x1n"]
    q_o = io["q1"]
    k_o = io["k1"]
    v_o = io["v1"]
    g_o = io["g1d"]

    SWAP = [(i + 1) if i % 2 == 0 else (i - 1) for i in range(32)]

    ctx = ExitStack()
    _pfx = "ph%d_" % _PHASE_N[0]
    _PHASE_N[0] += 1
    S = lambda n, sh, dt=F32: ctx.enter_context(nc.sbuf_tensor(_pfx + n, sh, dt))
    PS = lambda n, sh, dt=F32: ctx.enter_context(nc.psum_tensor(_pfx + n, sh, dt))
    with ctx:
        x_sb = S("x_sb", [128, 8, T])
        xnb = S("xnb", [128, 8, T], BF16)
        xnf_t = S("xnf_t", [128, 8, 512])
        sq_full = xnf_t
        wq_sb = S("wq_sb", [128, 8, D], BF16)
        wk_sb = S("wk_sb", [128, 8, D], BF16)
        wr_sb = S("wr_sb", [128, 8, NE1])
        sqk_sb = S("sqk_sb", [64, NH])
        qscol = S("qscol", [128, 1])
        kscol = S("kscol", [128, 1])
        cos_sb = S("cos_sb", [128, T])
        sin_sb = S("sin_sb", [128, T])
        rows = S("rows", [1, 512 * 6])
        qstage = S("qstage", [128, 2, 512], BF16)
        kstage = S("kstage", [128, 2, 512], BF16)
        vstage = S("vstage", [128, 2, 1024], BF16)
        qt = S("qt", [128, 512])
        qs = S("qs", [128, 512])
        qro = qt
        sqq = qs
        rn_rows = S("rn_rows", [1, 2, 512], BF16)
        rn_f = S("rn_f", [1, 2, 512])
        probs = S("probs", [128, NE1])
        sig_t = S("sig_t", [128, NE1])
        eq_t = S("eq_t", [128, NE1])
        tmq = S("tmq", [128, NE1])
        sel2 = S("sel2", [128, NE1])
        mx = S("mx", [128, 1])
        am = S("am", [128, 1])
        negm = S("negm", [128, 1])
        gT_sb = S("gT_sb", [16, 16, 128])

        pA = PS("pA", [1, 512])
        pB = PS("pB", [1, 512])
        pC = PS("pC", [128, 512])
        pD = PS("pD", [128, 512])
        pE = PS("pE", [128, 512])
        pF = PS("pF", [128, 512])

        ones_col = cs["c_ones_col"]
        ones_row = cs["c_ones_1x128"]
        rev16 = cs["c_rev16"]
        ident = cs["c_ident_f32"]

        ld_x = p.y(lambda y: y.dma_start(out=x_sb[:], in_=x_h.rearrange("p (c t) -> p c t", c=8)), inc=True)
        ld_wq = _ld_fm(p, wq_h, wq_sb, cast=True)
        ld_wk = _ld_fm(p, wk_h, wk_sb, cast=True)
        ld_wr = _ld_fm(p, wr_h, wr_sb)
        ld_sqk = p.y(lambda y: y.dma_start(out=sqk_sb[:], in_=sqk_h), inc=True)
        ld_cos = p.y(lambda y: y.dma_start(out=cos_sb[:], in_=cos_h), inc=True)
        ld_sin = p.y(lambda y: y.dma_start(out=sin_sb[:], in_=sin_h), inc=True)

        mean_r = rows[0:1, 0:512]
        msq_r = rows[0:1, 512:1024]
        var_r = rows[0:1, 1024:1536]
        std_r = rows[0:1, 1536:2048]
        rstd_r = rows[0:1, 2048:2560]
        tmp_r = rows[0:1, 2560:3072]

        for tc in range(4):
            ts = slice(512 * tc, 512 * (tc + 1))
            sq_evs = []
            for ci in range(8):
                sq_evs.append(p.s(lambda s, ci=ci, ts=ts: s.activation(
                    sq_full[:, ci, :], x_sb[:, ci, ts], AF.Square), wait=(ld_x,), inc=True))
            for ci in range(8):
                p.t(lambda t, ci=ci, ts=ts: t.matmul(
                    pA[0:1, :], ones_col[:], x_sb[:, ci, ts],
                    start=(ci == 0), stop=(ci == 7)), wait=(ld_x, ld_c))
            mm_ev = None
            for ci in range(8):
                mm_ev = p.t(lambda t, ci=ci: t.matmul(
                    pB[0:1, :], ones_col[:], sq_full[:, ci, :],
                    start=(ci == 0), stop=(ci == 7)), wait=(sq_evs[ci],), inc=(ci == 7))
            e1 = p.v(lambda v: v.tensor_scalar(mean_r, pA[0:1, :], 1.0 / D, None, ALU.mult), wait=(mm_ev,), inc=True)
            p.v(lambda v: v.tensor_scalar(msq_r, pB[0:1, :], 1.0 / D, None, ALU.mult))
            p.v(lambda v: v.tensor_tensor(tmp_r, mean_r, mean_r, ALU.mult))
            p.v(lambda v: v.tensor_tensor(var_r, msq_r, tmp_r, ALU.subtract))
            e3 = p.v(lambda v: v.tensor_scalar(var_r, var_r, 1e-5, None, ALU.add), inc=True)
            e4 = p.s(lambda s: s.activation(std_r, var_r, AF.Sqrt), wait=(e3,), inc=True)
            e5 = p.v(lambda v: v.reciprocal(rstd_r, std_r), wait=(e4,), inc=True)
            b1 = p.t(lambda t: t.matmul(pC[:], ones_row[:], mean_r, start=True, stop=True),
                     wait=(e1, e5), inc=True)
            b2 = p.t(lambda t: t.matmul(pD[:], ones_row[:], rstd_r, start=True, stop=True),
                     wait=(e5,), inc=True)
            ev_xn = None
            for ci in range(8):
                p.v(lambda v, ci=ci, ts=ts: v.tensor_tensor(xnf_t[:, ci, :], x_sb[:, ci, ts], pC[:], ALU.subtract), wait=(b1, b2))
                p.v(lambda v, ci=ci: v.tensor_tensor(xnf_t[:, ci, :], xnf_t[:, ci, :], pD[:], ALU.mult))
                ev_xn = p.v(lambda v, ci=ci, ts=ts: v.tensor_copy(xnb[:, ci, ts], xnf_t[:, ci, :]), inc=True)

            for st4 in range(4):
                tt = 4 * tc + st4
                for ci in range(8):
                    p.t(lambda t, ci=ci, st4=st4: t.matmul(
                        pC[0:128, 0:NE1], xnf_t[:, ci, 128 * st4:128 * (st4 + 1)], wr_sb[:, ci, :],
                        start=(ci == 0), stop=(ci == 7)), wait=(ev_xn, ld_wr))
                mlg = p.mark("t")
                sig = p.s(lambda s: s.activation(sig_t[:], pC[0:128, 0:NE1], AF.Sigmoid), wait=(mlg,), inc=True)
                cp = p.v(lambda v: v.tensor_copy(probs[:], pC[0:128, 0:NE1]), wait=(mlg,), inc=True)
                p.v(lambda v: v.memset(sel2[:], 0.0))
                ek = p.v(lambda v: v.engine_nop(), inc=True)
                for kk in range(2):
                    rr1 = p.v(lambda v: v.reduce_max(mx[:], probs[:], axis=AX.X), wait=(ek,), inc=True)
                    p.v(lambda v: v.tensor_scalar(eq_t[:], probs[:], mx[:], None, ALU.is_ge), wait=(rr1,))
                    p.v(lambda v: v.tensor_tensor(tmq[:], eq_t[:], rev16[:], ALU.mult), wait=(ld_c,))
                    rr2 = p.v(lambda v: v.reduce_max(am[:], tmq[:], axis=AX.X), inc=True)
                    p.v(lambda v: v.tensor_scalar(eq_t[:], tmq[:], am[:], None, ALU.is_ge), wait=(rr2,))
                    p.v(lambda v: v.tensor_tensor(sel2[:], sel2[:], eq_t[:], ALU.add))
                    ek = p.v(lambda v: v.scalar_tensor_tensor(probs[:], eq_t[:], -1e30, probs[:], ALU.mult, ALU.add), inc=True)
                gsel = p.v(lambda v: v.tensor_tensor(sel2[:], sel2[:], sig_t[:], ALU.mult), wait=(sig, ek), inc=True)
                trg = p.t(lambda t, tt=tt: t.transpose(pD[0:16, 0:128], sel2[:], ident[:]), wait=(gsel,), inc=True)
                p.v(lambda v, tt=tt: v.tensor_copy(gT_sb[:, tt, :], pD[0:16, 0:128]), wait=(trg,))
        ev_g = p.mark("v")
        st_g = p.y(lambda y: y.dma_start(out=g_o, in_=gT_sb[:].rearrange("p c t -> p (c t)")), wait=(ev_g,), inc=True)
        xn_all = ev_g
        ev_store = [st_g]
        ev_store.append(p.y(lambda y: y.dma_start(out=xn_o, in_=xnb[:].rearrange("p c t -> p (c t)")), wait=(xn_all,), inc=True))

        st8 = {"prev_pe": None, "prev_store": {0: None, 1: None}}

        def do_proj(w_sb, stage, out_hbm, ld_w, scol, sfac):
            prev_pe = st8["prev_pe"]
            prev_store = st8["prev_store"]
            for co in range(8):
                sc1 = p.v(lambda v, co=co, sfac=sfac, scol=scol: v.tensor_scalar(
                    scol[0:64, :], sqk_sb[:, 2 * co:2 * co + 1], sfac, None, ALU.mult), wait=(ld_sqk,), inc=True)
                sc2 = p.v(lambda v, co=co, sfac=sfac, scol=scol: v.tensor_scalar(
                    scol[64:128, :], sqk_sb[:, 2 * co + 1:2 * co + 2], sfac, None, ALU.mult), inc=True)
                for tc in range(4):
                    ts = slice(512 * tc, 512 * (tc + 1))
                    buf = (co * 4 + tc) % 2
                    for ci in range(8):
                        p.t(lambda t, ci=ci, co=co, ts=ts, w_sb=w_sb: t.matmul(
                            pE[:], w_sb[:, ci, 128 * co:128 * (co + 1)], xnb[:, ci, ts],
                            start=(ci == 0), stop=(ci == 7)),
                            wait=(xn_all, ld_w) + ((prev_pe,) if prev_pe else ()))
                    mm = p.mark("t")
                    c1 = p.v(lambda v: v.tensor_copy(qt[:], pE[:]), wait=(mm, ld_cos, ld_sin), inc=True)
                    prev_pe = c1
                    p.v(lambda v: v.stream_shuffle(qs[:], qt[:], mask=SWAP))
                    p.v(lambda v, ts=ts: v.tensor_tensor(qt[:], qt[:], cos_sb[:, ts], ALU.mult))
                    p.v(lambda v, ts=ts: v.tensor_tensor(qs[:], qs[:], sin_sb[:, ts], ALU.mult))
                    rr = p.v(lambda v: v.tensor_tensor(qt[:], qt[:], qs[:], ALU.add), inc=True)
                    sqe = p.s(lambda s: s.activation(sqq[:], qro[:], AF.Square), wait=(rr,), inc=True)
                    for hh in range(2):
                        pX = pA if hh == 0 else pB
                        mqn = p.t(lambda t, hh=hh, pX=pX: t.matmul(
                            pX[0:1, :], ones_col[64 * hh:64 * (hh + 1), :], sqq[64 * hh:64 * (hh + 1), :],
                            start=True, stop=True), wait=(sqe,), inc=True)
                        s1e = p.s(lambda s, hh=hh, pX=pX: s.activation(rn_f[0:1, hh, :], pX[0:1, :], AF.Sqrt), wait=(mqn,), inc=True)
                        p.v(lambda v, hh=hh: v.tensor_scalar(rn_f[0:1, hh, :], rn_f[0:1, hh, :], 1e-12, None, ALU.max), wait=(s1e,))
                        p.v(lambda v, hh=hh: v.reciprocal(rn_f[0:1, hh, :], rn_f[0:1, hh, :]))
                        p.v(lambda v, hh=hh: v.tensor_copy(rn_rows[0:1, hh, :], rn_f[0:1, hh, :]))
                    evr = p.mark("v")
                    bb1 = p.t(lambda t: t.matmul(pF[0:64, :], cs["c_ones_rowbf"][0:1, 0:64], rn_rows[0:1, 0, :],
                                                 start=True, stop=True, tile_position=(0, 0)), wait=(evr,), inc=True)
                    bb2 = p.t(lambda t: t.matmul(pF[64:128, :], cs["c_ones_rowbf"][0:1, 0:64], rn_rows[0:1, 1, :],
                                                 start=True, stop=True, tile_position=(0, 64), skip_group_check=True), inc=True)
                    p.v(lambda v: v.tensor_tensor(qro[:], qro[:], pF[:], ALU.mult), wait=(bb2, sc2))
                    fin = p.v(lambda v, buf=buf, stage=stage, scol=scol: v.tensor_scalar(
                        stage[:, buf, :], qro[:], scol[:], None, ALU.mult),
                        wait=((prev_store[buf],) if prev_store[buf] else ()), inc=True)
                    prev_store[buf] = p.y(lambda y, buf=buf, co=co, ts=ts, stage=stage, out_hbm=out_hbm:
                        y.dma_start(out=out_hbm.rearrange("p (c t) -> p c t", c=8)[:, co, ts],
                                    in_=stage[:, buf, :]), wait=(fin,), inc=True)
            st8["prev_pe"] = prev_pe
            st8["prev_store"] = prev_store
            return (prev_store[0], prev_store[1])

        k_done = do_proj(wk_sb, kstage, k_o, ld_wk, kscol, 32.0)
        ld_wv = _ld_fm(p, wv_h, wk_sb, cast=True, wait=(p.mark("t"),))
        prev_pf = p.mark("v")
        prev_vst = {0: None, 1: None}
        for tt in range(16):
            buf = tt % 2
            ev_half = []
            for half in range(2):
                for ci in range(8):
                    p.t(lambda t, tt=tt, half=half, ci=ci: t.matmul(
                        pF[:], xnb[:, ci, 128 * tt:128 * (tt + 1)], wk_sb[:, ci, 512 * half:512 * (half + 1)],
                        start=(ci == 0), stop=(ci == 7)), wait=(ld_wv,) + ((prev_pf,) if prev_pf else ()))
                mm = p.mark("t")
                ev_half.append(p.v(lambda v, buf=buf, half=half: v.tensor_copy(
                    vstage[:, buf, 512 * half:512 * (half + 1)], pF[:]),
                    wait=(mm,) + ((prev_vst[buf],) if prev_vst[buf] else ()), inc=True))
                prev_pf = ev_half[-1]
            ev_st = p.y(lambda y, tt=tt, buf=buf: y.dma_start(
                out=v_o.rearrange("p (c t) -> p c t", c=16)[:, tt, :], in_=vstage[:, buf, :]),
                wait=(ev_half[-1],), inc=True)
            prev_vst[buf] = ev_st
        ev_store.append(ev_st)
        v_done = (prev_vst[0], prev_vst[1])
        do_proj(wq_sb, qstage, q_o, ld_wq, qscol, 256.0)
        return {"k": k_done, "v": v_done}




def emit_attn1(nc, p, io, cs, ld_c):
    q_h = io["q1"]
    k_h = io["k1full"]
    v_h = io["v1full"]
    wo_h = io["wo_t"]

    s1_o = io["s1"]

    ctx = ExitStack()
    _pfx = "ph%d_" % _PHASE_N[0]
    _PHASE_N[0] += 1
    S = lambda n, sh, dt=F32: ctx.enter_context(nc.sbuf_tensor(_pfx + n, sh, dt))
    PS = lambda n, sh, dt=F32: ctx.enter_context(nc.psum_tensor(_pfx + n, sh, dt))
    with ctx:
        qfm = S("qfm", [128, 8, T], BF16)
        kbuf = S("kbuf", [128, 2, SEQ], BF16)
        vbuf = S("vbuf", [128, 2, 64 * 128], BF16)
        e_sb = S("e_sb", [128, 2, 2, 1536], BF16)
        y1 = S("y1", [128, 8, T], BF16)
        wo_sb = S("wo_sb", [128, 8, D], BF16)
        sstage = S("sstage", [128, 2, 512])
        Mt = S("Mt", [128, 2, 512])
        Mt2 = S("Mt2", [128, 2, 1024])
        mcols = S("mcols", [128, 8])
        negm = S("negm", [1, 2, 512], BF16)
        rz = S("rz", [1, 2, 512], BF16)
        rzf = S("rzf", [1, 2, 512])

        pAll = PS("pAll", [128, 4096])

        def pa1(j):
            return pAll[:, 512 * j:512 * (j + 1)]

        def pb1(j):
            return pAll[:, 2048 + 512 * j:2048 + 512 * (j + 1)]

        pSa = pAll[:, 0:1536]
        pSb = pAll[:, 1536:3072]
        pAV = pAll[:, 3072:3584]
        pZ = pAll[:, 3584:4096]

        ones_row = cs["c_ones_1x128"]
        ones_rowbf = cs["c_ones_rowbf"]
        ones_colbf = cs["c_ones_colbf"]
        ident = cs["c_ident_f32"]

        ld_q = p.y(lambda y: y.dma_start(out=qfm[:], in_=q_h.rearrange("p (c t) -> p c t", c=8)), inc=True)
        ld_wo = _ld_fm(p, wo_h, wo_sb, cast=True)

        NG = 22
        GS = [min(3, 64 - 3 * g) for g in range(NG)]
        ld_k = {}
        ld_v = {}
        for hp in range(2):
            ld_k[hp] = p.y(lambda y, hp=hp: y.dma_start(
                out=kbuf[:, hp % 2, :].rearrange("p (r t) -> p r t", r=4), in_=k_h[:, hp]), inc=True)
            ld_v[hp] = p.g(lambda g, hp=hp: g.dma_start(
                out=vbuf[:, hp % 2, :], in_=v_h[hp]), inc=True, dma=True)

        prev_evac = None
        prev_c1 = None
        prev_expa_l = None
        prev_exp_l = None
        pend = [None]
        for hp in range(8):
            kb = hp % 2
            ld_k_ev = ld_k[hp]
            ld_v_ev = ld_v[hp]
            for q4 in range(4):
                qsl = slice(512 * q4, 512 * (q4 + 1))
                ev_mp = {(0, 0): None, (0, 1): None, (1, 0): None, (1, 1): None}
                for g2 in range(32):
                    par = g2 % 2
                    if g2 == 1 and pend[0] is not None:
                        prev_evac, prev_c1 = pend[0]()
                        pend[0] = None
                    for j in range(2):
                        ks = 2 * g2 + j
                        if g2 == 0:
                            wA = ((ld_k_ev, ld_q) + ((prev_expa_l,) if prev_expa_l else ())) if j == 0 else ()
                            wB = ((prev_exp_l,) if prev_exp_l else ()) if j == 0 else ()
                        elif g2 == 1:
                            wA = ()
                            wB = ((prev_c1,) if (j == 0 and prev_c1) else ()) + \
                                 ((prev_evac,) if (j == 1 and prev_evac) else ())
                        else:
                            wA = (ev_mp[(0, par)],) if (j == 0 and ev_mp[(0, par)]) else ()
                            wB = (ev_mp[(1, par)],) if (j == 0 and ev_mp[(1, par)]) else ()
                        p.t(lambda t, ks=ks, j=j, kb=kb, hp=hp, qsl=qsl, par=par: t.matmul(
                            pa1(2 * par + j), kbuf[0:64, kb, 128 * ks:128 * (ks + 1)],
                            qfm[0:64, hp, qsl], start=True, stop=True, tile_position=(0, 0)), wait=wA)
                        p.t(lambda t, ks=ks, j=j, kb=kb, hp=hp, qsl=qsl, par=par: t.matmul(
                            pb1(2 * par + j), kbuf[64:128, kb, 128 * ks:128 * (ks + 1)],
                            qfm[64:128, hp, qsl], start=True, stop=True, tile_position=(64, 0)), wait=wB)
                    mg2 = p.mark("t")
                    for hh in range(2):
                        base = 1024 * par if hh == 0 else 2048 + 1024 * par
                        psl = pAll[:, base:base + 1024]
                        if g2 == 0:
                            p.v(lambda v, hh=hh, psl=psl: v.tensor_copy(Mt2[:, hh, :], psl), wait=(mg2,))
                        else:
                            p.v(lambda v, hh=hh, psl=psl: v.tensor_tensor(
                                Mt2[:, hh, :], Mt2[:, hh, :], psl, ALU.max), wait=(mg2,))
                        ev_mp[(hh, par)] = p.mark("v")
                p.v(lambda v: v.tensor_tensor(Mt[:, 0, :], Mt2[:, 0, 0:512], Mt2[:, 0, 512:1024], ALU.max))
                ev_mfa = p.mark("v")
                p.v(lambda v: v.tensor_tensor(Mt[:, 1, :], Mt2[:, 1, 0:512], Mt2[:, 1, 512:1024], ALU.max))
                ev_mfb = p.mark("v")
                w0 = (ev_mp[(0, 0)], ev_mp[(0, 1)], ev_mp[(1, 0)])
                for j in range(GS[0]):
                    p.t(lambda t, j=j, kb=kb, hp=hp, qsl=qsl: t.matmul(
                        pSa[:, 512 * j:512 * (j + 1)], kbuf[0:64, kb, 128 * j:128 * (j + 1)],
                        qfm[0:64, hp, qsl], start=True, stop=False, tile_position=(0, 0)),
                        wait=w0 if j == 0 else ())
                    p.t(lambda t, j=j, kb=kb, hp=hp, qsl=qsl: t.matmul(
                        pSb[:, 512 * j:512 * (j + 1)], kbuf[64:128, kb, 128 * j:128 * (j + 1)],
                        qfm[64:128, hp, qsl], start=True, stop=False, tile_position=(64, 0)))
                tr1 = None
                for c in range(8):
                    hh, j = c // 4, c % 4
                    w = (ev_mfa, ev_mfb) if c == 0 else ()
                    tr1 = p.t(lambda t, c=c, hh=hh, j=j: t.transpose(
                        pAll[:, 3072 + 128 * c:3072 + 128 * (c + 1)], Mt[:, hh, 128 * j:128 * (j + 1)], ident[:]),
                        wait=w, inc=(c == 7))
                rd = None
                for c in range(8):
                    rd = p.v(lambda v, c=c: v.reduce_max(mcols[:, c:c + 1],
                                                         pAll[:, 3072 + 128 * c:3072 + 128 * (c + 1)], axis=AX.X),
                             wait=(tr1,) if c == 0 else (), inc=(c == 7))
                tr2 = None
                for c in range(8):
                    tr2 = p.t(lambda t, c=c: t.transpose(
                        pAll[0:1, 3072 + 128 * c:3072 + 128 * (c + 1)], mcols[:, c:c + 1], ident[:]),
                        wait=(rd,) if c == 0 else (), inc=(c == 7))
                ev_negm = p.v(lambda v: v.tensor_scalar(negm[:].rearrange("o h t -> o (h t)"),
                                                        pAll[0:1, 3072:4096], -1.0, None, ALU.mult),
                              wait=(tr2,), inc=True)
                ev_ea = {}
                ev_eb = {}
                ev_av = [None] * NG
                for g in range(NG):
                    gb = g % 2
                    qa0 = [None] * GS[g]
                    qb0 = [None] * GS[g]
                    if g == 0:
                        for j in range(GS[0]):
                            p.t(lambda t, j=j: t.matmul(
                                pSa[:, 512 * j:512 * (j + 1)], ones_rowbf[:], negm[0:1, 0, :],
                                start=False, stop=True), wait=(ev_negm,) if j == 0 else ())
                            qa0[j] = p.mark("t")
                            p.t(lambda t, j=j: t.matmul(
                                pSb[:, 512 * j:512 * (j + 1)], ones_rowbf[:], negm[0:1, 1, :],
                                start=False, stop=True))
                            qb0[j] = p.mark("t")
                    else:
                        for j in range(GS[g]):
                            ks = 3 * g + j
                            p.t(lambda t, ks=ks, j=j, kb=kb, hp=hp, qsl=qsl: t.matmul(
                                pSa[:, 512 * j:512 * (j + 1)], kbuf[0:64, kb, 128 * ks:128 * (ks + 1)],
                                qfm[0:64, hp, qsl], start=True, stop=False, tile_position=(0, 0)),
                                wait=(ev_ea[(g - 1, j)],) if j < GS[g - 1] else ())
                            p.t(lambda t, j=j: t.matmul(
                                pSa[:, 512 * j:512 * (j + 1)], ones_rowbf[:], negm[0:1, 0, :],
                                start=False, stop=True))
                            qa0[j] = p.mark("t")
                            p.t(lambda t, ks=ks, j=j, kb=kb, hp=hp, qsl=qsl: t.matmul(
                                pSb[:, 512 * j:512 * (j + 1)], kbuf[64:128, kb, 128 * ks:128 * (ks + 1)],
                                qfm[64:128, hp, qsl], start=True, stop=False, tile_position=(64, 0)),
                                wait=(ev_eb[(g - 1, j)],) if j < GS[g - 1] else ())
                            p.t(lambda t, j=j: t.matmul(
                                pSb[:, 512 * j:512 * (j + 1)], ones_rowbf[:], negm[0:1, 1, :],
                                start=False, stop=True))
                            qb0[j] = p.mark("t")
                    for j in range(GS[g]):
                        w_exp = (qa0[j],)
                        if g >= 2 and j == 0:
                            w_exp = w_exp + ((ev_av[g - 2],) if ev_av[g - 2] else ())
                        ev_ea[(g, j)] = p.s(lambda s, gb=gb, j=j: s.activation(
                            e_sb[:, gb, 0, 512 * j:512 * (j + 1)], pSa[:, 512 * j:512 * (j + 1)],
                            AF.Exp), wait=w_exp, inc=True)
                        ev_eb[(g, j)] = p.s(lambda s, gb=gb, j=j: s.activation(
                            e_sb[:, gb, 1, 512 * j:512 * (j + 1)], pSb[:, 512 * j:512 * (j + 1)],
                            AF.Exp), wait=(qb0[j],), inc=True)
                    for j in range(GS[g]):
                        ks = 3 * g + j
                        first = (g == 0 and j == 0)
                        last = (g == NG - 1 and j == GS[g] - 1)
                        p.t(lambda t, ks=ks, j=j, gb=gb, kb=kb, first=first, last=last: t.matmul(
                            pAV[0:64, :], vbuf[:, kb, :].rearrange("p (tt d) -> p tt d", d=128)[:, ks, 0:64],
                            e_sb[:, gb, 0, 512 * j:512 * (j + 1)],
                            start=first, stop=last, tile_position=(0, 0)),
                            wait=(ev_ea[(g, j)],) + ((ld_v_ev,) if j == 0 else ()))
                        p.t(lambda t, ks=ks, j=j, gb=gb, kb=kb, first=first, last=last: t.matmul(
                            pAV[64:128, :], vbuf[:, kb, :].rearrange("p (tt d) -> p tt d", d=128)[:, ks, 64:128],
                            e_sb[:, gb, 1, 512 * j:512 * (j + 1)],
                            start=first, stop=last, tile_position=(0, 64), skip_group_check=True),
                            wait=(ev_eb[(g, j)],))
                        p.t(lambda t, j=j, gb=gb, first=first, last=last: t.matmul(
                            pZ[0:1, :], ones_colbf[:], e_sb[:, gb, 0, 512 * j:512 * (j + 1)],
                            start=first, stop=last, tile_position=(0, 0)))
                        p.t(lambda t, j=j, gb=gb, first=first, last=last: t.matmul(
                            pZ[64:65, :], ones_colbf[:], e_sb[:, gb, 1, 512 * j:512 * (j + 1)],
                            start=first, stop=last, tile_position=(0, 64), skip_group_check=True))
                    ev_av[g] = p.mark("t")
                sb2 = q4 % 2
                c1 = p.v(lambda v, sb2=sb2: v.tensor_copy(sstage[:, sb2, :], pAV[:]), wait=(ev_av[NG - 1],), inc=True)
                p.v(lambda v: v.reciprocal(rzf[0:1, 0, :], pZ[0:1, :]))
                p.v(lambda v: v.reciprocal(rzf[0:1, 1, :], pZ[64:65, :]))
                rcast = p.v(lambda v: v.tensor_copy(rz[:].rearrange("o h t -> o (h t)"), rzf[:].rearrange("o h t -> o (h t)")), inc=True)
                prev_expa_l = ev_ea[(NG - 1, 0)]
                prev_exp_l = ev_eb[(NG - 1, 0)]

                def mk_pend(sb2=sb2, hp=hp, qsl=qsl, rcast=rcast, c1=c1):
                    def emit():
                        p.t(lambda t: t.matmul(pZ[0:64, :], ones_rowbf[0:1, 0:64], rz[0:1, 0, :],
                                               start=True, stop=True, tile_position=(0, 0)),
                            wait=(rcast,))
                        b2 = p.t(lambda t: t.matmul(pZ[64:128, :], ones_rowbf[0:1, 0:64], rz[0:1, 1, :],
                                                    start=True, stop=True, tile_position=(0, 64),
                                                    skip_group_check=True), inc=True)
                        ev = p.v(lambda v: v.tensor_tensor(
                            y1[:, hp, qsl], sstage[:, sb2, :], pZ[:, 0:512], ALU.mult), wait=(b2,), inc=True)
                        return ev, c1
                    return emit
                pend[0] = mk_pend()
            if hp + 2 <= 7:
                nhp = hp + 2
                lastuse = p.mark("t")
                ld_k[nhp] = p.y(lambda y, nhp=nhp: y.dma_start(
                    out=kbuf[:, nhp % 2, :].rearrange("p (r t) -> p r t", r=4), in_=k_h[:, nhp]),
                    wait=(lastuse,), inc=True)
                ld_v[nhp] = p.g(lambda g, nhp=nhp: g.dma_start(
                    out=vbuf[:, nhp % 2, :], in_=v_h[nhp]),
                    wait=(lastuse,), inc=True, dma=True)
        ev_y1, _ = pend[0]()
        pend[0] = None

        prev_pe = None
        prev_sst = {0: None, 1: None}
        for co in range(8):
            for tc in range(4):
                ts = slice(512 * tc, 512 * (tc + 1))
                sb2 = (co * 4 + tc) % 2
                for ci in range(8):
                    p.t(lambda t, ci=ci, co=co, ts=ts: t.matmul(
                        pAll[:, 0:512], wo_sb[:, ci, 128 * co:128 * (co + 1)], y1[:, ci, ts],
                        start=(ci == 0), stop=(ci == 7)),
                        wait=(ev_y1, ld_wo) + ((prev_pe,) if prev_pe else ()))
                mm = p.mark("t")
                prev_pe = p.v(lambda v, sb2=sb2: v.tensor_copy(sstage[:, sb2, :], pAll[:, 0:512]),
                              wait=(mm,) + ((prev_sst[sb2],) if prev_sst[sb2] else ()), inc=True)
                prev_sst[sb2] = p.y(lambda y, co=co, ts=ts, sb2=sb2: y.dma_start(
                    out=s1_o.rearrange("p (c t) -> p c t", c=8)[:, co, ts], in_=sstage[:, sb2, :]),
                    wait=(prev_pe,), inc=True)




def emit_ffn1(nc, p, io, cs, ld_c):
    x_h = io["x1"]
    xn_h = io["x1n"]
    s1_h = io["s1"]
    wmh_h = io["wmh_t"]
    wmg_h = io["wmg_t"]
    kf_h = io["kffwd_r"]
    vf_h = io["vffwd"]
    g_h = io["g1d"]

    out_o = io["outp"]

    ctx = ExitStack()
    _pfx = "ph%d_" % _PHASE_N[0]
    _PHASE_N[0] += 1
    S = lambda n, sh, dt=F32: ctx.enter_context(nc.sbuf_tensor(_pfx + n, sh, dt))
    PS = lambda n, sh, dt=F32: ctx.enter_context(nc.psum_tensor(_pfx + n, sh, dt))
    with ctx:
        xnb = S("xnb", [128, 8, T], BF16)
        qm = S("qm", [128, 8, T], BF16)
        y_fm = S("y_fm", [128, 8, T], BF16)
        w_sb = S("w_sb", [128, 8, D], BF16)
        kv_k = S("kv_k", [128, 2, 16, S1], BF16)
        kv_v = S("kv_v", [128, 2, 16, 2, D1], BF16)
        g_bc = S("g_bc", [128, 16, 512], BF16)
        g_row16 = S("g_row16", [1, 16, 512], BF16)
        gel = S("gel", [128, 2, 512], BF16)
        stg = S("stg", [128, 2, 512])
        x1st = S("x1st", [128, T])
        t1f = S("t1f", [128, T])

        pAll = PS("pAll", [128, 4096])

        ones_row = cs["c_ones_1x128"]

        ld_xn = p.y(lambda y: y.dma_start(out=xnb[:], in_=xn_h.rearrange("p (c t) -> p c t", c=8)), inc=True)
        ld_w = _ld_fm(p, wmh_h, w_sb, cast=True)

        prev_pe = {0: None, 1: None}
        idx = 0
        for co in range(8):
            for tc in range(4):
                ts = slice(512 * tc, 512 * (tc + 1))
                bk = idx % 2
                base = 2048 + 512 * bk
                for ci in range(8):
                    w = ((ld_xn, ld_w) if idx == 0 and ci == 0 else ()) +                         ((prev_pe[bk],) if (ci == 0 and prev_pe[bk]) else ())
                    p.t(lambda t, ci=ci, co=co, ts=ts, base=base: t.matmul(
                        pAll[:, base:base + 512], w_sb[:, ci, 128 * co:128 * (co + 1)], xnb[:, ci, ts],
                        start=(ci == 0), stop=(ci == 7)), wait=w)
                mm = p.mark("t")
                prev_pe[bk] = p.v(lambda v, co=co, ts=ts, base=base: v.tensor_copy(
                    qm[:, co, ts], pAll[:, base:base + 512]), wait=(mm,), inc=True)
                idx += 1
        ev_qm = p.mark("v")

        prev_kv = {0: None, 1: None}
        prev_y = None
        prev_bg = None
        gcp = {2560: ev_qm, 3584: ev_qm}
        for tc in range(4):
            ts = slice(512 * tc, 512 * (tc + 1))
            lgs = [None] * NE1
            for e in range(NE1):
                lgs[e] = p.g(lambda g, e=e, ts=ts: g.dma_start(out=g_row16[0:1, e, :], in_=g_h[e:e + 1, ts]),
                             wait=(prev_bg,) if (e == 0 and prev_bg) else (), inc=True, dma=True)
            for e in range(NE1):
                base = 3584 if e % 2 == 0 else 2560
                bg = p.t(lambda t, e=e, base=base: t.matmul(
                    pAll[:, base:base + 512], cs["c_ones_rowbf"][:], g_row16[0:1, e, :],
                    start=True, stop=True), wait=(lgs[e],) + ((gcp[base],) if gcp[base] else ()), inc=True)
                gcp[base] = p.v(lambda v, e=e, base=base: v.tensor_copy(g_bc[:, e, :], pAll[:, base:base + 512]),
                                wait=(bg,), inc=True)
                if e == NE1 - 1:
                    prev_bg = bg
            ev_gbc = p.mark("v")
            for h in range(H1):
                hb = h % 2
                lk = p.g(lambda g, h=h, hb=hb: g.dma_start(
                    out=kv_k[:, hb], in_=kf_h[h].rearrange("e p s -> p e s")),
                    wait=(prev_kv[hb],) if prev_kv[hb] else (), inc=True, dma=True)
                lv = p.g(lambda g, h=h, hb=hb: g.dma_start(
                    out=kv_v[:, hb], in_=vf_h[h].rearrange("e (sc p) d -> p e sc d", p=128)),
                    inc=True, dma=True)
                ev_exp = {}
                for e in range(NE1):
                    sbk = e % 2
                    for sc in range(2):
                        w = (lk, ev_gbc) if (e == 0 and sc == 0) else ()
                        if e >= 2 and sc == 0:
                            w = w + (ev_exp[e - 2],)
                        p.t(lambda t, e=e, sc=sc, h=h, hb=hb, ts=ts, sbk=sbk: t.matmul(
                            pAll[:, 1024 * sbk + 512 * sc:1024 * sbk + 512 * (sc + 1)],
                            kv_k[:, hb, e, 128 * sc:128 * (sc + 1)], qm[:, h, ts],
                            start=True, stop=True), wait=w)
                    msc = p.mark("t")
                    ge1 = p.s(lambda s, sbk=sbk: s.activation(
                        gel[:, 0, :], pAll[:, 1024 * sbk:1024 * sbk + 512], AF.Gelu), wait=(msc,), inc=True)
                    ge2 = p.s(lambda s, sbk=sbk: s.activation(
                        gel[:, 1, :], pAll[:, 1024 * sbk + 512:1024 * sbk + 1024], AF.Gelu), inc=True)
                    p.v(lambda v, e=e: v.tensor_tensor(gel[:, 0, :], gel[:, 0, :], g_bc[:, e, :], ALU.mult), wait=(ge1,))
                    ev_exp[e] = p.v(lambda v, e=e: v.tensor_tensor(gel[:, 1, :], gel[:, 1, :], g_bc[:, e, :], ALU.mult), wait=(ge2,), inc=True)
                    for sc in range(2):
                        first = (e == 0 and sc == 0)
                        last = (e == NE1 - 1 and sc == 1)
                        w = ((ev_exp[e],) if sc == 0 else ()) + ((lv, prev_y) if (first and prev_y) else ((lv,) if first else ()))
                        p.t(lambda t, e=e, sc=sc, hb=hb, first=first, last=last: t.matmul(
                            pAll[:, 3072:3584], kv_v[:, hb, e, sc, :], gel[:, sc, :],
                            start=first, stop=last), wait=w)
                my = p.mark("t")
                prev_y = p.v(lambda v, h=h, ts=ts: v.tensor_copy(y_fm[:, h, ts], pAll[:, 3072:3584]), wait=(my,), inc=True)
                prev_kv[hb] = my
        ev_yfm = prev_y

        ld_wg = _ld_fm(p, wmg_h, w_sb, cast=True, wait=(p.mark("t"),))
        prev_st = None
        prev_sil = {0: None, 1: None}
        prev_ost = {0: None, 1: None}
        idx = 0
        for co in range(8):
            for tc in range(4):
                ts = slice(512 * tc, 512 * (tc + 1))
                sb2 = idx % 2
                base = 512 * sb2
                for ci in range(8):
                    w = ((ev_yfm, ld_wg) if idx == 0 and ci == 0 else ()) +                         ((prev_sil[sb2],) if (ci == 0 and prev_sil[sb2]) else ())
                    p.t(lambda t, ci=ci, co=co, ts=ts, base=base: t.matmul(
                        pAll[:, base:base + 512], w_sb[:, ci, 128 * co:128 * (co + 1)], y_fm[:, ci, ts],
                        start=(ci == 0), stop=(ci == 7)), wait=w)
                mm = p.mark("t")
                lx = p.y(lambda y, co=co, ts=ts: y.dma_start(
                    out=t1f[:, 0:512], in_=x_h.rearrange("p (c t) -> p c t", c=8)[:, co, ts]),
                    wait=(prev_st,) if prev_st else (), inc=True)
                ls = p.y(lambda y, co=co, ts=ts: y.dma_start(
                    out=x1st[:, 0:512], in_=s1_h.rearrange("p (c t) -> p c t", c=8)[:, co, ts]), inc=True)
                gg = p.s(lambda s, sb2=sb2, base=base: s.activation(stg[:, sb2, :], pAll[:, base:base + 512], AF.Silu),
                         wait=(mm,) + ((prev_ost[sb2],) if prev_ost[sb2] else ()), inc=True)
                prev_sil[sb2] = gg
                p.v(lambda v, sb2=sb2: v.tensor_tensor(stg[:, sb2, :], stg[:, sb2, :], x1st[:, 0:512], ALU.mult), wait=(gg, ls))
                p.v(lambda v, sb2=sb2: v.tensor_tensor(stg[:, sb2, :], stg[:, sb2, :], t1f[:, 0:512], ALU.add), wait=(lx,))
                ev_o = p.mark("v")
                prev_st = p.y(lambda y, co=co, ts=ts, sb2=sb2: y.dma_start(
                    out=out_o.rearrange("p (c t) -> p c t", c=8)[:, co, ts], in_=stg[:, sb2, :]),
                    wait=(ev_o,), inc=True)
                prev_ost[sb2] = prev_st
                idx += 1




ALL_CONSTS = ["c_ones_1x128", "c_ones_col", "c_ltriu", "c_rev64",
              "c_ident_f32", "c_rev16", "c_ones_colbf", "c_ones_rowbf"]


def build_fused(upto=99, dbg=False):
    nc = _nc()
    EI = lambda n, sh, dt=F32: nc.dram_tensor(n, sh, dt, kind="ExternalInput").ap()
    IN = lambda n, sh, dt=F32: nc.dram_tensor(n, sh, dt, kind="Internal").ap()

    x_fm = EI("x_fm", [D, T])
    wq_t = EI("wq_t", [D, D]); wk_t = EI("wk_t", [D, D]); wv_t = EI("wv_t", [D, D])
    gate_w = EI("gate_w", [D, NE0])
    core_mask = EI("core_mask", [8, 1])
    sgu_pairs = EI("sgu_pairs", [D, 16 * 256]); sdown = EI("sdown", [2 * D, D])
    sgate = EI("sgate", [128, 8])
    egu = EI("egu", [NE0, D, D]); edw = EI("edw", [NE0, CAP0, D])
    wq1_t = EI("wq1_t", [D, D]); wk1_t = EI("wk1_t", [D, D]); wv1_t = EI("wv1_t", [D, D])
    wo_t = EI("wo_t", [D, D]); sqk_t = EI("sqk_t", [64, NH])
    wr1_t = EI("wr1_t", [D, NE1])
    rope_cos = EI("rope_cos", [128, T]); rope_sin = EI("rope_sin", [128, T])
    wmh_t = EI("wmh_t", [D, D]); wmg_t = EI("wmg_t", [D, D])
    kffwd_r = EI("kffwd_r", [H1, NE1, D1, S1]); vffwd = EI("vffwd", [H1, NE1, S1, D1])

    out_o = nc.dram_tensor("outp", [128, 8 * T], F32, kind="ExternalOutput").ap()

    DI = (lambda n, sh, dt=F32: nc.dram_tensor(n, sh, dt, kind="ExternalOutput").ap()) if dbg else IN
    xn0 = DI("xn0", [128, 8 * T], BF16)
    q0 = DI("q0", [128, 8 * T], BF16)
    k0 = IN("k0", [128, 8 * T], BF16)
    v0 = IN("v0", [128, 16 * D], BF16)
    qn2 = DI("qn2", [NH, T])
    kmax2 = IN("kmax2", [1, NH])
    wd = DI("wd", [128, 16 * NE0])
    pos = DI("pos", [128, 16 * NE0])
    cnt = IN("cnt", [1, NE0])
    s0 = DI("s0", [128, 8 * T])
    x1 = DI("x1", [128, 8 * T])
    x1n = DI("x1n", [128, 8 * T], BF16)
    q1 = DI("q1", [128, 8 * T], BF16)
    k1 = IN("k1", [128, 8 * T], BF16)
    v1 = IN("v1", [128, 16 * D], BF16)
    g1d = DI("g1d", [NE1, T])
    s1 = DI("s1", [128, 8 * T])
    k0_ag = IN("k0_ag", [4 * 128, 8 * T], BF16)
    v0_ag = IN("v0_ag", [4 * 128, 16 * D], BF16)
    km_ag = IN("km_ag", [4, NH])
    cnt_ag = IN("cnt_ag", [8, NE0])
    v0_hp = IN("v0_hp", [8, 128, 64 * 128], BF16)
    k1_ag = IN("k1_ag", [4 * 128, 8 * T], BF16)
    v1_ag = IN("v1_ag", [4 * 128, 16 * D], BF16)
    v1_hp = IN("v1_hp", [8, 128, 64 * 128], BF16)

    GROUPS4 = [[0, 1, 2, 3], [4, 5, 6, 7]]
    GROUPS8 = [list(range(8))]

    ctx = ExitStack()
    with ctx:
        p = Prog(nc)
        S = lambda n, sh, dt=F32: ctx.enter_context(nc.sbuf_tensor(n, sh, dt))
        cs = {}
        ld_c = None
        for n in ALL_CONSTS:
            sh, dt = CONST_SHAPES[n]
            ap = nc.dram_tensor(n, sh, dt, kind="ExternalInput").ap()
            tl = S(n + "_sb", list(sh), dt)
            cs[n] = tl
            ld_c = p.y(lambda y, tl=tl, ap=ap: y.dma_start(out=tl[:], in_=ap), inc=True)

        kv0_done = emit_pre0(nc, p, {"x_fm": x_fm, "wq_t": wq_t, "wk_t": wk_t, "wv_t": wv_t,
                          "gate_w": gate_w, "xn0": xn0, "q0": q0, "k0": k0, "v0": v0,
                          "qn2": qn2, "kmax2": kmax2, "wd": wd, "pos": pos, "cnt": cnt},
                  cs, ld_c)
        if upto < 2:
            p.barrier()
            p.final_wait()
            p.emit()
            return nc
        p.g(lambda g: g.collective_compute("AllGather", ALU.bypass, replica_groups=GROUPS8,
                                           ins=[cnt.opt()], outs=[cnt_ag.opt()]), wait=kv0_done["cnt"])
        p.g(lambda g: g.collective_compute("AllGather", ALU.bypass, replica_groups=GROUPS4,
                                           ins=[k0.opt()], outs=[k0_ag.opt()]), wait=kv0_done["k"])
        p.g(lambda g: g.collective_compute("AllGather", ALU.bypass, replica_groups=GROUPS4,
                                           ins=[kmax2.opt()], outs=[km_ag.opt()]), wait=kv0_done["km"])
        agv0 = p.g(lambda g: g.collective_compute("AllGather", ALU.bypass, replica_groups=GROUPS4,
                                           ins=[v0.opt()], outs=[v0_ag.opt()]), wait=kv0_done["v"])
        for _h in range(8):
            for _r in range(4):
                p.g(lambda g, _h=_h, _r=_r: g.dma_start(
                    out=v0_hp.rearrange("h p (r tt x) -> h p r (tt x)", r=4, x=128)[_h][:, _r].rearrange("p (tt x) -> p tt x", x=128),
                    in_=v0_ag.rearrange("(r p) (tt h x) -> r h p tt x", p=128, h=8, x=128)[_r][_h]),
                    wait=(agv0,), inc=(_h == 7 and _r == 3), dma=True)
        p.barrier()
        if upto < 3:
            p.final_wait()
            p.emit()
            return nc
        emit_attn0(nc, p, {"q0": q0,
                           "kfull": k0_ag.rearrange("(r p) (c t) -> p c r t", p=128, c=8),
                           "vfull": v0_hp, "qn2": qn2,
                           "kmax2g": km_ag.rearrange("r h -> (r h)").rearrange("(o x) -> o x", o=1),
                           "s0": s0}, cs, ld_c)
        p.barrier()
        if upto < 4:
            p.final_wait()
            p.emit()
            return nc
        emit_moe0(nc, p, {"xn0": xn0, "x_fm": x_fm, "s0": s0, "wd": wd, "pos": pos,
                          "cnt_all": cnt_ag, "core_mask": core_mask,
                          "sgu_pairs": sgu_pairs, "sdown": sdown, "sgate": sgate,
                          "egu": egu, "edw": edw, "x1": x1}, cs, ld_c)
        p.barrier()
        if upto < 5:
            p.final_wait()
            p.emit()
            return nc
        kv_done = emit_pre1(nc, p, {"x1": x1, "wq1_t": wq1_t, "wk1_t": wk1_t, "wv1_t": wv1_t,
                          "wr1_t": wr1_t, "sqk_t": sqk_t, "rope_cos": rope_cos,
                          "rope_sin": rope_sin, "x1n": x1n, "q1": q1, "k1": k1,
                          "v1": v1, "g1d": g1d}, cs, ld_c)
        if upto < 6:
            p.barrier()
            p.final_wait()
            p.emit()
            return nc
        p.g(lambda g: g.collective_compute("AllGather", ALU.bypass, replica_groups=GROUPS4,
                                           ins=[k1.opt()], outs=[k1_ag.opt()]), wait=kv_done["k"])
        agv1 = p.g(lambda g: g.collective_compute("AllGather", ALU.bypass, replica_groups=GROUPS4,
                                           ins=[v1.opt()], outs=[v1_ag.opt()]), wait=kv_done["v"])
        for _h in range(8):
            for _r in range(4):
                p.g(lambda g, _h=_h, _r=_r: g.dma_start(
                    out=v1_hp.rearrange("h p (r tt x) -> h p r (tt x)", r=4, x=128)[_h][:, _r].rearrange("p (tt x) -> p tt x", x=128),
                    in_=v1_ag.rearrange("(r p) (tt h x) -> r h p tt x", p=128, h=8, x=128)[_r][_h]),
                    wait=(agv1,), inc=(_h == 7 and _r == 3), dma=True)
        p.barrier()
        if upto < 7:
            p.final_wait()
            p.emit()
            return nc
        emit_attn1(nc, p, {"q1": q1,
                           "k1full": k1_ag.rearrange("(r p) (c t) -> p c r t", p=128, c=8),
                           "v1full": v1_hp, "wo_t": wo_t, "s1": s1}, cs, ld_c)
        p.barrier()
        if upto < 8:
            p.final_wait()
            p.emit()
            return nc
        emit_ffn1(nc, p, {"x1": x1, "x1n": x1n, "s1": s1, "wmh_t": wmh_t, "wmg_t": wmg_t,
                          "kffwd_r": kffwd_r, "vffwd": vffwd, "g1d": g1d, "outp": out_o},
                  cs, ld_c)
        p.final_wait()
        p.emit()
    return nc


def kernel(**inputs):
    consts = _consts_np()
    x = np.asarray(inputs["L0_attn_x"], dtype=np.float32)
    wqkv = np.asarray(inputs["L0_attn_W_qkv"], dtype=np.float32)
    gate_w = np.ascontiguousarray(np.asarray(inputs["L0_ffn_gate_w"], dtype=np.float32))
    sgw = np.asarray(inputs["L0_ffn_shared_gate_w"], dtype=np.float32)
    sguw = np.asarray(inputs["L0_ffn_shared_gate_up_w"], dtype=np.float32)
    sdw = np.ascontiguousarray(np.asarray(inputs["L0_ffn_shared_down_w"], dtype=np.float32))
    eguw = np.ascontiguousarray(np.asarray(inputs["L0_ffn_expert_gate_up_w"], dtype=np.float32))
    edww = np.ascontiguousarray(np.asarray(inputs["L0_ffn_expert_down_w"], dtype=np.float32))
    wq1 = np.asarray(inputs["L1_attn_Wq"], dtype=np.float32)
    wk1 = np.asarray(inputs["L1_attn_Wk"], dtype=np.float32)
    wv1 = np.asarray(inputs["L1_attn_Wv"], dtype=np.float32)
    wo1 = np.asarray(inputs["L1_attn_Wo"], dtype=np.float32)
    sqk = np.asarray(inputs["L1_attn_s_qk"], dtype=np.float32)
    wr1 = np.asarray(inputs["L1_ffn_w_router"], dtype=np.float32)
    wmh = np.asarray(inputs["L1_ffn_w_mh"], dtype=np.float32)
    wmg = np.asarray(inputs["L1_ffn_w_mg"], dtype=np.float32)
    kff = np.asarray(inputs["L1_ffn_k_ffwd"], dtype=np.float32)
    vff = np.ascontiguousarray(np.asarray(inputs["L1_ffn_v_ffwd"], dtype=np.float32))

    x_parts = _host_split_fm(x)
    sgu_pairs = np.zeros((D, 16, 256), dtype=np.float32)
    for i in range(16):
        sgu_pairs[:, i, 0:128] = sguw[:, 128 * i:128 * (i + 1)]
        sgu_pairs[:, i, 128:256] = sguw[:, 2048 + 128 * i:2048 + 128 * (i + 1)]
    shared = {
        "wq_t": np.ascontiguousarray(wqkv[0:D].T),
        "wk_t": np.ascontiguousarray(wqkv[D:2 * D].T),
        "wv_t": np.ascontiguousarray(wqkv[2 * D:3 * D].T),
        "gate_w": gate_w,
        "sgu_pairs": np.ascontiguousarray(sgu_pairs.reshape(D, 16 * 256)),
        "sdown": sdw,
        "sgate": np.ascontiguousarray(sgw.reshape(8, 128).T),
        "egu": eguw, "edw": edww,
        "wq1_t": np.ascontiguousarray(wq1.T),
        "wk1_t": np.ascontiguousarray(wk1.T),
        "wv1_t": np.ascontiguousarray(wv1.T),
        "wo_t": np.ascontiguousarray(wo1.T),
        "sqk_t": np.ascontiguousarray(sqk.T),
        "wr1_t": np.ascontiguousarray(wr1.T),
        "wmh_t": np.ascontiguousarray(wmh.T),
        "wmg_t": np.ascontiguousarray(wmg.T),
        "kffwd_r": np.ascontiguousarray(kff.transpose(0, 1, 3, 2)),
        "vffwd": vff,
    }
    for kname in ALL_CONSTS:
        shared[kname] = consts[kname]

    ncf = build_fused()
    in_maps = []
    for c in range(NCORES):
        cos_t, sin_t = _rope_tables(c)
        m = dict(shared)
        m["x_fm"] = x_parts[c]
        m["core_mask"] = np.ascontiguousarray((np.arange(8) < c).astype(np.float32).reshape(8, 1))
        m["rope_cos"] = cos_t
        m["rope_sin"] = sin_t
        in_maps.append(m)
    res = run_bass_kernel_spmd(ncf, in_maps, core_ids=list(range(NCORES)))
    out = np.empty((2 * SEQ, D), dtype=np.float32)
    for c in range(NCORES):
        out[c * T:(c + 1) * T] = res.results[c]["outp"].reshape(128, 8, T).transpose(2, 1, 0).reshape(T, D)
    return out.reshape(2, SEQ, D)
